# version 11
# baseline (speedup 1.0000x reference)
import numpy as np
import ml_dtypes
import concourse.bass as bass
import concourse.mybir as mybir
from concourse.bass_utils import run_bass_kernel_spmd

F32 = mybir.dt.float32
BF16 = mybir.dt.bfloat16
I32 = mybir.dt.int32
AF = mybir.ActivationFunctionType
ALU = mybir.AluOpType
BF = ml_dtypes.bfloat16

T = 2048
D = 1024
G = 8
NH = 16
DH = 64
NKC = 16
NQG = 4
PIF = float(np.float32(np.pi))
TWO_PI = 2.0 * np.pi
CW1 = float(np.float32(TWO_PI)); CW2 = float(np.float32(TWO_PI - CW1))
CW3 = float(np.float32(TWO_PI - CW1 - CW2))
C_L0 = 8.0
C_L1 = 8.0
H0 = 3072
H1 = 4096

XA = 0
XB = 16384
KVO = 32768
QQ = 49152
OO = 65536
WSA = 81920
WSB = 86016
AUX = 90112
A_SIN = 90112
A_COS = 92160
A_RT = 94208
A_KVT = 90112
A_PT = 94208
A_LB = 96256
A_BC = 97280
CONST0 = 98304
ONESCOL = 98432
ONESROW = 98440
NBF = 98568

FQA = 0
FQB = 512
FQV = 1024
FQC = 1536
FQD = 2048
FST = 2560
FMS = 3072
NF32 = 3328
TB_TP = 0
TB_KT = 512
TB_TH = 1024
TB_T2 = 1536

NINT = 512


def host_constants():
    c = {}
    c["tpos"] = np.broadcast_to(np.arange(T, dtype=np.float64), (128, T)).astype(np.float32).copy()
    jmod = (np.arange(128) % 32).astype(np.float64)
    c["jarg"] = np.broadcast_to((-(jmod / 32.0)).astype(np.float32).reshape(128, 1), (128, 8)).copy()
    inv_freq = 1.0 / 10000.0 ** (np.arange(0, DH, 2, dtype=np.float64) / DH)
    theta = np.arange(T, dtype=np.float64)[None, :] * inv_freq[np.arange(128) % 32][:, None]
    c["ktab"] = np.round(theta / TWO_PI).astype(np.float32)
    c["ktabc"] = np.round((theta + np.pi / 2) / TWO_PI).astype(np.float32)
    r2t = np.zeros((128, 128), dtype=np.float32)
    for blk in range(2):
        o = 64 * blk
        for j in range(32):
            r2t[o + j + 32, o + j] = -1.0
            r2t[o + j, o + j + 32] = 1.0
    c["r2t"] = r2t.astype(BF)
    c["ident"] = np.eye(128, dtype=np.float32).astype(BF)
    c["identf32"] = np.eye(128, dtype=np.float32)
    ind2 = np.zeros((2, 128), dtype=np.float32)
    ind2[0, 0:64] = 1.0
    ind2[1, 64:128] = 1.0
    c["ind2"] = ind2
    c["ones1x128"] = np.ones((1, 128), dtype=np.float32)
    return c


def layout_lora(ins):
    return {
        "AfcT": np.ascontiguousarray(ins["L1_ffn_A_fc"].T),
        "BfcT": np.ascontiguousarray(ins["L1_ffn_B_fc"].T),
        "AprT": np.ascontiguousarray(ins["L1_ffn_A_proj"].T),
        "BprT": np.ascontiguousarray(ins["L1_ffn_B_proj"].T),
    }


class Chan:
    def __init__(self, sem):
        self.sem = sem
        self.n = 0


class Rec:
    ENGS = ("tensor", "vector", "scalar", "gpsimd", "sync")

    def __init__(self, nc):
        self.nc = nc
        self.ops = {e: [] for e in self.ENGS}
        self.last = {}
        self.drain = {}

    def do(self, eng, fn, wait=(), inc=(), bare=False):
        w = [(ch, (ch.n if v is None else v)) for ch, v in wait]
        ii = []
        for ch, k in inc:
            ch.n += k
            ii.append((ch, k))
        isdma = any(k == 16 for _, k in ii)
        dinc = None
        if not (bare or isdma) and eng in self.drain:
            w = [self.last.get(eng)] + w if self.last.get(eng) else w
            if ii:
                self.last[eng] = (ii[0][0], ii[0][0].n)
            else:
                dch = self.drain[eng]
                dch.n += 1
                dinc = (dch, 1)
                self.last[eng] = (dch, dch.n)

        def run(e, fn=fn, w=w, ii=ii, dinc=dinc):
            for ch, v in w:
                e.wait_ge(ch.sem, v)
            ins = fn(e)
            if dinc is not None:
                ins = ins.then_inc(dinc[0].sem, dinc[1])
            for ch, k in ii:
                ins = ins.then_inc(ch.sem, k)
        self.ops[eng].append(run)

    def emit_block(self):
        with self.nc.Block() as blk:
            for name in self.ENGS:
                ops = self.ops[name]
                if not ops:
                    continue
                getattr(blk, name)(lambda e, ops=ops: [f(e) for f in ops])
        self.ops = {e: [] for e in self.ENGS}


class Builder:
    def __init__(self, stage="all", debug=()):
        self.stage = stage
        self.debug = set(debug)
        self.nc = bass.Bass("TRN2", target_bir_lowering=False)
        nc = self.nc
        self.inp = {}
        self.dbg_out = {}
        self.inp["x"] = nc.dram_tensor("x", [T, D], F32, kind="ExternalInput")
        for nm, shp, dt in [
            ("L0_attn_Wq", [D, D], F32), ("L0_attn_Wkv", [D, D], F32),
            ("L0_attn_Wout", [D, D], F32), ("L0_attn_bout", [D], F32),
            ("L0_ffn_w_gate", [H0, D], I32), ("L0_ffn_w_up", [H0, D], I32),
            ("L0_ffn_w_down", [D, H0], I32),
            ("L1_attn_w_qkv", [3 * D, D], F32), ("L1_attn_w_proj", [D, D], F32),
            ("L1_attn_b_proj", [D], F32),
            ("L1_ffn_W_fc", [H1, D], F32), ("L1_ffn_b_fc", [H1], F32),
            ("L1_ffn_A_fc", [16, D], F32), ("L1_ffn_B_fc", [H1, 16], F32),
            ("L1_ffn_W_proj", [D, H1], F32), ("L1_ffn_b_proj", [D], F32),
            ("L1_ffn_A_proj", [16, H1], F32), ("L1_ffn_B_proj", [D, 16], F32),
        ]:
            self.inp[nm] = nc.dram_tensor(nm, shp, dt, kind="ExternalInput")
        for nm, shp in [("tpos", [128, T]), ("jarg", [128, 8]), ("ktab", [128, T]),
                        ("ktabc", [128, T]), ("ind2", [2, 128]), ("ones1x128", [1, 128])]:
            self.inp[nm] = nc.dram_tensor(nm, shp, F32, kind="ExternalInput")
        for nm, shp in [("r2t", [128, 128]), ("ident", [128, 128])]:
            self.inp[nm] = nc.dram_tensor(nm, shp, BF16, kind="ExternalInput")
        for nm, shp in [("AfcT", [D, 16]), ("BfcT", [16, H1]), ("AprT", [H1, 16]), ("BprT", [16, D]),
                        ("identf32", [128, 128])]:
            self.inp[nm] = nc.dram_tensor(nm, shp, F32, kind="ExternalInput")
        self.out = nc.dram_tensor("y", [T, D], F32, kind="ExternalOutput")
        self.wfcT = nc.dram_tensor("wfcT", [32, 128, 1024], BF16)
        self.wprT = nc.dram_tensor("wprT", [8, 128, 4096], BF16)
        self.rtrip = nc.dram_tensor("rtrip", [1, 512], F32)
        self.ostage = nc.dram_tensor("ostage", [T, D], BF16)

        self._ctx = []

    def dbg(self, name, shape, dtype=F32):
        t = self.nc.dram_tensor("dbg_" + name, shape, dtype, kind="ExternalOutput")
        self.dbg_out[name] = t
        return t

    def build(self):
        nc = self.nc
        import contextlib
        self._stack = contextlib.ExitStack()
        st = self._stack
        self.bf = st.enter_context(nc.sbuf_tensor("bfa", [128, NBF], BF16))
        self.f32 = st.enter_context(nc.sbuf_tensor("f32a", [128, NF32], F32))
        self.i32 = st.enter_context(nc.sbuf_tensor("i32a", [128, NINT], I32))
        self.ps = [st.enter_context(nc.psum_tensor(f"pb{i}", [128, 512], F32)) for i in range(7)]

        self.psb = st.enter_context(nc.psum_tensor("pbf", [128, 1024], BF16))
        self.ch = {}
        for nm in ["dma", "wdma", "mm", "act", "dve", "gp", "xb",
                   "wg0", "wg1", "wu0", "wu1", "bt0", "bt1"]:
            self.ch[nm] = Chan(st.enter_context(nc.semaphore("ch_" + nm)))
        self.rec = Rec(nc)
        for eng in ("vector", "scalar", "gpsimd"):
            self.rec.drain[eng] = Chan(st.enter_context(nc.semaphore("dr_" + eng)))

        self.phase_setup()
        if self.stage == "setup":
            self.finish(); return self
        if self.stage == "quant":
            chd = self.ch["dma"]
            self.emit_quant(self.inp["L1_ffn_W_fc"], 32, 1024, self.wfcT, FMS + 232)
            dq = self.dbg("qcells", [128, 16], F32)
            self.rec.do("scalar", lambda s: s.dma_start(dq[:, :], self.f32[:, FMS + 232:FMS + 248]),
                        wait=[(self.ch["chv"] if False else self.ch["dve"], None),
                              (self.ch["wdma"], None)], inc=[(chd, 16)])
            dw = self.dbg("qtiles", [128, 1024], mybir.dt.bfloat16)
            self.rec.do("gpsimd", lambda g: g.dma_start(dw[:, :],
                        bass.AP(self.wfcT, 0, [[1024, 128], [1, 1024]])), wait=[(self.ch["wdma"], None)], inc=[(chd, 16)])
            self.rec.do("scalar", lambda s: s.wait_ge(chd.sem, chd.n), bare=True)
            self.finish(); return self
        self.phase_l0_proj()
        if self.stage == "l0proj":
            self.finish(); return self
        self.phase_l0_attn()
        self.phase_l0_out_ln()
        if self.stage == "x2":
            self.finish(); return self
        self.phase_l0_ffn()
        if self.stage == "x3":
            self.finish(); return self
        self.phase_l1_proj()
        self.phase_l1_attn()
        self.phase_l1_out_ln()
        if self.stage == "x4":
            self.finish(); return self
        self.phase_l1_ffn()
        self.phase_output()
        self.finish()
        return self

    def finish(self):
        self.rec.emit_block()
        self._stack.close()

    def bfv(self, col, n, p0=0, np_=128):
        return self.bf[p0:p0 + np_, col:col + n]

    def xf(self, base, g, t0, nt, p0=0, np_=128):
        return self.bf[p0:p0 + np_, base + g * T + t0: base + g * T + t0 + nt]

    def phase_setup(self):
        nc, rec, ch, bf, f32 = self.nc, self.rec, self.ch, self.bf, self.f32
        inp = self.inp
        xsrc = inp["x"]
        chd = self.ch["dma"]
        for tc in range(16):
            dst = bf[:, XB + tc * 1024: XB + (tc + 1) * 1024]
            src = bass.AP(xsrc, tc * 128 * D, [[D, 128], [1, D]])
            rec.do("gpsimd", lambda g, d=dst, s=src: g.dma_start(d, s), inc=[(chd, 16)])
        rec.do("gpsimd", lambda g: g.dma_start(bf[:, CONST0:CONST0 + 128], inp["r2t"][:, :]), inc=[(chd, 16)])
        chg = self.ch["gp"]
        for off, val in ((48, -C_L0), (49, 1e-5), (50, 0.125), (51, -C_L1),
                         (52, 2.0), (53, 1.0 / 16129.0), (54, 1.0 / 127.0), (55, -0.5)):
            rec.do("gpsimd", lambda g, o=off, v=val: g.memset(f32[:, FMS + o:FMS + o + 1], v), inc=[(chg, 1)])
        rec.do("gpsimd", lambda g: g.memset(f32[:, FMS + 60:FMS + 61], 127.0), inc=[(chg, 1)])
        rec.do("gpsimd", lambda g: g.memset(bf[:, ONESCOL:ONESCOL + 8], 1.0), inc=[(chg, 1)])
        rec.do("gpsimd", lambda g: g.memset(bf[:, ONESROW:ONESROW + 128], 1.0), inc=[(chg, 1)])
        rec.do("gpsimd", lambda g: g.memset(f32[:, FMS:FMS + 1], 10000.0), inc=[(chg, 1)])
        rec.do("gpsimd", lambda g: g.dma_start(f32[:, FMS + 16:FMS + 24], inp["jarg"][:, :]), inc=[(chd, 16)])
        chm, cha, chv = self.ch["mm"], self.ch["act"], self.ch["dve"]
        rec.do("scalar", lambda s: s.activation(f32[:, FMS + 1:FMS + 2], f32[:, FMS:FMS + 1], AF.Ln),
               wait=[(chd, None), (self.ch["gp"], None)], inc=[(cha, 1)])
        rec.do("vector", lambda v: v.tensor_scalar_mul(f32[:, FMS + 17:FMS + 18], f32[:, FMS + 16:FMS + 17],
                                                       f32[:, FMS + 1:FMS + 2]),
               wait=[(cha, None), (chd, None)], inc=[(chv, 1)])
        rec.do("scalar", lambda s: s.activation(f32[:, FMS + 18:FMS + 19], f32[:, FMS + 17:FMS + 18], AF.Exp),
               wait=[(chv, None)], inc=[(cha, 1)])
        inv_col = FMS + 18
        rec.do("vector", lambda v: v.reciprocal(f32[:, FMS + 61:FMS + 62], f32[:, FMS + 60:FMS + 61]),
               wait=[(self.ch["gp"], None)], inc=[(chv, 1)])
        for qp in range(4):
            t0 = 512 * qp
            tp, kt, th, t2 = TB_TP, TB_KT, TB_TH, TB_T2
            rec.do("gpsimd", lambda g, t0=t0: g.dma_start(f32[:, tp:tp + 512],
                   bass.AP(inp["tpos"], t0, [[T, 128], [1, 512]])), wait=[(chv, None)], inc=[(chd, 16)])
            rec.do("gpsimd", lambda g, t0=t0: g.dma_start(f32[:, kt:kt + 512],
                   bass.AP(inp["ktab"], t0, [[T, 128], [1, 512]])), inc=[(chd, 16)])
            rec.do("vector", lambda v: v.tensor_scalar_mul(f32[:, th:th + 512], f32[:, tp:tp + 512],
                                                           f32[:, inv_col:inv_col + 1]),
                   wait=[(chd, None), (cha, None), (chv, None)], inc=[(chv, 1)])
            rec.do("vector", lambda v: v.scalar_tensor_tensor(f32[:, t2:t2 + 512], f32[:, kt:kt + 512],
                   -CW1, f32[:, th:th + 512], ALU.mult, ALU.add))
            rec.do("vector", lambda v: v.scalar_tensor_tensor(f32[:, tp:tp + 512], f32[:, kt:kt + 512],
                   -CW2, f32[:, t2:t2 + 512], ALU.mult, ALU.add))
            rec.do("vector", lambda v: v.scalar_tensor_tensor(f32[:, t2:t2 + 512], f32[:, kt:kt + 512],
                   -CW3, f32[:, tp:tp + 512], ALU.mult, ALU.add))
            rec.do("vector", lambda v: v.tensor_scalar_min(f32[:, tp:tp + 512], f32[:, t2:t2 + 512], PIF))
            rec.do("vector", lambda v: v.tensor_scalar_max(f32[:, t2:t2 + 512], f32[:, tp:tp + 512], -PIF),
                   inc=[(chv, 1)])
            rec.do("scalar", lambda s, t0=t0: s.activation(bf[:, A_SIN + t0:A_SIN + t0 + 512],
                   f32[:, t2:t2 + 512], AF.Sin), wait=[(chv, None)], inc=[(cha, 1)])
            rec.do("gpsimd", lambda g, t0=t0: g.dma_start(f32[:, kt:kt + 512],
                   bass.AP(inp["ktabc"], t0, [[T, 128], [1, 512]])), wait=[(cha, None)], inc=[(chd, 16)])
            rec.do("vector", lambda v: v.tensor_scalar_add(f32[:, tp:tp + 512], f32[:, th:th + 512],
                   float(np.float32(np.pi / 2))), wait=[(chd, None), (cha, None)])
            rec.do("vector", lambda v: v.scalar_tensor_tensor(f32[:, t2:t2 + 512], f32[:, kt:kt + 512],
                   -CW1, f32[:, tp:tp + 512], ALU.mult, ALU.add))
            rec.do("vector", lambda v: v.scalar_tensor_tensor(f32[:, tp:tp + 512], f32[:, kt:kt + 512],
                   -CW2, f32[:, t2:t2 + 512], ALU.mult, ALU.add))
            rec.do("vector", lambda v: v.scalar_tensor_tensor(f32[:, t2:t2 + 512], f32[:, kt:kt + 512],
                   -CW3, f32[:, tp:tp + 512], ALU.mult, ALU.add))
            rec.do("vector", lambda v: v.tensor_scalar_min(f32[:, tp:tp + 512], f32[:, t2:t2 + 512], PIF))
            rec.do("vector", lambda v: v.tensor_scalar_max(f32[:, t2:t2 + 512], f32[:, tp:tp + 512], -PIF),
                   inc=[(chv, 1)])
            rec.do("scalar", lambda s, t0=t0: s.activation(bf[:, A_COS + t0:A_COS + t0 + 512],
                   f32[:, t2:t2 + 512], AF.Sin), wait=[(chv, None)], inc=[(cha, 1)])
        chx = self.ch["xb"]
        for tc in range(16):
            inap = bf[:, XB + tc * 1024: XB + (tc + 1) * 1024].rearrange("p (g d) -> p g d", g=8)
            outap = bass.AP(self.bf, XA + tc * 128, [[NBF, 128], [T, 8], [1, 128]])
            rec.do("sync", lambda s, i=inap, o=outap: s.dma_start(o, i, transpose=True),
                   wait=[(chd, None)], inc=[(chx, 16)])
        chw = self.ch["wdma"]
        wq = inp["L0_attn_Wq"]
        for h, base in ((0, WSA), (1, WSB)):
            for ic in range(8):
                dst = bf[:, base + ic * 512: base + (ic + 1) * 512]
                src = bass.AP(wq, ic * 128 * D + h * 512, [[D, 128], [1, 512]])
                rec.do("gpsimd", lambda g, d=dst, s=src: g.dma_start(d, s), inc=[(chw, 16)])
        if "tabdeep" in self.debug:
            dinv = self.dbg("invf", [128, 8], F32)
            dth = self.dbg("theta", [128, 512], F32)
            rec.do("scalar", lambda s: s.dma_start(dinv[:, 0:8], f32[:, inv_col:inv_col + 8]),
                   wait=[(self.ch["act"], None)], inc=[(chd, 16)])
            rec.do("scalar", lambda s: s.dma_start(dth[:, :], f32[:, TB_TH:TB_TH + 512]), inc=[(chd, 16)])
            dj = self.dbg("jchain", [128, 8], F32)
            rec.do("vector", lambda v: v.tensor_copy(f32[:, FMS + 300:FMS + 301], self.ps[0][:, 0:1]), wait=[(self.ch["mm"], None)])
            rec.do("vector", lambda v: v.tensor_copy(f32[:, FMS + 301:FMS + 302], f32[:, FMS + 16:FMS + 17]))
            rec.do("vector", lambda v: v.tensor_copy(f32[:, FMS + 302:FMS + 303], f32[:, FMS + 17:FMS + 18]).then_inc(self.ch["dve"].sem, 1))
            rec.do("scalar", lambda s: s.dma_start(dj[:, 0:3], f32[:, FMS + 300:FMS + 303]), wait=[(self.ch["dve"], self.ch["dve"].n + 1)], inc=[(chd, 16)])
        if "tables" in self.debug:
            dsin = self.dbg("sin", [128, T], BF16); dcos = self.dbg("cos", [128, T], BF16)
            dxa = self.dbg("xa", [128, G * T], BF16); dxb = self.dbg("xb", [128, 16384], BF16)
            rec.do("scalar", lambda s: s.dma_start(dsin[:, :], bf[:, A_SIN:A_SIN + T]),
                   wait=[(self.ch["act"], None)], inc=[(chd, 16)])
            rec.do("scalar", lambda s: s.dma_start(dcos[:, :], bf[:, A_COS:A_COS + T]), inc=[(chd, 16)])
            rec.do("scalar", lambda s: s.dma_start(dxa[:, :], bf[:, XA:XA + 16384]),
                   wait=[(chx, None)], inc=[(chd, 16)])
            rec.do("scalar", lambda s: s.dma_start(dxb[:, :], bf[:, XB:XB + 16384]), inc=[(chd, 16)])
            rec.do("scalar", lambda s: s.wait_ge(chd.sem, chd.n), bare=True)
        self.rec.emit_block()

    def phase_l0_proj(self):
        nc, rec, bf = self.nc, self.rec, self.bf
        chm, cha, chv, chw = self.ch["mm"], self.ch["act"], self.ch["dve"], self.ch["wdma"]
        evac1_done = {}
        it = 0
        for proj, dstbase in (("q", QQ), ("kv", KVO)):
            if proj == "kv":
                mm_after_q = chm.n
                wkv = self.inp["L0_attn_Wkv"]
                for h, base in ((0, WSA), (1, WSB)):
                    for ic in range(8):
                        dst = bf[:, base + ic * 512: base + (ic + 1) * 512]
                        src = bass.AP(wkv, ic * 128 * D + h * 512, [[D, 128], [1, 512]])
                        rec.do("gpsimd", lambda g, d=dst, s=src: g.dma_start(d, s),
                               wait=[(chm, mm_after_q)] if ic == 0 and h == 0 else [], inc=[(chw, 16)])
            for oc in range(8):
                wbase = WSA if oc < 4 else WSB
                ocol = (oc % 4) * 128
                for tg in range(4):
                    par = it % 2
                    p0 = self.ps[par]
                    p1 = self.ps[2 + par]
                    w_pe = [(chw, None), (self.ch["xb"], None)]
                    if it - 2 in evac1_done:
                        w_pe.append((cha, evac1_done[it - 2]))
                    for ic in range(8):
                        lhsT = bf[:, wbase + ic * 512 + ocol: wbase + ic * 512 + ocol + 128]
                        rhs = bf[:, XA + ic * T + tg * 512: XA + ic * T + tg * 512 + 512]
                        rec.do("tensor", lambda t, o=p0, l=lhsT, r=rhs, s=(ic == 0), e=(ic == 7):
                               t.matmul(o[:, 0:512], l, r, start=s, stop=e),
                               wait=w_pe if ic == 0 else [],
                               inc=[(chm, 1)] if ic == 7 else [])
                    dcol = dstbase + oc * T + tg * 512
                    rec.do("scalar", lambda s, d=dcol, o=p0: s.activation(bf[:, d:d + 512], o[:, 0:512], AF.Copy),
                           wait=[(chm, None)], inc=[(cha, 1)])
                    rec.do("tensor", lambda t, d=dcol, o=p1: t.matmul(o[:, 0:512],
                           bf[:, CONST0:CONST0 + 128], bf[:, d:d + 512], start=True, stop=True),
                           wait=[(cha, None)], inc=[(chm, 1)])
                    t1c = A_RT + par * 512
                    rec.do("vector", lambda v, d=dcol, t1=t1c, tg=tg: v.tensor_mul(
                           bf[:, t1:t1 + 512], bf[:, d:d + 512], bf[:, A_COS + tg * 512: A_COS + tg * 512 + 512]),
                           wait=[(cha, None)])
                    swc = A_RT + 1024 + par * 512
                    rec.do("scalar", lambda s, o=p1, sw=swc: s.activation(bf[:, sw:sw + 512], o[:, 0:512], AF.Copy),
                           wait=[(chm, None)], inc=[(cha, 1)])
                    evac1_done[it] = cha.n
                    rec.do("vector", lambda v, sw=swc, tg=tg: v.tensor_mul(
                           bf[:, sw:sw + 512], bf[:, sw:sw + 512], bf[:, A_SIN + tg * 512: A_SIN + tg * 512 + 512]),
                           wait=[(cha, None)])
                    rec.do("vector", lambda v, d=dcol, t1=t1c, sw=swc: v.tensor_add(
                           bf[:, d:d + 512], bf[:, t1:t1 + 512], bf[:, sw:sw + 512]), inc=[(chv, 1)])
                    it += 1
        mm_all = chm.n
        wout = self.inp["L0_attn_Wout"]
        for h, base in ((0, WSA), (1, WSB)):
            for ic in range(8):
                dst = bf[:, base + ic * 512: base + (ic + 1) * 512]
                src = bass.AP(wout, ic * 128 * D + h * 512, [[D, 128], [1, 512]])
                rec.do("gpsimd", lambda g, d=dst, s=src: g.dma_start(d, s),
                       wait=[(chm, mm_all)] if ic == 0 and h == 0 else [], inc=[(chw, 16)])
        if "l0proj" in self.debug:
            dt2 = self.dbg("tiles", [128, 4096], BF16)
            rec.do("scalar", lambda s: s.dma_start(dt2[:, 0:2048], bf[:, A_RT:A_RT + 2048]),
                   wait=[(chv, None)], inc=[(self.ch["dma"], 16)])
            rec.do("scalar", lambda s: s.dma_start(dt2[:, 2048:4096], bf[:, A_SIN:A_SIN + 2048]), inc=[(self.ch["dma"], 16)])
            dws = self.dbg("wsa", [128, 8192], BF16)
            rec.do("scalar", lambda s: s.dma_start(dws[:, 0:4096], bf[:, WSA:WSA + 4096]),
                   wait=[(chw, None)], inc=[(self.ch["dma"], 16)])
            rec.do("scalar", lambda s: s.dma_start(dws[:, 4096:8192], bf[:, WSB:WSB + 4096]), inc=[(self.ch["dma"], 16)])
            dq = self.dbg("qrot", [128, G * T], BF16)
            dkv = self.dbg("kvrot", [128, G * T], BF16)
            rec.do("scalar", lambda s: s.dma_start(dq[:, :], bf[:, QQ:QQ + G * T]),
                   wait=[(chv, None)], inc=[(self.ch["dma"], 16)])
            rec.do("scalar", lambda s: s.dma_start(dkv[:, :], bf[:, KVO:KVO + G * T]), inc=[(self.ch["dma"], 16)])
            rec.do("scalar", lambda s: s.wait_ge(self.ch["dma"].sem, self.ch["dma"].n), bare=True)
        self.rec.emit_block()

    def phase_l0_attn(self):
        nc, rec, bf, f32 = self.nc, self.rec, self.bf, self.f32
        chm, cha, chv, chx, chg = self.ch["mm"], self.ch["act"], self.ch["dve"], self.ch["xb"], self.ch["gp"]
        rec.do("gpsimd", lambda g: g.memset(bf[:, A_KVT:A_KVT + 4096], 0.0), inc=[(chg, 1)])
        rec.do("gpsimd", lambda g: g.memset(bass.AP(bf, A_KVT + 64, [[NBF, 128], [256, 16]]), 1.0), inc=[(chg, 1)])
        rec.do("gpsimd", lambda g: g.memset(bass.AP(bf, A_KVT + 128, [[NBF, 128], [256, 16]]), 1.0), inc=[(chg, 1)])
        memset_done = chg.n
        expd = {}
        avd = {}
        evacd = 0
        xbar_n0 = chx.n
        for pr in range(8):
            wv = [(chg, memset_done)]
            if pr > 0:
                wv.append((chm, chm.n))
            for half, (p0, dcol) in enumerate(((0, A_KVT), (64, A_KVT + 192))):
                inap = bass.AP(bf, KVO + pr * T, [[NBF, 64], [128, 16], [1, 128]], ) if False else None
                src = self.bf[p0:p0 + 64, KVO + pr * T: KVO + (pr + 1) * T].rearrange("p (c k) -> p c k", c=16)
                dst = bass.AP(bf, dcol, [[NBF, 128], [256, 16], [1, 64]])
                rec.do("sync", lambda s, d=dst, i=src: s.dma_start(d, i, transpose=True),
                       wait=wv if half == 0 else [], inc=[(chx, 16)])
            for qg in range(4):
                ev_wait = evacd
                for c in range(16):
                    par = c % 2
                    psA, psB = self.ps[par], self.ps[2 + par]
                    w = []
                    if (0, par) in expd:
                        w.append((cha, expd[(0, par)]))
                    if (1, par) in expd:
                        w.append((cha, expd[(1, par)]))
                    if c == 0:
                        w.append((chv, ev_wait))
                        w.append((chx, chx.n))
                    qcol = QQ + pr * T + qg * 512
                    kcol = KVO + pr * T + c * 128
                    rec.do("tensor", lambda t, o=psA, k=kcol, q=qcol: t.matmul(
                           o[:, 0:512], bf[0:64, k:k + 128], bf[0:64, q:q + 512],
                           start=True, stop=True, tile_position=(0, 0)),
                           wait=w, inc=[(chm, 1)])
                    rec.do("tensor", lambda t, o=psB, k=kcol, q=qcol: t.matmul(
                           o[:, 0:512], bf[64:128, k:k + 128], bf[64:128, q:q + 512],
                           start=True, stop=True, tile_position=(64, 0)),
                           inc=[(chm, 1)])
                    mm_after_s = chm.n
                    for h, ps_ in ((0, psA), (1, psB)):
                        ptcol = A_PT + (2 * par + h) * 512
                        w2 = [(chm, mm_after_s)]
                        if c >= 2 and (c - 2) in avd:
                            w2.append((chm, avd[c - 2]))
                        rec.do("scalar", lambda s, p=ps_, pc=ptcol: s.activation(
                               bf[:, pc:pc + 512], p[:, 0:512], AF.Exp,
                               bias=f32[:, FMS + 48:FMS + 49], scale=f32[:, FMS + 50:FMS + 51]),
                               wait=w2, inc=[(cha, 1)])
                        expd[(h, par)] = cha.n
                    for h, avp in ((0, self.ps[4]), (1, self.ps[5])):
                        ptcol = A_PT + (2 * par + h) * 512
                        kv_l = A_KVT + c * 256 + (0 if h == 0 else 128)
                        rec.do("tensor", lambda t, o=avp, kl=kv_l, pc=ptcol, s0=(c == 0), s1=(c == 15):
                               t.matmul(o[:, 0:512], bf[:, kl:kl + 128], bf[:, pc:pc + 512],
                                        start=s0, stop=s1),
                               wait=[(cha, expd[(h, par)])], inc=[(chm, 1)])
                    avd[c] = chm.n
                av_stop = chm.n
                rec.do("vector", lambda v: v.reciprocal(f32[64:65, FQA:FQA + 512], self.ps[4][64:65, 0:512]),
                       wait=[(chm, av_stop)])
                rec.do("vector", lambda v: v.reciprocal(f32[0:1, FQA:FQA + 512], self.ps[5][0:1, 0:512]))
                rec.do("vector", lambda v: v.tensor_copy(bf[64:65, A_LB:A_LB + 512], f32[64:65, FQA:FQA + 512]))
                rec.do("vector", lambda v: v.tensor_copy(bf[0:1, A_LB:A_LB + 512], f32[0:1, FQA:FQA + 512]),
                       inc=[(chv, 1)])
                rows_ready = chv.n
                rec.do("tensor", lambda t: t.matmul(self.ps[0][:, 0:512], bf[64:65, ONESROW:ONESROW + 128],
                       bf[64:65, A_LB:A_LB + 512], start=True, stop=True),
                       wait=[(chv, rows_ready)], inc=[(chm, 1)])
                rec.do("tensor", lambda t: t.matmul(self.ps[1][:, 0:512], bf[0:1, ONESROW:ONESROW + 128],
                       bf[0:1, A_LB:A_LB + 512], start=True, stop=True), inc=[(chm, 1)])
                bc_done = chm.n
                rec.do("scalar", lambda s: s.activation(bf[:, A_BC:A_BC + 512], self.ps[0][:, 0:512], AF.Copy),
                       wait=[(chm, bc_done)])
                rec.do("scalar", lambda s: s.activation(bf[:, A_BC + 512:A_BC + 1024], self.ps[1][:, 0:512], AF.Copy),
                       inc=[(cha, 1)])
                bce = cha.n
                ocol = OO + pr * T + qg * 512
                rec.do("vector", lambda v, oc=ocol: v.tensor_mul(bf[0:64, oc:oc + 512],
                       self.ps[4][0:64, 0:512], bf[0:64, A_BC:A_BC + 512]),
                       wait=[(cha, bce)])
                rec.do("vector", lambda v, oc=ocol: v.tensor_mul(bf[64:128, oc:oc + 512],
                       self.ps[5][64:128, 0:512], bf[64:128, A_BC + 512:A_BC + 1024]),
                       inc=[(chv, 1)])
                evacd = chv.n
            self.rec.emit_block()
        if "l0attn" in self.debug:
            do_ = self.dbg("of", [128, G * T], BF16)
            rec.do("scalar", lambda s: s.dma_start(do_[:, :], bf[:, OO:OO + G * T]),
                   wait=[(chv, None)], inc=[(self.ch["dma"], 16)])
            rec.do("scalar", lambda s: s.wait_ge(self.ch["dma"].sem, self.ch["dma"].n), bare=True)
        self.rec.emit_block()

    def phase_l0_out_ln(self):
        self.emit_proj_residual_ln(src=OO, wname=None, bias="L0_attn_bout",
                                   stream=XA, rawdst=XB, newdst=XA, dbgkey="x2")

    def emit_proj_residual_ln(self, src, wname, bias, stream, rawdst, newdst, dbgkey):
        nc, rec, bf, f32 = self.nc, self.rec, self.bf, self.f32
        chm, cha, chv, chd = self.ch["mm"], self.ch["act"], self.ch["dve"], self.ch["dma"]
        chw = self.ch["wdma"]
        bt = self.inp[bias]
        rec.do("gpsimd", lambda g: g.dma_start(f32[:, FMS + 32:FMS + 40],
               bass.AP(bt, 0, [[1, 128], [128, 8]]), allow_slow_non_contiguous=True), inc=[(chd, 16)])
        bias_done = chd.n
        raw_done = {}
        for tg in range(4):
            for oc in range(8):
                par = (tg * 8 + oc) % 2
                p0 = self.ps[par]
                wbase = WSA if oc < 4 else WSB
                ocol = (oc % 4) * 128
                w = [(chw, None)] if (tg == 0 and oc == 0) else []
                if (tg, oc) not in (None,):
                    prev = tg * 8 + oc - 2
                    if prev >= 0:
                        w.append((chv, raw_done.get(prev, 0)))
                for ic in range(8):
                    lhsT = bf[:, wbase + ic * 512 + ocol: wbase + ic * 512 + ocol + 128]
                    rhs = bf[:, src + ic * T + tg * 512: src + ic * T + tg * 512 + 512]
                    rec.do("tensor", lambda t, o=p0, l=lhsT, r=rhs, s=(ic == 0), e=(ic == 7):
                           t.matmul(o[:, 0:512], l, r, start=s, stop=e),
                           wait=w if ic == 0 else [], inc=[(chm, 1)] if ic == 7 else [])
                rcol = rawdst + oc * T + tg * 512
                xcol = stream + oc * T + tg * 512
                wst = [(chm, None)]
                if tg == 0 and oc == 0:
                    wst.append((chd, bias_done))
                rec.do("vector", lambda v, o=p0, rc=rcol, xc=xcol, occ=oc: v.scalar_tensor_tensor(
                       bf[:, rc:rc + 512], o[:, 0:512], f32[:, FMS + 32 + occ:FMS + 33 + occ],
                       bf[:, xc:xc + 512], ALU.add, ALU.add),
                       wait=wst, inc=[(chv, 1)])
                raw_done[tg * 8 + oc] = chv.n
            self.emit_ln_group(tg, rawdst, newdst, raw_done[tg * 8 + 7])
        if dbgkey in self.debug:
            dr = self.dbg(dbgkey + "_raw", [128, G * T], BF16)
            dn = self.dbg(dbgkey, [128, G * T], BF16)
            rec.do("scalar", lambda s: s.dma_start(dr[:, :], bf[:, rawdst:rawdst + G * T]),
                   wait=[(chv, None)], inc=[(chd, 16)])
            rec.do("scalar", lambda s: s.dma_start(dn[:, :], bf[:, newdst:newdst + G * T]), inc=[(chd, 16)])
            rec.do("scalar", lambda s: s.wait_ge(chd.sem, chd.n), bare=True)
        self.rec.emit_block()

    def emit_ln_group(self, tg, rawdst, newdst, raw_ready_chv):
        rec, bf, f32 = self.rec, self.bf, self.f32
        chm, cha, chv = self.ch["mm"], self.ch["act"], self.ch["dve"]
        for ic in range(8):
            rcol = rawdst + ic * T + tg * 512
            w = [(chv, raw_ready_chv)] if ic == 0 else []
            rec.do("tensor", lambda t, rc=rcol, s=(ic == 0), e=(ic == 7): t.matmul(
                   self.ps[6][0:1, 0:512], bf[:, ONESCOL:ONESCOL + 1], bf[:, rc:rc + 512],
                   start=s, stop=e), wait=w, inc=[(chm, 1)] if ic == 7 else [])
        sqmm = {}
        for ic in range(8):
            rcol = rawdst + ic * T + tg * 512
            sq = A_PT + (ic % 2) * 512
            w = [(chv, raw_ready_chv)] if ic == 0 else []
            if ic >= 2:
                w.append((chm, sqmm[ic - 2]))
            rec.do("scalar", lambda s_, rc=rcol, q=sq: s_.activation(bf[:, q:q + 512], bf[:, rc:rc + 512],
                   AF.Square), wait=w, inc=[(cha, 1)])
            rec.do("tensor", lambda t, q=sq, s=(ic == 0), e=(ic == 7): t.matmul(
                   self.ps[5][0:1, 0:512], bf[:, ONESCOL:ONESCOL + 1], bf[:, q:q + 512],
                   start=s, stop=e), wait=[(cha, None)], inc=[(chm, 1)])
            sqmm[ic] = chm.n
        sums_done = chm.n
        rec.do("vector", lambda v: v.tensor_scalar_mul(f32[0:1, FQA:FQA + 512], self.ps[6][0:1, 0:512],
               1.0 / 1024.0), wait=[(chm, sums_done)])
        rec.do("vector", lambda v: v.tensor_scalar_mul(f32[0:1, FQB:FQB + 512], self.ps[5][0:1, 0:512],
               1.0 / 1024.0))
        rec.do("vector", lambda v: v.tensor_mul(f32[0:1, FQC:FQC + 512], f32[0:1, FQA:FQA + 512],
               f32[0:1, FQA:FQA + 512]))
        rec.do("vector", lambda v: v.tensor_sub(f32[0:1, FQB:FQB + 512], f32[0:1, FQB:FQB + 512],
               f32[0:1, FQC:FQC + 512]), inc=[(chv, 1)])
        rec.do("scalar", lambda s: s.activation(f32[0:1, FQC:FQC + 512], f32[0:1, FQB:FQB + 512],
               AF.Ln, bias=f32[0:1, FMS + 49:FMS + 50]), wait=[(chv, None)])
        rec.do("scalar", lambda s: s.activation(f32[0:1, FQB:FQB + 512], f32[0:1, FQC:FQC + 512],
               AF.Exp, scale=f32[0:1, FMS + 55:FMS + 56]), inc=[(cha, 1)])
        rec.do("vector", lambda v: v.engine_nop(), wait=[(cha, None)])
        rec.do("vector", lambda v: v.tensor_copy(bf[0:1, A_LB:A_LB + 512], f32[0:1, FQA:FQA + 512]))
        rec.do("vector", lambda v: v.tensor_copy(bf[0:1, A_LB + 512:A_LB + 1024],
               f32[0:1, FQB:FQB + 512]), inc=[(chv, 1)])
        rows_done = chv.n
        rec.do("tensor", lambda t: t.matmul(self.ps[2][:, 0:512], bf[0:1, ONESROW:ONESROW + 128],
               bf[0:1, A_LB:A_LB + 512], start=True, stop=True), wait=[(chv, rows_done)], inc=[(chm, 1)])
        rec.do("tensor", lambda t: t.matmul(self.ps[3][:, 0:512], bf[0:1, ONESROW:ONESROW + 128],
               bf[0:1, A_LB + 512:A_LB + 1024], start=True, stop=True), inc=[(chm, 1)])
        bc = chm.n
        rec.do("scalar", lambda s: s.activation(bf[:, A_BC:A_BC + 512], self.ps[2][:, 0:512], AF.Copy),
               wait=[(chm, bc), (chv, None)])
        rec.do("scalar", lambda s: s.activation(bf[:, A_BC + 512:A_BC + 1024], self.ps[3][:, 0:512], AF.Copy),
               inc=[(cha, 1)])
        bce = cha.n
        for ic in range(8):
            rcol = rawdst + ic * T + tg * 512
            ncol = newdst + ic * T + tg * 512
            tmp = A_PT + 1024
            rec.do("vector", lambda v, rc=rcol, tm=tmp: v.tensor_sub(bf[:, tm:tm + 512],
                   bf[:, rc:rc + 512], bf[:, A_BC:A_BC + 512]),
                   wait=[(cha, bce)] if ic == 0 else [])
            rec.do("vector", lambda v, nc_=ncol, tm=tmp: v.tensor_mul(bf[:, nc_:nc_ + 512],
                   bf[:, tm:tm + 512], bf[:, A_BC + 512:A_BC + 1024]),
                   inc=[(chv, 1)] if ic == 7 else [])

    def phase_l0_ffn(self):
        nc, rec, bf, f32 = self.nc, self.rec, self.bf, self.f32
        chm, cha, chv, chx, chw = self.ch["mm"], self.ch["act"], self.ch["dve"], self.ch["xb"], self.ch["wdma"]
        FH = KVO
        wg_t, wu_t, wd_t = self.inp["L0_ffn_w_gate"], self.inp["L0_ffn_w_up"], self.inp["L0_ffn_w_down"]
        NH0 = 24
        stg = {"g": WSA, "u": WSB, "d": WSA}
        for tg2 in range(2):
            ln_gate = cha.n
            xb_done = {}
            for hc in range(NH0):
                par = hc % 2
                for nm, wt, psg in (("g", wg_t, (self.ps[0], self.ps[1])), ("u", wu_t, (self.ps[2], self.ps[3]))):
                    sb = stg[nm] + par * 1024
                    tb = stg[nm] + 2048 + par * 1024
                    chs = self.ch[("wg" if nm == "g" else "wu") + str(par)]
                    if hc >= 2:
                        w = [(chx, xb_done[(nm, hc - 2)][0])]
                    else:
                        w = [(chx, chx.n)]
                    src = bass.AP(wt, hc * 128 * D, [[D, 128], [1, D]])
                    rec.do("gpsimd", lambda g, d=sb, s=src: g.dma_start(bf[:, d:d + 1024], s),
                           wait=w, inc=[(chs, 16)])
                    wdma_n = chs.n
                    mmprev = xb_done.get((nm, hc - 2), (0, chm.n))[1]
                    ii = bf[:, sb:sb + 1024].rearrange("p (c k) -> p c k", c=8)
                    oo = bf[:, tb:tb + 1024].rearrange("p (c k) -> p c k", c=8)
                    rec.do("sync", lambda s, i=ii, o=oo, wn=wdma_n, mp=mmprev, cs=chs: (
                           s.wait_ge(cs.sem, wn), s.wait_ge(chm.sem, mp),
                           s.dma_start(o, i, transpose=True))[-1], inc=[(chx, 16)])
                    xbn = chx.n
                    for th in range(2):
                        ps_ = psg[th]
                        for dc in range(8):
                            lhsT = bf[:, tb + dc * 128: tb + dc * 128 + 128]
                            rhs = bf[:, XA + dc * T + tg2 * 1024 + th * 512: XA + dc * T + tg2 * 1024 + th * 512 + 512]
                            wmm = [(chx, xbn)] if dc == 0 else []
                            if dc == 0 and hc == 0 and th == 0 and nm == "g":
                                wmm.append((cha, ln_gate))
                            if dc == 0 and th == 0 and nm == "g" and hc >= 1:
                                wmm.append((chv, chv.n))
                                wmm.append((cha, cha.n))
                            rec.do("tensor", lambda t, o=ps_, l=lhsT, r=rhs, s=(dc == 0), e=(dc == 7):
                                   t.matmul(o[:, 0:512], l, r, start=s, stop=e),
                                   wait=wmm, inc=[(chm, 1)] if dc == 7 else [])
                    xb_done[(nm, hc)] = (chx.n, chm.n)
                gu_done = chm.n
                for th in range(2):
                    gt = A_KVT + th * 512
                    rec.do("scalar", lambda s, g_=gt, p=self.ps[th]: s.activation(
                           bf[:, g_:g_ + 512], p[:, 0:512], AF.Silu, scale=f32[:, FMS + 54:FMS + 55]),
                           wait=[(chm, gu_done)], inc=[(cha, 1)])
                    hcol = FH + hc * 1024 + th * 512
                    rec.do("vector", lambda v, hcc=hcol, g_=gt, p=self.ps[2 + th]: v.tensor_mul(
                           bf[:, hcc:hcc + 512], p[:, 0:512], bf[:, g_:g_ + 512]),
                           wait=[(cha, None)], inc=[(chv, 1)])
            hdone = chv.n
            raw_chv = {}
            for oc in range(8):
                dn_w = [(chv, hdone)]
                if oc >= 1:
                    dn_w.append((chv, raw_chv[oc - 1]))
                mm_start = chm.n
                for p3 in range(3):
                    par = p3 % 2
                    sb = stg["d"] + par * 1024
                    tb = stg["d"] + 2048 + par * 1024
                    w = list(dn_w) if p3 == 0 else []
                    if (oc * 3 + p3) >= 2:
                        w.append((chx, chx.n))
                    src = bass.AP(wd_t, oc * 128 * H0 + p3 * 1024, [[H0, 128], [1, 1024]])
                    chs = self.ch["wg" + str(par)]
                    rec.do("gpsimd", lambda g, d=sb, s=src: g.dma_start(bf[:, d:d + 1024], s),
                           wait=w, inc=[(chs, 16)])
                    wdma_n = chs.n
                    ii = bf[:, sb:sb + 1024].rearrange("p (c k) -> p c k", c=8)
                    oo = bf[:, tb:tb + 1024].rearrange("p (c k) -> p c k", c=8)
                    mmprev = chm.n if (oc * 3 + p3) >= 2 else 0
                    rec.do("sync", lambda s, i=ii, o=oo, wn=wdma_n, mp=mmprev, cs=chs: (
                           s.wait_ge(cs.sem, wn), s.wait_ge(chm.sem, mp),
                           s.dma_start(o, i, transpose=True))[-1], inc=[(chx, 16)])
                    xbn = chx.n
                    for hin in range(8):
                        hc = p3 * 8 + hin
                        lhsT = bf[:, tb + hin * 128: tb + hin * 128 + 128]
                        for th in range(2):
                            ps_ = self.ps[4 + th]
                            rhs = bf[:, FH + hc * 1024 + th * 512: FH + hc * 1024 + th * 512 + 512]
                            wmm = [(chx, xbn)] if hin == 0 else []
                            rec.do("tensor", lambda t, o=ps_, l=lhsT, r=rhs, s=(hc == 0), e=(hc == 23):
                                   t.matmul(o[:, 0:512], l, r, start=s, stop=e),
                                   wait=wmm, inc=[(chm, 1)] if hc == 23 else [])
                for th in range(2):
                    rcol = XB + oc * T + tg2 * 1024 + th * 512
                    xcol = XA + oc * T + tg2 * 1024 + th * 512
                    rec.do("vector", lambda v, o=self.ps[4 + th], rc=rcol, xc=xcol: v.scalar_tensor_tensor(
                           bf[:, rc:rc + 512], o[:, 0:512], f32[:, FMS + 53:FMS + 54],
                           bf[:, xc:xc + 512], ALU.mult, ALU.add),
                           wait=[(chm, None)], inc=[(chv, 1)])
                raw_chv[oc] = chv.n
            for th in range(2):
                self.emit_ln_group(tg2 * 2 + th, XB, XA, raw_chv[7])
            self.rec.emit_block()
        if "x3" in self.debug:
            chd = self.ch["dma"]
            dh = self.dbg("h0", [128, 24 * 1024], BF16)
            rec.do("scalar", lambda s: s.dma_start(dh[:, :], bf[:, FH:FH + 24 * 1024]),
                   wait=[(chv, None)], inc=[(chd, 16)])
            dn = self.dbg("x3", [128, G * T], BF16)
            dr = self.dbg("x3_raw", [128, G * T], BF16)
            rec.do("scalar", lambda s: s.dma_start(dn[:, :], bf[:, XA:XA + G * T]),
                   wait=[(chv, None)], inc=[(chd, 16)])
            rec.do("scalar", lambda s: s.dma_start(dr[:, :], bf[:, XB:XB + G * T]), inc=[(chd, 16)])
            rec.do("scalar", lambda s: s.wait_ge(chd.sem, chd.n), bare=True)
        self.rec.emit_block()

    def phase_l1_proj(self):
        nc, rec, bf, f32 = self.nc, self.rec, self.bf, self.f32
        chm, cha, chv, chx, chw = self.ch["mm"], self.ch["act"], self.ch["dve"], self.ch["xb"], self.ch["wdma"]
        wt = self.inp["L1_attn_w_qkv"]
        xb_done = {}
        evac_done = {}
        for oc in range(24):
            par = oc % 2
            sb = WSA + par * 1024
            tb = WSA + 2048 + par * 1024
            chs = self.ch["wg" + str(par)]
            w = [(chx, xb_done[oc - 2][0])] if oc >= 2 else [(chx, chx.n)]
            src = bass.AP(wt, oc * 128 * D, [[D, 128], [1, D]])
            rec.do("gpsimd", lambda g, d=sb, s=src: g.dma_start(bf[:, d:d + 1024], s),
                   wait=w, inc=[(chs, 16)])
            wdma_n = chs.n
            mmprev = xb_done.get(oc - 2, (0, chm.n))[1]
            ii = bf[:, sb:sb + 1024].rearrange("p (c k) -> p c k", c=8)
            oo = bf[:, tb:tb + 1024].rearrange("p (c k) -> p c k", c=8)
            rec.do("sync", lambda s, i=ii, o=oo, wn=wdma_n, mp=mmprev, cs=chs: (
                   s.wait_ge(cs.sem, wn), s.wait_ge(chm.sem, mp),
                   s.dma_start(o, i, transpose=True))[-1], inc=[(chx, 16)])
            xbn = chx.n
            if oc < 8:
                dstb = QQ + oc * T
            elif oc < 16:
                dstb = KVO + (oc - 8) * T
            else:
                dstb = XB + (oc - 16) * T
            for tg in range(4):
                ppar = (oc * 4 + tg) % 2
                p0 = self.ps[ppar]
                wmm = [(chx, xbn)] if tg == 0 else []
                prev = oc * 4 + tg - 2
                if prev >= 0:
                    wmm.append((cha, evac_done.get(prev, 0)))
                for ic in range(8):
                    lhsT = bf[:, tb + ic * 128: tb + ic * 128 + 128]
                    rhs = bf[:, XA + ic * T + tg * 512: XA + ic * T + tg * 512 + 512]
                    rec.do("tensor", lambda t, o=p0, l=lhsT, r=rhs, st=(ic == 0), e=(ic == 7):
                           t.matmul(o[:, 0:512], l, r, start=st, stop=e),
                           wait=wmm if ic == 0 else [], inc=[(chm, 1)] if ic == 7 else [])
                dcol = dstb + tg * 512
                rec.do("scalar", lambda s, d=dcol, o=p0: s.activation(bf[:, d:d + 512], o[:, 0:512], AF.Copy),
                       wait=[(chm, None)], inc=[(cha, 1)])
                evac_done[oc * 4 + tg] = cha.n
            xb_done[oc] = (chx.n, chm.n)
        vdone = cha.n
        for dc in range(8):
            ii = self.bf[:, XB + dc * T: XB + (dc + 1) * T].rearrange("p (c k) -> p c k", c=16)
            oo = bass.AP(bf, OO + dc * 128, [[NBF, 128], [1024, 16], [1, 128]])
            rec.do("sync", lambda s, i=ii, o=oo: s.dma_start(o, i, transpose=True),
                   wait=[(cha, vdone)] if dc == 0 else [], inc=[(chx, 16)])
        self.rec.emit_block()

    def phase_l1_attn(self):
        nc, rec, bf, f32 = self.nc, self.rec, self.bf, self.f32
        chm, cha, chv, chx = self.ch["mm"], self.ch["act"], self.ch["dve"], self.ch["xb"]
        PT = AUX
        for qg in range(4):
            expd = {}
            avdone_prev = chm.n
            for c in range(16):
                par = c % 2
                psS = self.ps[par]
                w = []
                if (c - 2, "e") in expd:
                    w.append((cha, expd[(c - 2, "e")]))
                if c == 0:
                    w.append((chx, chx.n))
                for ic in range(8):
                    lhsT = bf[:, KVO + ic * T + c * 128: KVO + ic * T + c * 128 + 128]
                    rhs = bf[:, QQ + ic * T + qg * 512: QQ + ic * T + qg * 512 + 512]
                    rec.do("tensor", lambda t, o=psS, l=lhsT, r=rhs, st=(ic == 0), e=(ic == 7):
                           t.matmul(o[:, 0:512], l, r, start=st, stop=e),
                           wait=w if ic == 0 else [], inc=[(chm, 1)] if ic == 7 else [])
                w2 = [(chm, None)]
                if qg > 0 and c == 0:
                    w2.append((chm, avdone_prev))
                rec.do("scalar", lambda s, p=psS, pc=PT + c * 512: s.activation(
                       bf[:, pc:pc + 512], p[:, 0:512], AF.Exp,
                       bias=f32[:, FMS + 51:FMS + 52], scale=f32[:, FMS + 50:FMS + 51]),
                       wait=w2, inc=[(cha, 1)])
                expd[(c, "e")] = cha.n
            exp_all = cha.n
            for qt in range(4):
                for c in range(16):
                    lhsT = bf[:, PT + c * 512 + qt * 128: PT + c * 512 + qt * 128 + 128]
                    w = [(cha, exp_all)] if c == 0 else []
                    if c == 0:
                        w.append((chv, chv.n))
                    for dh in range(2):
                        rhs = bf[:, OO + c * 1024 + dh * 512: OO + c * 1024 + dh * 512 + 512]
                        rec.do("tensor", lambda t, o=self.ps[2 + dh], l=lhsT, r=rhs, st=(c == 0), e=(c == 15):
                               t.matmul(o[:, 0:512], l, r, start=st, stop=e),
                               wait=w if dh == 0 else [], inc=[(chm, 1)] if c == 15 else [])
                    rec.do("tensor", lambda t, l=lhsT, st=(c == 0), e=(c == 15):
                           t.matmul(self.ps[4][:, 0:1], l, bf[:, ONESCOL:ONESCOL + 1], start=st, stop=e),
                           inc=[(chm, 1)] if c == 15 else [])
                av_done = chm.n
                qi = qg * 4 + qt
                rec.do("vector", lambda v: v.reciprocal(f32[:, FMS + 104:FMS + 105], self.ps[4][:, 0:1]),
                       wait=[(chm, av_done)], inc=[(chv, 1)])
                for dh in range(2):
                    dcol = XB + qi * 1024 + dh * 512
                    rec.do("scalar", lambda s, d=dcol, o=self.ps[2 + dh]: s.activation(
                           bf[:, d:d + 512], o[:, 0:512], AF.Copy, scale=f32[:, FMS + 104:FMS + 105]),
                           wait=[(chv, None)], inc=[(cha, 1)] if dh == 1 else [])
                rec.do("vector", lambda v: v.engine_nop(), wait=[(cha, None)])
                rec.do("vector", lambda v: v.engine_nop(), inc=[(chv, 1)])
            self.rec.emit_block()
        if "l1attn" in self.debug:
            chd = self.ch["dma"]
            dob = self.dbg("otok_mid", [128, 16384], BF16)
            rec.do("scalar", lambda s: s.dma_start(dob[:, :], bf[:, XB:XB + 16384]),
                   wait=[(chv, None)], inc=[(chd, 16)])
            rec.do("scalar", lambda s: s.wait_ge(chd.sem, chd.n), bare=True)
        self.rec.emit_block()

    def phase_l1_out_ln(self):
        nc, rec, bf, f32 = self.nc, self.rec, self.bf, self.f32
        chm, cha, chv, chx, chw = self.ch["mm"], self.ch["act"], self.ch["dve"], self.ch["xb"], self.ch["wdma"]
        chd = self.ch["dma"]
        wt = self.inp["L1_attn_w_proj"]
        bt = self.inp["L1_attn_b_proj"]
        rec.do("gpsimd", lambda g: g.dma_start(f32[:, FMS + 96:FMS + 104],
               bass.AP(bt, 0, [[1, 128], [128, 8]]), allow_slow_non_contiguous=True), inc=[(chd, 16)])
        bias_done = chd.n
        xb_done = {}
        raw_chv = {}
        for oc in range(8):
            sb = AUX + (oc % 2) * 1024
            tb = (WSA + oc * 1024) if oc < 4 else (WSB + (oc - 4) * 1024)
            chs = self.ch["wg" + str(oc % 2)]
            w = [(chx, chx.n)] if oc < 2 else [(chx, xb_done[oc - 2][0])]
            rec.do("gpsimd", lambda g, d=sb, s=bass.AP(wt, oc * 128 * D, [[D, 128], [1, D]]):
                   g.dma_start(bf[:, d:d + 1024], s), wait=w, inc=[(chs, 16)])
            wdma_n = chs.n
            ii = bf[:, sb:sb + 1024].rearrange("p (c k) -> p c k", c=8)
            oo = bf[:, tb:tb + 1024].rearrange("p (c k) -> p c k", c=8)
            rec.do("sync", lambda s_, i=ii, o=oo, wn=wdma_n, cs=chs: (
                   s_.wait_ge(cs.sem, wn), s_.dma_start(o, i, transpose=True))[-1], inc=[(chx, 16)])
            xb_done[oc] = (chx.n, chm.n)
        tiles_ready = chx.n
        for ab in range(2):
            for r in range(2):
                for oc in range(8):
                    ppar = (ab * 16 + r * 8 + oc) % 2
                    p0 = self.ps[ppar]
                    tb = (WSA + oc * 1024) if oc < 4 else (WSB + (oc - 4) * 1024)
                    idx = ab * 16 + r * 8 + oc
                    wmm = [(chx, tiles_ready)] if idx == 0 else []
                    if idx - 2 >= 0:
                        wmm.append((chv, raw_chv.get(idx - 2, 0)))
                    for hc in range(8):
                        lhsT = bf[:, tb + hc * 128: tb + hc * 128 + 128]
                        rhs = bf[:, XB + (8 * r + hc) * 1024 + ab * 512: XB + (8 * r + hc) * 1024 + ab * 512 + 512]
                        rec.do("tensor", lambda t, o=p0, l=lhsT, rr=rhs, st=(hc == 0), e=(hc == 7):
                               t.matmul(o[:, 0:512], l, rr, start=st, stop=e),
                               wait=wmm if hc == 0 else [], inc=[(chm, 1)] if hc == 7 else [])
                    wst = [(chm, None)]
                    if idx == 0:
                        wst.append((chd, bias_done))
                    outap = bass.AP(bf, KVO + oc * T + 2 * ab * 512 + r, [[NBF, 128], [2, 512]])
                    xap = bass.AP(bf, XA + oc * T + 2 * ab * 512 + r, [[NBF, 128], [2, 512]])
                    rec.do("vector", lambda v, o=p0, oa=outap, xa_=xap, occ=oc: v.scalar_tensor_tensor(
                           oa, o[:, 0:512], f32[:, FMS + 96 + occ:FMS + 97 + occ], xa_,
                           ALU.add, ALU.add), wait=wst, inc=[(chv, 1)])
                    raw_chv[idx] = chv.n
            for th in range(2):
                self.emit_ln_group(2 * ab + th, KVO, XA, raw_chv[ab * 16 + 15])
            self.rec.emit_block()
        if "x4" in self.debug:
            dob = self.dbg("otok", [128, 16384], BF16)
            rec.do("scalar", lambda s: s.dma_start(dob[:, :], bf[:, XB:XB + 16384]),
                   wait=[(chv, None)], inc=[(chd, 16)])
            dn = self.dbg("x4", [128, G * T], BF16)
            dr = self.dbg("x4_raw", [128, G * T], BF16)
            rec.do("scalar", lambda s: s.dma_start(dn[:, :], bf[:, XA:XA + G * T]),
                   wait=[(chv, None)], inc=[(chd, 16)])
            rec.do("scalar", lambda s: s.dma_start(dr[:, :], bf[:, KVO:KVO + G * T]), inc=[(chd, 16)])
            rec.do("scalar", lambda s: s.wait_ge(chd.sem, chd.n), bare=True)
        self.rec.emit_block()

    def emit_quant(self, wt, nchunks, width, scratch, cells):
        nc, rec, bf, f32 = self.nc, self.rec, self.bf, self.f32
        chm, cha, chv, chx, chw, chd, chg = (self.ch["mm"], self.ch["act"], self.ch["dve"],
                                             self.ch["xb"], self.ch["wdma"], self.ch["dma"], self.ch["gp"])
        npieces = nchunks * (width // 1024)
        for eng in ("gpsimd", "scalar", "sync", "vector", "tensor"):
            rec.do(eng, lambda e, v=chd.n: e.wait_ge(chd.sem, v), bare=True)
            rec.do(eng, lambda e, v=chw.n: e.wait_ge(chw.sem, v), bare=True)
            rec.do(eng, lambda e, v=chx.n: e.wait_ge(chx.sem, v), bare=True)
        rec.do("gpsimd", lambda g: g.memset(f32[:, FMS + 112:FMS + 120], 0.0), inc=[(chg, 1)])
        mx_done = 0
        for pc in range(npieces):
            par = pc % 2
            base = FQA if par == 0 else FQB
            pbase = FQA if par == 0 else FQV
            col0 = 0 if par == 0 else 1024
            w = [(chv, mx_done), (chg, None)] if pc < 2 else [(chv, mx_done)]
            src = bass.AP(wt, pc * 128 * 1024 if width == 1024 else (pc // 4) * 128 * width + (pc % 4) * 1024,
                          [[width, 128], [1, 1024]])
            rec.do("gpsimd", lambda g, c0=col0, s=src: g.dma_start(f32[:, c0:c0 + 1024], s),
                   wait=w, inc=[(chd, 16)])
            rec.do("vector", lambda v, c0=col0: v.reduce_max(f32[:, FMS + 113:FMS + 114],
                   f32[:, c0:c0 + 1024], mybir.AxisListType.X, apply_absolute_value=True),
                   wait=[(chd, None)])

            rec.do("vector", lambda v: v.tensor_max(f32[:, FMS + 112:FMS + 113],
                   f32[:, FMS + 112:FMS + 113], f32[:, FMS + 113:FMS + 114]),
                   wait=[(chg, None)] if pc == 0 else [], inc=[(chv, 1)])
            mx_done = chv.n
        QX = FMS + 200
        self.rec.emit_block()
        rec.do("vector", lambda v: v.tensor_copy(f32[:, FQB:FQB + 4], f32[:, FMS + 112:FMS + 116]),
               wait=[(chv, mx_done)], inc=[(chv, 1)])
        rec.do("gpsimd", lambda s_, v=chv.n: (s_.wait_ge(chv.sem, v),
               s_.dma_start(bass.AP(self.rtrip, 0, [[512, 1], [1, 512]]), f32[:, FQB:FQB + 4]))[-1],
               inc=[(chd, 16)])
        rec.do("gpsimd", lambda s_, dv=chd.n: (s_.wait_ge(chd.sem, dv),
               s_.dma_start(f32[:, FQV:FQV + 512],
               bass.AP(self.rtrip, 0, [[0, 128], [1, 512]])))[-1], inc=[(chd, 16)])
        dnr = chd.n
        self.rec.emit_block()
        for j in range(4):
            rec.do("vector", lambda v, jj=j: v.reduce_max(f32[:, QX + jj:QX + jj + 1],
                   f32[:, FQV:FQV + 512], mybir.AxisListType.X),
                   wait=[(chd, dnr)])
        for j in range(4):
            rec.do("vector", lambda v, jj=j: v.tensor_copy(f32[:, QX + 24 + jj:QX + 25 + jj],
                   f32[:, FMS + 61:FMS + 62]))
        rec.do("vector", lambda v: v.tensor_copy(f32[:, FQD:FQD + 512], f32[:, FQV:FQV + 512]))
        rec.do("vector", lambda v, cc=cells: v.tensor_mul(f32[:, cc:cc + 4], f32[:, QX:QX + 4],
               f32[:, QX + 24:QX + 28]))
        rec.do("vector", lambda v: v.tensor_copy(f32[:, FQD:FQD + 512], f32[:, FQV:FQV + 512]))
        self.rec.emit_block()
        rec.do("vector", lambda v, cc=cells: v.reciprocal(f32[:, cc + 4:cc + 8], f32[:, cc:cc + 4]))
        rec.do("vector", lambda v: v.tensor_copy(f32[:, FQD:FQD + 512], f32[:, FQV:FQV + 512]))
        self.rec.emit_block()
        rec.do("vector", lambda v, cc=cells: v.tensor_scalar_mul(f32[:, cc + 8:cc + 12],
               f32[:, cc + 4:cc + 8], 2.0), inc=[(chv, 1)])
        self.rec.emit_block()
        if "qearly" in self.debug:
            dcf = self.dbg("qcf_early", [128, 8], F32)
            rec.do("vector", lambda v: v.tensor_copy(f32[:, QX + 8:QX + 12], f32[:, QX:QX + 4]), inc=[(chv, 1)])
            rec.do("scalar", lambda s_, cc=cells, v=chv.n: (s_.wait_ge(chv.sem, v),
                   s_.dma_start(dcf[:, 0:4], f32[:, cc:cc + 4]))[-1], inc=[(chd, 16)])
            rec.do("scalar", lambda s_: s_.dma_start(dcf[:, 4:8], f32[:, QX + 32:QX + 36]), inc=[(chd, 16)])
        if "qearly" in self.debug:
            dqe = self.dbg("qearly", [128, 90], F32)
            rec.do("scalar", lambda s_: s_.dma_start(dqe[:, 0:56], f32[:, FMS + 200:FMS + 256]),
                   wait=[(chv, None)], inc=[(chd, 16)])
            dbc = self.dbg("qbc", [128, 130], F32)
            rec.do("vector", lambda v: v.tensor_copy(f32[:, FQV:FQV + 128], self.ps[6][:, 0:128]),
                   inc=[(chv, 1)])
            rec.do("scalar", lambda s_: s_.dma_start(dbc[:, 0:128], f32[:, FQV:FQV + 128]),
                   wait=[(chv, None)], inc=[(chd, 16)])
            rec.do("scalar", lambda s_: s_.dma_start(dbc[:, 128:129], f32[:, FQA:FQA + 1], allow_slow_non_contiguous=True), inc=[(chd, 16)])
            rec.do("scalar", lambda s_: s_.dma_start(dbc[:, 129:130], f32[:, FMS + 112:FMS + 113], allow_slow_non_contiguous=True), inc=[(chd, 16)])
        cells_done = chv.n
        for pc in range(npieces):
            w = [(chv, cells_done)]
            if pc >= 1:
                w.append((chv, None))
                w.append((chx, None))
            src = bass.AP(wt, pc * 128 * 1024 if width == 1024 else (pc // 4) * 128 * width + (pc % 4) * 1024,
                          [[width, 128], [1, 1024]])
            rec.do("gpsimd", lambda g, s=src: g.dma_start(f32[:, 0:1024], s),
                   wait=w + [(chx, chx.n), (chw, chw.n)], inc=[(chd, 16)])
            dn = chd.n
            sb = WSB + (pc % 2) * 1024
            for hh in range(2):
                c0 = hh * 512
                wch = [(chd, dn), (chx, chx.n), (chw, chw.n)] if hh == 0 else []
                rec.do("vector", lambda v, cc=c0, cell=cells: v.tensor_scalar_mul(
                       f32[:, FQV:FQV + 512], f32[:, cc:cc + 512], f32[:, cell + 4:cell + 5]),
                       wait=wch)
                rec.do("vector", lambda v: v.tensor_copy(self.i32[:, 0:512], f32[:, FQV:FQV + 512]))
                rec.do("vector", lambda v: v.tensor_copy(f32[:, FQC:FQC + 512], self.i32[:, 0:512]))
                rec.do("vector", lambda v: v.tensor_sub(f32[:, FQD:FQD + 512], f32[:, FQV:FQV + 512],
                       f32[:, FQC:FQC + 512]))
                rec.do("vector", lambda v: v.scalar_tensor_tensor(f32[:, FST:FST + 512],
                       f32[:, FQD:FQD + 512], 0.5, f32[:, FQC:FQC + 512], ALU.is_ge, ALU.add))
                rec.do("vector", lambda v: v.tensor_single_scalar(f32[:, FQV:FQV + 512],
                       f32[:, FQD:FQD + 512], -0.5, ALU.is_le))
                rec.do("vector", lambda v, s_=sb, hh_=hh: v.tensor_sub(bf[:, s_ + hh_ * 512:s_ + hh_ * 512 + 512],
                       f32[:, FST:FST + 512], f32[:, FQV:FQV + 512]), inc=[(chv, 1)])
            qv = chv.n
            ii = bf[:, sb:sb + 1024].rearrange("p (c k) -> p c k", c=8)
            oo = bf[:, WSA + (pc % 2) * 1024:WSA + (pc % 2) * 1024 + 1024].rearrange("p (c k) -> p c k", c=8)
            rec.do("sync", lambda s_, i=ii, o=oo, v=qv, wv=chw.n, dv=chd.n: (s_.wait_ge(chv.sem, v),
                   s_.wait_ge(chw.sem, wv), s_.wait_ge(chd.sem, dv),
                   s_.dma_start(o, i, transpose=True))[-1], inc=[(chx, 16)])
            xn = chx.n
            if width == 1024:
                dst = bass.AP(scratch, pc * 128 * 1024, [[1024, 128], [1, 1024]])
            else:
                dst = bass.AP(scratch, (pc // 4) * 128 * 4096 + (pc % 4) * 1024, [[4096, 128], [1, 1024]])
            rec.do("gpsimd", lambda s_, d=dst, src2=bf[:, WSA + (pc % 2) * 1024:WSA + (pc % 2) * 1024 + 1024], xnn=xn:
                   (s_.wait_ge(chx.sem, xnn), s_.dma_start(d, src2))[-1], inc=[(chw, 16)])
            if pc == 16 and "qearly" in self.debug:
                dmid = self.dbg("qmid", [128, 16], F32)
                rec.do("scalar", lambda s_, cc=cells: s_.dma_start(dmid[:, 0:12], f32[:, cc:cc + 12]),
                       inc=[(chd, 16)])
            self.rec.emit_block()

    def phase_l1_ffn(self):
        nc, rec, bf, f32 = self.nc, self.rec, self.bf, self.f32
        chm, cha, chv, chx, chw, chd, chg = (self.ch["mm"], self.ch["act"], self.ch["dve"],
                                             self.ch["xb"], self.ch["wdma"], self.ch["dma"], self.ch["gp"])
        self.emit_quant(self.inp["L1_ffn_W_fc"], 32, 1024, self.wfcT, FMS + 232)
        self.rec.emit_block()
        self.emit_quant(self.inp["L1_ffn_W_proj"], 8, 4096, self.wprT, FMS + 244)
        self.rec.emit_block()
        AFC = AUX + 512
        APR = AUX + 640
        BST = AUX
        L1T = AUX + 1152
        L2T = AUX + 1664
        rec.do("gpsimd", lambda g: g.dma_start(
               bf[:, AFC:AFC + 128].rearrange("p (b f) -> p b f", b=8),
               bass.AP(self.inp["AfcT"], 0, [[16, 128], [2048, 8], [1, 16]])), inc=[(chw, 16)])
        rec.do("gpsimd", lambda g: g.dma_start(
               bf[:, APR:APR + 512].rearrange("p (b f) -> p b f", b=32),
               bass.AP(self.inp["AprT"], 0, [[16, 128], [2048, 32], [1, 16]])), inc=[(chw, 16)])
        rec.do("gpsimd", lambda g: g.dma_start(f32[:, FMS + 128:FMS + 160],
               bass.AP(self.inp["L1_ffn_b_fc"], 0, [[1, 128], [128, 32]]),
               allow_slow_non_contiguous=True), inc=[(chw, 16)])
        rec.do("gpsimd", lambda g: g.dma_start(f32[:, FMS + 160:FMS + 168],
               bass.AP(self.inp["L1_ffn_b_proj"], 0, [[1, 128], [128, 8]]),
               allow_slow_non_contiguous=True), inc=[(chw, 16)])
        consts_done = chw.n
        raw_chv = 0
        h2_marker = {}
        for tg in range(4):
            for ic in range(8):
                lhsT = bf[:, AFC + ic * 16: AFC + ic * 16 + 16]
                rhs = bf[:, XA + ic * T + tg * 512: XA + ic * T + tg * 512 + 512]
                w = [(chw, consts_done)] if ic == 0 else []
                if ic == 0:
                    w.append((cha, cha.n))
                rec.do("tensor", lambda t, l=lhsT, r=rhs, st=(ic == 0), e=(ic == 7):
                       t.matmul(self.ps[4][0:16, 0:512], l, r, start=st, stop=e),
                       wait=w, inc=[(chm, 1)] if ic == 7 else [])
            rec.do("scalar", lambda s: s.activation(bf[0:16, L1T:L1T + 512], self.ps[4][0:16, 0:512],
                   AF.Copy, scale=f32[0:16, FMS + 240:FMS + 241]), wait=[(chm, None)], inc=[(cha, 1)])
            l1_done = cha.n
            self.rec.emit_block()
            evac_done = {}
            for oc in range(32):
                par = oc % 2
                wsb = WSA + par * 1024
                chs = self.ch["wg" + str(par)]
                chb = self.ch["bt" + str(par)]
                w = [(chm, chm.n)]
                if oc >= 2:
                    w.append((cha, evac_done[oc - 2]))
                rec.do("gpsimd", lambda g, d=wsb, o=oc: g.dma_start(bf[:, d:d + 1024],
                       bass.AP(self.wfcT, o * 128 * 1024, [[1024, 128], [1, 1024]])),
                       wait=w, inc=[(chs, 16)])
                rec.do("gpsimd", lambda g, o=oc, p2=par: g.dma_start(bf[0:16, BST + p2 * 128:BST + p2 * 128 + 128],
                       bass.AP(self.inp["BfcT"], o * 128, [[H1, 16], [1, 128]])), inc=[(chb, 16)])
                wns, bns = chs.n, chb.n
                p0 = self.ps[par]
                for ic in range(8):
                    lhsT = bf[:, wsb + ic * 128: wsb + ic * 128 + 128]
                    rhs = bf[:, XA + ic * T + tg * 512: XA + ic * T + tg * 512 + 512]
                    wm = [(chs, wns), (cha, l1_done)] if ic == 0 else []
                    rec.do("tensor", lambda t, o=p0, l=lhsT, r=rhs, st=(ic == 0):
                           t.matmul(o[:, 0:512], l, r, start=st, stop=False),
                           wait=wm, inc=[])
                rec.do("tensor", lambda t, o=p0, p2=par: t.matmul(o[:, 0:512],
                       bf[0:16, BST + p2 * 128:BST + p2 * 128 + 128], bf[0:16, L1T:L1T + 512],
                       start=False, stop=True), wait=[(chb, bns)], inc=[(chm, 1)])
                rec.do("scalar", lambda s, o=p0, oc_=oc: s.activation(
                       bf[:, QQ + oc_ * 512: QQ + oc_ * 512 + 512], o[:, 0:512], AF.Gelu,
                       scale=f32[:, FMS + 232:FMS + 233], bias=f32[:, FMS + 128 + oc_:FMS + 129 + oc_]),
                       wait=[(chm, None)], inc=[(cha, 1)])
                evac_done[oc] = cha.n
            h2_done = cha.n
            self.rec.emit_block()
            for hc in range(32):
                lhsT = bf[:, APR + hc * 16: APR + hc * 16 + 16]
                rhs = bf[:, QQ + hc * 512: QQ + hc * 512 + 512]
                w = [(cha, h2_done)] if hc == 0 else []
                rec.do("tensor", lambda t, l=lhsT, r=rhs, st=(hc == 0), e=(hc == 31):
                       t.matmul(self.ps[4][0:16, 0:512], l, r, start=st, stop=e),
                       wait=w, inc=[(chm, 1)] if hc == 31 else [])
            rec.do("scalar", lambda s: s.activation(bf[0:16, L2T:L2T + 512], self.ps[4][0:16, 0:512],
                   AF.Copy, scale=f32[0:16, FMS + 252:FMS + 253]), wait=[(chm, None)], inc=[(cha, 1)])
            l2_done = cha.n
            self.rec.emit_block()
            prev_evac = {}
            subdone = {}
            for oc in range(8):
                p0 = self.ps[oc % 2]
                for sub in range(4):
                    gsub = oc * 4 + sub
                    par = gsub % 2
                    wsb = WSA + 2048 + par * 1024
                    chs = self.ch["wg" + str(par)]
                    w = [(chm, subdone.get(gsub - 2, 0))]
                    if tg == 0 and gsub == 0:
                        w.append((chw, chw.n))
                    rec.do("gpsimd", lambda g, d=wsb, o=oc, s2=sub: g.dma_start(bf[:, d:d + 1024],
                           bass.AP(self.wprT, o * 128 * 4096 + s2 * 1024, [[4096, 128], [1, 1024]])),
                           wait=w, inc=[(chs, 16)])
                    wn = chs.n
                    for hin in range(8):
                        hc = sub * 8 + hin
                        lhsT = bf[:, wsb + hin * 128: wsb + hin * 128 + 128]
                        rhs = bf[:, QQ + hc * 512: QQ + hc * 512 + 512]
                        wm = [(chs, wn), (cha, l2_done)] if hin == 0 else []
                        if oc >= 2 and hin == 0 and sub == 0:
                            wm.append((chv, prev_evac[oc - 2]))
                        rec.do("tensor", lambda t, o=p0, l=lhsT, r=rhs, st=(hc == 0):
                               t.matmul(o[:, 0:512], l, r, start=st, stop=False),
                               wait=wm, inc=[(chm, 1)] if hin == 7 else [])
                    subdone[gsub] = chm.n
                chb = self.ch["bt0"]
                rec.do("gpsimd", lambda g, o=oc: g.dma_start(bf[0:16, BST + 256:BST + 384],
                       bass.AP(self.inp["BprT"], o * 128, [[D, 16], [1, 128]])),
                       wait=[(chm, chm.n)], inc=[(chb, 16)])
                bw = chb.n
                rec.do("tensor", lambda t, o=p0: t.matmul(o[:, 0:512],
                       bf[0:16, BST + 256:BST + 384],
                       bf[0:16, L2T:L2T + 512], start=False, stop=True),
                       wait=[(chb, bw)], inc=[(chm, 1)])
                t1 = A_PT + 1536
                rec.do("vector", lambda v, o=p0, oc_=oc: v.tensor_scalar(
                       bf[:, t1:t1 + 512] if False else bf[:, A_PT + 1536:A_PT + 2048], o[:, 0:512],
                       f32[:, FMS + 244:FMS + 245], f32[:, FMS + 160 + oc_:FMS + 161 + oc_], ALU.mult, ALU.add),
                       wait=[(chm, None)])
                rcol = XB + oc * T + tg * 512
                xcol = XA + oc * T + tg * 512
                rec.do("vector", lambda v, rc=rcol, xc=xcol: v.tensor_add(
                       bf[:, rc:rc + 512], bf[:, A_PT + 1536:A_PT + 2048], bf[:, xc:xc + 512]),
                       inc=[(chv, 1)])
                prev_evac[oc] = chv.n
                self.rec.emit_block()
            raw_chv = chv.n
            self.rec.emit_block()
            self.emit_ln_group(tg, XB, XA, raw_chv)
            self.rec.emit_block()
        if "x5" in self.debug:
            chd2 = self.ch["dma"]
            dn = self.dbg("x5", [128, G * T], BF16)
            dr = self.dbg("x5_raw", [128, G * T], BF16)
            dh2 = self.dbg("h2", [128, 16384], BF16)
            dc_ = self.dbg("cells", [128, 32], F32)
            rec.do("scalar", lambda s: s.dma_start(dn[:, :], bf[:, XA:XA + G * T]),
                   wait=[(chv, None)], inc=[(chd2, 16)])
            rec.do("scalar", lambda s: s.dma_start(dr[:, :], bf[:, XB:XB + G * T]), inc=[(chd2, 16)])
            rec.do("scalar", lambda s: s.dma_start(dh2[:, :], bf[:, QQ:QQ + 16384]), inc=[(chd2, 16)])
            rec.do("scalar", lambda s: s.dma_start(dc_[:, 0:24], f32[:, FMS + 232:FMS + 256]), inc=[(chd2, 16)])

            rec.do("scalar", lambda s: s.wait_ge(chd2.sem, chd2.n), bare=True)
        self.rec.emit_block()

    def phase_output(self):
        nc, rec, bf = self.nc, self.rec, self.bf
        chx, chd = self.ch["xb"], self.ch["dma"]
        for g in range(8):
            ii = bf[:, XA + g * T: XA + (g + 1) * T].rearrange("p (c k) -> p c k", c=16)
            oo = bass.AP(bf, XB + g * 128, [[NBF, 128], [1024, 16], [1, 128]])
            rec.do("sync", lambda s, i=ii, o=oo: s.dma_start(o, i, transpose=True), inc=[(chx, 16)])
        xall = chx.n
        for tc in range(16):
            src = bf[:, XB + tc * 1024: XB + (tc + 1) * 1024]
            dst = bass.AP(self.out, tc * 128 * D, [[D, 128], [1, D]])
            rec.do("gpsimd", lambda g, d=dst, s=src: g.dma_start(d, s),
                   wait=[(chx, xall)] if tc == 0 else [], inc=[(chd, 16)])
        rec.do("gpsimd", lambda g: g.wait_ge(chd.sem, chd.n), bare=True)
        self.rec.emit_block()




def kernel(**inputs):
    n = 8
    x_full = np.ascontiguousarray(inputs["L0_attn_x"])
    b = Builder(stage="all", debug=())
    b.build()
    consts = host_constants()
    lora = layout_lora(inputs)
    weight_names = [k for k in inputs.keys() if k != "L0_attn_x"]
    in_maps = []
    for c in range(n):
        m = {"x": np.ascontiguousarray(x_full[c])}
        for k in weight_names:
            m[k] = inputs[k]
        m.update(consts)
        m.update(lora)
        in_maps.append(m)
    res = run_bass_kernel_spmd(b.nc, in_maps, core_ids=list(range(n)))
    y = np.stack([np.asarray(res.results[c]["y"]) for c in range(n)], axis=0)
    return y.astype(np.float32)


# version 12
# speedup vs baseline: 1.0005x; 1.0005x over previous
import numpy as np
import ml_dtypes
import concourse.bass as bass
import concourse.mybir as mybir
from concourse.bass_utils import run_bass_kernel_spmd

F32 = mybir.dt.float32
BF16 = mybir.dt.bfloat16
I32 = mybir.dt.int32
AF = mybir.ActivationFunctionType
ALU = mybir.AluOpType
BF = ml_dtypes.bfloat16

T = 2048
D = 1024
G = 8
NH = 16
DH = 64
NKC = 16
NQG = 4
PIF = float(np.float32(np.pi))
TWO_PI = 2.0 * np.pi
CW1 = float(np.float32(TWO_PI)); CW2 = float(np.float32(TWO_PI - CW1))
CW3 = float(np.float32(TWO_PI - CW1 - CW2))
C_L0 = 8.0
C_L1 = 8.0
H0 = 3072
H1 = 4096

XA = 0
XB = 16384
KVO = 32768
QQ = 49152
OO = 65536
WSA = 81920
WSB = 86016
AUX = 90112
A_SIN = 90112
A_COS = 92160
A_RT = 94208
A_KVT = 90112
A_PT = 94208
A_LB = 96256
A_BC = 97280
CONST0 = 98304
ONESCOL = 98432
ONESROW = 98440
NBF = 98568

FQA = 0
FQB = 512
FQV = 1024
FQC = 1536
FQD = 2048
FST = 2560
FMS = 3072
NF32 = 3328
TB_TP = 0
TB_KT = 512
TB_TH = 1024
TB_T2 = 1536

NINT = 512


def host_constants():
    c = {}
    c["tpos"] = np.broadcast_to(np.arange(T, dtype=np.float64), (128, T)).astype(np.float32).copy()
    jmod = (np.arange(128) % 32).astype(np.float64)
    c["jarg"] = np.broadcast_to((-(jmod / 32.0)).astype(np.float32).reshape(128, 1), (128, 8)).copy()
    inv_freq = 1.0 / 10000.0 ** (np.arange(0, DH, 2, dtype=np.float64) / DH)
    theta = np.arange(T, dtype=np.float64)[None, :] * inv_freq[np.arange(128) % 32][:, None]
    c["ktab"] = np.round(theta / TWO_PI).astype(np.float32)
    c["ktabc"] = np.round((theta + np.pi / 2) / TWO_PI).astype(np.float32)
    r2t = np.zeros((128, 128), dtype=np.float32)
    for blk in range(2):
        o = 64 * blk
        for j in range(32):
            r2t[o + j + 32, o + j] = -1.0
            r2t[o + j, o + j + 32] = 1.0
    c["r2t"] = r2t.astype(BF)
    c["ident"] = np.eye(128, dtype=np.float32).astype(BF)
    c["identf32"] = np.eye(128, dtype=np.float32)
    ind2 = np.zeros((2, 128), dtype=np.float32)
    ind2[0, 0:64] = 1.0
    ind2[1, 64:128] = 1.0
    c["ind2"] = ind2
    c["ones1x128"] = np.ones((1, 128), dtype=np.float32)
    return c


def layout_lora(ins):
    return {
        "AfcT": np.ascontiguousarray(ins["L1_ffn_A_fc"].T),
        "BfcT": np.ascontiguousarray(ins["L1_ffn_B_fc"].T),
        "AprT": np.ascontiguousarray(ins["L1_ffn_A_proj"].T),
        "BprT": np.ascontiguousarray(ins["L1_ffn_B_proj"].T),
    }


class Chan:
    def __init__(self, sem):
        self.sem = sem
        self.n = 0


class Rec:
    ENGS = ("tensor", "vector", "scalar", "gpsimd", "sync")

    def __init__(self, nc):
        self.nc = nc
        self.ops = {e: [] for e in self.ENGS}
        self.last = {}
        self.drain = {}

    def do(self, eng, fn, wait=(), inc=(), bare=False):
        w = [(ch, (ch.n if v is None else v)) for ch, v in wait]
        ii = []
        for ch, k in inc:
            ch.n += k
            ii.append((ch, k))
        isdma = any(k == 16 for _, k in ii)
        dinc = None
        if not (bare or isdma) and eng in self.drain:
            w = [self.last.get(eng)] + w if self.last.get(eng) else w
            if ii:
                self.last[eng] = (ii[0][0], ii[0][0].n)
            else:
                dch = self.drain[eng]
                dch.n += 1
                dinc = (dch, 1)
                self.last[eng] = (dch, dch.n)

        def run(e, fn=fn, w=w, ii=ii, dinc=dinc):
            for ch, v in w:
                e.wait_ge(ch.sem, v)
            ins = fn(e)
            if dinc is not None:
                ins = ins.then_inc(dinc[0].sem, dinc[1])
            for ch, k in ii:
                ins = ins.then_inc(ch.sem, k)
        self.ops[eng].append(run)

    def emit_block(self):
        with self.nc.Block() as blk:
            for name in self.ENGS:
                ops = self.ops[name]
                if not ops:
                    continue
                getattr(blk, name)(lambda e, ops=ops: [f(e) for f in ops])
        self.ops = {e: [] for e in self.ENGS}


class Builder:
    def __init__(self, stage="all", debug=()):
        self.stage = stage
        self.debug = set(debug)
        self.nc = bass.Bass("TRN2", target_bir_lowering=False)
        nc = self.nc
        self.inp = {}
        self.dbg_out = {}
        self.inp["x"] = nc.dram_tensor("x", [T, D], F32, kind="ExternalInput")
        for nm, shp, dt in [
            ("L0_attn_Wq", [D, D], F32), ("L0_attn_Wkv", [D, D], F32),
            ("L0_attn_Wout", [D, D], F32), ("L0_attn_bout", [D], F32),
            ("L0_ffn_w_gate", [H0, D], I32), ("L0_ffn_w_up", [H0, D], I32),
            ("L0_ffn_w_down", [D, H0], I32),
            ("L1_attn_w_qkv", [3 * D, D], F32), ("L1_attn_w_proj", [D, D], F32),
            ("L1_attn_b_proj", [D], F32),
            ("L1_ffn_W_fc", [H1, D], F32), ("L1_ffn_b_fc", [H1], F32),
            ("L1_ffn_A_fc", [16, D], F32), ("L1_ffn_B_fc", [H1, 16], F32),
            ("L1_ffn_W_proj", [D, H1], F32), ("L1_ffn_b_proj", [D], F32),
            ("L1_ffn_A_proj", [16, H1], F32), ("L1_ffn_B_proj", [D, 16], F32),
        ]:
            self.inp[nm] = nc.dram_tensor(nm, shp, dt, kind="ExternalInput")
        for nm, shp in [("tpos", [128, T]), ("jarg", [128, 8]), ("ktab", [128, T]),
                        ("ktabc", [128, T]), ("ind2", [2, 128]), ("ones1x128", [1, 128])]:
            self.inp[nm] = nc.dram_tensor(nm, shp, F32, kind="ExternalInput")
        for nm, shp in [("r2t", [128, 128]), ("ident", [128, 128])]:
            self.inp[nm] = nc.dram_tensor(nm, shp, BF16, kind="ExternalInput")
        for nm, shp in [("AfcT", [D, 16]), ("BfcT", [16, H1]), ("AprT", [H1, 16]), ("BprT", [16, D]),
                        ("identf32", [128, 128])]:
            self.inp[nm] = nc.dram_tensor(nm, shp, F32, kind="ExternalInput")
        self.out = nc.dram_tensor("y", [T, D], F32, kind="ExternalOutput")
        self.wfcT = nc.dram_tensor("wfcT", [32, 128, 1024], BF16)
        self.wprT = nc.dram_tensor("wprT", [8, 128, 4096], BF16)
        self.rtrip = nc.dram_tensor("rtrip", [1, 512], F32)
        self.ostage = nc.dram_tensor("ostage", [T, D], BF16)

        self._ctx = []

    def dbg(self, name, shape, dtype=F32):
        t = self.nc.dram_tensor("dbg_" + name, shape, dtype, kind="ExternalOutput")
        self.dbg_out[name] = t
        return t

    def build(self):
        nc = self.nc
        import contextlib
        self._stack = contextlib.ExitStack()
        st = self._stack
        self.bf = st.enter_context(nc.sbuf_tensor("bfa", [128, NBF], BF16))
        self.f32 = st.enter_context(nc.sbuf_tensor("f32a", [128, NF32], F32))
        self.i32 = st.enter_context(nc.sbuf_tensor("i32a", [128, NINT], I32))
        self.ps = [st.enter_context(nc.psum_tensor(f"pb{i}", [128, 512], F32)) for i in range(7)]

        self.psb = st.enter_context(nc.psum_tensor("pbf", [128, 1024], BF16))
        self.ch = {}
        for nm in ["dma", "wdma", "mm", "act", "dve", "gp", "xb",
                   "wg0", "wg1", "wu0", "wu1", "bt0", "bt1"]:
            self.ch[nm] = Chan(st.enter_context(nc.semaphore("ch_" + nm)))
        self.rec = Rec(nc)
        for eng in ("vector", "scalar", "gpsimd"):
            self.rec.drain[eng] = Chan(st.enter_context(nc.semaphore("dr_" + eng)))

        self.phase_setup()
        if self.stage == "setup":
            self.finish(); return self
        if self.stage == "quant":
            chd = self.ch["dma"]
            self.emit_quant(self.inp["L1_ffn_W_fc"], 32, 1024, self.wfcT, FMS + 232)
            dq = self.dbg("qcells", [128, 16], F32)
            self.rec.do("scalar", lambda s: s.dma_start(dq[:, :], self.f32[:, FMS + 232:FMS + 248]),
                        wait=[(self.ch["chv"] if False else self.ch["dve"], None),
                              (self.ch["wdma"], None)], inc=[(chd, 16)])
            dw = self.dbg("qtiles", [128, 1024], mybir.dt.bfloat16)
            self.rec.do("gpsimd", lambda g: g.dma_start(dw[:, :],
                        bass.AP(self.wfcT, 0, [[1024, 128], [1, 1024]])), wait=[(self.ch["wdma"], None)], inc=[(chd, 16)])
            self.rec.do("scalar", lambda s: s.wait_ge(chd.sem, chd.n), bare=True)
            self.finish(); return self
        self.phase_l0_proj()
        if self.stage == "l0proj":
            self.finish(); return self
        self.phase_l0_attn()
        self.phase_l0_out_ln()
        if self.stage == "x2":
            self.finish(); return self
        self.phase_l0_ffn()
        if self.stage == "x3":
            self.finish(); return self
        self.phase_l1_proj()
        self.phase_l1_attn()
        self.phase_l1_out_ln()
        if self.stage == "x4":
            self.finish(); return self
        self.phase_l1_ffn()
        self.phase_output()
        self.finish()
        return self

    def finish(self):
        self.rec.emit_block()
        self._stack.close()

    def bfv(self, col, n, p0=0, np_=128):
        return self.bf[p0:p0 + np_, col:col + n]

    def xf(self, base, g, t0, nt, p0=0, np_=128):
        return self.bf[p0:p0 + np_, base + g * T + t0: base + g * T + t0 + nt]

    def phase_setup(self):
        nc, rec, ch, bf, f32 = self.nc, self.rec, self.ch, self.bf, self.f32
        inp = self.inp
        xsrc = inp["x"]
        chd = self.ch["dma"]
        for tc in range(16):
            dst = bf[:, XB + tc * 1024: XB + (tc + 1) * 1024]
            src = bass.AP(xsrc, tc * 128 * D, [[D, 128], [1, D]])
            rec.do("gpsimd", lambda g, d=dst, s=src: g.dma_start(d, s), inc=[(chd, 16)])
        rec.do("gpsimd", lambda g: g.dma_start(bf[:, CONST0:CONST0 + 128], inp["r2t"][:, :]), inc=[(chd, 16)])
        chg = self.ch["gp"]
        for off, val in ((48, -C_L0), (49, 1e-5), (50, 0.125), (51, -C_L1),
                         (52, 2.0), (53, 1.0 / 16129.0), (54, 1.0 / 127.0), (55, -0.5)):
            rec.do("gpsimd", lambda g, o=off, v=val: g.memset(f32[:, FMS + o:FMS + o + 1], v), inc=[(chg, 1)])
        rec.do("gpsimd", lambda g: g.memset(f32[:, FMS + 60:FMS + 61], 127.0), inc=[(chg, 1)])
        rec.do("gpsimd", lambda g: g.memset(bf[:, ONESCOL:ONESCOL + 8], 1.0), inc=[(chg, 1)])
        rec.do("gpsimd", lambda g: g.memset(bf[:, ONESROW:ONESROW + 128], 1.0), inc=[(chg, 1)])
        rec.do("gpsimd", lambda g: g.memset(f32[:, FMS:FMS + 1], 10000.0), inc=[(chg, 1)])
        rec.do("gpsimd", lambda g: g.dma_start(f32[:, FMS + 16:FMS + 24], inp["jarg"][:, :]), inc=[(chd, 16)])
        chm, cha, chv = self.ch["mm"], self.ch["act"], self.ch["dve"]
        rec.do("scalar", lambda s: s.activation(f32[:, FMS + 1:FMS + 2], f32[:, FMS:FMS + 1], AF.Ln),
               wait=[(chd, None), (self.ch["gp"], None)], inc=[(cha, 1)])
        rec.do("vector", lambda v: v.tensor_scalar_mul(f32[:, FMS + 17:FMS + 18], f32[:, FMS + 16:FMS + 17],
                                                       f32[:, FMS + 1:FMS + 2]),
               wait=[(cha, None), (chd, None)], inc=[(chv, 1)])
        rec.do("scalar", lambda s: s.activation(f32[:, FMS + 18:FMS + 19], f32[:, FMS + 17:FMS + 18], AF.Exp),
               wait=[(chv, None)], inc=[(cha, 1)])
        inv_col = FMS + 18
        rec.do("vector", lambda v: v.reciprocal(f32[:, FMS + 61:FMS + 62], f32[:, FMS + 60:FMS + 61]),
               wait=[(self.ch["gp"], None)], inc=[(chv, 1)])
        for qp in range(4):
            t0 = 512 * qp
            tp, kt, th, t2 = TB_TP, TB_KT, TB_TH, TB_T2
            rec.do("gpsimd", lambda g, t0=t0: g.dma_start(f32[:, tp:tp + 512],
                   bass.AP(inp["tpos"], t0, [[T, 128], [1, 512]])), wait=[(chv, None)], inc=[(chd, 16)])
            rec.do("gpsimd", lambda g, t0=t0: g.dma_start(f32[:, kt:kt + 512],
                   bass.AP(inp["ktab"], t0, [[T, 128], [1, 512]])), inc=[(chd, 16)])
            rec.do("vector", lambda v: v.tensor_scalar_mul(f32[:, th:th + 512], f32[:, tp:tp + 512],
                                                           f32[:, inv_col:inv_col + 1]),
                   wait=[(chd, None), (cha, None), (chv, None)], inc=[(chv, 1)])
            rec.do("vector", lambda v: v.scalar_tensor_tensor(f32[:, t2:t2 + 512], f32[:, kt:kt + 512],
                   -CW1, f32[:, th:th + 512], ALU.mult, ALU.add))
            rec.do("vector", lambda v: v.scalar_tensor_tensor(f32[:, tp:tp + 512], f32[:, kt:kt + 512],
                   -CW2, f32[:, t2:t2 + 512], ALU.mult, ALU.add))
            rec.do("vector", lambda v: v.scalar_tensor_tensor(f32[:, t2:t2 + 512], f32[:, kt:kt + 512],
                   -CW3, f32[:, tp:tp + 512], ALU.mult, ALU.add))
            rec.do("vector", lambda v: v.tensor_scalar_min(f32[:, tp:tp + 512], f32[:, t2:t2 + 512], PIF))
            rec.do("vector", lambda v: v.tensor_scalar_max(f32[:, t2:t2 + 512], f32[:, tp:tp + 512], -PIF),
                   inc=[(chv, 1)])
            rec.do("scalar", lambda s, t0=t0: s.activation(bf[:, A_SIN + t0:A_SIN + t0 + 512],
                   f32[:, t2:t2 + 512], AF.Sin), wait=[(chv, None)], inc=[(cha, 1)])
            rec.do("gpsimd", lambda g, t0=t0: g.dma_start(f32[:, kt:kt + 512],
                   bass.AP(inp["ktabc"], t0, [[T, 128], [1, 512]])), wait=[(cha, None)], inc=[(chd, 16)])
            rec.do("vector", lambda v: v.tensor_scalar_add(f32[:, tp:tp + 512], f32[:, th:th + 512],
                   float(np.float32(np.pi / 2))), wait=[(chd, None), (cha, None)])
            rec.do("vector", lambda v: v.scalar_tensor_tensor(f32[:, t2:t2 + 512], f32[:, kt:kt + 512],
                   -CW1, f32[:, tp:tp + 512], ALU.mult, ALU.add))
            rec.do("vector", lambda v: v.scalar_tensor_tensor(f32[:, tp:tp + 512], f32[:, kt:kt + 512],
                   -CW2, f32[:, t2:t2 + 512], ALU.mult, ALU.add))
            rec.do("vector", lambda v: v.scalar_tensor_tensor(f32[:, t2:t2 + 512], f32[:, kt:kt + 512],
                   -CW3, f32[:, tp:tp + 512], ALU.mult, ALU.add))
            rec.do("vector", lambda v: v.tensor_scalar_min(f32[:, tp:tp + 512], f32[:, t2:t2 + 512], PIF))
            rec.do("vector", lambda v: v.tensor_scalar_max(f32[:, t2:t2 + 512], f32[:, tp:tp + 512], -PIF),
                   inc=[(chv, 1)])
            rec.do("scalar", lambda s, t0=t0: s.activation(bf[:, A_COS + t0:A_COS + t0 + 512],
                   f32[:, t2:t2 + 512], AF.Sin), wait=[(chv, None)], inc=[(cha, 1)])
        chx = self.ch["xb"]
        for tc in range(16):
            inap = bf[:, XB + tc * 1024: XB + (tc + 1) * 1024].rearrange("p (g d) -> p g d", g=8)
            outap = bass.AP(self.bf, XA + tc * 128, [[NBF, 128], [T, 8], [1, 128]])
            rec.do("sync", lambda s, i=inap, o=outap: s.dma_start(o, i, transpose=True),
                   wait=[(chd, None)], inc=[(chx, 16)])
        chw = self.ch["wdma"]
        wq = inp["L0_attn_Wq"]
        for h, base in ((0, WSA), (1, WSB)):
            for ic in range(8):
                dst = bf[:, base + ic * 512: base + (ic + 1) * 512]
                src = bass.AP(wq, ic * 128 * D + h * 512, [[D, 128], [1, 512]])
                rec.do("gpsimd", lambda g, d=dst, s=src: g.dma_start(d, s), inc=[(chw, 16)])
        if "tabdeep" in self.debug:
            dinv = self.dbg("invf", [128, 8], F32)
            dth = self.dbg("theta", [128, 512], F32)
            rec.do("scalar", lambda s: s.dma_start(dinv[:, 0:8], f32[:, inv_col:inv_col + 8]),
                   wait=[(self.ch["act"], None)], inc=[(chd, 16)])
            rec.do("scalar", lambda s: s.dma_start(dth[:, :], f32[:, TB_TH:TB_TH + 512]), inc=[(chd, 16)])
            dj = self.dbg("jchain", [128, 8], F32)
            rec.do("vector", lambda v: v.tensor_copy(f32[:, FMS + 300:FMS + 301], self.ps[0][:, 0:1]), wait=[(self.ch["mm"], None)])
            rec.do("vector", lambda v: v.tensor_copy(f32[:, FMS + 301:FMS + 302], f32[:, FMS + 16:FMS + 17]))
            rec.do("vector", lambda v: v.tensor_copy(f32[:, FMS + 302:FMS + 303], f32[:, FMS + 17:FMS + 18]).then_inc(self.ch["dve"].sem, 1))
            rec.do("scalar", lambda s: s.dma_start(dj[:, 0:3], f32[:, FMS + 300:FMS + 303]), wait=[(self.ch["dve"], self.ch["dve"].n + 1)], inc=[(chd, 16)])
        if "tables" in self.debug:
            dsin = self.dbg("sin", [128, T], BF16); dcos = self.dbg("cos", [128, T], BF16)
            dxa = self.dbg("xa", [128, G * T], BF16); dxb = self.dbg("xb", [128, 16384], BF16)
            rec.do("scalar", lambda s: s.dma_start(dsin[:, :], bf[:, A_SIN:A_SIN + T]),
                   wait=[(self.ch["act"], None)], inc=[(chd, 16)])
            rec.do("scalar", lambda s: s.dma_start(dcos[:, :], bf[:, A_COS:A_COS + T]), inc=[(chd, 16)])
            rec.do("scalar", lambda s: s.dma_start(dxa[:, :], bf[:, XA:XA + 16384]),
                   wait=[(chx, None)], inc=[(chd, 16)])
            rec.do("scalar", lambda s: s.dma_start(dxb[:, :], bf[:, XB:XB + 16384]), inc=[(chd, 16)])
            rec.do("scalar", lambda s: s.wait_ge(chd.sem, chd.n), bare=True)
        self.rec.emit_block()

    def phase_l0_proj(self):
        nc, rec, bf = self.nc, self.rec, self.bf
        chm, cha, chv, chw = self.ch["mm"], self.ch["act"], self.ch["dve"], self.ch["wdma"]
        evac1_done = {}
        it = 0
        for proj, dstbase in (("q", QQ), ("kv", KVO)):
            if proj == "kv":
                mm_after_q = chm.n
                wkv = self.inp["L0_attn_Wkv"]
                for h, base in ((0, WSA), (1, WSB)):
                    for ic in range(8):
                        dst = bf[:, base + ic * 512: base + (ic + 1) * 512]
                        src = bass.AP(wkv, ic * 128 * D + h * 512, [[D, 128], [1, 512]])
                        rec.do("gpsimd", lambda g, d=dst, s=src: g.dma_start(d, s),
                               wait=[(chm, mm_after_q)] if ic == 0 and h == 0 else [], inc=[(chw, 16)])
            for oc in range(8):
                wbase = WSA if oc < 4 else WSB
                ocol = (oc % 4) * 128
                for tg in range(4):
                    par = it % 2
                    p0 = self.ps[par]
                    p1 = self.ps[2 + par]
                    w_pe = [(chw, None), (self.ch["xb"], None)]
                    if it - 2 in evac1_done:
                        w_pe.append((cha, evac1_done[it - 2]))
                    for ic in range(8):
                        lhsT = bf[:, wbase + ic * 512 + ocol: wbase + ic * 512 + ocol + 128]
                        rhs = bf[:, XA + ic * T + tg * 512: XA + ic * T + tg * 512 + 512]
                        rec.do("tensor", lambda t, o=p0, l=lhsT, r=rhs, s=(ic == 0), e=(ic == 7):
                               t.matmul(o[:, 0:512], l, r, start=s, stop=e),
                               wait=w_pe if ic == 0 else [],
                               inc=[(chm, 1)] if ic == 7 else [])
                    dcol = dstbase + oc * T + tg * 512
                    rec.do("scalar", lambda s, d=dcol, o=p0: s.activation(bf[:, d:d + 512], o[:, 0:512], AF.Copy),
                           wait=[(chm, None)], inc=[(cha, 1)])
                    rec.do("tensor", lambda t, d=dcol, o=p1: t.matmul(o[:, 0:512],
                           bf[:, CONST0:CONST0 + 128], bf[:, d:d + 512], start=True, stop=True),
                           wait=[(cha, None)], inc=[(chm, 1)])
                    t1c = A_RT + par * 512
                    rec.do("vector", lambda v, d=dcol, t1=t1c, tg=tg: v.tensor_mul(
                           bf[:, t1:t1 + 512], bf[:, d:d + 512], bf[:, A_COS + tg * 512: A_COS + tg * 512 + 512]),
                           wait=[(cha, None)])
                    swc = A_RT + 1024 + par * 512
                    rec.do("scalar", lambda s, o=p1, sw=swc: s.activation(bf[:, sw:sw + 512], o[:, 0:512], AF.Copy),
                           wait=[(chm, None), (chv, None)], inc=[(cha, 1)])
                    evac1_done[it] = cha.n
                    rec.do("vector", lambda v, sw=swc, tg=tg: v.tensor_mul(
                           bf[:, sw:sw + 512], bf[:, sw:sw + 512], bf[:, A_SIN + tg * 512: A_SIN + tg * 512 + 512]),
                           wait=[(cha, None)])
                    rec.do("vector", lambda v, d=dcol, t1=t1c, sw=swc: v.tensor_add(
                           bf[:, d:d + 512], bf[:, t1:t1 + 512], bf[:, sw:sw + 512]), inc=[(chv, 1)])
                    it += 1
        mm_all = chm.n
        wout = self.inp["L0_attn_Wout"]
        for h, base in ((0, WSA), (1, WSB)):
            for ic in range(8):
                dst = bf[:, base + ic * 512: base + (ic + 1) * 512]
                src = bass.AP(wout, ic * 128 * D + h * 512, [[D, 128], [1, 512]])
                rec.do("gpsimd", lambda g, d=dst, s=src: g.dma_start(d, s),
                       wait=[(chm, mm_all)] if ic == 0 and h == 0 else [], inc=[(chw, 16)])
        if "l0proj" in self.debug:
            dt2 = self.dbg("tiles", [128, 4096], BF16)
            rec.do("scalar", lambda s: s.dma_start(dt2[:, 0:2048], bf[:, A_RT:A_RT + 2048]),
                   wait=[(chv, None)], inc=[(self.ch["dma"], 16)])
            rec.do("scalar", lambda s: s.dma_start(dt2[:, 2048:4096], bf[:, A_SIN:A_SIN + 2048]), inc=[(self.ch["dma"], 16)])
            dws = self.dbg("wsa", [128, 8192], BF16)
            rec.do("scalar", lambda s: s.dma_start(dws[:, 0:4096], bf[:, WSA:WSA + 4096]),
                   wait=[(chw, None)], inc=[(self.ch["dma"], 16)])
            rec.do("scalar", lambda s: s.dma_start(dws[:, 4096:8192], bf[:, WSB:WSB + 4096]), inc=[(self.ch["dma"], 16)])
            dq = self.dbg("qrot", [128, G * T], BF16)
            dkv = self.dbg("kvrot", [128, G * T], BF16)
            rec.do("scalar", lambda s: s.dma_start(dq[:, :], bf[:, QQ:QQ + G * T]),
                   wait=[(chv, None)], inc=[(self.ch["dma"], 16)])
            rec.do("scalar", lambda s: s.dma_start(dkv[:, :], bf[:, KVO:KVO + G * T]), inc=[(self.ch["dma"], 16)])
            rec.do("scalar", lambda s: s.wait_ge(self.ch["dma"].sem, self.ch["dma"].n), bare=True)
        self.rec.emit_block()

    def phase_l0_attn(self):
        nc, rec, bf, f32 = self.nc, self.rec, self.bf, self.f32
        chm, cha, chv, chx, chg = self.ch["mm"], self.ch["act"], self.ch["dve"], self.ch["xb"], self.ch["gp"]
        rec.do("gpsimd", lambda g: g.memset(bf[:, A_KVT:A_KVT + 4096], 0.0), inc=[(chg, 1)])
        rec.do("gpsimd", lambda g: g.memset(bass.AP(bf, A_KVT + 64, [[NBF, 128], [256, 16]]), 1.0), inc=[(chg, 1)])
        rec.do("gpsimd", lambda g: g.memset(bass.AP(bf, A_KVT + 128, [[NBF, 128], [256, 16]]), 1.0), inc=[(chg, 1)])
        memset_done = chg.n
        expd = {}
        avd = {}
        evacd = 0
        xbar_n0 = chx.n
        for pr in range(8):
            wv = [(chg, memset_done)]
            if pr > 0:
                wv.append((chm, chm.n))
            for half, (p0, dcol) in enumerate(((0, A_KVT), (64, A_KVT + 192))):
                inap = bass.AP(bf, KVO + pr * T, [[NBF, 64], [128, 16], [1, 128]], ) if False else None
                src = self.bf[p0:p0 + 64, KVO + pr * T: KVO + (pr + 1) * T].rearrange("p (c k) -> p c k", c=16)
                dst = bass.AP(bf, dcol, [[NBF, 128], [256, 16], [1, 64]])
                rec.do("sync", lambda s, d=dst, i=src: s.dma_start(d, i, transpose=True),
                       wait=wv if half == 0 else [], inc=[(chx, 16)])
            for qg in range(4):
                ev_wait = evacd
                for c in range(16):
                    par = c % 2
                    psA, psB = self.ps[par], self.ps[2 + par]
                    w = []
                    if (0, par) in expd:
                        w.append((cha, expd[(0, par)]))
                    if (1, par) in expd:
                        w.append((cha, expd[(1, par)]))
                    if c == 0:
                        w.append((chv, ev_wait))
                        w.append((chx, chx.n))
                    qcol = QQ + pr * T + qg * 512
                    kcol = KVO + pr * T + c * 128
                    rec.do("tensor", lambda t, o=psA, k=kcol, q=qcol: t.matmul(
                           o[:, 0:512], bf[0:64, k:k + 128], bf[0:64, q:q + 512],
                           start=True, stop=True, tile_position=(0, 0)),
                           wait=w, inc=[(chm, 1)])
                    rec.do("tensor", lambda t, o=psB, k=kcol, q=qcol: t.matmul(
                           o[:, 0:512], bf[64:128, k:k + 128], bf[64:128, q:q + 512],
                           start=True, stop=True, tile_position=(64, 0)),
                           inc=[(chm, 1)])
                    mm_after_s = chm.n
                    for h, ps_ in ((0, psA), (1, psB)):
                        ptcol = A_PT + (2 * par + h) * 512
                        w2 = [(chm, mm_after_s)]
                        if c >= 2 and (c - 2) in avd:
                            w2.append((chm, avd[c - 2]))
                        rec.do("scalar", lambda s, p=ps_, pc=ptcol: s.activation(
                               bf[:, pc:pc + 512], p[:, 0:512], AF.Exp,
                               bias=f32[:, FMS + 48:FMS + 49], scale=f32[:, FMS + 50:FMS + 51]),
                               wait=w2, inc=[(cha, 1)])
                        expd[(h, par)] = cha.n
                    for h, avp in ((0, self.ps[4]), (1, self.ps[5])):
                        ptcol = A_PT + (2 * par + h) * 512
                        kv_l = A_KVT + c * 256 + (0 if h == 0 else 128)
                        rec.do("tensor", lambda t, o=avp, kl=kv_l, pc=ptcol, s0=(c == 0), s1=(c == 15):
                               t.matmul(o[:, 0:512], bf[:, kl:kl + 128], bf[:, pc:pc + 512],
                                        start=s0, stop=s1),
                               wait=[(cha, expd[(h, par)])], inc=[(chm, 1)])
                    avd[c] = chm.n
                av_stop = chm.n
                rec.do("vector", lambda v: v.reciprocal(f32[64:65, FQA:FQA + 512], self.ps[4][64:65, 0:512]),
                       wait=[(chm, av_stop)])
                rec.do("vector", lambda v: v.reciprocal(f32[0:1, FQA:FQA + 512], self.ps[5][0:1, 0:512]))
                rec.do("vector", lambda v: v.tensor_copy(bf[64:65, A_LB:A_LB + 512], f32[64:65, FQA:FQA + 512]))
                rec.do("vector", lambda v: v.tensor_copy(bf[0:1, A_LB:A_LB + 512], f32[0:1, FQA:FQA + 512]),
                       inc=[(chv, 1)])
                rows_ready = chv.n
                rec.do("tensor", lambda t: t.matmul(self.ps[0][:, 0:512], bf[64:65, ONESROW:ONESROW + 128],
                       bf[64:65, A_LB:A_LB + 512], start=True, stop=True),
                       wait=[(chv, rows_ready)], inc=[(chm, 1)])
                rec.do("tensor", lambda t: t.matmul(self.ps[1][:, 0:512], bf[0:1, ONESROW:ONESROW + 128],
                       bf[0:1, A_LB:A_LB + 512], start=True, stop=True), inc=[(chm, 1)])
                bc_done = chm.n
                rec.do("scalar", lambda s: s.activation(bf[:, A_BC:A_BC + 512], self.ps[0][:, 0:512], AF.Copy),
                       wait=[(chm, bc_done)])
                rec.do("scalar", lambda s: s.activation(bf[:, A_BC + 512:A_BC + 1024], self.ps[1][:, 0:512], AF.Copy),
                       inc=[(cha, 1)])
                bce = cha.n
                ocol = OO + pr * T + qg * 512
                rec.do("vector", lambda v, oc=ocol: v.tensor_mul(bf[0:64, oc:oc + 512],
                       self.ps[4][0:64, 0:512], bf[0:64, A_BC:A_BC + 512]),
                       wait=[(cha, bce)])
                rec.do("vector", lambda v, oc=ocol: v.tensor_mul(bf[64:128, oc:oc + 512],
                       self.ps[5][64:128, 0:512], bf[64:128, A_BC + 512:A_BC + 1024]),
                       inc=[(chv, 1)])
                evacd = chv.n
            self.rec.emit_block()
        if "l0attn" in self.debug:
            do_ = self.dbg("of", [128, G * T], BF16)
            rec.do("scalar", lambda s: s.dma_start(do_[:, :], bf[:, OO:OO + G * T]),
                   wait=[(chv, None)], inc=[(self.ch["dma"], 16)])
            rec.do("scalar", lambda s: s.wait_ge(self.ch["dma"].sem, self.ch["dma"].n), bare=True)
        self.rec.emit_block()

    def phase_l0_out_ln(self):
        self.emit_proj_residual_ln(src=OO, wname=None, bias="L0_attn_bout",
                                   stream=XA, rawdst=XB, newdst=XA, dbgkey="x2")

    def emit_proj_residual_ln(self, src, wname, bias, stream, rawdst, newdst, dbgkey):
        nc, rec, bf, f32 = self.nc, self.rec, self.bf, self.f32
        chm, cha, chv, chd = self.ch["mm"], self.ch["act"], self.ch["dve"], self.ch["dma"]
        chw = self.ch["wdma"]
        bt = self.inp[bias]
        rec.do("gpsimd", lambda g: g.dma_start(f32[:, FMS + 32:FMS + 40],
               bass.AP(bt, 0, [[1, 128], [128, 8]]), allow_slow_non_contiguous=True), inc=[(chd, 16)])
        bias_done = chd.n
        raw_done = {}
        for tg in range(4):
            for oc in range(8):
                par = (tg * 8 + oc) % 2
                p0 = self.ps[par]
                wbase = WSA if oc < 4 else WSB
                ocol = (oc % 4) * 128
                w = [(chw, None)] if (tg == 0 and oc == 0) else []
                if (tg, oc) not in (None,):
                    prev = tg * 8 + oc - 2
                    if prev >= 0:
                        w.append((chv, raw_done.get(prev, 0)))
                for ic in range(8):
                    lhsT = bf[:, wbase + ic * 512 + ocol: wbase + ic * 512 + ocol + 128]
                    rhs = bf[:, src + ic * T + tg * 512: src + ic * T + tg * 512 + 512]
                    rec.do("tensor", lambda t, o=p0, l=lhsT, r=rhs, s=(ic == 0), e=(ic == 7):
                           t.matmul(o[:, 0:512], l, r, start=s, stop=e),
                           wait=w if ic == 0 else [], inc=[(chm, 1)] if ic == 7 else [])
                rcol = rawdst + oc * T + tg * 512
                xcol = stream + oc * T + tg * 512
                wst = [(chm, None)]
                if tg == 0 and oc == 0:
                    wst.append((chd, bias_done))
                rec.do("vector", lambda v, o=p0, rc=rcol, xc=xcol, occ=oc: v.scalar_tensor_tensor(
                       bf[:, rc:rc + 512], o[:, 0:512], f32[:, FMS + 32 + occ:FMS + 33 + occ],
                       bf[:, xc:xc + 512], ALU.add, ALU.add),
                       wait=wst, inc=[(chv, 1)])
                raw_done[tg * 8 + oc] = chv.n
            self.emit_ln_group(tg, rawdst, newdst, raw_done[tg * 8 + 7])
        if dbgkey in self.debug:
            dr = self.dbg(dbgkey + "_raw", [128, G * T], BF16)
            dn = self.dbg(dbgkey, [128, G * T], BF16)
            rec.do("scalar", lambda s: s.dma_start(dr[:, :], bf[:, rawdst:rawdst + G * T]),
                   wait=[(chv, None)], inc=[(chd, 16)])
            rec.do("scalar", lambda s: s.dma_start(dn[:, :], bf[:, newdst:newdst + G * T]), inc=[(chd, 16)])
            rec.do("scalar", lambda s: s.wait_ge(chd.sem, chd.n), bare=True)
        self.rec.emit_block()

    def emit_ln_group(self, tg, rawdst, newdst, raw_ready_chv):
        rec, bf, f32 = self.rec, self.bf, self.f32
        chm, cha, chv = self.ch["mm"], self.ch["act"], self.ch["dve"]
        for ic in range(8):
            rcol = rawdst + ic * T + tg * 512
            w = [(chv, raw_ready_chv)] if ic == 0 else []
            rec.do("tensor", lambda t, rc=rcol, s=(ic == 0), e=(ic == 7): t.matmul(
                   self.ps[6][0:1, 0:512], bf[:, ONESCOL:ONESCOL + 1], bf[:, rc:rc + 512],
                   start=s, stop=e), wait=w, inc=[(chm, 1)] if ic == 7 else [])
        sqmm = {}
        for ic in range(8):
            rcol = rawdst + ic * T + tg * 512
            sq = A_PT + (ic % 2) * 512
            w = [(chv, raw_ready_chv)] if ic == 0 else []
            if ic >= 2:
                w.append((chm, sqmm[ic - 2]))
            rec.do("scalar", lambda s_, rc=rcol, q=sq: s_.activation(bf[:, q:q + 512], bf[:, rc:rc + 512],
                   AF.Square), wait=w, inc=[(cha, 1)])
            rec.do("tensor", lambda t, q=sq, s=(ic == 0), e=(ic == 7): t.matmul(
                   self.ps[5][0:1, 0:512], bf[:, ONESCOL:ONESCOL + 1], bf[:, q:q + 512],
                   start=s, stop=e), wait=[(cha, None)], inc=[(chm, 1)])
            sqmm[ic] = chm.n
        sums_done = chm.n
        rec.do("vector", lambda v: v.tensor_scalar_mul(f32[0:1, FQA:FQA + 512], self.ps[6][0:1, 0:512],
               1.0 / 1024.0), wait=[(chm, sums_done)])
        rec.do("vector", lambda v: v.tensor_scalar_mul(f32[0:1, FQB:FQB + 512], self.ps[5][0:1, 0:512],
               1.0 / 1024.0))
        rec.do("vector", lambda v: v.tensor_mul(f32[0:1, FQC:FQC + 512], f32[0:1, FQA:FQA + 512],
               f32[0:1, FQA:FQA + 512]))
        rec.do("vector", lambda v: v.tensor_sub(f32[0:1, FQB:FQB + 512], f32[0:1, FQB:FQB + 512],
               f32[0:1, FQC:FQC + 512]), inc=[(chv, 1)])
        rec.do("scalar", lambda s: s.activation(f32[0:1, FQC:FQC + 512], f32[0:1, FQB:FQB + 512],
               AF.Ln, bias=f32[0:1, FMS + 49:FMS + 50]), wait=[(chv, None)])
        rec.do("scalar", lambda s: s.activation(f32[0:1, FQB:FQB + 512], f32[0:1, FQC:FQC + 512],
               AF.Exp, scale=f32[0:1, FMS + 55:FMS + 56]), inc=[(cha, 1)])
        rec.do("vector", lambda v: v.engine_nop(), wait=[(cha, None)])
        rec.do("vector", lambda v: v.tensor_copy(bf[0:1, A_LB:A_LB + 512], f32[0:1, FQA:FQA + 512]))
        rec.do("vector", lambda v: v.tensor_copy(bf[0:1, A_LB + 512:A_LB + 1024],
               f32[0:1, FQB:FQB + 512]), inc=[(chv, 1)])
        rows_done = chv.n
        rec.do("tensor", lambda t: t.matmul(self.ps[2][:, 0:512], bf[0:1, ONESROW:ONESROW + 128],
               bf[0:1, A_LB:A_LB + 512], start=True, stop=True), wait=[(chv, rows_done)], inc=[(chm, 1)])
        rec.do("tensor", lambda t: t.matmul(self.ps[3][:, 0:512], bf[0:1, ONESROW:ONESROW + 128],
               bf[0:1, A_LB + 512:A_LB + 1024], start=True, stop=True), inc=[(chm, 1)])
        bc = chm.n
        rec.do("scalar", lambda s: s.activation(bf[:, A_BC:A_BC + 512], self.ps[2][:, 0:512], AF.Copy),
               wait=[(chm, bc), (chv, None)])
        rec.do("scalar", lambda s: s.activation(bf[:, A_BC + 512:A_BC + 1024], self.ps[3][:, 0:512], AF.Copy),
               inc=[(cha, 1)])
        bce = cha.n
        for ic in range(8):
            rcol = rawdst + ic * T + tg * 512
            ncol = newdst + ic * T + tg * 512
            tmp = A_PT + 1024
            rec.do("vector", lambda v, rc=rcol, tm=tmp: v.tensor_sub(bf[:, tm:tm + 512],
                   bf[:, rc:rc + 512], bf[:, A_BC:A_BC + 512]),
                   wait=[(cha, bce)] if ic == 0 else [])
            rec.do("vector", lambda v, nc_=ncol, tm=tmp: v.tensor_mul(bf[:, nc_:nc_ + 512],
                   bf[:, tm:tm + 512], bf[:, A_BC + 512:A_BC + 1024]),
                   inc=[(chv, 1)] if ic == 7 else [])

    def phase_l0_ffn(self):
        nc, rec, bf, f32 = self.nc, self.rec, self.bf, self.f32
        chm, cha, chv, chx, chw = self.ch["mm"], self.ch["act"], self.ch["dve"], self.ch["xb"], self.ch["wdma"]
        FH = KVO
        wg_t, wu_t, wd_t = self.inp["L0_ffn_w_gate"], self.inp["L0_ffn_w_up"], self.inp["L0_ffn_w_down"]
        NH0 = 24
        stg = {"g": WSA, "u": WSB, "d": WSA}
        for tg2 in range(2):
            ln_gate = cha.n
            xb_done = {}
            for hc in range(NH0):
                par = hc % 2
                for nm, wt, psg in (("g", wg_t, (self.ps[0], self.ps[1])), ("u", wu_t, (self.ps[2], self.ps[3]))):
                    sb = stg[nm] + par * 1024
                    tb = stg[nm] + 2048 + par * 1024
                    chs = self.ch[("wg" if nm == "g" else "wu") + str(par)]
                    if hc >= 2:
                        w = [(chx, xb_done[(nm, hc - 2)][0])]
                    else:
                        w = [(chx, chx.n)]
                    src = bass.AP(wt, hc * 128 * D, [[D, 128], [1, D]])
                    rec.do("gpsimd", lambda g, d=sb, s=src: g.dma_start(bf[:, d:d + 1024], s),
                           wait=w, inc=[(chs, 16)])
                    wdma_n = chs.n
                    mmprev = xb_done.get((nm, hc - 2), (0, chm.n))[1]
                    ii = bf[:, sb:sb + 1024].rearrange("p (c k) -> p c k", c=8)
                    oo = bf[:, tb:tb + 1024].rearrange("p (c k) -> p c k", c=8)
                    rec.do("sync", lambda s, i=ii, o=oo, wn=wdma_n, mp=mmprev, cs=chs: (
                           s.wait_ge(cs.sem, wn), s.wait_ge(chm.sem, mp),
                           s.dma_start(o, i, transpose=True))[-1], inc=[(chx, 16)])
                    xbn = chx.n
                    for th in range(2):
                        ps_ = psg[th]
                        for dc in range(8):
                            lhsT = bf[:, tb + dc * 128: tb + dc * 128 + 128]
                            rhs = bf[:, XA + dc * T + tg2 * 1024 + th * 512: XA + dc * T + tg2 * 1024 + th * 512 + 512]
                            wmm = [(chx, xbn)] if dc == 0 else []
                            if dc == 0 and hc == 0 and th == 0 and nm == "g":
                                wmm.append((cha, ln_gate))
                            if dc == 0 and th == 0 and nm == "g" and hc >= 1:
                                wmm.append((chv, chv.n))
                                wmm.append((cha, cha.n))
                            rec.do("tensor", lambda t, o=ps_, l=lhsT, r=rhs, s=(dc == 0), e=(dc == 7):
                                   t.matmul(o[:, 0:512], l, r, start=s, stop=e),
                                   wait=wmm, inc=[(chm, 1)] if dc == 7 else [])
                    xb_done[(nm, hc)] = (chx.n, chm.n)
                gu_done = chm.n
                for th in range(2):
                    gt = A_KVT + th * 512
                    rec.do("scalar", lambda s, g_=gt, p=self.ps[th]: s.activation(
                           bf[:, g_:g_ + 512], p[:, 0:512], AF.Silu, scale=f32[:, FMS + 54:FMS + 55]),
                           wait=[(chm, gu_done)], inc=[(cha, 1)])
                    hcol = FH + hc * 1024 + th * 512
                    rec.do("vector", lambda v, hcc=hcol, g_=gt, p=self.ps[2 + th]: v.tensor_mul(
                           bf[:, hcc:hcc + 512], p[:, 0:512], bf[:, g_:g_ + 512]),
                           wait=[(cha, None)], inc=[(chv, 1)])
            hdone = chv.n
            raw_chv = {}
            for oc in range(8):
                dn_w = [(chv, hdone)]
                if oc >= 1:
                    dn_w.append((chv, raw_chv[oc - 1]))
                mm_start = chm.n
                for p3 in range(3):
                    par = p3 % 2
                    sb = stg["d"] + par * 1024
                    tb = stg["d"] + 2048 + par * 1024
                    w = list(dn_w) if p3 == 0 else []
                    if (oc * 3 + p3) >= 2:
                        w.append((chx, chx.n))
                    src = bass.AP(wd_t, oc * 128 * H0 + p3 * 1024, [[H0, 128], [1, 1024]])
                    chs = self.ch["wg" + str(par)]
                    rec.do("gpsimd", lambda g, d=sb, s=src: g.dma_start(bf[:, d:d + 1024], s),
                           wait=w, inc=[(chs, 16)])
                    wdma_n = chs.n
                    ii = bf[:, sb:sb + 1024].rearrange("p (c k) -> p c k", c=8)
                    oo = bf[:, tb:tb + 1024].rearrange("p (c k) -> p c k", c=8)
                    mmprev = chm.n if (oc * 3 + p3) >= 2 else 0
                    rec.do("sync", lambda s, i=ii, o=oo, wn=wdma_n, mp=mmprev, cs=chs: (
                           s.wait_ge(cs.sem, wn), s.wait_ge(chm.sem, mp),
                           s.dma_start(o, i, transpose=True))[-1], inc=[(chx, 16)])
                    xbn = chx.n
                    for hin in range(8):
                        hc = p3 * 8 + hin
                        lhsT = bf[:, tb + hin * 128: tb + hin * 128 + 128]
                        for th in range(2):
                            ps_ = self.ps[4 + th]
                            rhs = bf[:, FH + hc * 1024 + th * 512: FH + hc * 1024 + th * 512 + 512]
                            wmm = [(chx, xbn)] if hin == 0 else []
                            rec.do("tensor", lambda t, o=ps_, l=lhsT, r=rhs, s=(hc == 0), e=(hc == 23):
                                   t.matmul(o[:, 0:512], l, r, start=s, stop=e),
                                   wait=wmm, inc=[(chm, 1)] if hc == 23 else [])
                for th in range(2):
                    rcol = XB + oc * T + tg2 * 1024 + th * 512
                    xcol = XA + oc * T + tg2 * 1024 + th * 512
                    rec.do("vector", lambda v, o=self.ps[4 + th], rc=rcol, xc=xcol: v.scalar_tensor_tensor(
                           bf[:, rc:rc + 512], o[:, 0:512], f32[:, FMS + 53:FMS + 54],
                           bf[:, xc:xc + 512], ALU.mult, ALU.add),
                           wait=[(chm, None)], inc=[(chv, 1)])
                raw_chv[oc] = chv.n
            for th in range(2):
                self.emit_ln_group(tg2 * 2 + th, XB, XA, raw_chv[7])
            self.rec.emit_block()
        if "x3" in self.debug:
            chd = self.ch["dma"]
            dh = self.dbg("h0", [128, 24 * 1024], BF16)
            rec.do("scalar", lambda s: s.dma_start(dh[:, :], bf[:, FH:FH + 24 * 1024]),
                   wait=[(chv, None)], inc=[(chd, 16)])
            dn = self.dbg("x3", [128, G * T], BF16)
            dr = self.dbg("x3_raw", [128, G * T], BF16)
            rec.do("scalar", lambda s: s.dma_start(dn[:, :], bf[:, XA:XA + G * T]),
                   wait=[(chv, None)], inc=[(chd, 16)])
            rec.do("scalar", lambda s: s.dma_start(dr[:, :], bf[:, XB:XB + G * T]), inc=[(chd, 16)])
            rec.do("scalar", lambda s: s.wait_ge(chd.sem, chd.n), bare=True)
        self.rec.emit_block()

    def phase_l1_proj(self):
        nc, rec, bf, f32 = self.nc, self.rec, self.bf, self.f32
        chm, cha, chv, chx, chw = self.ch["mm"], self.ch["act"], self.ch["dve"], self.ch["xb"], self.ch["wdma"]
        wt = self.inp["L1_attn_w_qkv"]
        xb_done = {}
        evac_done = {}
        for oc in range(24):
            par = oc % 2
            sb = WSA + par * 1024
            tb = WSA + 2048 + par * 1024
            chs = self.ch["wg" + str(par)]
            w = [(chx, xb_done[oc - 2][0])] if oc >= 2 else [(chx, chx.n)]
            src = bass.AP(wt, oc * 128 * D, [[D, 128], [1, D]])
            rec.do("gpsimd", lambda g, d=sb, s=src: g.dma_start(bf[:, d:d + 1024], s),
                   wait=w, inc=[(chs, 16)])
            wdma_n = chs.n
            mmprev = xb_done.get(oc - 2, (0, chm.n))[1]
            ii = bf[:, sb:sb + 1024].rearrange("p (c k) -> p c k", c=8)
            oo = bf[:, tb:tb + 1024].rearrange("p (c k) -> p c k", c=8)
            rec.do("sync", lambda s, i=ii, o=oo, wn=wdma_n, mp=mmprev, cs=chs: (
                   s.wait_ge(cs.sem, wn), s.wait_ge(chm.sem, mp),
                   s.dma_start(o, i, transpose=True))[-1], inc=[(chx, 16)])
            xbn = chx.n
            if oc < 8:
                dstb = QQ + oc * T
            elif oc < 16:
                dstb = KVO + (oc - 8) * T
            else:
                dstb = XB + (oc - 16) * T
            for tg in range(4):
                ppar = (oc * 4 + tg) % 2
                p0 = self.ps[ppar]
                wmm = [(chx, xbn)] if tg == 0 else []
                prev = oc * 4 + tg - 2
                if prev >= 0:
                    wmm.append((cha, evac_done.get(prev, 0)))
                for ic in range(8):
                    lhsT = bf[:, tb + ic * 128: tb + ic * 128 + 128]
                    rhs = bf[:, XA + ic * T + tg * 512: XA + ic * T + tg * 512 + 512]
                    rec.do("tensor", lambda t, o=p0, l=lhsT, r=rhs, st=(ic == 0), e=(ic == 7):
                           t.matmul(o[:, 0:512], l, r, start=st, stop=e),
                           wait=wmm if ic == 0 else [], inc=[(chm, 1)] if ic == 7 else [])
                dcol = dstb + tg * 512
                rec.do("scalar", lambda s, d=dcol, o=p0: s.activation(bf[:, d:d + 512], o[:, 0:512], AF.Copy),
                       wait=[(chm, None)], inc=[(cha, 1)])
                evac_done[oc * 4 + tg] = cha.n
            xb_done[oc] = (chx.n, chm.n)
        vdone = cha.n
        for dc in range(8):
            ii = self.bf[:, XB + dc * T: XB + (dc + 1) * T].rearrange("p (c k) -> p c k", c=16)
            oo = bass.AP(bf, OO + dc * 128, [[NBF, 128], [1024, 16], [1, 128]])
            rec.do("sync", lambda s, i=ii, o=oo: s.dma_start(o, i, transpose=True),
                   wait=[(cha, vdone)] if dc == 0 else [], inc=[(chx, 16)])
        self.rec.emit_block()

    def phase_l1_attn(self):
        nc, rec, bf, f32 = self.nc, self.rec, self.bf, self.f32
        chm, cha, chv, chx = self.ch["mm"], self.ch["act"], self.ch["dve"], self.ch["xb"]
        PT = AUX
        for qg in range(4):
            expd = {}
            avdone_prev = chm.n
            for c in range(16):
                par = c % 2
                psS = self.ps[par]
                w = []
                if (c - 2, "e") in expd:
                    w.append((cha, expd[(c - 2, "e")]))
                if c == 0:
                    w.append((chx, chx.n))
                for ic in range(8):
                    lhsT = bf[:, KVO + ic * T + c * 128: KVO + ic * T + c * 128 + 128]
                    rhs = bf[:, QQ + ic * T + qg * 512: QQ + ic * T + qg * 512 + 512]
                    rec.do("tensor", lambda t, o=psS, l=lhsT, r=rhs, st=(ic == 0), e=(ic == 7):
                           t.matmul(o[:, 0:512], l, r, start=st, stop=e),
                           wait=w if ic == 0 else [], inc=[(chm, 1)] if ic == 7 else [])
                w2 = [(chm, None)]
                if qg > 0 and c == 0:
                    w2.append((chm, avdone_prev))
                rec.do("scalar", lambda s, p=psS, pc=PT + c * 512: s.activation(
                       bf[:, pc:pc + 512], p[:, 0:512], AF.Exp,
                       bias=f32[:, FMS + 51:FMS + 52], scale=f32[:, FMS + 50:FMS + 51]),
                       wait=w2, inc=[(cha, 1)])
                expd[(c, "e")] = cha.n
            exp_all = cha.n
            for qt in range(4):
                for c in range(16):
                    lhsT = bf[:, PT + c * 512 + qt * 128: PT + c * 512 + qt * 128 + 128]
                    w = [(cha, exp_all)] if c == 0 else []
                    if c == 0:
                        w.append((chv, chv.n))
                    for dh in range(2):
                        rhs = bf[:, OO + c * 1024 + dh * 512: OO + c * 1024 + dh * 512 + 512]
                        rec.do("tensor", lambda t, o=self.ps[2 + dh], l=lhsT, r=rhs, st=(c == 0), e=(c == 15):
                               t.matmul(o[:, 0:512], l, r, start=st, stop=e),
                               wait=w if dh == 0 else [], inc=[(chm, 1)] if c == 15 else [])
                    rec.do("tensor", lambda t, l=lhsT, st=(c == 0), e=(c == 15):
                           t.matmul(self.ps[4][:, 0:1], l, bf[:, ONESCOL:ONESCOL + 1], start=st, stop=e),
                           inc=[(chm, 1)] if c == 15 else [])
                av_done = chm.n
                qi = qg * 4 + qt
                rec.do("vector", lambda v: v.reciprocal(f32[:, FMS + 104:FMS + 105], self.ps[4][:, 0:1]),
                       wait=[(chm, av_done)], inc=[(chv, 1)])
                for dh in range(2):
                    dcol = XB + qi * 1024 + dh * 512
                    rec.do("scalar", lambda s, d=dcol, o=self.ps[2 + dh]: s.activation(
                           bf[:, d:d + 512], o[:, 0:512], AF.Copy, scale=f32[:, FMS + 104:FMS + 105]),
                           wait=[(chv, None)], inc=[(cha, 1)] if dh == 1 else [])
                rec.do("vector", lambda v: v.engine_nop(), wait=[(cha, None)])
                rec.do("vector", lambda v: v.engine_nop(), inc=[(chv, 1)])
            self.rec.emit_block()
        if "l1attn" in self.debug:
            chd = self.ch["dma"]
            dob = self.dbg("otok_mid", [128, 16384], BF16)
            rec.do("scalar", lambda s: s.dma_start(dob[:, :], bf[:, XB:XB + 16384]),
                   wait=[(chv, None)], inc=[(chd, 16)])
            rec.do("scalar", lambda s: s.wait_ge(chd.sem, chd.n), bare=True)
        self.rec.emit_block()

    def phase_l1_out_ln(self):
        nc, rec, bf, f32 = self.nc, self.rec, self.bf, self.f32
        chm, cha, chv, chx, chw = self.ch["mm"], self.ch["act"], self.ch["dve"], self.ch["xb"], self.ch["wdma"]
        chd = self.ch["dma"]
        wt = self.inp["L1_attn_w_proj"]
        bt = self.inp["L1_attn_b_proj"]
        rec.do("gpsimd", lambda g: g.dma_start(f32[:, FMS + 96:FMS + 104],
               bass.AP(bt, 0, [[1, 128], [128, 8]]), allow_slow_non_contiguous=True), inc=[(chd, 16)])
        bias_done = chd.n
        xb_done = {}
        raw_chv = {}
        for oc in range(8):
            sb = AUX + (oc % 2) * 1024
            tb = (WSA + oc * 1024) if oc < 4 else (WSB + (oc - 4) * 1024)
            chs = self.ch["wg" + str(oc % 2)]
            w = [(chx, chx.n)] if oc < 2 else [(chx, xb_done[oc - 2][0])]
            rec.do("gpsimd", lambda g, d=sb, s=bass.AP(wt, oc * 128 * D, [[D, 128], [1, D]]):
                   g.dma_start(bf[:, d:d + 1024], s), wait=w, inc=[(chs, 16)])
            wdma_n = chs.n
            ii = bf[:, sb:sb + 1024].rearrange("p (c k) -> p c k", c=8)
            oo = bf[:, tb:tb + 1024].rearrange("p (c k) -> p c k", c=8)
            rec.do("sync", lambda s_, i=ii, o=oo, wn=wdma_n, cs=chs: (
                   s_.wait_ge(cs.sem, wn), s_.dma_start(o, i, transpose=True))[-1], inc=[(chx, 16)])
            xb_done[oc] = (chx.n, chm.n)
        tiles_ready = chx.n
        for ab in range(2):
            for r in range(2):
                for oc in range(8):
                    ppar = (ab * 16 + r * 8 + oc) % 2
                    p0 = self.ps[ppar]
                    tb = (WSA + oc * 1024) if oc < 4 else (WSB + (oc - 4) * 1024)
                    idx = ab * 16 + r * 8 + oc
                    wmm = [(chx, tiles_ready)] if idx == 0 else []
                    if idx - 2 >= 0:
                        wmm.append((chv, raw_chv.get(idx - 2, 0)))
                    for hc in range(8):
                        lhsT = bf[:, tb + hc * 128: tb + hc * 128 + 128]
                        rhs = bf[:, XB + (8 * r + hc) * 1024 + ab * 512: XB + (8 * r + hc) * 1024 + ab * 512 + 512]
                        rec.do("tensor", lambda t, o=p0, l=lhsT, rr=rhs, st=(hc == 0), e=(hc == 7):
                               t.matmul(o[:, 0:512], l, rr, start=st, stop=e),
                               wait=wmm if hc == 0 else [], inc=[(chm, 1)] if hc == 7 else [])
                    wst = [(chm, None)]
                    if idx == 0:
                        wst.append((chd, bias_done))
                    outap = bass.AP(bf, KVO + oc * T + 2 * ab * 512 + r, [[NBF, 128], [2, 512]])
                    xap = bass.AP(bf, XA + oc * T + 2 * ab * 512 + r, [[NBF, 128], [2, 512]])
                    rec.do("vector", lambda v, o=p0, oa=outap, xa_=xap, occ=oc: v.scalar_tensor_tensor(
                           oa, o[:, 0:512], f32[:, FMS + 96 + occ:FMS + 97 + occ], xa_,
                           ALU.add, ALU.add), wait=wst, inc=[(chv, 1)])
                    raw_chv[idx] = chv.n
            for th in range(2):
                self.emit_ln_group(2 * ab + th, KVO, XA, raw_chv[ab * 16 + 15])
            self.rec.emit_block()
        if "x4" in self.debug:
            dob = self.dbg("otok", [128, 16384], BF16)
            rec.do("scalar", lambda s: s.dma_start(dob[:, :], bf[:, XB:XB + 16384]),
                   wait=[(chv, None)], inc=[(chd, 16)])
            dn = self.dbg("x4", [128, G * T], BF16)
            dr = self.dbg("x4_raw", [128, G * T], BF16)
            rec.do("scalar", lambda s: s.dma_start(dn[:, :], bf[:, XA:XA + G * T]),
                   wait=[(chv, None)], inc=[(chd, 16)])
            rec.do("scalar", lambda s: s.dma_start(dr[:, :], bf[:, KVO:KVO + G * T]), inc=[(chd, 16)])
            rec.do("scalar", lambda s: s.wait_ge(chd.sem, chd.n), bare=True)
        self.rec.emit_block()

    def emit_quant(self, wt, nchunks, width, scratch, cells):
        nc, rec, bf, f32 = self.nc, self.rec, self.bf, self.f32
        chm, cha, chv, chx, chw, chd, chg = (self.ch["mm"], self.ch["act"], self.ch["dve"],
                                             self.ch["xb"], self.ch["wdma"], self.ch["dma"], self.ch["gp"])
        npieces = nchunks * (width // 1024)
        for eng in ("gpsimd", "scalar", "sync", "vector", "tensor"):
            rec.do(eng, lambda e, v=chd.n: e.wait_ge(chd.sem, v), bare=True)
            rec.do(eng, lambda e, v=chw.n: e.wait_ge(chw.sem, v), bare=True)
            rec.do(eng, lambda e, v=chx.n: e.wait_ge(chx.sem, v), bare=True)
        rec.do("gpsimd", lambda g: g.memset(f32[:, FMS + 112:FMS + 120], 0.0), inc=[(chg, 1)])
        mx_done = 0
        for pc in range(npieces):
            par = pc % 2
            base = FQA if par == 0 else FQB
            pbase = FQA if par == 0 else FQV
            col0 = 0 if par == 0 else 1024
            w = [(chv, mx_done), (chg, None)] if pc < 2 else [(chv, mx_done)]
            src = bass.AP(wt, pc * 128 * 1024 if width == 1024 else (pc // 4) * 128 * width + (pc % 4) * 1024,
                          [[width, 128], [1, 1024]])
            rec.do("gpsimd", lambda g, c0=col0, s=src: g.dma_start(f32[:, c0:c0 + 1024], s),
                   wait=w, inc=[(chd, 16)])
            rec.do("vector", lambda v, c0=col0: v.reduce_max(f32[:, FMS + 113:FMS + 114],
                   f32[:, c0:c0 + 1024], mybir.AxisListType.X, apply_absolute_value=True),
                   wait=[(chd, None)])

            rec.do("vector", lambda v: v.tensor_max(f32[:, FMS + 112:FMS + 113],
                   f32[:, FMS + 112:FMS + 113], f32[:, FMS + 113:FMS + 114]),
                   wait=[(chg, None)] if pc == 0 else [], inc=[(chv, 1)])
            mx_done = chv.n
        QX = FMS + 200
        self.rec.emit_block()
        rec.do("vector", lambda v: v.tensor_copy(f32[:, FQB:FQB + 4], f32[:, FMS + 112:FMS + 116]),
               wait=[(chv, mx_done)], inc=[(chv, 1)])
        rec.do("gpsimd", lambda s_, v=chv.n: (s_.wait_ge(chv.sem, v),
               s_.dma_start(bass.AP(self.rtrip, 0, [[512, 1], [1, 512]]), f32[:, FQB:FQB + 4]))[-1],
               inc=[(chd, 16)])
        rec.do("gpsimd", lambda s_, dv=chd.n: (s_.wait_ge(chd.sem, dv),
               s_.dma_start(f32[:, FQV:FQV + 512],
               bass.AP(self.rtrip, 0, [[0, 128], [1, 512]])))[-1], inc=[(chd, 16)])
        dnr = chd.n
        self.rec.emit_block()
        for j in range(4):
            rec.do("vector", lambda v, jj=j: v.reduce_max(f32[:, QX + jj:QX + jj + 1],
                   f32[:, FQV:FQV + 512], mybir.AxisListType.X),
                   wait=[(chd, dnr)])
        for j in range(4):
            rec.do("vector", lambda v, jj=j: v.tensor_copy(f32[:, QX + 24 + jj:QX + 25 + jj],
                   f32[:, FMS + 61:FMS + 62]))
        rec.do("vector", lambda v: v.tensor_copy(f32[:, FQD:FQD + 512], f32[:, FQV:FQV + 512]))
        rec.do("vector", lambda v, cc=cells: v.tensor_mul(f32[:, cc:cc + 4], f32[:, QX:QX + 4],
               f32[:, QX + 24:QX + 28]))
        rec.do("vector", lambda v: v.tensor_copy(f32[:, FQD:FQD + 512], f32[:, FQV:FQV + 512]))
        self.rec.emit_block()
        rec.do("vector", lambda v, cc=cells: v.reciprocal(f32[:, cc + 4:cc + 8], f32[:, cc:cc + 4]))
        rec.do("vector", lambda v: v.tensor_copy(f32[:, FQD:FQD + 512], f32[:, FQV:FQV + 512]))
        self.rec.emit_block()
        rec.do("vector", lambda v, cc=cells: v.tensor_scalar_mul(f32[:, cc + 8:cc + 12],
               f32[:, cc + 4:cc + 8], 2.0), inc=[(chv, 1)])
        self.rec.emit_block()
        if "qearly" in self.debug:
            dcf = self.dbg("qcf_early", [128, 8], F32)
            rec.do("vector", lambda v: v.tensor_copy(f32[:, QX + 8:QX + 12], f32[:, QX:QX + 4]), inc=[(chv, 1)])
            rec.do("scalar", lambda s_, cc=cells, v=chv.n: (s_.wait_ge(chv.sem, v),
                   s_.dma_start(dcf[:, 0:4], f32[:, cc:cc + 4]))[-1], inc=[(chd, 16)])
            rec.do("scalar", lambda s_: s_.dma_start(dcf[:, 4:8], f32[:, QX + 32:QX + 36]), inc=[(chd, 16)])
        if "qearly" in self.debug:
            dqe = self.dbg("qearly", [128, 90], F32)
            rec.do("scalar", lambda s_: s_.dma_start(dqe[:, 0:56], f32[:, FMS + 200:FMS + 256]),
                   wait=[(chv, None)], inc=[(chd, 16)])
            dbc = self.dbg("qbc", [128, 130], F32)
            rec.do("vector", lambda v: v.tensor_copy(f32[:, FQV:FQV + 128], self.ps[6][:, 0:128]),
                   inc=[(chv, 1)])
            rec.do("scalar", lambda s_: s_.dma_start(dbc[:, 0:128], f32[:, FQV:FQV + 128]),
                   wait=[(chv, None)], inc=[(chd, 16)])
            rec.do("scalar", lambda s_: s_.dma_start(dbc[:, 128:129], f32[:, FQA:FQA + 1], allow_slow_non_contiguous=True), inc=[(chd, 16)])
            rec.do("scalar", lambda s_: s_.dma_start(dbc[:, 129:130], f32[:, FMS + 112:FMS + 113], allow_slow_non_contiguous=True), inc=[(chd, 16)])
        cells_done = chv.n
        for pc in range(npieces):
            w = [(chv, cells_done)]
            if pc >= 1:
                w.append((chv, None))
                w.append((chx, None))
            src = bass.AP(wt, pc * 128 * 1024 if width == 1024 else (pc // 4) * 128 * width + (pc % 4) * 1024,
                          [[width, 128], [1, 1024]])
            rec.do("gpsimd", lambda g, s=src: g.dma_start(f32[:, 0:1024], s),
                   wait=w + [(chx, chx.n), (chw, chw.n)], inc=[(chd, 16)])
            dn = chd.n
            sb = WSB + (pc % 2) * 1024
            for hh in range(2):
                c0 = hh * 512
                wch = [(chd, dn), (chx, chx.n), (chw, chw.n)] if hh == 0 else []
                rec.do("vector", lambda v, cc=c0, cell=cells: v.tensor_scalar_mul(
                       f32[:, FQV:FQV + 512], f32[:, cc:cc + 512], f32[:, cell + 4:cell + 5]),
                       wait=wch)
                rec.do("vector", lambda v: v.tensor_copy(self.i32[:, 0:512], f32[:, FQV:FQV + 512]))
                rec.do("vector", lambda v: v.tensor_copy(f32[:, FQC:FQC + 512], self.i32[:, 0:512]))
                rec.do("vector", lambda v: v.tensor_sub(f32[:, FQD:FQD + 512], f32[:, FQV:FQV + 512],
                       f32[:, FQC:FQC + 512]))
                rec.do("vector", lambda v: v.scalar_tensor_tensor(f32[:, FST:FST + 512],
                       f32[:, FQD:FQD + 512], 0.5, f32[:, FQC:FQC + 512], ALU.is_ge, ALU.add))
                rec.do("vector", lambda v: v.tensor_single_scalar(f32[:, FQV:FQV + 512],
                       f32[:, FQD:FQD + 512], -0.5, ALU.is_le))
                rec.do("vector", lambda v, s_=sb, hh_=hh: v.tensor_sub(bf[:, s_ + hh_ * 512:s_ + hh_ * 512 + 512],
                       f32[:, FST:FST + 512], f32[:, FQV:FQV + 512]), inc=[(chv, 1)])
            qv = chv.n
            ii = bf[:, sb:sb + 1024].rearrange("p (c k) -> p c k", c=8)
            oo = bf[:, WSA + (pc % 2) * 1024:WSA + (pc % 2) * 1024 + 1024].rearrange("p (c k) -> p c k", c=8)
            rec.do("sync", lambda s_, i=ii, o=oo, v=qv, wv=chw.n, dv=chd.n: (s_.wait_ge(chv.sem, v),
                   s_.wait_ge(chw.sem, wv), s_.wait_ge(chd.sem, dv),
                   s_.dma_start(o, i, transpose=True))[-1], inc=[(chx, 16)])
            xn = chx.n
            if width == 1024:
                dst = bass.AP(scratch, pc * 128 * 1024, [[1024, 128], [1, 1024]])
            else:
                dst = bass.AP(scratch, (pc // 4) * 128 * 4096 + (pc % 4) * 1024, [[4096, 128], [1, 1024]])
            rec.do("gpsimd", lambda s_, d=dst, src2=bf[:, WSA + (pc % 2) * 1024:WSA + (pc % 2) * 1024 + 1024], xnn=xn:
                   (s_.wait_ge(chx.sem, xnn), s_.dma_start(d, src2))[-1], inc=[(chw, 16)])
            if pc == 16 and "qearly" in self.debug:
                dmid = self.dbg("qmid", [128, 16], F32)
                rec.do("scalar", lambda s_, cc=cells: s_.dma_start(dmid[:, 0:12], f32[:, cc:cc + 12]),
                       inc=[(chd, 16)])
            self.rec.emit_block()

    def phase_l1_ffn(self):
        nc, rec, bf, f32 = self.nc, self.rec, self.bf, self.f32
        chm, cha, chv, chx, chw, chd, chg = (self.ch["mm"], self.ch["act"], self.ch["dve"],
                                             self.ch["xb"], self.ch["wdma"], self.ch["dma"], self.ch["gp"])
        self.emit_quant(self.inp["L1_ffn_W_fc"], 32, 1024, self.wfcT, FMS + 232)
        self.rec.emit_block()
        self.emit_quant(self.inp["L1_ffn_W_proj"], 8, 4096, self.wprT, FMS + 244)
        self.rec.emit_block()
        AFC = AUX + 512
        APR = AUX + 640
        BST = AUX
        L1T = AUX + 1152
        L2T = AUX + 1664
        rec.do("gpsimd", lambda g: g.dma_start(
               bf[:, AFC:AFC + 128].rearrange("p (b f) -> p b f", b=8),
               bass.AP(self.inp["AfcT"], 0, [[16, 128], [2048, 8], [1, 16]])), inc=[(chw, 16)])
        rec.do("gpsimd", lambda g: g.dma_start(
               bf[:, APR:APR + 512].rearrange("p (b f) -> p b f", b=32),
               bass.AP(self.inp["AprT"], 0, [[16, 128], [2048, 32], [1, 16]])), inc=[(chw, 16)])
        rec.do("gpsimd", lambda g: g.dma_start(f32[:, FMS + 128:FMS + 160],
               bass.AP(self.inp["L1_ffn_b_fc"], 0, [[1, 128], [128, 32]]),
               allow_slow_non_contiguous=True), inc=[(chw, 16)])
        rec.do("gpsimd", lambda g: g.dma_start(f32[:, FMS + 160:FMS + 168],
               bass.AP(self.inp["L1_ffn_b_proj"], 0, [[1, 128], [128, 8]]),
               allow_slow_non_contiguous=True), inc=[(chw, 16)])
        consts_done = chw.n
        raw_chv = 0
        h2_marker = {}
        for tg in range(4):
            for ic in range(8):
                lhsT = bf[:, AFC + ic * 16: AFC + ic * 16 + 16]
                rhs = bf[:, XA + ic * T + tg * 512: XA + ic * T + tg * 512 + 512]
                w = [(chw, consts_done)] if ic == 0 else []
                if ic == 0:
                    w.append((cha, cha.n))
                rec.do("tensor", lambda t, l=lhsT, r=rhs, st=(ic == 0), e=(ic == 7):
                       t.matmul(self.ps[4][0:16, 0:512], l, r, start=st, stop=e),
                       wait=w, inc=[(chm, 1)] if ic == 7 else [])
            rec.do("scalar", lambda s: s.activation(bf[0:16, L1T:L1T + 512], self.ps[4][0:16, 0:512],
                   AF.Copy, scale=f32[0:16, FMS + 240:FMS + 241]), wait=[(chm, None)], inc=[(cha, 1)])
            l1_done = cha.n
            self.rec.emit_block()
            evac_done = {}
            for oc in range(32):
                par = oc % 2
                wsb = WSA + par * 1024
                chs = self.ch["wg" + str(par)]
                chb = self.ch["bt" + str(par)]
                w = [(chm, chm.n)]
                if oc >= 2:
                    w.append((cha, evac_done[oc - 2]))
                rec.do("gpsimd", lambda g, d=wsb, o=oc: g.dma_start(bf[:, d:d + 1024],
                       bass.AP(self.wfcT, o * 128 * 1024, [[1024, 128], [1, 1024]])),
                       wait=w, inc=[(chs, 16)])
                rec.do("gpsimd", lambda g, o=oc, p2=par: g.dma_start(bf[0:16, BST + p2 * 128:BST + p2 * 128 + 128],
                       bass.AP(self.inp["BfcT"], o * 128, [[H1, 16], [1, 128]])), inc=[(chb, 16)])
                wns, bns = chs.n, chb.n
                p0 = self.ps[par]
                for ic in range(8):
                    lhsT = bf[:, wsb + ic * 128: wsb + ic * 128 + 128]
                    rhs = bf[:, XA + ic * T + tg * 512: XA + ic * T + tg * 512 + 512]
                    wm = [(chs, wns), (cha, l1_done)] if ic == 0 else []
                    rec.do("tensor", lambda t, o=p0, l=lhsT, r=rhs, st=(ic == 0):
                           t.matmul(o[:, 0:512], l, r, start=st, stop=False),
                           wait=wm, inc=[])
                rec.do("tensor", lambda t, o=p0, p2=par: t.matmul(o[:, 0:512],
                       bf[0:16, BST + p2 * 128:BST + p2 * 128 + 128], bf[0:16, L1T:L1T + 512],
                       start=False, stop=True), wait=[(chb, bns)], inc=[(chm, 1)])
                rec.do("scalar", lambda s, o=p0, oc_=oc: s.activation(
                       bf[:, QQ + oc_ * 512: QQ + oc_ * 512 + 512], o[:, 0:512], AF.Gelu,
                       scale=f32[:, FMS + 232:FMS + 233], bias=f32[:, FMS + 128 + oc_:FMS + 129 + oc_]),
                       wait=[(chm, None)], inc=[(cha, 1)])
                evac_done[oc] = cha.n
            h2_done = cha.n
            self.rec.emit_block()
            for hc in range(32):
                lhsT = bf[:, APR + hc * 16: APR + hc * 16 + 16]
                rhs = bf[:, QQ + hc * 512: QQ + hc * 512 + 512]
                w = [(cha, h2_done)] if hc == 0 else []
                rec.do("tensor", lambda t, l=lhsT, r=rhs, st=(hc == 0), e=(hc == 31):
                       t.matmul(self.ps[4][0:16, 0:512], l, r, start=st, stop=e),
                       wait=w, inc=[(chm, 1)] if hc == 31 else [])
            rec.do("scalar", lambda s: s.activation(bf[0:16, L2T:L2T + 512], self.ps[4][0:16, 0:512],
                   AF.Copy, scale=f32[0:16, FMS + 252:FMS + 253]), wait=[(chm, None)], inc=[(cha, 1)])
            l2_done = cha.n
            self.rec.emit_block()
            prev_evac = {}
            subdone = {}
            for oc in range(8):
                p0 = self.ps[oc % 2]
                for sub in range(4):
                    gsub = oc * 4 + sub
                    par = gsub % 2
                    wsb = WSA + 2048 + par * 1024
                    chs = self.ch["wg" + str(par)]
                    w = [(chm, subdone.get(gsub - 2, 0))]
                    if tg == 0 and gsub == 0:
                        w.append((chw, chw.n))
                    rec.do("gpsimd", lambda g, d=wsb, o=oc, s2=sub: g.dma_start(bf[:, d:d + 1024],
                           bass.AP(self.wprT, o * 128 * 4096 + s2 * 1024, [[4096, 128], [1, 1024]])),
                           wait=w, inc=[(chs, 16)])
                    wn = chs.n
                    for hin in range(8):
                        hc = sub * 8 + hin
                        lhsT = bf[:, wsb + hin * 128: wsb + hin * 128 + 128]
                        rhs = bf[:, QQ + hc * 512: QQ + hc * 512 + 512]
                        wm = [(chs, wn), (cha, l2_done)] if hin == 0 else []
                        if oc >= 2 and hin == 0 and sub == 0:
                            wm.append((chv, prev_evac[oc - 2]))
                        rec.do("tensor", lambda t, o=p0, l=lhsT, r=rhs, st=(hc == 0):
                               t.matmul(o[:, 0:512], l, r, start=st, stop=False),
                               wait=wm, inc=[(chm, 1)] if hin == 7 else [])
                    subdone[gsub] = chm.n
                chb = self.ch["bt0"]
                rec.do("gpsimd", lambda g, o=oc: g.dma_start(bf[0:16, BST + 256:BST + 384],
                       bass.AP(self.inp["BprT"], o * 128, [[D, 16], [1, 128]])),
                       wait=[(chm, chm.n)], inc=[(chb, 16)])
                bw = chb.n
                rec.do("tensor", lambda t, o=p0: t.matmul(o[:, 0:512],
                       bf[0:16, BST + 256:BST + 384],
                       bf[0:16, L2T:L2T + 512], start=False, stop=True),
                       wait=[(chb, bw)], inc=[(chm, 1)])
                t1 = A_PT + 1536
                rec.do("vector", lambda v, o=p0, oc_=oc: v.tensor_scalar(
                       bf[:, t1:t1 + 512] if False else bf[:, A_PT + 1536:A_PT + 2048], o[:, 0:512],
                       f32[:, FMS + 244:FMS + 245], f32[:, FMS + 160 + oc_:FMS + 161 + oc_], ALU.mult, ALU.add),
                       wait=[(chm, None)])
                rcol = XB + oc * T + tg * 512
                xcol = XA + oc * T + tg * 512
                rec.do("vector", lambda v, rc=rcol, xc=xcol: v.tensor_add(
                       bf[:, rc:rc + 512], bf[:, A_PT + 1536:A_PT + 2048], bf[:, xc:xc + 512]),
                       inc=[(chv, 1)])
                prev_evac[oc] = chv.n
                self.rec.emit_block()
            raw_chv = chv.n
            self.rec.emit_block()
            self.emit_ln_group(tg, XB, XA, raw_chv)
            self.rec.emit_block()
        if "x5" in self.debug:
            chd2 = self.ch["dma"]
            dn = self.dbg("x5", [128, G * T], BF16)
            dr = self.dbg("x5_raw", [128, G * T], BF16)
            dh2 = self.dbg("h2", [128, 16384], BF16)
            dc_ = self.dbg("cells", [128, 32], F32)
            rec.do("scalar", lambda s: s.dma_start(dn[:, :], bf[:, XA:XA + G * T]),
                   wait=[(chv, None)], inc=[(chd2, 16)])
            rec.do("scalar", lambda s: s.dma_start(dr[:, :], bf[:, XB:XB + G * T]), inc=[(chd2, 16)])
            rec.do("scalar", lambda s: s.dma_start(dh2[:, :], bf[:, QQ:QQ + 16384]), inc=[(chd2, 16)])
            rec.do("scalar", lambda s: s.dma_start(dc_[:, 0:24], f32[:, FMS + 232:FMS + 256]), inc=[(chd2, 16)])

            rec.do("scalar", lambda s: s.wait_ge(chd2.sem, chd2.n), bare=True)
        self.rec.emit_block()

    def phase_output(self):
        nc, rec, bf = self.nc, self.rec, self.bf
        chx, chd = self.ch["xb"], self.ch["dma"]
        for g in range(8):
            ii = bf[:, XA + g * T: XA + (g + 1) * T].rearrange("p (c k) -> p c k", c=16)
            oo = bass.AP(bf, XB + g * 128, [[NBF, 128], [1024, 16], [1, 128]])
            rec.do("sync", lambda s, i=ii, o=oo: s.dma_start(o, i, transpose=True), inc=[(chx, 16)])
        xall = chx.n
        for tc in range(16):
            src = bf[:, XB + tc * 1024: XB + (tc + 1) * 1024]
            dst = bass.AP(self.out, tc * 128 * D, [[D, 128], [1, D]])
            rec.do("gpsimd", lambda g, d=dst, s=src: g.dma_start(d, s),
                   wait=[(chx, xall)] if tc == 0 else [], inc=[(chd, 16)])
        rec.do("gpsimd", lambda g: g.wait_ge(chd.sem, chd.n), bare=True)
        self.rec.emit_block()




def kernel(**inputs):
    n = 8
    x_full = np.ascontiguousarray(inputs["L0_attn_x"])
    b = Builder(stage="all", debug=())
    b.build()
    consts = host_constants()
    lora = layout_lora(inputs)
    weight_names = [k for k in inputs.keys() if k != "L0_attn_x"]
    in_maps = []
    for c in range(n):
        m = {"x": np.ascontiguousarray(x_full[c])}
        for k in weight_names:
            m[k] = inputs[k]
        m.update(consts)
        m.update(lora)
        in_maps.append(m)
    res = run_bass_kernel_spmd(b.nc, in_maps, core_ids=list(range(n)))
    y = np.stack([np.asarray(res.results[c]["y"]) for c in range(n)], axis=0)
    return y.astype(np.float32)


# version 13
# speedup vs baseline: 1.0133x; 1.0128x over previous
import numpy as np
import ml_dtypes
import concourse.bass as bass
import concourse.mybir as mybir
from concourse.bass_utils import run_bass_kernel_spmd

F32 = mybir.dt.float32
BF16 = mybir.dt.bfloat16
I32 = mybir.dt.int32
AF = mybir.ActivationFunctionType
ALU = mybir.AluOpType
BF = ml_dtypes.bfloat16

T = 2048
D = 1024
G = 8
NH = 16
DH = 64
NKC = 16
NQG = 4
PIF = float(np.float32(np.pi))
TWO_PI = 2.0 * np.pi
CW1 = float(np.float32(TWO_PI)); CW2 = float(np.float32(TWO_PI - CW1))
CW3 = float(np.float32(TWO_PI - CW1 - CW2))
C_L0 = 8.0
C_L1 = 8.0
H0 = 3072
H1 = 4096

XA = 0
XB = 16384
KVO = 32768
QQ = 49152
OO = 65536
WSA = 81920
WSB = 86016
AUX = 90112
A_SIN = 90112
A_COS = 92160
A_RT = 94208
A_KVT = 90112
A_PT = 94208
A_LB = 96256
A_BC = 97280
CONST0 = 98304
ONESCOL = 98432
ONESROW = 98440
NBF = 98568

FQA = 0
FQB = 512
FQV = 1024
FQC = 1536
FQD = 2048
FST = 2560
FMS = 3072
NF32 = 3328
TB_TP = 0
TB_KT = 512
TB_TH = 1024
TB_T2 = 1536

NINT = 512


def host_constants():
    c = {}
    c["tpos"] = np.broadcast_to(np.arange(T, dtype=np.float64), (128, T)).astype(np.float32).copy()
    jmod = (np.arange(128) % 32).astype(np.float64)
    c["jarg"] = np.broadcast_to((-(jmod / 32.0)).astype(np.float32).reshape(128, 1), (128, 8)).copy()
    inv_freq = 1.0 / 10000.0 ** (np.arange(0, DH, 2, dtype=np.float64) / DH)
    theta = np.arange(T, dtype=np.float64)[None, :] * inv_freq[np.arange(128) % 32][:, None]
    c["ktab"] = np.round(theta / TWO_PI).astype(np.float32)
    c["ktabc"] = np.round((theta + np.pi / 2) / TWO_PI).astype(np.float32)
    r2t = np.zeros((128, 128), dtype=np.float32)
    for blk in range(2):
        o = 64 * blk
        for j in range(32):
            r2t[o + j + 32, o + j] = -1.0
            r2t[o + j, o + j + 32] = 1.0
    c["r2t"] = r2t.astype(BF)
    c["ident"] = np.eye(128, dtype=np.float32).astype(BF)
    c["identf32"] = np.eye(128, dtype=np.float32)
    ind2 = np.zeros((2, 128), dtype=np.float32)
    ind2[0, 0:64] = 1.0
    ind2[1, 64:128] = 1.0
    c["ind2"] = ind2
    c["ones1x128"] = np.ones((1, 128), dtype=np.float32)
    return c


def layout_lora(ins):
    return {
        "AfcT": np.ascontiguousarray(ins["L1_ffn_A_fc"].T),
        "BfcT": np.ascontiguousarray(ins["L1_ffn_B_fc"].T),
        "AprT": np.ascontiguousarray(ins["L1_ffn_A_proj"].T),
        "BprT": np.ascontiguousarray(ins["L1_ffn_B_proj"].T),
    }


class Chan:
    def __init__(self, sem):
        self.sem = sem
        self.n = 0


class Rec:
    ENGS = ("tensor", "vector", "scalar", "gpsimd", "sync")

    def __init__(self, nc):
        self.nc = nc
        self.ops = {e: [] for e in self.ENGS}
        self.last = {}
        self.drain = {}

    def do(self, eng, fn, wait=(), inc=(), bare=False):
        w = [(ch, (ch.n if v is None else v)) for ch, v in wait]
        ii = []
        for ch, k in inc:
            ch.n += k
            ii.append((ch, k))
        isdma = any(k == 16 for _, k in ii)
        dinc = None
        if not (bare or isdma) and eng in self.drain:
            w = [self.last.get(eng)] + w if self.last.get(eng) else w
            if ii:
                self.last[eng] = (ii[0][0], ii[0][0].n)
            else:
                dch = self.drain[eng]
                dch.n += 1
                dinc = (dch, 1)
                self.last[eng] = (dch, dch.n)

        def run(e, fn=fn, w=w, ii=ii, dinc=dinc):
            for ch, v in w:
                e.wait_ge(ch.sem, v)
            ins = fn(e)
            if dinc is not None:
                ins = ins.then_inc(dinc[0].sem, dinc[1])
            for ch, k in ii:
                ins = ins.then_inc(ch.sem, k)
        self.ops[eng].append(run)

    def emit_block(self):
        with self.nc.Block() as blk:
            for name in self.ENGS:
                ops = self.ops[name]
                if not ops:
                    continue
                getattr(blk, name)(lambda e, ops=ops: [f(e) for f in ops])
        self.ops = {e: [] for e in self.ENGS}


class Builder:
    def __init__(self, stage="all", debug=()):
        self.stage = stage
        self.debug = set(debug)
        self.nc = bass.Bass("TRN2", target_bir_lowering=False)
        nc = self.nc
        self.inp = {}
        self.dbg_out = {}
        self.inp["x"] = nc.dram_tensor("x", [T, D], F32, kind="ExternalInput")
        for nm, shp, dt in [
            ("L0_attn_Wq", [D, D], F32), ("L0_attn_Wkv", [D, D], F32),
            ("L0_attn_Wout", [D, D], F32), ("L0_attn_bout", [D], F32),
            ("L0_ffn_w_gate", [H0, D], I32), ("L0_ffn_w_up", [H0, D], I32),
            ("L0_ffn_w_down", [D, H0], I32),
            ("L1_attn_w_qkv", [3 * D, D], F32), ("L1_attn_w_proj", [D, D], F32),
            ("L1_attn_b_proj", [D], F32),
            ("L1_ffn_W_fc", [H1, D], F32), ("L1_ffn_b_fc", [H1], F32),
            ("L1_ffn_A_fc", [16, D], F32), ("L1_ffn_B_fc", [H1, 16], F32),
            ("L1_ffn_W_proj", [D, H1], F32), ("L1_ffn_b_proj", [D], F32),
            ("L1_ffn_A_proj", [16, H1], F32), ("L1_ffn_B_proj", [D, 16], F32),
        ]:
            self.inp[nm] = nc.dram_tensor(nm, shp, dt, kind="ExternalInput")
        for nm, shp in [("tpos", [128, T]), ("jarg", [128, 8]), ("ktab", [128, T]),
                        ("ktabc", [128, T]), ("ind2", [2, 128]), ("ones1x128", [1, 128])]:
            self.inp[nm] = nc.dram_tensor(nm, shp, F32, kind="ExternalInput")
        for nm, shp in [("r2t", [128, 128]), ("ident", [128, 128])]:
            self.inp[nm] = nc.dram_tensor(nm, shp, BF16, kind="ExternalInput")
        for nm, shp in [("AfcT", [D, 16]), ("BfcT", [16, H1]), ("AprT", [H1, 16]), ("BprT", [16, D]),
                        ("identf32", [128, 128])]:
            self.inp[nm] = nc.dram_tensor(nm, shp, F32, kind="ExternalInput")
        self.out = nc.dram_tensor("y", [T, D], F32, kind="ExternalOutput")
        self.wfcT = nc.dram_tensor("wfcT", [32, 128, 1024], BF16)
        self.wprT = nc.dram_tensor("wprT", [8, 128, 4096], BF16)
        self.rtrip = nc.dram_tensor("rtrip", [1, 512], F32)
        self.ostage = nc.dram_tensor("ostage", [T, D], BF16)

        self._ctx = []

    def dbg(self, name, shape, dtype=F32):
        t = self.nc.dram_tensor("dbg_" + name, shape, dtype, kind="ExternalOutput")
        self.dbg_out[name] = t
        return t

    def build(self):
        nc = self.nc
        import contextlib
        self._stack = contextlib.ExitStack()
        st = self._stack
        self.bf = st.enter_context(nc.sbuf_tensor("bfa", [128, NBF], BF16))
        self.f32 = st.enter_context(nc.sbuf_tensor("f32a", [128, NF32], F32))
        self.i32 = st.enter_context(nc.sbuf_tensor("i32a", [128, NINT], I32))
        self.ps = [st.enter_context(nc.psum_tensor(f"pb{i}", [128, 512], F32)) for i in range(7)]

        self.psb = st.enter_context(nc.psum_tensor("pbf", [128, 1024], BF16))
        self.ch = {}
        for nm in ["dma", "wdma", "mm", "act", "dve", "gp", "xb",
                   "wg0", "wg1", "wu0", "wu1", "bt0", "bt1"]:
            self.ch[nm] = Chan(st.enter_context(nc.semaphore("ch_" + nm)))
        self.rec = Rec(nc)
        for eng in ("vector", "scalar", "gpsimd"):
            self.rec.drain[eng] = Chan(st.enter_context(nc.semaphore("dr_" + eng)))

        self.phase_setup()
        if self.stage == "setup":
            self.finish(); return self
        if self.stage == "quant":
            chd = self.ch["dma"]
            self.emit_quant(self.inp["L1_ffn_W_fc"], 32, 1024, self.wfcT, FMS + 232)
            dq = self.dbg("qcells", [128, 16], F32)
            self.rec.do("scalar", lambda s: s.dma_start(dq[:, :], self.f32[:, FMS + 232:FMS + 248]),
                        wait=[(self.ch["chv"] if False else self.ch["dve"], None),
                              (self.ch["wdma"], None)], inc=[(chd, 16)])
            dw = self.dbg("qtiles", [128, 1024], mybir.dt.bfloat16)
            self.rec.do("gpsimd", lambda g: g.dma_start(dw[:, :],
                        bass.AP(self.wfcT, 0, [[1024, 128], [1, 1024]])), wait=[(self.ch["wdma"], None)], inc=[(chd, 16)])
            self.rec.do("scalar", lambda s: s.wait_ge(chd.sem, chd.n), bare=True)
            self.finish(); return self
        self.phase_l0_proj()
        if self.stage == "l0proj":
            self.finish(); return self
        self.phase_l0_attn()
        self.phase_l0_out_ln()
        if self.stage == "x2":
            self.finish(); return self
        self.phase_l0_ffn()
        if self.stage == "x3":
            self.finish(); return self
        self.phase_l1_proj()
        self.phase_l1_attn()
        self.phase_l1_out_ln()
        if self.stage == "x4":
            self.finish(); return self
        self.phase_l1_ffn()
        self.phase_output()
        self.finish()
        return self

    def finish(self):
        self.rec.emit_block()
        self._stack.close()

    def bfv(self, col, n, p0=0, np_=128):
        return self.bf[p0:p0 + np_, col:col + n]

    def xf(self, base, g, t0, nt, p0=0, np_=128):
        return self.bf[p0:p0 + np_, base + g * T + t0: base + g * T + t0 + nt]

    def phase_setup(self):
        nc, rec, ch, bf, f32 = self.nc, self.rec, self.ch, self.bf, self.f32
        inp = self.inp
        xsrc = inp["x"]
        chd = self.ch["dma"]
        for tc in range(16):
            dst = bf[:, XB + tc * 1024: XB + (tc + 1) * 1024]
            src = bass.AP(xsrc, tc * 128 * D, [[D, 128], [1, D]])
            rec.do("gpsimd", lambda g, d=dst, s=src: g.dma_start(d, s), inc=[(chd, 16)])
        rec.do("gpsimd", lambda g: g.dma_start(bf[:, CONST0:CONST0 + 128], inp["r2t"][:, :]), inc=[(chd, 16)])
        chg = self.ch["gp"]
        for off, val in ((48, -C_L0), (49, 1e-5), (50, 0.125), (51, -C_L1),
                         (52, 2.0), (53, 1.0 / 16129.0), (54, 1.0 / 127.0), (55, -0.5)):
            rec.do("gpsimd", lambda g, o=off, v=val: g.memset(f32[:, FMS + o:FMS + o + 1], v), inc=[(chg, 1)])
        rec.do("gpsimd", lambda g: g.memset(f32[:, FMS + 60:FMS + 61], 127.0), inc=[(chg, 1)])
        rec.do("gpsimd", lambda g: g.memset(bf[:, ONESCOL:ONESCOL + 8], 1.0), inc=[(chg, 1)])
        rec.do("gpsimd", lambda g: g.memset(bf[:, ONESROW:ONESROW + 128], 1.0), inc=[(chg, 1)])
        rec.do("gpsimd", lambda g: g.memset(f32[:, FMS:FMS + 1], 10000.0), inc=[(chg, 1)])
        rec.do("gpsimd", lambda g: g.dma_start(f32[:, FMS + 16:FMS + 24], inp["jarg"][:, :]), inc=[(chd, 16)])
        chm, cha, chv = self.ch["mm"], self.ch["act"], self.ch["dve"]
        rec.do("scalar", lambda s: s.activation(f32[:, FMS + 1:FMS + 2], f32[:, FMS:FMS + 1], AF.Ln),
               wait=[(chd, None), (self.ch["gp"], None)], inc=[(cha, 1)])
        rec.do("vector", lambda v: v.tensor_scalar_mul(f32[:, FMS + 17:FMS + 18], f32[:, FMS + 16:FMS + 17],
                                                       f32[:, FMS + 1:FMS + 2]),
               wait=[(cha, None), (chd, None)], inc=[(chv, 1)])
        rec.do("scalar", lambda s: s.activation(f32[:, FMS + 18:FMS + 19], f32[:, FMS + 17:FMS + 18], AF.Exp),
               wait=[(chv, None)], inc=[(cha, 1)])
        inv_col = FMS + 18
        rec.do("vector", lambda v: v.reciprocal(f32[:, FMS + 61:FMS + 62], f32[:, FMS + 60:FMS + 61]),
               wait=[(self.ch["gp"], None)], inc=[(chv, 1)])
        for qp in range(4):
            t0 = 512 * qp
            tp, kt, th, t2 = TB_TP, TB_KT, TB_TH, TB_T2
            rec.do("gpsimd", lambda g, t0=t0: g.dma_start(f32[:, tp:tp + 512],
                   bass.AP(inp["tpos"], t0, [[T, 128], [1, 512]])), wait=[(chv, None)], inc=[(chd, 16)])
            rec.do("gpsimd", lambda g, t0=t0: g.dma_start(f32[:, kt:kt + 512],
                   bass.AP(inp["ktab"], t0, [[T, 128], [1, 512]])), inc=[(chd, 16)])
            rec.do("vector", lambda v: v.tensor_scalar_mul(f32[:, th:th + 512], f32[:, tp:tp + 512],
                                                           f32[:, inv_col:inv_col + 1]),
                   wait=[(chd, None), (cha, None), (chv, None)], inc=[(chv, 1)])
            rec.do("vector", lambda v: v.scalar_tensor_tensor(f32[:, t2:t2 + 512], f32[:, kt:kt + 512],
                   -CW1, f32[:, th:th + 512], ALU.mult, ALU.add))
            rec.do("vector", lambda v: v.scalar_tensor_tensor(f32[:, tp:tp + 512], f32[:, kt:kt + 512],
                   -CW2, f32[:, t2:t2 + 512], ALU.mult, ALU.add))
            rec.do("vector", lambda v: v.scalar_tensor_tensor(f32[:, t2:t2 + 512], f32[:, kt:kt + 512],
                   -CW3, f32[:, tp:tp + 512], ALU.mult, ALU.add))
            rec.do("vector", lambda v: v.tensor_scalar_min(f32[:, tp:tp + 512], f32[:, t2:t2 + 512], PIF))
            rec.do("vector", lambda v: v.tensor_scalar_max(f32[:, t2:t2 + 512], f32[:, tp:tp + 512], -PIF),
                   inc=[(chv, 1)])
            rec.do("scalar", lambda s, t0=t0: s.activation(bf[:, A_SIN + t0:A_SIN + t0 + 512],
                   f32[:, t2:t2 + 512], AF.Sin), wait=[(chv, None)], inc=[(cha, 1)])
            rec.do("gpsimd", lambda g, t0=t0: g.dma_start(f32[:, kt:kt + 512],
                   bass.AP(inp["ktabc"], t0, [[T, 128], [1, 512]])), wait=[(cha, None)], inc=[(chd, 16)])
            rec.do("vector", lambda v: v.tensor_scalar_add(f32[:, tp:tp + 512], f32[:, th:th + 512],
                   float(np.float32(np.pi / 2))), wait=[(chd, None), (cha, None)])
            rec.do("vector", lambda v: v.scalar_tensor_tensor(f32[:, t2:t2 + 512], f32[:, kt:kt + 512],
                   -CW1, f32[:, tp:tp + 512], ALU.mult, ALU.add))
            rec.do("vector", lambda v: v.scalar_tensor_tensor(f32[:, tp:tp + 512], f32[:, kt:kt + 512],
                   -CW2, f32[:, t2:t2 + 512], ALU.mult, ALU.add))
            rec.do("vector", lambda v: v.scalar_tensor_tensor(f32[:, t2:t2 + 512], f32[:, kt:kt + 512],
                   -CW3, f32[:, tp:tp + 512], ALU.mult, ALU.add))
            rec.do("vector", lambda v: v.tensor_scalar_min(f32[:, tp:tp + 512], f32[:, t2:t2 + 512], PIF))
            rec.do("vector", lambda v: v.tensor_scalar_max(f32[:, t2:t2 + 512], f32[:, tp:tp + 512], -PIF),
                   inc=[(chv, 1)])
            rec.do("scalar", lambda s, t0=t0: s.activation(bf[:, A_COS + t0:A_COS + t0 + 512],
                   f32[:, t2:t2 + 512], AF.Sin), wait=[(chv, None)], inc=[(cha, 1)])
        chx = self.ch["xb"]
        for tc in range(16):
            inap = bf[:, XB + tc * 1024: XB + (tc + 1) * 1024].rearrange("p (g d) -> p g d", g=8)
            outap = bass.AP(self.bf, XA + tc * 128, [[NBF, 128], [T, 8], [1, 128]])
            rec.do("sync", lambda s, i=inap, o=outap: s.dma_start(o, i, transpose=True),
                   wait=[(chd, None)], inc=[(chx, 16)])
        chw = self.ch["wdma"]
        wq = inp["L0_attn_Wq"]
        for h, base in ((0, WSA), (1, WSB)):
            for ic in range(8):
                dst = bf[:, base + ic * 512: base + (ic + 1) * 512]
                src = bass.AP(wq, ic * 128 * D + h * 512, [[D, 128], [1, 512]])
                rec.do("gpsimd", lambda g, d=dst, s=src: g.dma_start(d, s), inc=[(chw, 16)])
        if "tabdeep" in self.debug:
            dinv = self.dbg("invf", [128, 8], F32)
            dth = self.dbg("theta", [128, 512], F32)
            rec.do("scalar", lambda s: s.dma_start(dinv[:, 0:8], f32[:, inv_col:inv_col + 8]),
                   wait=[(self.ch["act"], None)], inc=[(chd, 16)])
            rec.do("scalar", lambda s: s.dma_start(dth[:, :], f32[:, TB_TH:TB_TH + 512]), inc=[(chd, 16)])
            dj = self.dbg("jchain", [128, 8], F32)
            rec.do("vector", lambda v: v.tensor_copy(f32[:, FMS + 300:FMS + 301], self.ps[0][:, 0:1]), wait=[(self.ch["mm"], None)])
            rec.do("vector", lambda v: v.tensor_copy(f32[:, FMS + 301:FMS + 302], f32[:, FMS + 16:FMS + 17]))
            rec.do("vector", lambda v: v.tensor_copy(f32[:, FMS + 302:FMS + 303], f32[:, FMS + 17:FMS + 18]).then_inc(self.ch["dve"].sem, 1))
            rec.do("scalar", lambda s: s.dma_start(dj[:, 0:3], f32[:, FMS + 300:FMS + 303]), wait=[(self.ch["dve"], self.ch["dve"].n + 1)], inc=[(chd, 16)])
        if "tables" in self.debug:
            dsin = self.dbg("sin", [128, T], BF16); dcos = self.dbg("cos", [128, T], BF16)
            dxa = self.dbg("xa", [128, G * T], BF16); dxb = self.dbg("xb", [128, 16384], BF16)
            rec.do("scalar", lambda s: s.dma_start(dsin[:, :], bf[:, A_SIN:A_SIN + T]),
                   wait=[(self.ch["act"], None)], inc=[(chd, 16)])
            rec.do("scalar", lambda s: s.dma_start(dcos[:, :], bf[:, A_COS:A_COS + T]), inc=[(chd, 16)])
            rec.do("scalar", lambda s: s.dma_start(dxa[:, :], bf[:, XA:XA + 16384]),
                   wait=[(chx, None)], inc=[(chd, 16)])
            rec.do("scalar", lambda s: s.dma_start(dxb[:, :], bf[:, XB:XB + 16384]), inc=[(chd, 16)])
            rec.do("scalar", lambda s: s.wait_ge(chd.sem, chd.n), bare=True)
        self.rec.emit_block()

    def phase_l0_proj(self):
        nc, rec, bf = self.nc, self.rec, self.bf
        chm, cha, chv, chw = self.ch["mm"], self.ch["act"], self.ch["dve"], self.ch["wdma"]
        evac1_done = {}
        it = 0
        for proj, dstbase in (("q", QQ), ("kv", KVO)):
            if proj == "kv":
                mm_after_q = chm.n
                wkv = self.inp["L0_attn_Wkv"]
                for h, base in ((0, WSA), (1, WSB)):
                    for ic in range(8):
                        dst = bf[:, base + ic * 512: base + (ic + 1) * 512]
                        src = bass.AP(wkv, ic * 128 * D + h * 512, [[D, 128], [1, 512]])
                        rec.do("gpsimd", lambda g, d=dst, s=src: g.dma_start(d, s),
                               wait=[(chm, mm_after_q)] if ic == 0 and h == 0 else [], inc=[(chw, 16)])
            for oc in range(8):
                wbase = WSA if oc < 4 else WSB
                ocol = (oc % 4) * 128
                for tg in range(4):
                    par = it % 2
                    p0 = self.ps[par]
                    p1 = self.ps[2 + par]
                    w_pe = [(chw, None), (self.ch["xb"], None)]
                    if it - 2 in evac1_done:
                        w_pe.append((cha, evac1_done[it - 2]))
                    for ic in range(8):
                        lhsT = bf[:, wbase + ic * 512 + ocol: wbase + ic * 512 + ocol + 128]
                        rhs = bf[:, XA + ic * T + tg * 512: XA + ic * T + tg * 512 + 512]
                        rec.do("tensor", lambda t, o=p0, l=lhsT, r=rhs, s=(ic == 0), e=(ic == 7):
                               t.matmul(o[:, 0:512], l, r, start=s, stop=e),
                               wait=w_pe if ic == 0 else [],
                               inc=[(chm, 1)] if ic == 7 else [])
                    dcol = dstbase + oc * T + tg * 512
                    rec.do("scalar", lambda s, d=dcol, o=p0: s.activation(bf[:, d:d + 512], o[:, 0:512], AF.Copy),
                           wait=[(chm, None)], inc=[(cha, 1)])
                    rec.do("tensor", lambda t, d=dcol, o=p1: t.matmul(o[:, 0:512],
                           bf[:, CONST0:CONST0 + 128], bf[:, d:d + 512], start=True, stop=True),
                           wait=[(cha, None)], inc=[(chm, 1)])
                    t1c = A_RT + par * 512
                    rec.do("vector", lambda v, d=dcol, t1=t1c, tg=tg: v.tensor_mul(
                           bf[:, t1:t1 + 512], bf[:, d:d + 512], bf[:, A_COS + tg * 512: A_COS + tg * 512 + 512]),
                           wait=[(cha, None)])
                    swc = A_RT + 1024 + par * 512
                    rec.do("scalar", lambda s, o=p1, sw=swc: s.activation(bf[:, sw:sw + 512], o[:, 0:512], AF.Copy),
                           wait=[(chm, None), (chv, None)], inc=[(cha, 1)])
                    evac1_done[it] = cha.n
                    rec.do("vector", lambda v, sw=swc, tg=tg: v.tensor_mul(
                           bf[:, sw:sw + 512], bf[:, sw:sw + 512], bf[:, A_SIN + tg * 512: A_SIN + tg * 512 + 512]),
                           wait=[(cha, None)])
                    rec.do("vector", lambda v, d=dcol, t1=t1c, sw=swc: v.tensor_add(
                           bf[:, d:d + 512], bf[:, t1:t1 + 512], bf[:, sw:sw + 512]), inc=[(chv, 1)])
                    it += 1
        mm_all = chm.n
        wout = self.inp["L0_attn_Wout"]
        for h, base in ((0, WSA), (1, WSB)):
            for ic in range(8):
                dst = bf[:, base + ic * 512: base + (ic + 1) * 512]
                src = bass.AP(wout, ic * 128 * D + h * 512, [[D, 128], [1, 512]])
                rec.do("gpsimd", lambda g, d=dst, s=src: g.dma_start(d, s),
                       wait=[(chm, mm_all)] if ic == 0 and h == 0 else [], inc=[(chw, 16)])
        if "l0proj" in self.debug:
            dt2 = self.dbg("tiles", [128, 4096], BF16)
            rec.do("scalar", lambda s: s.dma_start(dt2[:, 0:2048], bf[:, A_RT:A_RT + 2048]),
                   wait=[(chv, None)], inc=[(self.ch["dma"], 16)])
            rec.do("scalar", lambda s: s.dma_start(dt2[:, 2048:4096], bf[:, A_SIN:A_SIN + 2048]), inc=[(self.ch["dma"], 16)])
            dws = self.dbg("wsa", [128, 8192], BF16)
            rec.do("scalar", lambda s: s.dma_start(dws[:, 0:4096], bf[:, WSA:WSA + 4096]),
                   wait=[(chw, None)], inc=[(self.ch["dma"], 16)])
            rec.do("scalar", lambda s: s.dma_start(dws[:, 4096:8192], bf[:, WSB:WSB + 4096]), inc=[(self.ch["dma"], 16)])
            dq = self.dbg("qrot", [128, G * T], BF16)
            dkv = self.dbg("kvrot", [128, G * T], BF16)
            rec.do("scalar", lambda s: s.dma_start(dq[:, :], bf[:, QQ:QQ + G * T]),
                   wait=[(chv, None)], inc=[(self.ch["dma"], 16)])
            rec.do("scalar", lambda s: s.dma_start(dkv[:, :], bf[:, KVO:KVO + G * T]), inc=[(self.ch["dma"], 16)])
            rec.do("scalar", lambda s: s.wait_ge(self.ch["dma"].sem, self.ch["dma"].n), bare=True)
        self.rec.emit_block()

    def phase_l0_attn(self):
        nc, rec, bf, f32 = self.nc, self.rec, self.bf, self.f32
        chm, cha, chv, chx, chg = self.ch["mm"], self.ch["act"], self.ch["dve"], self.ch["xb"], self.ch["gp"]
        rec.do("gpsimd", lambda g: g.memset(bf[:, A_KVT:A_KVT + 4096], 0.0), inc=[(chg, 1)])
        rec.do("gpsimd", lambda g: g.memset(bass.AP(bf, A_KVT + 64, [[NBF, 128], [256, 16]]), 1.0), inc=[(chg, 1)])
        rec.do("gpsimd", lambda g: g.memset(bass.AP(bf, A_KVT + 128, [[NBF, 128], [256, 16]]), 1.0), inc=[(chg, 1)])
        memset_done = chg.n
        expd = {}
        avd = {}
        evacd = 0
        xbar_n0 = chx.n
        for pr in range(8):
            wv = [(chg, memset_done)]
            if pr > 0:
                wv.append((chm, chm.n))
            for half, (p0, dcol) in enumerate(((0, A_KVT), (64, A_KVT + 192))):
                inap = bass.AP(bf, KVO + pr * T, [[NBF, 64], [128, 16], [1, 128]], ) if False else None
                src = self.bf[p0:p0 + 64, KVO + pr * T: KVO + (pr + 1) * T].rearrange("p (c k) -> p c k", c=16)
                dst = bass.AP(bf, dcol, [[NBF, 128], [256, 16], [1, 64]])
                rec.do("sync", lambda s, d=dst, i=src: s.dma_start(d, i, transpose=True),
                       wait=wv if half == 0 else [], inc=[(chx, 16)])
            for qg in range(4):
                ev_wait = evacd
                for c in range(16):
                    par = c % 2
                    psA, psB = self.ps[par], self.ps[2 + par]
                    w = []
                    if (0, par) in expd:
                        w.append((cha, expd[(0, par)]))
                    if (1, par) in expd:
                        w.append((cha, expd[(1, par)]))
                    if c == 0:
                        w.append((chv, ev_wait))
                        w.append((chx, chx.n))
                    qcol = QQ + pr * T + qg * 512
                    kcol = KVO + pr * T + c * 128
                    rec.do("tensor", lambda t, o=psA, k=kcol, q=qcol: t.matmul(
                           o[:, 0:512], bf[0:64, k:k + 128], bf[0:64, q:q + 512],
                           start=True, stop=True, tile_position=(0, 0)),
                           wait=w, inc=[(chm, 1)])
                    rec.do("tensor", lambda t, o=psB, k=kcol, q=qcol: t.matmul(
                           o[:, 0:512], bf[64:128, k:k + 128], bf[64:128, q:q + 512],
                           start=True, stop=True, tile_position=(64, 0)),
                           inc=[(chm, 1)])
                    mm_after_s = chm.n
                    for h, ps_ in ((0, psA), (1, psB)):
                        ptcol = A_PT + (2 * par + h) * 512
                        w2 = [(chm, mm_after_s)]
                        if c >= 2 and (c - 2) in avd:
                            w2.append((chm, avd[c - 2]))
                        rec.do("scalar", lambda s, p=ps_, pc=ptcol: s.activation(
                               bf[:, pc:pc + 512], p[:, 0:512], AF.Exp,
                               bias=f32[:, FMS + 48:FMS + 49], scale=f32[:, FMS + 50:FMS + 51]),
                               wait=w2, inc=[(cha, 1)])
                        expd[(h, par)] = cha.n
                    for h, avp in ((0, self.ps[4]), (1, self.ps[5])):
                        ptcol = A_PT + (2 * par + h) * 512
                        kv_l = A_KVT + c * 256 + (0 if h == 0 else 128)
                        rec.do("tensor", lambda t, o=avp, kl=kv_l, pc=ptcol, s0=(c == 0), s1=(c == 15):
                               t.matmul(o[:, 0:512], bf[:, kl:kl + 128], bf[:, pc:pc + 512],
                                        start=s0, stop=s1),
                               wait=[(cha, expd[(h, par)])], inc=[(chm, 1)])
                    avd[c] = chm.n
                av_stop = chm.n
                rec.do("vector", lambda v: v.reciprocal(f32[64:65, FQA:FQA + 512], self.ps[4][64:65, 0:512]),
                       wait=[(chm, av_stop)])
                rec.do("vector", lambda v: v.reciprocal(f32[0:1, FQA:FQA + 512], self.ps[5][0:1, 0:512]))
                rec.do("vector", lambda v: v.tensor_copy(bf[64:65, A_LB:A_LB + 512], f32[64:65, FQA:FQA + 512]))
                rec.do("vector", lambda v: v.tensor_copy(bf[0:1, A_LB:A_LB + 512], f32[0:1, FQA:FQA + 512]),
                       inc=[(chv, 1)])
                rows_ready = chv.n
                rec.do("tensor", lambda t: t.matmul(self.ps[0][:, 0:512], bf[64:65, ONESROW:ONESROW + 128],
                       bf[64:65, A_LB:A_LB + 512], start=True, stop=True),
                       wait=[(chv, rows_ready)], inc=[(chm, 1)])
                rec.do("tensor", lambda t: t.matmul(self.ps[1][:, 0:512], bf[0:1, ONESROW:ONESROW + 128],
                       bf[0:1, A_LB:A_LB + 512], start=True, stop=True), inc=[(chm, 1)])
                bc_done = chm.n
                rec.do("scalar", lambda s: s.activation(bf[:, A_BC:A_BC + 512], self.ps[0][:, 0:512], AF.Copy),
                       wait=[(chm, bc_done)])
                rec.do("scalar", lambda s: s.activation(bf[:, A_BC + 512:A_BC + 1024], self.ps[1][:, 0:512], AF.Copy),
                       inc=[(cha, 1)])
                bce = cha.n
                ocol = OO + pr * T + qg * 512
                rec.do("vector", lambda v, oc=ocol: v.tensor_mul(bf[0:64, oc:oc + 512],
                       self.ps[4][0:64, 0:512], bf[0:64, A_BC:A_BC + 512]),
                       wait=[(cha, bce)])
                rec.do("vector", lambda v, oc=ocol: v.tensor_mul(bf[64:128, oc:oc + 512],
                       self.ps[5][64:128, 0:512], bf[64:128, A_BC + 512:A_BC + 1024]),
                       inc=[(chv, 1)])
                evacd = chv.n
        if "l0attn" in self.debug:
            do_ = self.dbg("of", [128, G * T], BF16)
            rec.do("scalar", lambda s: s.dma_start(do_[:, :], bf[:, OO:OO + G * T]),
                   wait=[(chv, None)], inc=[(self.ch["dma"], 16)])
            rec.do("scalar", lambda s: s.wait_ge(self.ch["dma"].sem, self.ch["dma"].n), bare=True)
        self.rec.emit_block()

    def phase_l0_out_ln(self):
        self.emit_proj_residual_ln(src=OO, wname=None, bias="L0_attn_bout",
                                   stream=XA, rawdst=XB, newdst=XA, dbgkey="x2")

    def emit_proj_residual_ln(self, src, wname, bias, stream, rawdst, newdst, dbgkey):
        nc, rec, bf, f32 = self.nc, self.rec, self.bf, self.f32
        chm, cha, chv, chd = self.ch["mm"], self.ch["act"], self.ch["dve"], self.ch["dma"]
        chw = self.ch["wdma"]
        bt = self.inp[bias]
        rec.do("gpsimd", lambda g: g.dma_start(f32[:, FMS + 32:FMS + 40],
               bass.AP(bt, 0, [[1, 128], [128, 8]]), allow_slow_non_contiguous=True), inc=[(chd, 16)])
        bias_done = chd.n
        raw_done = {}
        for tg in range(4):
            for oc in range(8):
                par = (tg * 8 + oc) % 2
                p0 = self.ps[par]
                wbase = WSA if oc < 4 else WSB
                ocol = (oc % 4) * 128
                w = [(chw, None)] if (tg == 0 and oc == 0) else []
                if (tg, oc) not in (None,):
                    prev = tg * 8 + oc - 2
                    if prev >= 0:
                        w.append((chv, raw_done.get(prev, 0)))
                for ic in range(8):
                    lhsT = bf[:, wbase + ic * 512 + ocol: wbase + ic * 512 + ocol + 128]
                    rhs = bf[:, src + ic * T + tg * 512: src + ic * T + tg * 512 + 512]
                    rec.do("tensor", lambda t, o=p0, l=lhsT, r=rhs, s=(ic == 0), e=(ic == 7):
                           t.matmul(o[:, 0:512], l, r, start=s, stop=e),
                           wait=w if ic == 0 else [], inc=[(chm, 1)] if ic == 7 else [])
                rcol = rawdst + oc * T + tg * 512
                xcol = stream + oc * T + tg * 512
                wst = [(chm, None)]
                if tg == 0 and oc == 0:
                    wst.append((chd, bias_done))
                rec.do("vector", lambda v, o=p0, rc=rcol, xc=xcol, occ=oc: v.scalar_tensor_tensor(
                       bf[:, rc:rc + 512], o[:, 0:512], f32[:, FMS + 32 + occ:FMS + 33 + occ],
                       bf[:, xc:xc + 512], ALU.add, ALU.add),
                       wait=wst, inc=[(chv, 1)])
                raw_done[tg * 8 + oc] = chv.n
            self.emit_ln_group(tg, rawdst, newdst, raw_done[tg * 8 + 7])
        if dbgkey in self.debug:
            dr = self.dbg(dbgkey + "_raw", [128, G * T], BF16)
            dn = self.dbg(dbgkey, [128, G * T], BF16)
            rec.do("scalar", lambda s: s.dma_start(dr[:, :], bf[:, rawdst:rawdst + G * T]),
                   wait=[(chv, None)], inc=[(chd, 16)])
            rec.do("scalar", lambda s: s.dma_start(dn[:, :], bf[:, newdst:newdst + G * T]), inc=[(chd, 16)])
            rec.do("scalar", lambda s: s.wait_ge(chd.sem, chd.n), bare=True)
        self.rec.emit_block()

    def emit_ln_group(self, tg, rawdst, newdst, raw_ready_chv):
        rec, bf, f32 = self.rec, self.bf, self.f32
        chm, cha, chv = self.ch["mm"], self.ch["act"], self.ch["dve"]
        for ic in range(8):
            rcol = rawdst + ic * T + tg * 512
            w = [(chv, raw_ready_chv)] if ic == 0 else []
            rec.do("tensor", lambda t, rc=rcol, s=(ic == 0), e=(ic == 7): t.matmul(
                   self.ps[6][0:1, 0:512], bf[:, ONESCOL:ONESCOL + 1], bf[:, rc:rc + 512],
                   start=s, stop=e), wait=w, inc=[(chm, 1)] if ic == 7 else [])
        sqmm = {}
        for ic in range(8):
            rcol = rawdst + ic * T + tg * 512
            sq = A_PT + (ic % 2) * 512
            w = [(chv, raw_ready_chv)] if ic == 0 else []
            if ic >= 2:
                w.append((chm, sqmm[ic - 2]))
            else:
                w.append((chm, chm.n))
            rec.do("scalar", lambda s_, rc=rcol, q=sq: s_.activation(bf[:, q:q + 512], bf[:, rc:rc + 512],
                   AF.Square), wait=w, inc=[(cha, 1)])
            rec.do("tensor", lambda t, q=sq, s=(ic == 0), e=(ic == 7): t.matmul(
                   self.ps[5][0:1, 0:512], bf[:, ONESCOL:ONESCOL + 1], bf[:, q:q + 512],
                   start=s, stop=e), wait=[(cha, None)], inc=[(chm, 1)])
            sqmm[ic] = chm.n
        sums_done = chm.n
        rec.do("vector", lambda v: v.tensor_scalar_mul(f32[0:1, FQA:FQA + 512], self.ps[6][0:1, 0:512],
               1.0 / 1024.0), wait=[(chm, sums_done)])
        rec.do("vector", lambda v: v.tensor_scalar_mul(f32[0:1, FQB:FQB + 512], self.ps[5][0:1, 0:512],
               1.0 / 1024.0))
        rec.do("vector", lambda v: v.tensor_mul(f32[0:1, FQC:FQC + 512], f32[0:1, FQA:FQA + 512],
               f32[0:1, FQA:FQA + 512]))
        rec.do("vector", lambda v: v.tensor_sub(f32[0:1, FQB:FQB + 512], f32[0:1, FQB:FQB + 512],
               f32[0:1, FQC:FQC + 512]), inc=[(chv, 1)])
        rec.do("scalar", lambda s: s.activation(f32[0:1, FQC:FQC + 512], f32[0:1, FQB:FQB + 512],
               AF.Ln, bias=f32[0:1, FMS + 49:FMS + 50]), wait=[(chv, None)])
        rec.do("scalar", lambda s: s.activation(f32[0:1, FQB:FQB + 512], f32[0:1, FQC:FQC + 512],
               AF.Exp, scale=f32[0:1, FMS + 55:FMS + 56]), inc=[(cha, 1)])
        rec.do("vector", lambda v: v.engine_nop(), wait=[(cha, None)])
        rec.do("vector", lambda v: v.tensor_copy(bf[0:1, A_LB:A_LB + 512], f32[0:1, FQA:FQA + 512]))
        rec.do("vector", lambda v: v.tensor_copy(bf[0:1, A_LB + 512:A_LB + 1024],
               f32[0:1, FQB:FQB + 512]), inc=[(chv, 1)])
        rows_done = chv.n
        rec.do("tensor", lambda t: t.matmul(self.ps[2][:, 0:512], bf[0:1, ONESROW:ONESROW + 128],
               bf[0:1, A_LB:A_LB + 512], start=True, stop=True),
               wait=[(chv, rows_done), (cha, cha.n)], inc=[(chm, 1)])
        rec.do("tensor", lambda t: t.matmul(self.ps[3][:, 0:512], bf[0:1, ONESROW:ONESROW + 128],
               bf[0:1, A_LB + 512:A_LB + 1024], start=True, stop=True), inc=[(chm, 1)])
        bc = chm.n
        rec.do("scalar", lambda s: s.activation(bf[:, A_BC:A_BC + 512], self.ps[2][:, 0:512], AF.Copy),
               wait=[(chm, bc), (chv, None)])
        rec.do("scalar", lambda s: s.activation(bf[:, A_BC + 512:A_BC + 1024], self.ps[3][:, 0:512], AF.Copy),
               inc=[(cha, 1)])
        bce = cha.n
        for ic in range(8):
            rcol = rawdst + ic * T + tg * 512
            ncol = newdst + ic * T + tg * 512
            tmp = A_PT + 1024
            rec.do("vector", lambda v, rc=rcol, tm=tmp: v.tensor_sub(bf[:, tm:tm + 512],
                   bf[:, rc:rc + 512], bf[:, A_BC:A_BC + 512]),
                   wait=[(cha, bce)] if ic == 0 else [])
            rec.do("vector", lambda v, nc_=ncol, tm=tmp: v.tensor_mul(bf[:, nc_:nc_ + 512],
                   bf[:, tm:tm + 512], bf[:, A_BC + 512:A_BC + 1024]),
                   inc=[(chv, 1)] if ic == 7 else [])

    def phase_l0_ffn(self):
        nc, rec, bf, f32 = self.nc, self.rec, self.bf, self.f32
        chm, cha, chv, chx, chw = self.ch["mm"], self.ch["act"], self.ch["dve"], self.ch["xb"], self.ch["wdma"]
        FH = KVO
        wg_t, wu_t, wd_t = self.inp["L0_ffn_w_gate"], self.inp["L0_ffn_w_up"], self.inp["L0_ffn_w_down"]
        NH0 = 24
        stg = {"g": WSA, "u": WSB, "d": WSA}
        for tg2 in range(2):
            ln_gate = cha.n
            xb_done = {}
            for hc in range(NH0):
                par = hc % 2
                for nm, wt, psg in (("g", wg_t, (self.ps[0], self.ps[1])), ("u", wu_t, (self.ps[2], self.ps[3]))):
                    sb = stg[nm] + par * 1024
                    tb = stg[nm] + 2048 + par * 1024
                    chs = self.ch[("wg" if nm == "g" else "wu") + str(par)]
                    if hc >= 2:
                        w = [(chx, xb_done[(nm, hc - 2)][0])]
                    else:
                        w = [(chx, chx.n)]
                    src = bass.AP(wt, hc * 128 * D, [[D, 128], [1, D]])
                    rec.do("gpsimd", lambda g, d=sb, s=src: g.dma_start(bf[:, d:d + 1024], s),
                           wait=w, inc=[(chs, 16)])
                    wdma_n = chs.n
                    mmprev = xb_done.get((nm, hc - 2), (0, chm.n))[1]
                    ii = bf[:, sb:sb + 1024].rearrange("p (c k) -> p c k", c=8)
                    oo = bf[:, tb:tb + 1024].rearrange("p (c k) -> p c k", c=8)
                    rec.do("sync", lambda s, i=ii, o=oo, wn=wdma_n, mp=mmprev, cs=chs: (
                           s.wait_ge(cs.sem, wn), s.wait_ge(chm.sem, mp),
                           s.dma_start(o, i, transpose=True))[-1], inc=[(chx, 16)])
                    xbn = chx.n
                    for th in range(2):
                        ps_ = psg[th]
                        for dc in range(8):
                            lhsT = bf[:, tb + dc * 128: tb + dc * 128 + 128]
                            rhs = bf[:, XA + dc * T + tg2 * 1024 + th * 512: XA + dc * T + tg2 * 1024 + th * 512 + 512]
                            wmm = [(chx, xbn)] if dc == 0 else []
                            if dc == 0 and hc == 0 and th == 0 and nm == "g":
                                wmm.append((cha, ln_gate))
                            if dc == 0 and th == 0 and nm == "g" and hc >= 1:
                                wmm.append((chv, chv.n))
                                wmm.append((cha, cha.n))
                            rec.do("tensor", lambda t, o=ps_, l=lhsT, r=rhs, s=(dc == 0), e=(dc == 7):
                                   t.matmul(o[:, 0:512], l, r, start=s, stop=e),
                                   wait=wmm, inc=[(chm, 1)] if dc == 7 else [])
                    xb_done[(nm, hc)] = (chx.n, chm.n)
                gu_done = chm.n
                for th in range(2):
                    gt = A_KVT + th * 512
                    rec.do("scalar", lambda s, g_=gt, p=self.ps[th]: s.activation(
                           bf[:, g_:g_ + 512], p[:, 0:512], AF.Silu, scale=f32[:, FMS + 54:FMS + 55]),
                           wait=[(chm, gu_done)], inc=[(cha, 1)])
                    hcol = FH + hc * 1024 + th * 512
                    rec.do("vector", lambda v, hcc=hcol, g_=gt, p=self.ps[2 + th]: v.tensor_mul(
                           bf[:, hcc:hcc + 512], p[:, 0:512], bf[:, g_:g_ + 512]),
                           wait=[(cha, None)], inc=[(chv, 1)])
            hdone = chv.n
            raw_chv = {}
            for oc in range(8):
                dn_w = [(chv, hdone)]
                if oc >= 1:
                    dn_w.append((chv, raw_chv[oc - 1]))
                mm_start = chm.n
                for p3 in range(3):
                    par = p3 % 2
                    sb = stg["d"] + par * 1024
                    tb = stg["d"] + 2048 + par * 1024
                    w = list(dn_w) if p3 == 0 else []
                    if (oc * 3 + p3) >= 2:
                        w.append((chx, chx.n))
                    src = bass.AP(wd_t, oc * 128 * H0 + p3 * 1024, [[H0, 128], [1, 1024]])
                    chs = self.ch["wg" + str(par)]
                    rec.do("gpsimd", lambda g, d=sb, s=src: g.dma_start(bf[:, d:d + 1024], s),
                           wait=w, inc=[(chs, 16)])
                    wdma_n = chs.n
                    ii = bf[:, sb:sb + 1024].rearrange("p (c k) -> p c k", c=8)
                    oo = bf[:, tb:tb + 1024].rearrange("p (c k) -> p c k", c=8)
                    mmprev = chm.n if (oc * 3 + p3) >= 2 else 0
                    rec.do("sync", lambda s, i=ii, o=oo, wn=wdma_n, mp=mmprev, cs=chs: (
                           s.wait_ge(cs.sem, wn), s.wait_ge(chm.sem, mp),
                           s.dma_start(o, i, transpose=True))[-1], inc=[(chx, 16)])
                    xbn = chx.n
                    for hin in range(8):
                        hc = p3 * 8 + hin
                        lhsT = bf[:, tb + hin * 128: tb + hin * 128 + 128]
                        for th in range(2):
                            ps_ = self.ps[4 + th]
                            rhs = bf[:, FH + hc * 1024 + th * 512: FH + hc * 1024 + th * 512 + 512]
                            wmm = [(chx, xbn)] if hin == 0 else []
                            rec.do("tensor", lambda t, o=ps_, l=lhsT, r=rhs, s=(hc == 0), e=(hc == 23):
                                   t.matmul(o[:, 0:512], l, r, start=s, stop=e),
                                   wait=wmm, inc=[(chm, 1)] if hc == 23 else [])
                for th in range(2):
                    rcol = XB + oc * T + tg2 * 1024 + th * 512
                    xcol = XA + oc * T + tg2 * 1024 + th * 512
                    rec.do("vector", lambda v, o=self.ps[4 + th], rc=rcol, xc=xcol: v.scalar_tensor_tensor(
                           bf[:, rc:rc + 512], o[:, 0:512], f32[:, FMS + 53:FMS + 54],
                           bf[:, xc:xc + 512], ALU.mult, ALU.add),
                           wait=[(chm, None)], inc=[(chv, 1)])
                raw_chv[oc] = chv.n
            for th in range(2):
                self.emit_ln_group(tg2 * 2 + th, XB, XA, raw_chv[7])
            self.rec.emit_block()
        if "x3" in self.debug:
            chd = self.ch["dma"]
            dh = self.dbg("h0", [128, 24 * 1024], BF16)
            rec.do("scalar", lambda s: s.dma_start(dh[:, :], bf[:, FH:FH + 24 * 1024]),
                   wait=[(chv, None)], inc=[(chd, 16)])
            dn = self.dbg("x3", [128, G * T], BF16)
            dr = self.dbg("x3_raw", [128, G * T], BF16)
            rec.do("scalar", lambda s: s.dma_start(dn[:, :], bf[:, XA:XA + G * T]),
                   wait=[(chv, None)], inc=[(chd, 16)])
            rec.do("scalar", lambda s: s.dma_start(dr[:, :], bf[:, XB:XB + G * T]), inc=[(chd, 16)])
            rec.do("scalar", lambda s: s.wait_ge(chd.sem, chd.n), bare=True)
        self.rec.emit_block()

    def phase_l1_proj(self):
        nc, rec, bf, f32 = self.nc, self.rec, self.bf, self.f32
        chm, cha, chv, chx, chw = self.ch["mm"], self.ch["act"], self.ch["dve"], self.ch["xb"], self.ch["wdma"]
        wt = self.inp["L1_attn_w_qkv"]
        xb_done = {}
        evac_done = {}
        for oc in range(24):
            par = oc % 2
            sb = WSA + par * 1024
            tb = WSA + 2048 + par * 1024
            chs = self.ch["wg" + str(par)]
            w = [(chx, xb_done[oc - 2][0])] if oc >= 2 else [(chx, chx.n)]
            src = bass.AP(wt, oc * 128 * D, [[D, 128], [1, D]])
            rec.do("gpsimd", lambda g, d=sb, s=src: g.dma_start(bf[:, d:d + 1024], s),
                   wait=w, inc=[(chs, 16)])
            wdma_n = chs.n
            mmprev = xb_done.get(oc - 2, (0, chm.n))[1]
            ii = bf[:, sb:sb + 1024].rearrange("p (c k) -> p c k", c=8)
            oo = bf[:, tb:tb + 1024].rearrange("p (c k) -> p c k", c=8)
            rec.do("sync", lambda s, i=ii, o=oo, wn=wdma_n, mp=mmprev, cs=chs: (
                   s.wait_ge(cs.sem, wn), s.wait_ge(chm.sem, mp),
                   s.dma_start(o, i, transpose=True))[-1], inc=[(chx, 16)])
            xbn = chx.n
            if oc < 8:
                dstb = QQ + oc * T
            elif oc < 16:
                dstb = KVO + (oc - 8) * T
            else:
                dstb = XB + (oc - 16) * T
            for tg in range(4):
                ppar = (oc * 4 + tg) % 2
                p0 = self.ps[ppar]
                wmm = [(chx, xbn)] if tg == 0 else []
                prev = oc * 4 + tg - 2
                if prev >= 0:
                    wmm.append((cha, evac_done.get(prev, 0)))
                for ic in range(8):
                    lhsT = bf[:, tb + ic * 128: tb + ic * 128 + 128]
                    rhs = bf[:, XA + ic * T + tg * 512: XA + ic * T + tg * 512 + 512]
                    rec.do("tensor", lambda t, o=p0, l=lhsT, r=rhs, st=(ic == 0), e=(ic == 7):
                           t.matmul(o[:, 0:512], l, r, start=st, stop=e),
                           wait=wmm if ic == 0 else [], inc=[(chm, 1)] if ic == 7 else [])
                dcol = dstb + tg * 512
                rec.do("scalar", lambda s, d=dcol, o=p0: s.activation(bf[:, d:d + 512], o[:, 0:512], AF.Copy),
                       wait=[(chm, None)], inc=[(cha, 1)])
                evac_done[oc * 4 + tg] = cha.n
            xb_done[oc] = (chx.n, chm.n)
        vdone = cha.n
        for dc in range(8):
            ii = self.bf[:, XB + dc * T: XB + (dc + 1) * T].rearrange("p (c k) -> p c k", c=16)
            oo = bass.AP(bf, OO + dc * 128, [[NBF, 128], [1024, 16], [1, 128]])
            rec.do("sync", lambda s, i=ii, o=oo: s.dma_start(o, i, transpose=True),
                   wait=[(cha, vdone)] if dc == 0 else [], inc=[(chx, 16)])
        self.rec.emit_block()

    def phase_l1_attn(self):
        nc, rec, bf, f32 = self.nc, self.rec, self.bf, self.f32
        chm, cha, chv, chx = self.ch["mm"], self.ch["act"], self.ch["dve"], self.ch["xb"]
        PT = AUX
        for qg in range(4):
            expd = {}
            avdone_prev = chm.n
            for c in range(16):
                par = c % 2
                psS = self.ps[par]
                w = []
                if (c - 2, "e") in expd:
                    w.append((cha, expd[(c - 2, "e")]))
                if c == 0:
                    w.append((chx, chx.n))
                for ic in range(8):
                    lhsT = bf[:, KVO + ic * T + c * 128: KVO + ic * T + c * 128 + 128]
                    rhs = bf[:, QQ + ic * T + qg * 512: QQ + ic * T + qg * 512 + 512]
                    rec.do("tensor", lambda t, o=psS, l=lhsT, r=rhs, st=(ic == 0), e=(ic == 7):
                           t.matmul(o[:, 0:512], l, r, start=st, stop=e),
                           wait=w if ic == 0 else [], inc=[(chm, 1)] if ic == 7 else [])
                w2 = [(chm, None)]
                if qg > 0 and c == 0:
                    w2.append((chm, avdone_prev))
                rec.do("scalar", lambda s, p=psS, pc=PT + c * 512: s.activation(
                       bf[:, pc:pc + 512], p[:, 0:512], AF.Exp,
                       bias=f32[:, FMS + 51:FMS + 52], scale=f32[:, FMS + 50:FMS + 51]),
                       wait=w2, inc=[(cha, 1)])
                expd[(c, "e")] = cha.n
            exp_all = cha.n
            for qt in range(4):
                for c in range(16):
                    lhsT = bf[:, PT + c * 512 + qt * 128: PT + c * 512 + qt * 128 + 128]
                    w = [(cha, exp_all)] if c == 0 else []
                    if c == 0:
                        w.append((chv, chv.n))
                    for dh in range(2):
                        rhs = bf[:, OO + c * 1024 + dh * 512: OO + c * 1024 + dh * 512 + 512]
                        rec.do("tensor", lambda t, o=self.ps[2 + dh], l=lhsT, r=rhs, st=(c == 0), e=(c == 15):
                               t.matmul(o[:, 0:512], l, r, start=st, stop=e),
                               wait=w if dh == 0 else [], inc=[(chm, 1)] if c == 15 else [])
                    rec.do("tensor", lambda t, l=lhsT, st=(c == 0), e=(c == 15):
                           t.matmul(self.ps[4][:, 0:1], l, bf[:, ONESCOL:ONESCOL + 1], start=st, stop=e),
                           inc=[(chm, 1)] if c == 15 else [])
                av_done = chm.n
                qi = qg * 4 + qt
                rec.do("vector", lambda v: v.reciprocal(f32[:, FMS + 104:FMS + 105], self.ps[4][:, 0:1]),
                       wait=[(chm, av_done)], inc=[(chv, 1)])
                for dh in range(2):
                    dcol = XB + qi * 1024 + dh * 512
                    rec.do("scalar", lambda s, d=dcol, o=self.ps[2 + dh]: s.activation(
                           bf[:, d:d + 512], o[:, 0:512], AF.Copy, scale=f32[:, FMS + 104:FMS + 105]),
                           wait=[(chv, None)], inc=[(cha, 1)] if dh == 1 else [])
                rec.do("vector", lambda v: v.engine_nop(), wait=[(cha, None)])
                rec.do("vector", lambda v: v.engine_nop(), inc=[(chv, 1)])
            self.rec.emit_block()
        if "l1attn" in self.debug:
            chd = self.ch["dma"]
            dob = self.dbg("otok_mid", [128, 16384], BF16)
            rec.do("scalar", lambda s: s.dma_start(dob[:, :], bf[:, XB:XB + 16384]),
                   wait=[(chv, None)], inc=[(chd, 16)])
            rec.do("scalar", lambda s: s.wait_ge(chd.sem, chd.n), bare=True)
        self.rec.emit_block()

    def phase_l1_out_ln(self):
        nc, rec, bf, f32 = self.nc, self.rec, self.bf, self.f32
        chm, cha, chv, chx, chw = self.ch["mm"], self.ch["act"], self.ch["dve"], self.ch["xb"], self.ch["wdma"]
        chd = self.ch["dma"]
        wt = self.inp["L1_attn_w_proj"]
        bt = self.inp["L1_attn_b_proj"]
        rec.do("gpsimd", lambda g: g.dma_start(f32[:, FMS + 96:FMS + 104],
               bass.AP(bt, 0, [[1, 128], [128, 8]]), allow_slow_non_contiguous=True), inc=[(chd, 16)])
        bias_done = chd.n
        xb_done = {}
        raw_chv = {}
        for oc in range(8):
            sb = AUX + (oc % 2) * 1024
            tb = (WSA + oc * 1024) if oc < 4 else (WSB + (oc - 4) * 1024)
            chs = self.ch["wg" + str(oc % 2)]
            w = [(chx, chx.n)] if oc < 2 else [(chx, xb_done[oc - 2][0])]
            rec.do("gpsimd", lambda g, d=sb, s=bass.AP(wt, oc * 128 * D, [[D, 128], [1, D]]):
                   g.dma_start(bf[:, d:d + 1024], s), wait=w, inc=[(chs, 16)])
            wdma_n = chs.n
            ii = bf[:, sb:sb + 1024].rearrange("p (c k) -> p c k", c=8)
            oo = bf[:, tb:tb + 1024].rearrange("p (c k) -> p c k", c=8)
            rec.do("sync", lambda s_, i=ii, o=oo, wn=wdma_n, cs=chs: (
                   s_.wait_ge(cs.sem, wn), s_.dma_start(o, i, transpose=True))[-1], inc=[(chx, 16)])
            xb_done[oc] = (chx.n, chm.n)
        tiles_ready = chx.n
        for ab in range(2):
            for r in range(2):
                for oc in range(8):
                    ppar = (ab * 16 + r * 8 + oc) % 2
                    p0 = self.ps[ppar]
                    tb = (WSA + oc * 1024) if oc < 4 else (WSB + (oc - 4) * 1024)
                    idx = ab * 16 + r * 8 + oc
                    wmm = [(chx, tiles_ready)] if idx == 0 else []
                    if idx - 2 >= 0:
                        wmm.append((chv, raw_chv.get(idx - 2, 0)))
                    for hc in range(8):
                        lhsT = bf[:, tb + hc * 128: tb + hc * 128 + 128]
                        rhs = bf[:, XB + (8 * r + hc) * 1024 + ab * 512: XB + (8 * r + hc) * 1024 + ab * 512 + 512]
                        rec.do("tensor", lambda t, o=p0, l=lhsT, rr=rhs, st=(hc == 0), e=(hc == 7):
                               t.matmul(o[:, 0:512], l, rr, start=st, stop=e),
                               wait=wmm if hc == 0 else [], inc=[(chm, 1)] if hc == 7 else [])
                    wst = [(chm, None)]
                    if idx == 0:
                        wst.append((chd, bias_done))
                    outap = bass.AP(bf, KVO + oc * T + 2 * ab * 512 + r, [[NBF, 128], [2, 512]])
                    xap = bass.AP(bf, XA + oc * T + 2 * ab * 512 + r, [[NBF, 128], [2, 512]])
                    rec.do("vector", lambda v, o=p0, oa=outap, xa_=xap, occ=oc: v.scalar_tensor_tensor(
                           oa, o[:, 0:512], f32[:, FMS + 96 + occ:FMS + 97 + occ], xa_,
                           ALU.add, ALU.add), wait=wst, inc=[(chv, 1)])
                    raw_chv[idx] = chv.n
            for th in range(2):
                self.emit_ln_group(2 * ab + th, KVO, XA, raw_chv[ab * 16 + 15])
            self.rec.emit_block()
        if "x4" in self.debug:
            dob = self.dbg("otok", [128, 16384], BF16)
            rec.do("scalar", lambda s: s.dma_start(dob[:, :], bf[:, XB:XB + 16384]),
                   wait=[(chv, None)], inc=[(chd, 16)])
            dn = self.dbg("x4", [128, G * T], BF16)
            dr = self.dbg("x4_raw", [128, G * T], BF16)
            rec.do("scalar", lambda s: s.dma_start(dn[:, :], bf[:, XA:XA + G * T]),
                   wait=[(chv, None)], inc=[(chd, 16)])
            rec.do("scalar", lambda s: s.dma_start(dr[:, :], bf[:, KVO:KVO + G * T]), inc=[(chd, 16)])
            rec.do("scalar", lambda s: s.wait_ge(chd.sem, chd.n), bare=True)
        self.rec.emit_block()

    def emit_quant(self, wt, nchunks, width, scratch, cells):
        nc, rec, bf, f32 = self.nc, self.rec, self.bf, self.f32
        chm, cha, chv, chx, chw, chd, chg = (self.ch["mm"], self.ch["act"], self.ch["dve"],
                                             self.ch["xb"], self.ch["wdma"], self.ch["dma"], self.ch["gp"])
        npieces = nchunks * (width // 1024)
        for eng in ("gpsimd", "scalar", "sync", "vector", "tensor"):
            rec.do(eng, lambda e, v=chd.n: e.wait_ge(chd.sem, v), bare=True)
            rec.do(eng, lambda e, v=chw.n: e.wait_ge(chw.sem, v), bare=True)
            rec.do(eng, lambda e, v=chx.n: e.wait_ge(chx.sem, v), bare=True)
        rec.do("gpsimd", lambda g: g.memset(f32[:, FMS + 112:FMS + 120], 0.0), inc=[(chg, 1)])
        mx_done = 0
        for pc in range(npieces):
            par = pc % 2
            base = FQA if par == 0 else FQB
            pbase = FQA if par == 0 else FQV
            col0 = 0 if par == 0 else 1024
            w = [(chv, mx_done), (chg, None)] if pc < 2 else [(chv, mx_done)]
            src = bass.AP(wt, pc * 128 * 1024 if width == 1024 else (pc // 4) * 128 * width + (pc % 4) * 1024,
                          [[width, 128], [1, 1024]])
            rec.do("gpsimd", lambda g, c0=col0, s=src: g.dma_start(f32[:, c0:c0 + 1024], s),
                   wait=w, inc=[(chd, 16)])
            rec.do("vector", lambda v, c0=col0: v.reduce_max(f32[:, FMS + 113:FMS + 114],
                   f32[:, c0:c0 + 1024], mybir.AxisListType.X, apply_absolute_value=True),
                   wait=[(chd, None)])

            rec.do("vector", lambda v: v.tensor_max(f32[:, FMS + 112:FMS + 113],
                   f32[:, FMS + 112:FMS + 113], f32[:, FMS + 113:FMS + 114]),
                   wait=[(chg, None)] if pc == 0 else [], inc=[(chv, 1)])
            mx_done = chv.n
        QX = FMS + 200
        self.rec.emit_block()
        rec.do("vector", lambda v: v.tensor_copy(f32[:, FQB:FQB + 4], f32[:, FMS + 112:FMS + 116]),
               wait=[(chv, mx_done)], inc=[(chv, 1)])
        rec.do("gpsimd", lambda s_, v=chv.n: (s_.wait_ge(chv.sem, v),
               s_.dma_start(bass.AP(self.rtrip, 0, [[512, 1], [1, 512]]), f32[:, FQB:FQB + 4]))[-1],
               inc=[(chd, 16)])
        rec.do("gpsimd", lambda s_, dv=chd.n: (s_.wait_ge(chd.sem, dv),
               s_.dma_start(f32[:, FQV:FQV + 512],
               bass.AP(self.rtrip, 0, [[0, 128], [1, 512]])))[-1], inc=[(chd, 16)])
        dnr = chd.n
        self.rec.emit_block()
        for j in range(4):
            rec.do("vector", lambda v, jj=j: v.reduce_max(f32[:, QX + jj:QX + jj + 1],
                   f32[:, FQV:FQV + 512], mybir.AxisListType.X),
                   wait=[(chd, dnr)])
        for j in range(4):
            rec.do("vector", lambda v, jj=j: v.tensor_copy(f32[:, QX + 24 + jj:QX + 25 + jj],
                   f32[:, FMS + 61:FMS + 62]))
        rec.do("vector", lambda v: v.tensor_copy(f32[:, FQD:FQD + 512], f32[:, FQV:FQV + 512]))
        rec.do("vector", lambda v, cc=cells: v.tensor_mul(f32[:, cc:cc + 4], f32[:, QX:QX + 4],
               f32[:, QX + 24:QX + 28]))
        rec.do("vector", lambda v: v.tensor_copy(f32[:, FQD:FQD + 512], f32[:, FQV:FQV + 512]))
        self.rec.emit_block()
        rec.do("vector", lambda v, cc=cells: v.reciprocal(f32[:, cc + 4:cc + 8], f32[:, cc:cc + 4]))
        rec.do("vector", lambda v: v.tensor_copy(f32[:, FQD:FQD + 512], f32[:, FQV:FQV + 512]))
        self.rec.emit_block()
        rec.do("vector", lambda v, cc=cells: v.tensor_scalar_mul(f32[:, cc + 8:cc + 12],
               f32[:, cc + 4:cc + 8], 2.0), inc=[(chv, 1)])
        self.rec.emit_block()
        if "qearly" in self.debug:
            dcf = self.dbg("qcf_early", [128, 8], F32)
            rec.do("vector", lambda v: v.tensor_copy(f32[:, QX + 8:QX + 12], f32[:, QX:QX + 4]), inc=[(chv, 1)])
            rec.do("scalar", lambda s_, cc=cells, v=chv.n: (s_.wait_ge(chv.sem, v),
                   s_.dma_start(dcf[:, 0:4], f32[:, cc:cc + 4]))[-1], inc=[(chd, 16)])
            rec.do("scalar", lambda s_: s_.dma_start(dcf[:, 4:8], f32[:, QX + 32:QX + 36]), inc=[(chd, 16)])
        if "qearly" in self.debug:
            dqe = self.dbg("qearly", [128, 90], F32)
            rec.do("scalar", lambda s_: s_.dma_start(dqe[:, 0:56], f32[:, FMS + 200:FMS + 256]),
                   wait=[(chv, None)], inc=[(chd, 16)])
            dbc = self.dbg("qbc", [128, 130], F32)
            rec.do("vector", lambda v: v.tensor_copy(f32[:, FQV:FQV + 128], self.ps[6][:, 0:128]),
                   inc=[(chv, 1)])
            rec.do("scalar", lambda s_: s_.dma_start(dbc[:, 0:128], f32[:, FQV:FQV + 128]),
                   wait=[(chv, None)], inc=[(chd, 16)])
            rec.do("scalar", lambda s_: s_.dma_start(dbc[:, 128:129], f32[:, FQA:FQA + 1], allow_slow_non_contiguous=True), inc=[(chd, 16)])
            rec.do("scalar", lambda s_: s_.dma_start(dbc[:, 129:130], f32[:, FMS + 112:FMS + 113], allow_slow_non_contiguous=True), inc=[(chd, 16)])
        cells_done = chv.n
        for pc in range(npieces):
            w = [(chv, cells_done)]
            if pc >= 1:
                w.append((chv, None))
                w.append((chx, None))
            src = bass.AP(wt, pc * 128 * 1024 if width == 1024 else (pc // 4) * 128 * width + (pc % 4) * 1024,
                          [[width, 128], [1, 1024]])
            rec.do("gpsimd", lambda g, s=src: g.dma_start(f32[:, 0:1024], s),
                   wait=w + [(chx, chx.n), (chw, chw.n)], inc=[(chd, 16)])
            dn = chd.n
            sb = WSB + (pc % 2) * 1024
            for hh in range(2):
                c0 = hh * 512
                wch = [(chd, dn), (chx, chx.n), (chw, chw.n)] if hh == 0 else []
                rec.do("vector", lambda v, cc=c0, cell=cells: v.tensor_scalar_mul(
                       f32[:, FQV:FQV + 512], f32[:, cc:cc + 512], f32[:, cell + 4:cell + 5]),
                       wait=wch)
                rec.do("vector", lambda v: v.tensor_copy(self.i32[:, 0:512], f32[:, FQV:FQV + 512]))
                rec.do("vector", lambda v: v.tensor_copy(f32[:, FQC:FQC + 512], self.i32[:, 0:512]))
                rec.do("vector", lambda v: v.tensor_sub(f32[:, FQD:FQD + 512], f32[:, FQV:FQV + 512],
                       f32[:, FQC:FQC + 512]))
                rec.do("vector", lambda v: v.scalar_tensor_tensor(f32[:, FST:FST + 512],
                       f32[:, FQD:FQD + 512], 0.5, f32[:, FQC:FQC + 512], ALU.is_ge, ALU.add))
                rec.do("vector", lambda v: v.tensor_single_scalar(f32[:, FQV:FQV + 512],
                       f32[:, FQD:FQD + 512], -0.5, ALU.is_le))
                rec.do("vector", lambda v, s_=sb, hh_=hh: v.tensor_sub(bf[:, s_ + hh_ * 512:s_ + hh_ * 512 + 512],
                       f32[:, FST:FST + 512], f32[:, FQV:FQV + 512]), inc=[(chv, 1)])
            qv = chv.n
            ii = bf[:, sb:sb + 1024].rearrange("p (c k) -> p c k", c=8)
            oo = bf[:, WSA + (pc % 2) * 1024:WSA + (pc % 2) * 1024 + 1024].rearrange("p (c k) -> p c k", c=8)
            rec.do("sync", lambda s_, i=ii, o=oo, v=qv, wv=chw.n, dv=chd.n: (s_.wait_ge(chv.sem, v),
                   s_.wait_ge(chw.sem, wv), s_.wait_ge(chd.sem, dv),
                   s_.dma_start(o, i, transpose=True))[-1], inc=[(chx, 16)])
            xn = chx.n
            if width == 1024:
                dst = bass.AP(scratch, pc * 128 * 1024, [[1024, 128], [1, 1024]])
            else:
                dst = bass.AP(scratch, (pc // 4) * 128 * 4096 + (pc % 4) * 1024, [[4096, 128], [1, 1024]])
            rec.do("gpsimd", lambda s_, d=dst, src2=bf[:, WSA + (pc % 2) * 1024:WSA + (pc % 2) * 1024 + 1024], xnn=xn:
                   (s_.wait_ge(chx.sem, xnn), s_.dma_start(d, src2))[-1], inc=[(chw, 16)])
            if pc == 16 and "qearly" in self.debug:
                dmid = self.dbg("qmid", [128, 16], F32)
                rec.do("scalar", lambda s_, cc=cells: s_.dma_start(dmid[:, 0:12], f32[:, cc:cc + 12]),
                       inc=[(chd, 16)])
            self.rec.emit_block()

    def phase_l1_ffn(self):
        nc, rec, bf, f32 = self.nc, self.rec, self.bf, self.f32
        chm, cha, chv, chx, chw, chd, chg = (self.ch["mm"], self.ch["act"], self.ch["dve"],
                                             self.ch["xb"], self.ch["wdma"], self.ch["dma"], self.ch["gp"])
        self.emit_quant(self.inp["L1_ffn_W_fc"], 32, 1024, self.wfcT, FMS + 232)
        self.rec.emit_block()
        self.emit_quant(self.inp["L1_ffn_W_proj"], 8, 4096, self.wprT, FMS + 244)
        self.rec.emit_block()
        AFC = AUX + 512
        APR = AUX + 640
        BST = AUX
        L1T = AUX + 1152
        L2T = AUX + 1664
        rec.do("gpsimd", lambda g: g.dma_start(
               bf[:, AFC:AFC + 128].rearrange("p (b f) -> p b f", b=8),
               bass.AP(self.inp["AfcT"], 0, [[16, 128], [2048, 8], [1, 16]])), inc=[(chw, 16)])
        rec.do("gpsimd", lambda g: g.dma_start(
               bf[:, APR:APR + 512].rearrange("p (b f) -> p b f", b=32),
               bass.AP(self.inp["AprT"], 0, [[16, 128], [2048, 32], [1, 16]])), inc=[(chw, 16)])
        rec.do("gpsimd", lambda g: g.dma_start(f32[:, FMS + 128:FMS + 160],
               bass.AP(self.inp["L1_ffn_b_fc"], 0, [[1, 128], [128, 32]]),
               allow_slow_non_contiguous=True), inc=[(chw, 16)])
        rec.do("gpsimd", lambda g: g.dma_start(f32[:, FMS + 160:FMS + 168],
               bass.AP(self.inp["L1_ffn_b_proj"], 0, [[1, 128], [128, 8]]),
               allow_slow_non_contiguous=True), inc=[(chw, 16)])
        consts_done = chw.n
        raw_chv = 0
        h2_marker = {}
        for tg in range(4):
            for ic in range(8):
                lhsT = bf[:, AFC + ic * 16: AFC + ic * 16 + 16]
                rhs = bf[:, XA + ic * T + tg * 512: XA + ic * T + tg * 512 + 512]
                w = [(chw, consts_done)] if ic == 0 else []
                if ic == 0:
                    w.append((cha, cha.n))
                rec.do("tensor", lambda t, l=lhsT, r=rhs, st=(ic == 0), e=(ic == 7):
                       t.matmul(self.ps[4][0:16, 0:512], l, r, start=st, stop=e),
                       wait=w, inc=[(chm, 1)] if ic == 7 else [])
            rec.do("scalar", lambda s: s.activation(bf[0:16, L1T:L1T + 512], self.ps[4][0:16, 0:512],
                   AF.Copy, scale=f32[0:16, FMS + 240:FMS + 241]), wait=[(chm, None)], inc=[(cha, 1)])
            l1_done = cha.n
            evac_done = {}
            for oc in range(32):
                par = oc % 2
                wsb = WSA + par * 1024
                chs = self.ch["wg" + str(par)]
                chb = self.ch["bt" + str(par)]
                w = [(chm, chm.n)]
                if oc >= 2:
                    w.append((cha, evac_done[oc - 2]))
                rec.do("gpsimd", lambda g, d=wsb, o=oc: g.dma_start(bf[:, d:d + 1024],
                       bass.AP(self.wfcT, o * 128 * 1024, [[1024, 128], [1, 1024]])),
                       wait=w, inc=[(chs, 16)])
                rec.do("gpsimd", lambda g, o=oc, p2=par: g.dma_start(bf[0:16, BST + p2 * 128:BST + p2 * 128 + 128],
                       bass.AP(self.inp["BfcT"], o * 128, [[H1, 16], [1, 128]])), inc=[(chb, 16)])
                wns, bns = chs.n, chb.n
                p0 = self.ps[par]
                for ic in range(8):
                    lhsT = bf[:, wsb + ic * 128: wsb + ic * 128 + 128]
                    rhs = bf[:, XA + ic * T + tg * 512: XA + ic * T + tg * 512 + 512]
                    wm = [(chs, wns), (cha, l1_done)] if ic == 0 else []
                    rec.do("tensor", lambda t, o=p0, l=lhsT, r=rhs, st=(ic == 0):
                           t.matmul(o[:, 0:512], l, r, start=st, stop=False),
                           wait=wm, inc=[])
                rec.do("tensor", lambda t, o=p0, p2=par: t.matmul(o[:, 0:512],
                       bf[0:16, BST + p2 * 128:BST + p2 * 128 + 128], bf[0:16, L1T:L1T + 512],
                       start=False, stop=True), wait=[(chb, bns)], inc=[(chm, 1)])
                rec.do("scalar", lambda s, o=p0, oc_=oc: s.activation(
                       bf[:, QQ + oc_ * 512: QQ + oc_ * 512 + 512], o[:, 0:512], AF.Gelu,
                       scale=f32[:, FMS + 232:FMS + 233], bias=f32[:, FMS + 128 + oc_:FMS + 129 + oc_]),
                       wait=[(chm, None)], inc=[(cha, 1)])
                evac_done[oc] = cha.n
            h2_done = cha.n
            for hc in range(32):
                lhsT = bf[:, APR + hc * 16: APR + hc * 16 + 16]
                rhs = bf[:, QQ + hc * 512: QQ + hc * 512 + 512]
                w = [(cha, h2_done)] if hc == 0 else []
                rec.do("tensor", lambda t, l=lhsT, r=rhs, st=(hc == 0), e=(hc == 31):
                       t.matmul(self.ps[4][0:16, 0:512], l, r, start=st, stop=e),
                       wait=w, inc=[(chm, 1)] if hc == 31 else [])
            rec.do("scalar", lambda s: s.activation(bf[0:16, L2T:L2T + 512], self.ps[4][0:16, 0:512],
                   AF.Copy, scale=f32[0:16, FMS + 252:FMS + 253]), wait=[(chm, None)], inc=[(cha, 1)])
            l2_done = cha.n
            prev_evac = {}
            subdone = {}
            for oc in range(8):
                p0 = self.ps[oc % 2]
                for sub in range(4):
                    gsub = oc * 4 + sub
                    par = gsub % 2
                    wsb = WSA + 2048 + par * 1024
                    chs = self.ch["wg" + str(par)]
                    w = [(chm, subdone.get(gsub - 2) if gsub >= 2 else chm.n)]
                    if tg == 0 and gsub == 0:
                        w.append((chw, chw.n))
                    rec.do("gpsimd", lambda g, d=wsb, o=oc, s2=sub: g.dma_start(bf[:, d:d + 1024],
                           bass.AP(self.wprT, o * 128 * 4096 + s2 * 1024, [[4096, 128], [1, 1024]])),
                           wait=w, inc=[(chs, 16)])
                    wn = chs.n
                    for hin in range(8):
                        hc = sub * 8 + hin
                        lhsT = bf[:, wsb + hin * 128: wsb + hin * 128 + 128]
                        rhs = bf[:, QQ + hc * 512: QQ + hc * 512 + 512]
                        wm = [(chs, wn), (cha, l2_done)] if hin == 0 else []
                        if oc >= 2 and hin == 0 and sub == 0:
                            wm.append((chv, prev_evac[oc - 2]))
                        rec.do("tensor", lambda t, o=p0, l=lhsT, r=rhs, st=(hc == 0):
                               t.matmul(o[:, 0:512], l, r, start=st, stop=False),
                               wait=wm, inc=[(chm, 1)] if hin == 7 else [])
                    subdone[gsub] = chm.n
                chb = self.ch["bt0"]
                rec.do("gpsimd", lambda g, o=oc: g.dma_start(bf[0:16, BST + 256:BST + 384],
                       bass.AP(self.inp["BprT"], o * 128, [[D, 16], [1, 128]])),
                       wait=[(chm, chm.n)], inc=[(chb, 16)])
                bw = chb.n
                rec.do("tensor", lambda t, o=p0: t.matmul(o[:, 0:512],
                       bf[0:16, BST + 256:BST + 384],
                       bf[0:16, L2T:L2T + 512], start=False, stop=True),
                       wait=[(chb, bw)], inc=[(chm, 1)])
                t1 = A_PT + 1536
                rec.do("vector", lambda v, o=p0, oc_=oc: v.tensor_scalar(
                       bf[:, t1:t1 + 512] if False else bf[:, A_PT + 1536:A_PT + 2048], o[:, 0:512],
                       f32[:, FMS + 244:FMS + 245], f32[:, FMS + 160 + oc_:FMS + 161 + oc_], ALU.mult, ALU.add),
                       wait=[(chm, None)])
                rcol = XB + oc * T + tg * 512
                xcol = XA + oc * T + tg * 512
                rec.do("vector", lambda v, rc=rcol, xc=xcol: v.tensor_add(
                       bf[:, rc:rc + 512], bf[:, A_PT + 1536:A_PT + 2048], bf[:, xc:xc + 512]),
                       inc=[(chv, 1)])
                prev_evac[oc] = chv.n
            raw_chv = chv.n
            self.emit_ln_group(tg, XB, XA, raw_chv)
        if "x5" in self.debug:
            chd2 = self.ch["dma"]
            dn = self.dbg("x5", [128, G * T], BF16)
            dr = self.dbg("x5_raw", [128, G * T], BF16)
            dh2 = self.dbg("h2", [128, 16384], BF16)
            dc_ = self.dbg("cells", [128, 32], F32)
            rec.do("scalar", lambda s: s.dma_start(dn[:, :], bf[:, XA:XA + G * T]),
                   wait=[(chv, None)], inc=[(chd2, 16)])
            rec.do("scalar", lambda s: s.dma_start(dr[:, :], bf[:, XB:XB + G * T]), inc=[(chd2, 16)])
            rec.do("scalar", lambda s: s.dma_start(dh2[:, :], bf[:, QQ:QQ + 16384]), inc=[(chd2, 16)])
            rec.do("scalar", lambda s: s.dma_start(dc_[:, 0:24], f32[:, FMS + 232:FMS + 256]), inc=[(chd2, 16)])

            rec.do("scalar", lambda s: s.wait_ge(chd2.sem, chd2.n), bare=True)
        self.rec.emit_block()

    def phase_output(self):
        nc, rec, bf = self.nc, self.rec, self.bf
        chx, chd = self.ch["xb"], self.ch["dma"]
        for g in range(8):
            ii = bf[:, XA + g * T: XA + (g + 1) * T].rearrange("p (c k) -> p c k", c=16)
            oo = bass.AP(bf, XB + g * 128, [[NBF, 128], [1024, 16], [1, 128]])
            rec.do("sync", lambda s, i=ii, o=oo: s.dma_start(o, i, transpose=True), inc=[(chx, 16)])
        xall = chx.n
        for tc in range(16):
            src = bf[:, XB + tc * 1024: XB + (tc + 1) * 1024]
            dst = bass.AP(self.out, tc * 128 * D, [[D, 128], [1, D]])
            rec.do("gpsimd", lambda g, d=dst, s=src: g.dma_start(d, s),
                   wait=[(chx, xall)] if tc == 0 else [], inc=[(chd, 16)])
        rec.do("gpsimd", lambda g: g.wait_ge(chd.sem, chd.n), bare=True)
        self.rec.emit_block()




def kernel(**inputs):
    n = 8
    x_full = np.ascontiguousarray(inputs["L0_attn_x"])
    b = Builder(stage="all", debug=())
    b.build()
    consts = host_constants()
    lora = layout_lora(inputs)
    weight_names = [k for k in inputs.keys() if k != "L0_attn_x"]
    in_maps = []
    for c in range(n):
        m = {"x": np.ascontiguousarray(x_full[c])}
        for k in weight_names:
            m[k] = inputs[k]
        m.update(consts)
        m.update(lora)
        in_maps.append(m)
    res = run_bass_kernel_spmd(b.nc, in_maps, core_ids=list(range(n)))
    y = np.stack([np.asarray(res.results[c]["y"]) for c in range(n)], axis=0)
    return y.astype(np.float32)


# version 14
# speedup vs baseline: 1.1192x; 1.1045x over previous
import numpy as np
import ml_dtypes
import concourse.bass as bass
import concourse.mybir as mybir
from concourse.bass_utils import run_bass_kernel_spmd

F32 = mybir.dt.float32
BF16 = mybir.dt.bfloat16
I32 = mybir.dt.int32
AF = mybir.ActivationFunctionType
ALU = mybir.AluOpType
BF = ml_dtypes.bfloat16

T = 2048
D = 1024
G = 8
NH = 16
DH = 64
NKC = 16
NQG = 4
PIF = float(np.float32(np.pi))
TWO_PI = 2.0 * np.pi
CW1 = float(np.float32(TWO_PI)); CW2 = float(np.float32(TWO_PI - CW1))
CW3 = float(np.float32(TWO_PI - CW1 - CW2))
C_L0 = 8.0
C_L1 = 8.0
H0 = 3072
H1 = 4096

XA = 0
XB = 16384
KVO = 32768
QQ = 49152
OO = 65536
WSA = 81920
WSB = 86016
AUX = 90112
A_SIN = 90112
A_COS = 92160
A_RT = 94208
A_KVT = 90112
A_PT = 94208
A_LB = 96256
A_BC = 97280
CONST0 = 98304
ONESCOL = 98432
ONESROW = 98440
NBF = 98568

FQA = 0
FQB = 512
FQV = 1024
FQC = 1536
FQD = 2048
FST = 2560
FMS = 3072
NF32 = 3328
TB_TP = 0
TB_KT = 512
TB_TH = 1024
TB_T2 = 1536

NINT = 512


def host_constants():
    c = {}
    c["tpos"] = np.broadcast_to(np.arange(T, dtype=np.float64), (128, T)).astype(np.float32).copy()
    jmod = (np.arange(128) % 32).astype(np.float64)
    c["jarg"] = np.broadcast_to((-(jmod / 32.0)).astype(np.float32).reshape(128, 1), (128, 8)).copy()
    inv_freq = 1.0 / 10000.0 ** (np.arange(0, DH, 2, dtype=np.float64) / DH)
    theta = np.arange(T, dtype=np.float64)[None, :] * inv_freq[np.arange(128) % 32][:, None]
    c["ktab"] = np.round(theta / TWO_PI).astype(np.float32)
    c["ktabc"] = np.round((theta + np.pi / 2) / TWO_PI).astype(np.float32)
    r2t = np.zeros((128, 128), dtype=np.float32)
    for blk in range(2):
        o = 64 * blk
        for j in range(32):
            r2t[o + j + 32, o + j] = -1.0
            r2t[o + j, o + j + 32] = 1.0
    c["r2t"] = r2t.astype(BF)
    c["ident"] = np.eye(128, dtype=np.float32).astype(BF)
    c["identf32"] = np.eye(128, dtype=np.float32)
    ind2 = np.zeros((2, 128), dtype=np.float32)
    ind2[0, 0:64] = 1.0
    ind2[1, 64:128] = 1.0
    c["ind2"] = ind2
    c["ones1x128"] = np.ones((1, 128), dtype=np.float32)
    return c


def layout_lora(ins):
    return {
        "AfcT": np.ascontiguousarray(ins["L1_ffn_A_fc"].T),
        "BfcT": np.ascontiguousarray(ins["L1_ffn_B_fc"].T),
        "AprT": np.ascontiguousarray(ins["L1_ffn_A_proj"].T),
        "BprT": np.ascontiguousarray(ins["L1_ffn_B_proj"].T),
    }


class Chan:
    def __init__(self, sem):
        self.sem = sem
        self.n = 0


class Rec:
    ENGS = ("tensor", "vector", "scalar", "gpsimd", "sync")

    def __init__(self, nc):
        self.nc = nc
        self.ops = {e: [] for e in self.ENGS}
        self.last = {}
        self.drain = {}

    def do(self, eng, fn, wait=(), inc=(), bare=False):
        w = [(ch, (ch.n if v is None else v)) for ch, v in wait]
        ii = []
        for ch, k in inc:
            ch.n += k
            ii.append((ch, k))
        isdma = any(k == 16 for _, k in ii)
        dinc = None
        if not (bare or isdma) and eng in self.drain:
            w = [self.last.get(eng)] + w if self.last.get(eng) else w
            if ii:
                self.last[eng] = (ii[0][0], ii[0][0].n)
            else:
                dch = self.drain[eng]
                dch.n += 1
                dinc = (dch, 1)
                self.last[eng] = (dch, dch.n)

        def run(e, fn=fn, w=w, ii=ii, dinc=dinc):
            for ch, v in w:
                e.wait_ge(ch.sem, v)
            ins = fn(e)
            if dinc is not None:
                ins = ins.then_inc(dinc[0].sem, dinc[1])
            for ch, k in ii:
                ins = ins.then_inc(ch.sem, k)
        self.ops[eng].append(run)

    def emit_block(self):
        with self.nc.Block() as blk:
            for name in self.ENGS:
                ops = self.ops[name]
                if not ops:
                    continue
                getattr(blk, name)(lambda e, ops=ops: [f(e) for f in ops])
        self.ops = {e: [] for e in self.ENGS}


class Builder:
    def __init__(self, stage="all", debug=()):
        self.stage = stage
        self.debug = set(debug)
        self.nc = bass.Bass("TRN2", target_bir_lowering=False)
        nc = self.nc
        self.inp = {}
        self.dbg_out = {}
        self.inp["x"] = nc.dram_tensor("x", [T, D], F32, kind="ExternalInput")
        for nm, shp, dt in [
            ("L0_attn_Wq", [D, D], F32), ("L0_attn_Wkv", [D, D], F32),
            ("L0_attn_Wout", [D, D], F32), ("L0_attn_bout", [D], F32),
            ("L0_ffn_w_gate", [H0, D], I32), ("L0_ffn_w_up", [H0, D], I32),
            ("L0_ffn_w_down", [D, H0], I32),
            ("L1_attn_w_qkv", [3 * D, D], F32), ("L1_attn_w_proj", [D, D], F32),
            ("L1_attn_b_proj", [D], F32),
            ("L1_ffn_W_fc", [H1, D], F32), ("L1_ffn_b_fc", [H1], F32),
            ("L1_ffn_A_fc", [16, D], F32), ("L1_ffn_B_fc", [H1, 16], F32),
            ("L1_ffn_W_proj", [D, H1], F32), ("L1_ffn_b_proj", [D], F32),
            ("L1_ffn_A_proj", [16, H1], F32), ("L1_ffn_B_proj", [D, 16], F32),
        ]:
            self.inp[nm] = nc.dram_tensor(nm, shp, dt, kind="ExternalInput")
        for nm, shp in [("tpos", [128, T]), ("jarg", [128, 8]), ("ktab", [128, T]),
                        ("ktabc", [128, T]), ("ind2", [2, 128]), ("ones1x128", [1, 128])]:
            self.inp[nm] = nc.dram_tensor(nm, shp, F32, kind="ExternalInput")
        for nm, shp in [("r2t", [128, 128]), ("ident", [128, 128])]:
            self.inp[nm] = nc.dram_tensor(nm, shp, BF16, kind="ExternalInput")
        for nm, shp in [("AfcT", [D, 16]), ("BfcT", [16, H1]), ("AprT", [H1, 16]), ("BprT", [16, D]),
                        ("identf32", [128, 128])]:
            self.inp[nm] = nc.dram_tensor(nm, shp, F32, kind="ExternalInput")
        self.out = nc.dram_tensor("y", [T, D], F32, kind="ExternalOutput")
        self.wfcT = nc.dram_tensor("wfcT", [32, 128, 1024], BF16)
        self.wprT = nc.dram_tensor("wprT", [8, 128, 4096], BF16)
        self.rtrip = nc.dram_tensor("rtrip", [1, 512], F32)
        self.ostage = nc.dram_tensor("ostage", [T, D], BF16)

        self._ctx = []

    def dbg(self, name, shape, dtype=F32):
        t = self.nc.dram_tensor("dbg_" + name, shape, dtype, kind="ExternalOutput")
        self.dbg_out[name] = t
        return t

    def build(self):
        nc = self.nc
        import contextlib
        self._stack = contextlib.ExitStack()
        st = self._stack
        self.bf = st.enter_context(nc.sbuf_tensor("bfa", [128, NBF], BF16))
        self.f32 = st.enter_context(nc.sbuf_tensor("f32a", [128, NF32], F32))
        self.i32 = st.enter_context(nc.sbuf_tensor("i32a", [128, NINT], I32))
        self.ps = [st.enter_context(nc.psum_tensor(f"pb{i}", [128, 512], F32)) for i in range(7)]

        self.psb = st.enter_context(nc.psum_tensor("pbf", [128, 1024], BF16))
        self.ch = {}
        for nm in ["dma", "wdma", "mm", "act", "dve", "gp", "xb",
                   "wg0", "wg1", "wu0", "wu1", "bt0", "bt1"]:
            self.ch[nm] = Chan(st.enter_context(nc.semaphore("ch_" + nm)))
        self.rec = Rec(nc)
        for eng in ("vector", "scalar", "gpsimd"):
            self.rec.drain[eng] = Chan(st.enter_context(nc.semaphore("dr_" + eng)))

        self.phase_setup()
        if self.stage == "setup":
            self.finish(); return self
        if self.stage == "quant":
            chd = self.ch["dma"]
            self.emit_quant(self.inp["L1_ffn_W_fc"], 32, 1024, self.wfcT, FMS + 232)
            dq = self.dbg("qcells", [128, 16], F32)
            self.rec.do("scalar", lambda s: s.dma_start(dq[:, :], self.f32[:, FMS + 232:FMS + 248]),
                        wait=[(self.ch["chv"] if False else self.ch["dve"], None),
                              (self.ch["wdma"], None)], inc=[(chd, 16)])
            dw = self.dbg("qtiles", [128, 1024], mybir.dt.bfloat16)
            self.rec.do("gpsimd", lambda g: g.dma_start(dw[:, :],
                        bass.AP(self.wfcT, 0, [[1024, 128], [1, 1024]])), wait=[(self.ch["wdma"], None)], inc=[(chd, 16)])
            self.rec.do("scalar", lambda s: s.wait_ge(chd.sem, chd.n), bare=True)
            self.finish(); return self
        self.phase_l0_proj()
        if self.stage == "l0proj":
            self.finish(); return self
        self.phase_l0_attn()
        self.phase_l0_out_ln()
        if self.stage == "x2":
            self.finish(); return self
        self.phase_l0_ffn()
        if self.stage == "x3":
            self.finish(); return self
        self.phase_l1_proj()
        self.phase_l1_attn()
        self.phase_l1_out_ln()
        if self.stage == "x4":
            self.finish(); return self
        self.phase_l1_ffn()
        self.phase_output()
        self.finish()
        return self

    def finish(self):
        self.rec.emit_block()
        self._stack.close()

    def bfv(self, col, n, p0=0, np_=128):
        return self.bf[p0:p0 + np_, col:col + n]

    def xf(self, base, g, t0, nt, p0=0, np_=128):
        return self.bf[p0:p0 + np_, base + g * T + t0: base + g * T + t0 + nt]

    def phase_setup(self):
        nc, rec, ch, bf, f32 = self.nc, self.rec, self.ch, self.bf, self.f32
        inp = self.inp
        xsrc = inp["x"]
        chd = self.ch["dma"]
        for tc in range(16):
            dst = bf[:, XB + tc * 1024: XB + (tc + 1) * 1024]
            src = bass.AP(xsrc, tc * 128 * D, [[D, 128], [1, D]])
            rec.do("gpsimd", lambda g, d=dst, s=src: g.dma_start(d, s), inc=[(chd, 16)])
        rec.do("gpsimd", lambda g: g.dma_start(bf[:, CONST0:CONST0 + 128], inp["r2t"][:, :]), inc=[(chd, 16)])
        chg = self.ch["gp"]
        for off, val in ((48, -C_L0), (49, 1e-5), (50, 0.125), (51, -C_L1),
                         (52, 2.0), (53, 1.0 / 16129.0), (54, 1.0 / 127.0), (55, -0.5)):
            rec.do("gpsimd", lambda g, o=off, v=val: g.memset(f32[:, FMS + o:FMS + o + 1], v), inc=[(chg, 1)])
        rec.do("gpsimd", lambda g: g.memset(f32[:, FMS + 60:FMS + 61], 127.0), inc=[(chg, 1)])
        rec.do("gpsimd", lambda g: g.memset(bf[:, ONESCOL:ONESCOL + 8], 1.0), inc=[(chg, 1)])
        rec.do("gpsimd", lambda g: g.memset(bf[:, ONESROW:ONESROW + 128], 1.0), inc=[(chg, 1)])
        rec.do("gpsimd", lambda g: g.memset(f32[:, FMS:FMS + 1], 10000.0), inc=[(chg, 1)])
        rec.do("gpsimd", lambda g: g.dma_start(f32[:, FMS + 16:FMS + 24], inp["jarg"][:, :]), inc=[(chd, 16)])
        chm, cha, chv = self.ch["mm"], self.ch["act"], self.ch["dve"]
        rec.do("scalar", lambda s: s.activation(f32[:, FMS + 1:FMS + 2], f32[:, FMS:FMS + 1], AF.Ln),
               wait=[(chd, None), (self.ch["gp"], None)], inc=[(cha, 1)])
        rec.do("vector", lambda v: v.tensor_scalar_mul(f32[:, FMS + 17:FMS + 18], f32[:, FMS + 16:FMS + 17],
                                                       f32[:, FMS + 1:FMS + 2]),
               wait=[(cha, None), (chd, None)], inc=[(chv, 1)])
        rec.do("scalar", lambda s: s.activation(f32[:, FMS + 18:FMS + 19], f32[:, FMS + 17:FMS + 18], AF.Exp),
               wait=[(chv, None)], inc=[(cha, 1)])
        inv_col = FMS + 18
        rec.do("vector", lambda v: v.reciprocal(f32[:, FMS + 61:FMS + 62], f32[:, FMS + 60:FMS + 61]),
               wait=[(self.ch["gp"], None)], inc=[(chv, 1)])
        for qp in range(4):
            t0 = 512 * qp
            tp, kt, th, t2 = TB_TP, TB_KT, TB_TH, TB_T2
            rec.do("gpsimd", lambda g, t0=t0: g.dma_start(f32[:, tp:tp + 512],
                   bass.AP(inp["tpos"], t0, [[T, 128], [1, 512]])), wait=[(chv, None)], inc=[(chd, 16)])
            rec.do("gpsimd", lambda g, t0=t0: g.dma_start(f32[:, kt:kt + 512],
                   bass.AP(inp["ktab"], t0, [[T, 128], [1, 512]])), inc=[(chd, 16)])
            rec.do("vector", lambda v: v.tensor_scalar_mul(f32[:, th:th + 512], f32[:, tp:tp + 512],
                                                           f32[:, inv_col:inv_col + 1]),
                   wait=[(chd, None), (cha, None), (chv, None)], inc=[(chv, 1)])
            rec.do("vector", lambda v: v.scalar_tensor_tensor(f32[:, t2:t2 + 512], f32[:, kt:kt + 512],
                   -CW1, f32[:, th:th + 512], ALU.mult, ALU.add))
            rec.do("vector", lambda v: v.scalar_tensor_tensor(f32[:, tp:tp + 512], f32[:, kt:kt + 512],
                   -CW2, f32[:, t2:t2 + 512], ALU.mult, ALU.add))
            rec.do("vector", lambda v: v.scalar_tensor_tensor(f32[:, t2:t2 + 512], f32[:, kt:kt + 512],
                   -CW3, f32[:, tp:tp + 512], ALU.mult, ALU.add))
            rec.do("vector", lambda v: v.tensor_scalar_min(f32[:, tp:tp + 512], f32[:, t2:t2 + 512], PIF))
            rec.do("vector", lambda v: v.tensor_scalar_max(f32[:, t2:t2 + 512], f32[:, tp:tp + 512], -PIF),
                   inc=[(chv, 1)])
            rec.do("scalar", lambda s, t0=t0: s.activation(bf[:, A_SIN + t0:A_SIN + t0 + 512],
                   f32[:, t2:t2 + 512], AF.Sin), wait=[(chv, None)], inc=[(cha, 1)])
            rec.do("gpsimd", lambda g, t0=t0: g.dma_start(f32[:, kt:kt + 512],
                   bass.AP(inp["ktabc"], t0, [[T, 128], [1, 512]])), wait=[(cha, None)], inc=[(chd, 16)])
            rec.do("vector", lambda v: v.tensor_scalar_add(f32[:, tp:tp + 512], f32[:, th:th + 512],
                   float(np.float32(np.pi / 2))), wait=[(chd, None), (cha, None)])
            rec.do("vector", lambda v: v.scalar_tensor_tensor(f32[:, t2:t2 + 512], f32[:, kt:kt + 512],
                   -CW1, f32[:, tp:tp + 512], ALU.mult, ALU.add))
            rec.do("vector", lambda v: v.scalar_tensor_tensor(f32[:, tp:tp + 512], f32[:, kt:kt + 512],
                   -CW2, f32[:, t2:t2 + 512], ALU.mult, ALU.add))
            rec.do("vector", lambda v: v.scalar_tensor_tensor(f32[:, t2:t2 + 512], f32[:, kt:kt + 512],
                   -CW3, f32[:, tp:tp + 512], ALU.mult, ALU.add))
            rec.do("vector", lambda v: v.tensor_scalar_min(f32[:, tp:tp + 512], f32[:, t2:t2 + 512], PIF))
            rec.do("vector", lambda v: v.tensor_scalar_max(f32[:, t2:t2 + 512], f32[:, tp:tp + 512], -PIF),
                   inc=[(chv, 1)])
            rec.do("scalar", lambda s, t0=t0: s.activation(bf[:, A_COS + t0:A_COS + t0 + 512],
                   f32[:, t2:t2 + 512], AF.Sin), wait=[(chv, None)], inc=[(cha, 1)])
        chx = self.ch["xb"]
        for tc in range(16):
            inap = bf[:, XB + tc * 1024: XB + (tc + 1) * 1024].rearrange("p (g d) -> p g d", g=8)
            outap = bass.AP(self.bf, XA + tc * 128, [[NBF, 128], [T, 8], [1, 128]])
            rec.do("sync", lambda s, i=inap, o=outap: s.dma_start(o, i, transpose=True),
                   wait=[(chd, None)], inc=[(chx, 16)])
        chw = self.ch["wdma"]
        wq = inp["L0_attn_Wq"]
        for h, base in ((0, WSA), (1, WSB)):
            for ic in range(8):
                dst = bf[:, base + ic * 512: base + (ic + 1) * 512]
                src = bass.AP(wq, ic * 128 * D + h * 512, [[D, 128], [1, 512]])
                rec.do("gpsimd", lambda g, d=dst, s=src: g.dma_start(d, s), inc=[(chw, 16)])
        if "tabdeep" in self.debug:
            dinv = self.dbg("invf", [128, 8], F32)
            dth = self.dbg("theta", [128, 512], F32)
            rec.do("scalar", lambda s: s.dma_start(dinv[:, 0:8], f32[:, inv_col:inv_col + 8]),
                   wait=[(self.ch["act"], None)], inc=[(chd, 16)])
            rec.do("scalar", lambda s: s.dma_start(dth[:, :], f32[:, TB_TH:TB_TH + 512]), inc=[(chd, 16)])
            dj = self.dbg("jchain", [128, 8], F32)
            rec.do("vector", lambda v: v.tensor_copy(f32[:, FMS + 300:FMS + 301], self.ps[0][:, 0:1]), wait=[(self.ch["mm"], None)])
            rec.do("vector", lambda v: v.tensor_copy(f32[:, FMS + 301:FMS + 302], f32[:, FMS + 16:FMS + 17]))
            rec.do("vector", lambda v: v.tensor_copy(f32[:, FMS + 302:FMS + 303], f32[:, FMS + 17:FMS + 18]).then_inc(self.ch["dve"].sem, 1))
            rec.do("scalar", lambda s: s.dma_start(dj[:, 0:3], f32[:, FMS + 300:FMS + 303]), wait=[(self.ch["dve"], self.ch["dve"].n + 1)], inc=[(chd, 16)])
        if "tables" in self.debug:
            dsin = self.dbg("sin", [128, T], BF16); dcos = self.dbg("cos", [128, T], BF16)
            dxa = self.dbg("xa", [128, G * T], BF16); dxb = self.dbg("xb", [128, 16384], BF16)
            rec.do("scalar", lambda s: s.dma_start(dsin[:, :], bf[:, A_SIN:A_SIN + T]),
                   wait=[(self.ch["act"], None)], inc=[(chd, 16)])
            rec.do("scalar", lambda s: s.dma_start(dcos[:, :], bf[:, A_COS:A_COS + T]), inc=[(chd, 16)])
            rec.do("scalar", lambda s: s.dma_start(dxa[:, :], bf[:, XA:XA + 16384]),
                   wait=[(chx, None)], inc=[(chd, 16)])
            rec.do("scalar", lambda s: s.dma_start(dxb[:, :], bf[:, XB:XB + 16384]), inc=[(chd, 16)])
            rec.do("scalar", lambda s: s.wait_ge(chd.sem, chd.n), bare=True)
        self.rec.emit_block()

    def phase_l0_proj(self):
        nc, rec, bf = self.nc, self.rec, self.bf
        chm, cha, chv, chw = self.ch["mm"], self.ch["act"], self.ch["dve"], self.ch["wdma"]
        evac1_done = {}
        it = 0
        for proj, dstbase in (("q", QQ), ("kv", KVO)):
            if proj == "kv":
                mm_after_q = chm.n
                wkv = self.inp["L0_attn_Wkv"]
                for h, base in ((0, WSA), (1, WSB)):
                    for ic in range(8):
                        dst = bf[:, base + ic * 512: base + (ic + 1) * 512]
                        src = bass.AP(wkv, ic * 128 * D + h * 512, [[D, 128], [1, 512]])
                        rec.do("gpsimd", lambda g, d=dst, s=src: g.dma_start(d, s),
                               wait=[(chm, mm_after_q)] if ic == 0 and h == 0 else [], inc=[(chw, 16)])
            for oc in range(8):
                wbase = WSA if oc < 4 else WSB
                ocol = (oc % 4) * 128
                for tg in range(4):
                    par = it % 2
                    p0 = self.ps[par]
                    p1 = self.ps[2 + par]
                    w_pe = [(chw, None), (self.ch["xb"], None)]
                    if it - 2 in evac1_done:
                        w_pe.append((cha, evac1_done[it - 2]))
                    for ic in range(8):
                        lhsT = bf[:, wbase + ic * 512 + ocol: wbase + ic * 512 + ocol + 128]
                        rhs = bf[:, XA + ic * T + tg * 512: XA + ic * T + tg * 512 + 512]
                        rec.do("tensor", lambda t, o=p0, l=lhsT, r=rhs, s=(ic == 0), e=(ic == 7):
                               t.matmul(o[:, 0:512], l, r, start=s, stop=e),
                               wait=w_pe if ic == 0 else [],
                               inc=[(chm, 1)] if ic == 7 else [])
                    dcol = dstbase + oc * T + tg * 512
                    rec.do("scalar", lambda s, d=dcol, o=p0: s.activation(bf[:, d:d + 512], o[:, 0:512], AF.Copy),
                           wait=[(chm, None)], inc=[(cha, 1)])
                    rec.do("tensor", lambda t, d=dcol, o=p1: t.matmul(o[:, 0:512],
                           bf[:, CONST0:CONST0 + 128], bf[:, d:d + 512], start=True, stop=True),
                           wait=[(cha, None)], inc=[(chm, 1)])
                    t1c = A_RT + par * 512
                    rec.do("vector", lambda v, d=dcol, t1=t1c, tg=tg: v.tensor_mul(
                           bf[:, t1:t1 + 512], bf[:, d:d + 512], bf[:, A_COS + tg * 512: A_COS + tg * 512 + 512]),
                           wait=[(cha, None)])
                    swc = A_RT + 1024 + par * 512
                    rec.do("scalar", lambda s, o=p1, sw=swc: s.activation(bf[:, sw:sw + 512], o[:, 0:512], AF.Copy),
                           wait=[(chm, None), (chv, None)], inc=[(cha, 1)])
                    evac1_done[it] = cha.n
                    rec.do("vector", lambda v, sw=swc, tg=tg: v.tensor_mul(
                           bf[:, sw:sw + 512], bf[:, sw:sw + 512], bf[:, A_SIN + tg * 512: A_SIN + tg * 512 + 512]),
                           wait=[(cha, None)])
                    rec.do("vector", lambda v, d=dcol, t1=t1c, sw=swc: v.tensor_add(
                           bf[:, d:d + 512], bf[:, t1:t1 + 512], bf[:, sw:sw + 512]), inc=[(chv, 1)])
                    it += 1
        mm_all = chm.n
        wout = self.inp["L0_attn_Wout"]
        for h, base in ((0, WSA), (1, WSB)):
            for ic in range(8):
                dst = bf[:, base + ic * 512: base + (ic + 1) * 512]
                src = bass.AP(wout, ic * 128 * D + h * 512, [[D, 128], [1, 512]])
                rec.do("gpsimd", lambda g, d=dst, s=src: g.dma_start(d, s),
                       wait=[(chm, mm_all)] if ic == 0 and h == 0 else [], inc=[(chw, 16)])
        if "l0proj" in self.debug:
            dt2 = self.dbg("tiles", [128, 4096], BF16)
            rec.do("scalar", lambda s: s.dma_start(dt2[:, 0:2048], bf[:, A_RT:A_RT + 2048]),
                   wait=[(chv, None)], inc=[(self.ch["dma"], 16)])
            rec.do("scalar", lambda s: s.dma_start(dt2[:, 2048:4096], bf[:, A_SIN:A_SIN + 2048]), inc=[(self.ch["dma"], 16)])
            dws = self.dbg("wsa", [128, 8192], BF16)
            rec.do("scalar", lambda s: s.dma_start(dws[:, 0:4096], bf[:, WSA:WSA + 4096]),
                   wait=[(chw, None)], inc=[(self.ch["dma"], 16)])
            rec.do("scalar", lambda s: s.dma_start(dws[:, 4096:8192], bf[:, WSB:WSB + 4096]), inc=[(self.ch["dma"], 16)])
            dq = self.dbg("qrot", [128, G * T], BF16)
            dkv = self.dbg("kvrot", [128, G * T], BF16)
            rec.do("scalar", lambda s: s.dma_start(dq[:, :], bf[:, QQ:QQ + G * T]),
                   wait=[(chv, None)], inc=[(self.ch["dma"], 16)])
            rec.do("scalar", lambda s: s.dma_start(dkv[:, :], bf[:, KVO:KVO + G * T]), inc=[(self.ch["dma"], 16)])
            rec.do("scalar", lambda s: s.wait_ge(self.ch["dma"].sem, self.ch["dma"].n), bare=True)
        self.rec.emit_block()

    def phase_l0_attn(self):
        nc, rec, bf, f32 = self.nc, self.rec, self.bf, self.f32
        chm, cha, chv, chx, chg = self.ch["mm"], self.ch["act"], self.ch["dve"], self.ch["xb"], self.ch["gp"]
        rec.do("gpsimd", lambda g: g.memset(bf[:, A_KVT:A_KVT + 4096], 0.0), inc=[(chg, 1)])
        rec.do("gpsimd", lambda g: g.memset(bass.AP(bf, A_KVT + 64, [[NBF, 128], [256, 16]]), 1.0), inc=[(chg, 1)])
        rec.do("gpsimd", lambda g: g.memset(bass.AP(bf, A_KVT + 128, [[NBF, 128], [256, 16]]), 1.0), inc=[(chg, 1)])
        memset_done = chg.n
        expd = {}
        avd = {}
        evacd = 0
        xbar_n0 = chx.n
        for pr in range(8):
            wv = [(chg, memset_done)]
            if pr > 0:
                wv.append((chm, chm.n))
            for half, (p0, dcol) in enumerate(((0, A_KVT), (64, A_KVT + 192))):
                inap = bass.AP(bf, KVO + pr * T, [[NBF, 64], [128, 16], [1, 128]], ) if False else None
                src = self.bf[p0:p0 + 64, KVO + pr * T: KVO + (pr + 1) * T].rearrange("p (c k) -> p c k", c=16)
                dst = bass.AP(bf, dcol, [[NBF, 128], [256, 16], [1, 64]])
                rec.do("sync", lambda s, d=dst, i=src: s.dma_start(d, i, transpose=True),
                       wait=wv if half == 0 else [], inc=[(chx, 16)])
            for qg in range(4):
                ev_wait = evacd
                for c in range(16):
                    par = c % 2
                    psA, psB = self.ps[par], self.ps[2 + par]
                    w = []
                    if (0, par) in expd:
                        w.append((cha, expd[(0, par)]))
                    if (1, par) in expd:
                        w.append((cha, expd[(1, par)]))
                    if c == 0:
                        w.append((chv, ev_wait))
                        w.append((chx, chx.n))
                    qcol = QQ + pr * T + qg * 512
                    kcol = KVO + pr * T + c * 128
                    rec.do("tensor", lambda t, o=psA, k=kcol, q=qcol: t.matmul(
                           o[:, 0:512], bf[0:64, k:k + 128], bf[0:64, q:q + 512],
                           start=True, stop=True, tile_position=(0, 0)),
                           wait=w, inc=[(chm, 1)])
                    rec.do("tensor", lambda t, o=psB, k=kcol, q=qcol: t.matmul(
                           o[:, 0:512], bf[64:128, k:k + 128], bf[64:128, q:q + 512],
                           start=True, stop=True, tile_position=(64, 0)),
                           inc=[(chm, 1)])
                    mm_after_s = chm.n
                    for h, ps_ in ((0, psA), (1, psB)):
                        ptcol = A_PT + (2 * par + h) * 512
                        w2 = [(chm, mm_after_s)]
                        if c >= 2 and (c - 2) in avd:
                            w2.append((chm, avd[c - 2]))
                        rec.do("scalar", lambda s, p=ps_, pc=ptcol: s.activation(
                               bf[:, pc:pc + 512], p[:, 0:512], AF.Exp,
                               bias=f32[:, FMS + 48:FMS + 49], scale=f32[:, FMS + 50:FMS + 51]),
                               wait=w2, inc=[(cha, 1)])
                        expd[(h, par)] = cha.n
                    for h, avp in ((0, self.ps[4]), (1, self.ps[5])):
                        ptcol = A_PT + (2 * par + h) * 512
                        kv_l = A_KVT + c * 256 + (0 if h == 0 else 128)
                        rec.do("tensor", lambda t, o=avp, kl=kv_l, pc=ptcol, s0=(c == 0), s1=(c == 15):
                               t.matmul(o[:, 0:512], bf[:, kl:kl + 128], bf[:, pc:pc + 512],
                                        start=s0, stop=s1),
                               wait=[(cha, expd[(h, par)])], inc=[(chm, 1)])
                    avd[c] = chm.n
                av_stop = chm.n
                rec.do("vector", lambda v: v.reciprocal(f32[64:65, FQA:FQA + 512], self.ps[4][64:65, 0:512]),
                       wait=[(chm, av_stop)])
                rec.do("vector", lambda v: v.reciprocal(f32[0:1, FQA:FQA + 512], self.ps[5][0:1, 0:512]))
                rec.do("vector", lambda v: v.tensor_copy(bf[64:65, A_LB:A_LB + 512], f32[64:65, FQA:FQA + 512]))
                rec.do("vector", lambda v: v.tensor_copy(bf[0:1, A_LB:A_LB + 512], f32[0:1, FQA:FQA + 512]),
                       inc=[(chv, 1)])
                rows_ready = chv.n
                rec.do("tensor", lambda t: t.matmul(self.ps[0][:, 0:512], bf[64:65, ONESROW:ONESROW + 128],
                       bf[64:65, A_LB:A_LB + 512], start=True, stop=True),
                       wait=[(chv, rows_ready)], inc=[(chm, 1)])
                rec.do("tensor", lambda t: t.matmul(self.ps[1][:, 0:512], bf[0:1, ONESROW:ONESROW + 128],
                       bf[0:1, A_LB:A_LB + 512], start=True, stop=True), inc=[(chm, 1)])
                bc_done = chm.n
                rec.do("scalar", lambda s: s.activation(bf[:, A_BC:A_BC + 512], self.ps[0][:, 0:512], AF.Copy),
                       wait=[(chm, bc_done)])
                rec.do("scalar", lambda s: s.activation(bf[:, A_BC + 512:A_BC + 1024], self.ps[1][:, 0:512], AF.Copy),
                       inc=[(cha, 1)])
                bce = cha.n
                ocol = OO + pr * T + qg * 512
                rec.do("vector", lambda v, oc=ocol: v.tensor_mul(bf[0:64, oc:oc + 512],
                       self.ps[4][0:64, 0:512], bf[0:64, A_BC:A_BC + 512]),
                       wait=[(cha, bce)])
                rec.do("vector", lambda v, oc=ocol: v.tensor_mul(bf[64:128, oc:oc + 512],
                       self.ps[5][64:128, 0:512], bf[64:128, A_BC + 512:A_BC + 1024]),
                       inc=[(chv, 1)])
                evacd = chv.n
        if "l0attn" in self.debug:
            do_ = self.dbg("of", [128, G * T], BF16)
            rec.do("scalar", lambda s: s.dma_start(do_[:, :], bf[:, OO:OO + G * T]),
                   wait=[(chv, None)], inc=[(self.ch["dma"], 16)])
            rec.do("scalar", lambda s: s.wait_ge(self.ch["dma"].sem, self.ch["dma"].n), bare=True)
        self.rec.emit_block()

    def phase_l0_out_ln(self):
        self.emit_proj_residual_ln(src=OO, wname=None, bias="L0_attn_bout",
                                   stream=XA, rawdst=XB, newdst=XA, dbgkey="x2")

    def emit_proj_residual_ln(self, src, wname, bias, stream, rawdst, newdst, dbgkey):
        nc, rec, bf, f32 = self.nc, self.rec, self.bf, self.f32
        chm, cha, chv, chd = self.ch["mm"], self.ch["act"], self.ch["dve"], self.ch["dma"]
        chw = self.ch["wdma"]
        bt = self.inp[bias]
        rec.do("gpsimd", lambda g: g.dma_start(f32[:, FMS + 32:FMS + 40],
               bass.AP(bt, 0, [[1, 128], [128, 8]]), allow_slow_non_contiguous=True), inc=[(chd, 16)])
        bias_done = chd.n
        raw_done = {}
        for tg in range(4):
            for oc in range(8):
                par = (tg * 8 + oc) % 2
                p0 = self.ps[par]
                wbase = WSA if oc < 4 else WSB
                ocol = (oc % 4) * 128
                w = [(chw, None)] if (tg == 0 and oc == 0) else []
                if (tg, oc) not in (None,):
                    prev = tg * 8 + oc - 2
                    if prev >= 0:
                        w.append((chv, raw_done.get(prev, 0)))
                for ic in range(8):
                    lhsT = bf[:, wbase + ic * 512 + ocol: wbase + ic * 512 + ocol + 128]
                    rhs = bf[:, src + ic * T + tg * 512: src + ic * T + tg * 512 + 512]
                    rec.do("tensor", lambda t, o=p0, l=lhsT, r=rhs, s=(ic == 0), e=(ic == 7):
                           t.matmul(o[:, 0:512], l, r, start=s, stop=e),
                           wait=w if ic == 0 else [], inc=[(chm, 1)] if ic == 7 else [])
                rcol = rawdst + oc * T + tg * 512
                xcol = stream + oc * T + tg * 512
                wst = [(chm, None)]
                if tg == 0 and oc == 0:
                    wst.append((chd, bias_done))
                rec.do("vector", lambda v, o=p0, rc=rcol, xc=xcol, occ=oc: v.scalar_tensor_tensor(
                       bf[:, rc:rc + 512], o[:, 0:512], f32[:, FMS + 32 + occ:FMS + 33 + occ],
                       bf[:, xc:xc + 512], ALU.add, ALU.add),
                       wait=wst, inc=[(chv, 1)])
                raw_done[tg * 8 + oc] = chv.n
            self.emit_ln_group(tg, rawdst, newdst, raw_done[tg * 8 + 7])
        if dbgkey in self.debug:
            dr = self.dbg(dbgkey + "_raw", [128, G * T], BF16)
            dn = self.dbg(dbgkey, [128, G * T], BF16)
            rec.do("scalar", lambda s: s.dma_start(dr[:, :], bf[:, rawdst:rawdst + G * T]),
                   wait=[(chv, None)], inc=[(chd, 16)])
            rec.do("scalar", lambda s: s.dma_start(dn[:, :], bf[:, newdst:newdst + G * T]), inc=[(chd, 16)])
            rec.do("scalar", lambda s: s.wait_ge(chd.sem, chd.n), bare=True)
        self.rec.emit_block()

    def emit_ln_group(self, tg, rawdst, newdst, raw_ready_chv):
        rec, bf, f32 = self.rec, self.bf, self.f32
        chm, cha, chv = self.ch["mm"], self.ch["act"], self.ch["dve"]
        for ic in range(8):
            rcol = rawdst + ic * T + tg * 512
            w = [(chv, raw_ready_chv)] if ic == 0 else []
            rec.do("tensor", lambda t, rc=rcol, s=(ic == 0), e=(ic == 7): t.matmul(
                   self.ps[6][0:1, 0:512], bf[:, ONESCOL:ONESCOL + 1], bf[:, rc:rc + 512],
                   start=s, stop=e), wait=w, inc=[(chm, 1)] if ic == 7 else [])
        sqmm = {}
        for ic in range(8):
            rcol = rawdst + ic * T + tg * 512
            sq = A_PT + (ic % 2) * 512
            w = [(chv, raw_ready_chv)] if ic == 0 else []
            if ic >= 2:
                w.append((chm, sqmm[ic - 2]))
            else:
                w.append((chm, chm.n))
            rec.do("scalar", lambda s_, rc=rcol, q=sq: s_.activation(bf[:, q:q + 512], bf[:, rc:rc + 512],
                   AF.Square), wait=w, inc=[(cha, 1)])
            rec.do("tensor", lambda t, q=sq, s=(ic == 0), e=(ic == 7): t.matmul(
                   self.ps[5][0:1, 0:512], bf[:, ONESCOL:ONESCOL + 1], bf[:, q:q + 512],
                   start=s, stop=e), wait=[(cha, None)], inc=[(chm, 1)])
            sqmm[ic] = chm.n
        sums_done = chm.n
        rec.do("vector", lambda v: v.tensor_scalar_mul(f32[0:1, FQA:FQA + 512], self.ps[6][0:1, 0:512],
               1.0 / 1024.0), wait=[(chm, sums_done)])
        rec.do("vector", lambda v: v.tensor_scalar_mul(f32[0:1, FQB:FQB + 512], self.ps[5][0:1, 0:512],
               1.0 / 1024.0))
        rec.do("vector", lambda v: v.tensor_mul(f32[0:1, FQC:FQC + 512], f32[0:1, FQA:FQA + 512],
               f32[0:1, FQA:FQA + 512]))
        rec.do("vector", lambda v: v.tensor_sub(f32[0:1, FQB:FQB + 512], f32[0:1, FQB:FQB + 512],
               f32[0:1, FQC:FQC + 512]), inc=[(chv, 1)])
        rec.do("scalar", lambda s: s.activation(f32[0:1, FQC:FQC + 512], f32[0:1, FQB:FQB + 512],
               AF.Ln, bias=f32[0:1, FMS + 49:FMS + 50]), wait=[(chv, None)])
        rec.do("scalar", lambda s: s.activation(f32[0:1, FQB:FQB + 512], f32[0:1, FQC:FQC + 512],
               AF.Exp, scale=f32[0:1, FMS + 55:FMS + 56]), inc=[(cha, 1)])
        rec.do("vector", lambda v: v.engine_nop(), wait=[(cha, None)])
        rec.do("vector", lambda v: v.tensor_copy(bf[0:1, A_LB:A_LB + 512], f32[0:1, FQA:FQA + 512]))
        rec.do("vector", lambda v: v.tensor_copy(bf[0:1, A_LB + 512:A_LB + 1024],
               f32[0:1, FQB:FQB + 512]), inc=[(chv, 1)])
        rows_done = chv.n
        rec.do("tensor", lambda t: t.matmul(self.ps[2][:, 0:512], bf[0:1, ONESROW:ONESROW + 128],
               bf[0:1, A_LB:A_LB + 512], start=True, stop=True),
               wait=[(chv, rows_done), (cha, cha.n)], inc=[(chm, 1)])
        rec.do("tensor", lambda t: t.matmul(self.ps[3][:, 0:512], bf[0:1, ONESROW:ONESROW + 128],
               bf[0:1, A_LB + 512:A_LB + 1024], start=True, stop=True), inc=[(chm, 1)])
        bc = chm.n
        rec.do("scalar", lambda s: s.activation(bf[:, A_BC:A_BC + 512], self.ps[2][:, 0:512], AF.Copy),
               wait=[(chm, bc), (chv, None)])
        rec.do("scalar", lambda s: s.activation(bf[:, A_BC + 512:A_BC + 1024], self.ps[3][:, 0:512], AF.Copy),
               inc=[(cha, 1)])
        bce = cha.n
        for ic in range(8):
            rcol = rawdst + ic * T + tg * 512
            ncol = newdst + ic * T + tg * 512
            tmp = A_PT + 1024
            rec.do("vector", lambda v, rc=rcol, tm=tmp: v.tensor_sub(bf[:, tm:tm + 512],
                   bf[:, rc:rc + 512], bf[:, A_BC:A_BC + 512]),
                   wait=[(cha, bce)] if ic == 0 else [])
            rec.do("vector", lambda v, nc_=ncol, tm=tmp: v.tensor_mul(bf[:, nc_:nc_ + 512],
                   bf[:, tm:tm + 512], bf[:, A_BC + 512:A_BC + 1024]),
                   inc=[(chv, 1)] if ic == 7 else [])

    def phase_l0_ffn(self):
        nc, rec, bf, f32 = self.nc, self.rec, self.bf, self.f32
        chm, cha, chv, chx, chw = self.ch["mm"], self.ch["act"], self.ch["dve"], self.ch["xb"], self.ch["wdma"]
        FH = KVO
        wg_t, wu_t, wd_t = self.inp["L0_ffn_w_gate"], self.inp["L0_ffn_w_up"], self.inp["L0_ffn_w_down"]
        NH0 = 24
        stg = {"g": WSA, "u": WSB, "d": WSA}
        for tg2 in range(2):
            ln_gate = cha.n
            xb_done = {}
            for hc in range(NH0):
                par = hc % 2
                for nm, wt, psg in (("g", wg_t, (self.ps[0], self.ps[1])), ("u", wu_t, (self.ps[2], self.ps[3]))):
                    sb = stg[nm] + par * 1024
                    tb = stg[nm] + 2048 + par * 1024
                    chs = self.ch[("wg" if nm == "g" else "wu") + str(par)]
                    if hc >= 2:
                        w = [(chx, xb_done[(nm, hc - 2)][0])]
                    else:
                        w = [(chx, chx.n)]
                    src = bass.AP(wt, hc * 128 * D, [[D, 128], [1, D]])
                    rec.do("gpsimd", lambda g, d=sb, s=src: g.dma_start(bf[:, d:d + 1024], s),
                           wait=w, inc=[(chs, 16)])
                    wdma_n = chs.n
                    mmprev = xb_done.get((nm, hc - 2), (0, chm.n))[1]
                    ii = bf[:, sb:sb + 1024].rearrange("p (c k) -> p c k", c=8)
                    oo = bf[:, tb:tb + 1024].rearrange("p (c k) -> p c k", c=8)
                    rec.do("sync", lambda s, i=ii, o=oo, wn=wdma_n, mp=mmprev, cs=chs: (
                           s.wait_ge(cs.sem, wn), s.wait_ge(chm.sem, mp),
                           s.dma_start(o, i, transpose=True))[-1], inc=[(chx, 16)])
                    xbn = chx.n
                    for th in range(2):
                        ps_ = psg[th]
                        for dc in range(8):
                            lhsT = bf[:, tb + dc * 128: tb + dc * 128 + 128]
                            rhs = bf[:, XA + dc * T + tg2 * 1024 + th * 512: XA + dc * T + tg2 * 1024 + th * 512 + 512]
                            wmm = [(chx, xbn)] if dc == 0 else []
                            if dc == 0 and hc == 0 and th == 0 and nm == "g":
                                wmm.append((cha, ln_gate))
                            if dc == 0 and th == 0 and nm == "g" and hc >= 1:
                                wmm.append((chv, chv.n))
                                wmm.append((cha, cha.n))
                            rec.do("tensor", lambda t, o=ps_, l=lhsT, r=rhs, s=(dc == 0), e=(dc == 7):
                                   t.matmul(o[:, 0:512], l, r, start=s, stop=e),
                                   wait=wmm, inc=[(chm, 1)] if dc == 7 else [])
                    xb_done[(nm, hc)] = (chx.n, chm.n)
                gu_done = chm.n
                for th in range(2):
                    gt = A_KVT + th * 512
                    rec.do("scalar", lambda s, g_=gt, p=self.ps[th]: s.activation(
                           bf[:, g_:g_ + 512], p[:, 0:512], AF.Silu, scale=f32[:, FMS + 54:FMS + 55]),
                           wait=[(chm, gu_done)], inc=[(cha, 1)])
                    hcol = FH + hc * 1024 + th * 512
                    rec.do("vector", lambda v, hcc=hcol, g_=gt, p=self.ps[2 + th]: v.tensor_mul(
                           bf[:, hcc:hcc + 512], p[:, 0:512], bf[:, g_:g_ + 512]),
                           wait=[(cha, None)], inc=[(chv, 1)])
            hdone = chv.n
            raw_chv = {}
            for oc in range(8):
                dn_w = [(chv, hdone)]
                if oc >= 1:
                    dn_w.append((chv, raw_chv[oc - 1]))
                mm_start = chm.n
                for p3 in range(3):
                    par = p3 % 2
                    sb = stg["d"] + par * 1024
                    tb = stg["d"] + 2048 + par * 1024
                    w = list(dn_w) if p3 == 0 else []
                    if (oc * 3 + p3) >= 2:
                        w.append((chx, chx.n))
                    src = bass.AP(wd_t, oc * 128 * H0 + p3 * 1024, [[H0, 128], [1, 1024]])
                    chs = self.ch["wg" + str(par)]
                    rec.do("gpsimd", lambda g, d=sb, s=src: g.dma_start(bf[:, d:d + 1024], s),
                           wait=w, inc=[(chs, 16)])
                    wdma_n = chs.n
                    ii = bf[:, sb:sb + 1024].rearrange("p (c k) -> p c k", c=8)
                    oo = bf[:, tb:tb + 1024].rearrange("p (c k) -> p c k", c=8)
                    mmprev = chm.n if (oc * 3 + p3) >= 2 else 0
                    rec.do("sync", lambda s, i=ii, o=oo, wn=wdma_n, mp=mmprev, cs=chs: (
                           s.wait_ge(cs.sem, wn), s.wait_ge(chm.sem, mp),
                           s.dma_start(o, i, transpose=True))[-1], inc=[(chx, 16)])
                    xbn = chx.n
                    for hin in range(8):
                        hc = p3 * 8 + hin
                        lhsT = bf[:, tb + hin * 128: tb + hin * 128 + 128]
                        for th in range(2):
                            ps_ = self.ps[4 + th]
                            rhs = bf[:, FH + hc * 1024 + th * 512: FH + hc * 1024 + th * 512 + 512]
                            wmm = [(chx, xbn)] if hin == 0 else []
                            rec.do("tensor", lambda t, o=ps_, l=lhsT, r=rhs, s=(hc == 0), e=(hc == 23):
                                   t.matmul(o[:, 0:512], l, r, start=s, stop=e),
                                   wait=wmm, inc=[(chm, 1)] if hc == 23 else [])
                for th in range(2):
                    rcol = XB + oc * T + tg2 * 1024 + th * 512
                    xcol = XA + oc * T + tg2 * 1024 + th * 512
                    rec.do("vector", lambda v, o=self.ps[4 + th], rc=rcol, xc=xcol: v.scalar_tensor_tensor(
                           bf[:, rc:rc + 512], o[:, 0:512], f32[:, FMS + 53:FMS + 54],
                           bf[:, xc:xc + 512], ALU.mult, ALU.add),
                           wait=[(chm, None)], inc=[(chv, 1)])
                raw_chv[oc] = chv.n
            for th in range(2):
                self.emit_ln_group(tg2 * 2 + th, XB, XA, raw_chv[7])
            self.rec.emit_block()
        if "x3" in self.debug:
            chd = self.ch["dma"]
            dh = self.dbg("h0", [128, 24 * 1024], BF16)
            rec.do("scalar", lambda s: s.dma_start(dh[:, :], bf[:, FH:FH + 24 * 1024]),
                   wait=[(chv, None)], inc=[(chd, 16)])
            dn = self.dbg("x3", [128, G * T], BF16)
            dr = self.dbg("x3_raw", [128, G * T], BF16)
            rec.do("scalar", lambda s: s.dma_start(dn[:, :], bf[:, XA:XA + G * T]),
                   wait=[(chv, None)], inc=[(chd, 16)])
            rec.do("scalar", lambda s: s.dma_start(dr[:, :], bf[:, XB:XB + G * T]), inc=[(chd, 16)])
            rec.do("scalar", lambda s: s.wait_ge(chd.sem, chd.n), bare=True)
        self.rec.emit_block()

    def phase_l1_proj(self):
        nc, rec, bf, f32 = self.nc, self.rec, self.bf, self.f32
        chm, cha, chv, chx, chw = self.ch["mm"], self.ch["act"], self.ch["dve"], self.ch["xb"], self.ch["wdma"]
        wt = self.inp["L1_attn_w_qkv"]
        xb_done = {}
        evac_done = {}
        for oc in range(24):
            par = oc % 2
            sb = WSA + par * 1024
            tb = WSA + 2048 + par * 1024
            chs = self.ch["wg" + str(par)]
            w = [(chx, xb_done[oc - 2][0])] if oc >= 2 else [(chx, chx.n)]
            src = bass.AP(wt, oc * 128 * D, [[D, 128], [1, D]])
            rec.do("gpsimd", lambda g, d=sb, s=src: g.dma_start(bf[:, d:d + 1024], s),
                   wait=w, inc=[(chs, 16)])
            wdma_n = chs.n
            mmprev = xb_done.get(oc - 2, (0, chm.n))[1]
            ii = bf[:, sb:sb + 1024].rearrange("p (c k) -> p c k", c=8)
            oo = bf[:, tb:tb + 1024].rearrange("p (c k) -> p c k", c=8)
            rec.do("sync", lambda s, i=ii, o=oo, wn=wdma_n, mp=mmprev, cs=chs: (
                   s.wait_ge(cs.sem, wn), s.wait_ge(chm.sem, mp),
                   s.dma_start(o, i, transpose=True))[-1], inc=[(chx, 16)])
            xbn = chx.n
            if oc < 8:
                dstb = QQ + oc * T
            elif oc < 16:
                dstb = KVO + (oc - 8) * T
            else:
                dstb = XB + (oc - 16) * T
            for tg in range(4):
                ppar = (oc * 4 + tg) % 2
                p0 = self.ps[ppar]
                wmm = [(chx, xbn)] if tg == 0 else []
                prev = oc * 4 + tg - 2
                if prev >= 0:
                    wmm.append((cha, evac_done.get(prev, 0)))
                for ic in range(8):
                    lhsT = bf[:, tb + ic * 128: tb + ic * 128 + 128]
                    rhs = bf[:, XA + ic * T + tg * 512: XA + ic * T + tg * 512 + 512]
                    rec.do("tensor", lambda t, o=p0, l=lhsT, r=rhs, st=(ic == 0), e=(ic == 7):
                           t.matmul(o[:, 0:512], l, r, start=st, stop=e),
                           wait=wmm if ic == 0 else [], inc=[(chm, 1)] if ic == 7 else [])
                dcol = dstb + tg * 512
                rec.do("scalar", lambda s, d=dcol, o=p0: s.activation(bf[:, d:d + 512], o[:, 0:512], AF.Copy),
                       wait=[(chm, None)], inc=[(cha, 1)])
                evac_done[oc * 4 + tg] = cha.n
            xb_done[oc] = (chx.n, chm.n)
        vdone = cha.n
        for dc in range(8):
            ii = self.bf[:, XB + dc * T: XB + (dc + 1) * T].rearrange("p (c k) -> p c k", c=16)
            oo = bass.AP(bf, OO + dc * 128, [[NBF, 128], [1024, 16], [1, 128]])
            rec.do("sync", lambda s, i=ii, o=oo: s.dma_start(o, i, transpose=True),
                   wait=[(cha, vdone)] if dc == 0 else [], inc=[(chx, 16)])
        self.rec.emit_block()

    def phase_l1_attn(self):
        nc, rec, bf, f32 = self.nc, self.rec, self.bf, self.f32
        chm, cha, chv, chx = self.ch["mm"], self.ch["act"], self.ch["dve"], self.ch["xb"]
        PT = AUX
        for qg in range(4):
            expd = {}
            avdone_prev = chm.n
            for c in range(16):
                par = c % 2
                psS = self.ps[par]
                w = []
                if (c - 2, "e") in expd:
                    w.append((cha, expd[(c - 2, "e")]))
                if c == 0:
                    w.append((chx, chx.n))
                for ic in range(8):
                    lhsT = bf[:, KVO + ic * T + c * 128: KVO + ic * T + c * 128 + 128]
                    rhs = bf[:, QQ + ic * T + qg * 512: QQ + ic * T + qg * 512 + 512]
                    rec.do("tensor", lambda t, o=psS, l=lhsT, r=rhs, st=(ic == 0), e=(ic == 7):
                           t.matmul(o[:, 0:512], l, r, start=st, stop=e),
                           wait=w if ic == 0 else [], inc=[(chm, 1)] if ic == 7 else [])
                w2 = [(chm, None)]
                if qg > 0 and c == 0:
                    w2.append((chm, avdone_prev))
                rec.do("scalar", lambda s, p=psS, pc=PT + c * 512: s.activation(
                       bf[:, pc:pc + 512], p[:, 0:512], AF.Exp,
                       bias=f32[:, FMS + 51:FMS + 52], scale=f32[:, FMS + 50:FMS + 51]),
                       wait=w2, inc=[(cha, 1)])
                expd[(c, "e")] = cha.n
            exp_all = cha.n
            for qt in range(4):
                for c in range(16):
                    lhsT = bf[:, PT + c * 512 + qt * 128: PT + c * 512 + qt * 128 + 128]
                    w = [(cha, exp_all)] if c == 0 else []
                    if c == 0:
                        w.append((chv, chv.n))
                    for dh in range(2):
                        rhs = bf[:, OO + c * 1024 + dh * 512: OO + c * 1024 + dh * 512 + 512]
                        rec.do("tensor", lambda t, o=self.ps[2 + dh], l=lhsT, r=rhs, st=(c == 0), e=(c == 15):
                               t.matmul(o[:, 0:512], l, r, start=st, stop=e),
                               wait=w if dh == 0 else [], inc=[(chm, 1)] if c == 15 else [])
                    rec.do("tensor", lambda t, l=lhsT, st=(c == 0), e=(c == 15):
                           t.matmul(self.ps[4][:, 0:1], l, bf[:, ONESCOL:ONESCOL + 1], start=st, stop=e),
                           inc=[(chm, 1)] if c == 15 else [])
                av_done = chm.n
                qi = qg * 4 + qt
                rec.do("vector", lambda v: v.reciprocal(f32[:, FMS + 104:FMS + 105], self.ps[4][:, 0:1]),
                       wait=[(chm, av_done)], inc=[(chv, 1)])
                for dh in range(2):
                    dcol = XB + qi * 1024 + dh * 512
                    rec.do("scalar", lambda s, d=dcol, o=self.ps[2 + dh]: s.activation(
                           bf[:, d:d + 512], o[:, 0:512], AF.Copy, scale=f32[:, FMS + 104:FMS + 105]),
                           wait=[(chv, None)], inc=[(cha, 1)] if dh == 1 else [])
                rec.do("vector", lambda v: v.engine_nop(), wait=[(cha, None)])
                rec.do("vector", lambda v: v.engine_nop(), inc=[(chv, 1)])
            self.rec.emit_block()
        if "l1attn" in self.debug:
            chd = self.ch["dma"]
            dob = self.dbg("otok_mid", [128, 16384], BF16)
            rec.do("scalar", lambda s: s.dma_start(dob[:, :], bf[:, XB:XB + 16384]),
                   wait=[(chv, None)], inc=[(chd, 16)])
            rec.do("scalar", lambda s: s.wait_ge(chd.sem, chd.n), bare=True)
        self.rec.emit_block()

    def phase_l1_out_ln(self):
        nc, rec, bf, f32 = self.nc, self.rec, self.bf, self.f32
        chm, cha, chv, chx, chw = self.ch["mm"], self.ch["act"], self.ch["dve"], self.ch["xb"], self.ch["wdma"]
        chd = self.ch["dma"]
        wt = self.inp["L1_attn_w_proj"]
        bt = self.inp["L1_attn_b_proj"]
        rec.do("gpsimd", lambda g: g.dma_start(f32[:, FMS + 96:FMS + 104],
               bass.AP(bt, 0, [[1, 128], [128, 8]]), allow_slow_non_contiguous=True), inc=[(chd, 16)])
        bias_done = chd.n
        xb_done = {}
        raw_chv = {}
        for oc in range(8):
            sb = AUX + (oc % 2) * 1024
            tb = (WSA + oc * 1024) if oc < 4 else (WSB + (oc - 4) * 1024)
            chs = self.ch["wg" + str(oc % 2)]
            w = [(chx, chx.n)] if oc < 2 else [(chx, xb_done[oc - 2][0])]
            rec.do("gpsimd", lambda g, d=sb, s=bass.AP(wt, oc * 128 * D, [[D, 128], [1, D]]):
                   g.dma_start(bf[:, d:d + 1024], s), wait=w, inc=[(chs, 16)])
            wdma_n = chs.n
            ii = bf[:, sb:sb + 1024].rearrange("p (c k) -> p c k", c=8)
            oo = bf[:, tb:tb + 1024].rearrange("p (c k) -> p c k", c=8)
            rec.do("sync", lambda s_, i=ii, o=oo, wn=wdma_n, cs=chs: (
                   s_.wait_ge(cs.sem, wn), s_.dma_start(o, i, transpose=True))[-1], inc=[(chx, 16)])
            xb_done[oc] = (chx.n, chm.n)
        tiles_ready = chx.n
        for ab in range(2):
            for r in range(2):
                for oc in range(8):
                    ppar = (ab * 16 + r * 8 + oc) % 2
                    p0 = self.ps[ppar]
                    tb = (WSA + oc * 1024) if oc < 4 else (WSB + (oc - 4) * 1024)
                    idx = ab * 16 + r * 8 + oc
                    wmm = [(chx, tiles_ready)] if idx == 0 else []
                    if idx - 2 >= 0:
                        wmm.append((chv, raw_chv.get(idx - 2, 0)))
                    for hc in range(8):
                        lhsT = bf[:, tb + hc * 128: tb + hc * 128 + 128]
                        rhs = bf[:, XB + (8 * r + hc) * 1024 + ab * 512: XB + (8 * r + hc) * 1024 + ab * 512 + 512]
                        rec.do("tensor", lambda t, o=p0, l=lhsT, rr=rhs, st=(hc == 0), e=(hc == 7):
                               t.matmul(o[:, 0:512], l, rr, start=st, stop=e),
                               wait=wmm if hc == 0 else [], inc=[(chm, 1)] if hc == 7 else [])
                    wst = [(chm, None)]
                    if idx == 0:
                        wst.append((chd, bias_done))
                    outap = bass.AP(bf, KVO + oc * T + 2 * ab * 512 + r, [[NBF, 128], [2, 512]])
                    xap = bass.AP(bf, XA + oc * T + 2 * ab * 512 + r, [[NBF, 128], [2, 512]])
                    rec.do("vector", lambda v, o=p0, oa=outap, xa_=xap, occ=oc: v.scalar_tensor_tensor(
                           oa, o[:, 0:512], f32[:, FMS + 96 + occ:FMS + 97 + occ], xa_,
                           ALU.add, ALU.add), wait=wst, inc=[(chv, 1)])
                    raw_chv[idx] = chv.n
            for th in range(2):
                self.emit_ln_group(2 * ab + th, KVO, XA, raw_chv[ab * 16 + 15])
            self.rec.emit_block()
        if "x4" in self.debug:
            dob = self.dbg("otok", [128, 16384], BF16)
            rec.do("scalar", lambda s: s.dma_start(dob[:, :], bf[:, XB:XB + 16384]),
                   wait=[(chv, None)], inc=[(chd, 16)])
            dn = self.dbg("x4", [128, G * T], BF16)
            dr = self.dbg("x4_raw", [128, G * T], BF16)
            rec.do("scalar", lambda s: s.dma_start(dn[:, :], bf[:, XA:XA + G * T]),
                   wait=[(chv, None)], inc=[(chd, 16)])
            rec.do("scalar", lambda s: s.dma_start(dr[:, :], bf[:, KVO:KVO + G * T]), inc=[(chd, 16)])
            rec.do("scalar", lambda s: s.wait_ge(chd.sem, chd.n), bare=True)
        self.rec.emit_block()

    def emit_quant(self, wt, nchunks, width, scratch, cells):
        nc, rec, bf, f32 = self.nc, self.rec, self.bf, self.f32
        chm, cha, chv, chx, chw, chd, chg = (self.ch["mm"], self.ch["act"], self.ch["dve"],
                                             self.ch["xb"], self.ch["wdma"], self.ch["dma"], self.ch["gp"])
        npieces = nchunks * (width // 1024)
        for eng in ("gpsimd", "scalar", "sync", "vector", "tensor"):
            rec.do(eng, lambda e, v=chd.n: e.wait_ge(chd.sem, v), bare=True)
            rec.do(eng, lambda e, v=chw.n: e.wait_ge(chw.sem, v), bare=True)
            rec.do(eng, lambda e, v=chx.n: e.wait_ge(chx.sem, v), bare=True)
        rec.do("gpsimd", lambda g: g.memset(f32[:, FMS + 112:FMS + 120], 0.0), inc=[(chg, 1)])
        mx_done = {}
        for pc in range(npieces):
            par = pc % 2
            col0 = par * 1024
            chs = self.ch["wg" + str(par)]
            w = [(chv, mx_done.get(par, 0))]
            if pc < 2:
                w.append((chg, None))
            src = bass.AP(wt, pc * 128 * 1024 if width == 1024 else (pc // 4) * 128 * width + (pc % 4) * 1024,
                          [[width, 128], [1, 1024]])
            rec.do("gpsimd", lambda g, c0=col0, s=src: g.dma_start(f32[:, c0:c0 + 1024], s),
                   wait=w, inc=[(chs, 16)])
            ldn = chs.n
            rec.do("vector", lambda v, c0=col0, cs=chs, ln_=ldn: v.reduce_max(f32[:, FMS + 113:FMS + 114],
                   f32[:, c0:c0 + 1024], mybir.AxisListType.X, apply_absolute_value=True),
                   wait=[(chs, ldn)])

            rec.do("vector", lambda v: v.tensor_max(f32[:, FMS + 112:FMS + 113],
                   f32[:, FMS + 112:FMS + 113], f32[:, FMS + 113:FMS + 114]),
                   wait=[(chg, None)] if pc == 0 else [], inc=[(chv, 1)])
            mx_done[par] = chv.n
        mx_all = chv.n
        QX = FMS + 200
        self.rec.emit_block()
        rec.do("vector", lambda v: v.tensor_copy(f32[:, FQB:FQB + 4], f32[:, FMS + 112:FMS + 116]),
               wait=[(chv, mx_all)], inc=[(chv, 1)])
        rec.do("gpsimd", lambda s_, v=chv.n: (s_.wait_ge(chv.sem, v),
               s_.dma_start(bass.AP(self.rtrip, 0, [[512, 1], [1, 512]]), f32[:, FQB:FQB + 4]))[-1],
               inc=[(chd, 16)])
        rec.do("gpsimd", lambda s_, dv=chd.n: (s_.wait_ge(chd.sem, dv),
               s_.dma_start(f32[:, FQV:FQV + 512],
               bass.AP(self.rtrip, 0, [[0, 128], [1, 512]])))[-1], inc=[(chd, 16)])
        dnr = chd.n
        self.rec.emit_block()
        for j in range(4):
            rec.do("vector", lambda v, jj=j: v.reduce_max(f32[:, QX + jj:QX + jj + 1],
                   f32[:, FQV:FQV + 512], mybir.AxisListType.X),
                   wait=[(chd, dnr)])
        for j in range(4):
            rec.do("vector", lambda v, jj=j: v.tensor_copy(f32[:, QX + 24 + jj:QX + 25 + jj],
                   f32[:, FMS + 61:FMS + 62]))
        rec.do("vector", lambda v: v.tensor_copy(f32[:, FQD:FQD + 512], f32[:, FQV:FQV + 512]))
        rec.do("vector", lambda v, cc=cells: v.tensor_mul(f32[:, cc:cc + 4], f32[:, QX:QX + 4],
               f32[:, QX + 24:QX + 28]))
        rec.do("vector", lambda v: v.tensor_copy(f32[:, FQD:FQD + 512], f32[:, FQV:FQV + 512]))
        self.rec.emit_block()
        rec.do("vector", lambda v, cc=cells: v.reciprocal(f32[:, cc + 4:cc + 8], f32[:, cc:cc + 4]))
        rec.do("vector", lambda v: v.tensor_copy(f32[:, FQD:FQD + 512], f32[:, FQV:FQV + 512]))
        self.rec.emit_block()
        rec.do("vector", lambda v, cc=cells: v.tensor_scalar_mul(f32[:, cc + 8:cc + 12],
               f32[:, cc + 4:cc + 8], 2.0), inc=[(chv, 1)])
        self.rec.emit_block()
        if "qearly" in self.debug:
            dcf = self.dbg("qcf_early", [128, 8], F32)
            rec.do("vector", lambda v: v.tensor_copy(f32[:, QX + 8:QX + 12], f32[:, QX:QX + 4]), inc=[(chv, 1)])
            rec.do("scalar", lambda s_, cc=cells, v=chv.n: (s_.wait_ge(chv.sem, v),
                   s_.dma_start(dcf[:, 0:4], f32[:, cc:cc + 4]))[-1], inc=[(chd, 16)])
            rec.do("scalar", lambda s_: s_.dma_start(dcf[:, 4:8], f32[:, QX + 32:QX + 36]), inc=[(chd, 16)])
        if "qearly" in self.debug:
            dqe = self.dbg("qearly", [128, 90], F32)
            rec.do("scalar", lambda s_: s_.dma_start(dqe[:, 0:56], f32[:, FMS + 200:FMS + 256]),
                   wait=[(chv, None)], inc=[(chd, 16)])
            dbc = self.dbg("qbc", [128, 130], F32)
            rec.do("vector", lambda v: v.tensor_copy(f32[:, FQV:FQV + 128], self.ps[6][:, 0:128]),
                   inc=[(chv, 1)])
            rec.do("scalar", lambda s_: s_.dma_start(dbc[:, 0:128], f32[:, FQV:FQV + 128]),
                   wait=[(chv, None)], inc=[(chd, 16)])
            rec.do("scalar", lambda s_: s_.dma_start(dbc[:, 128:129], f32[:, FQA:FQA + 1], allow_slow_non_contiguous=True), inc=[(chd, 16)])
            rec.do("scalar", lambda s_: s_.dma_start(dbc[:, 129:130], f32[:, FMS + 112:FMS + 113], allow_slow_non_contiguous=True), inc=[(chd, 16)])
        cells_done = chv.n
        buffree = cells_done
        for pc in range(npieces):
            w = [(chv, cells_done if pc == 0 else buffree)]
            src = bass.AP(wt, pc * 128 * 1024 if width == 1024 else (pc // 4) * 128 * width + (pc % 4) * 1024,
                          [[width, 128], [1, 1024]])
            rec.do("gpsimd", lambda g, s=src: g.dma_start(f32[:, 0:1024], s),
                   wait=w, inc=[(chd, 16)])
            dn = chd.n
            sb = WSB + (pc % 2) * 1024
            for hh in range(2):
                c0 = hh * 512
                wch = [(chd, dn), (chx, chx.n)] if hh == 0 else []
                rec.do("vector", lambda v, cc=c0, cell=cells: v.tensor_scalar_mul(
                       f32[:, FQV:FQV + 512], f32[:, cc:cc + 512], f32[:, cell + 4:cell + 5]),
                       wait=wch, inc=[(chv, 1)] if hh == 1 else [])
                if hh == 1:
                    buffree = chv.n
                rec.do("vector", lambda v: v.tensor_copy(self.i32[:, 0:512], f32[:, FQV:FQV + 512]))
                rec.do("vector", lambda v: v.tensor_copy(f32[:, FQC:FQC + 512], self.i32[:, 0:512]))
                rec.do("vector", lambda v: v.tensor_sub(f32[:, FQD:FQD + 512], f32[:, FQV:FQV + 512],
                       f32[:, FQC:FQC + 512]))
                rec.do("vector", lambda v: v.scalar_tensor_tensor(f32[:, FST:FST + 512],
                       f32[:, FQD:FQD + 512], 0.5, f32[:, FQC:FQC + 512], ALU.is_ge, ALU.add))
                rec.do("vector", lambda v: v.tensor_single_scalar(f32[:, FQV:FQV + 512],
                       f32[:, FQD:FQD + 512], -0.5, ALU.is_le))
                rec.do("vector", lambda v, s_=sb, hh_=hh: v.tensor_sub(bf[:, s_ + hh_ * 512:s_ + hh_ * 512 + 512],
                       f32[:, FST:FST + 512], f32[:, FQV:FQV + 512]), inc=[(chv, 1)])
            qv = chv.n
            ii = bf[:, sb:sb + 1024].rearrange("p (c k) -> p c k", c=8)
            oo = bf[:, WSA + (pc % 2) * 1024:WSA + (pc % 2) * 1024 + 1024].rearrange("p (c k) -> p c k", c=8)
            rec.do("sync", lambda s_, i=ii, o=oo, v=qv, wv=chw.n, dv=chd.n: (s_.wait_ge(chv.sem, v),
                   s_.wait_ge(chw.sem, wv), s_.wait_ge(chd.sem, dv),
                   s_.dma_start(o, i, transpose=True))[-1], inc=[(chx, 16)])
            xn = chx.n
            if width == 1024:
                dst = bass.AP(scratch, pc * 128 * 1024, [[1024, 128], [1, 1024]])
            else:
                dst = bass.AP(scratch, (pc // 4) * 128 * 4096 + (pc % 4) * 1024, [[4096, 128], [1, 1024]])
            rec.do("scalar", lambda s_, d=dst, src2=bf[:, WSA + (pc % 2) * 1024:WSA + (pc % 2) * 1024 + 1024], xnn=xn:
                   (s_.wait_ge(chx.sem, xnn), s_.dma_start(d, src2))[-1], inc=[(chw, 16)])
            if pc == 16 and "qearly" in self.debug:
                dmid = self.dbg("qmid", [128, 16], F32)
                rec.do("scalar", lambda s_, cc=cells: s_.dma_start(dmid[:, 0:12], f32[:, cc:cc + 12]),
                       inc=[(chd, 16)])

    def phase_l1_ffn(self):
        nc, rec, bf, f32 = self.nc, self.rec, self.bf, self.f32
        chm, cha, chv, chx, chw, chd, chg = (self.ch["mm"], self.ch["act"], self.ch["dve"],
                                             self.ch["xb"], self.ch["wdma"], self.ch["dma"], self.ch["gp"])
        self.emit_quant(self.inp["L1_ffn_W_fc"], 32, 1024, self.wfcT, FMS + 232)
        self.rec.emit_block()
        self.emit_quant(self.inp["L1_ffn_W_proj"], 8, 4096, self.wprT, FMS + 244)
        self.rec.emit_block()
        AFC = AUX + 512
        APR = AUX + 640
        BST = AUX
        L1T = AUX + 1152
        L2T = AUX + 1664
        rec.do("gpsimd", lambda g: g.dma_start(
               bf[:, AFC:AFC + 128].rearrange("p (b f) -> p b f", b=8),
               bass.AP(self.inp["AfcT"], 0, [[16, 128], [2048, 8], [1, 16]])), inc=[(chw, 16)])
        rec.do("gpsimd", lambda g: g.dma_start(
               bf[:, APR:APR + 512].rearrange("p (b f) -> p b f", b=32),
               bass.AP(self.inp["AprT"], 0, [[16, 128], [2048, 32], [1, 16]])), inc=[(chw, 16)])
        rec.do("gpsimd", lambda g: g.dma_start(f32[:, FMS + 128:FMS + 160],
               bass.AP(self.inp["L1_ffn_b_fc"], 0, [[1, 128], [128, 32]]),
               allow_slow_non_contiguous=True), inc=[(chw, 16)])
        rec.do("gpsimd", lambda g: g.dma_start(f32[:, FMS + 160:FMS + 168],
               bass.AP(self.inp["L1_ffn_b_proj"], 0, [[1, 128], [128, 8]]),
               allow_slow_non_contiguous=True), inc=[(chw, 16)])
        consts_done = chw.n
        raw_chv = 0
        h2_marker = {}
        for tg in range(4):
            for ic in range(8):
                lhsT = bf[:, AFC + ic * 16: AFC + ic * 16 + 16]
                rhs = bf[:, XA + ic * T + tg * 512: XA + ic * T + tg * 512 + 512]
                w = [(chw, consts_done)] if ic == 0 else []
                if ic == 0:
                    w.append((cha, cha.n))
                rec.do("tensor", lambda t, l=lhsT, r=rhs, st=(ic == 0), e=(ic == 7):
                       t.matmul(self.ps[4][0:16, 0:512], l, r, start=st, stop=e),
                       wait=w, inc=[(chm, 1)] if ic == 7 else [])
            rec.do("scalar", lambda s: s.activation(bf[0:16, L1T:L1T + 512], self.ps[4][0:16, 0:512],
                   AF.Copy, scale=f32[0:16, FMS + 240:FMS + 241]), wait=[(chm, None)], inc=[(cha, 1)])
            l1_done = cha.n
            evac_done = {}
            for oc in range(32):
                par = oc % 2
                wsb = WSA + par * 1024
                chs = self.ch["wg" + str(par)]
                chb = self.ch["bt" + str(par)]
                w = [(chm, chm.n)]
                if oc >= 2:
                    w.append((cha, evac_done[oc - 2]))
                rec.do("gpsimd", lambda g, d=wsb, o=oc: g.dma_start(bf[:, d:d + 1024],
                       bass.AP(self.wfcT, o * 128 * 1024, [[1024, 128], [1, 1024]])),
                       wait=w, inc=[(chs, 16)])
                rec.do("gpsimd", lambda g, o=oc, p2=par: g.dma_start(bf[0:16, BST + p2 * 128:BST + p2 * 128 + 128],
                       bass.AP(self.inp["BfcT"], o * 128, [[H1, 16], [1, 128]])), inc=[(chb, 16)])
                wns, bns = chs.n, chb.n
                p0 = self.ps[par]
                for ic in range(8):
                    lhsT = bf[:, wsb + ic * 128: wsb + ic * 128 + 128]
                    rhs = bf[:, XA + ic * T + tg * 512: XA + ic * T + tg * 512 + 512]
                    wm = [(chs, wns), (cha, l1_done)] if ic == 0 else []
                    rec.do("tensor", lambda t, o=p0, l=lhsT, r=rhs, st=(ic == 0):
                           t.matmul(o[:, 0:512], l, r, start=st, stop=False),
                           wait=wm, inc=[])
                rec.do("tensor", lambda t, o=p0, p2=par: t.matmul(o[:, 0:512],
                       bf[0:16, BST + p2 * 128:BST + p2 * 128 + 128], bf[0:16, L1T:L1T + 512],
                       start=False, stop=True), wait=[(chb, bns)], inc=[(chm, 1)])
                rec.do("scalar", lambda s, o=p0, oc_=oc: s.activation(
                       bf[:, QQ + oc_ * 512: QQ + oc_ * 512 + 512], o[:, 0:512], AF.Gelu,
                       scale=f32[:, FMS + 232:FMS + 233], bias=f32[:, FMS + 128 + oc_:FMS + 129 + oc_]),
                       wait=[(chm, None)], inc=[(cha, 1)])
                evac_done[oc] = cha.n
            h2_done = cha.n
            for hc in range(32):
                lhsT = bf[:, APR + hc * 16: APR + hc * 16 + 16]
                rhs = bf[:, QQ + hc * 512: QQ + hc * 512 + 512]
                w = [(cha, h2_done)] if hc == 0 else []
                rec.do("tensor", lambda t, l=lhsT, r=rhs, st=(hc == 0), e=(hc == 31):
                       t.matmul(self.ps[4][0:16, 0:512], l, r, start=st, stop=e),
                       wait=w, inc=[(chm, 1)] if hc == 31 else [])
            rec.do("scalar", lambda s: s.activation(bf[0:16, L2T:L2T + 512], self.ps[4][0:16, 0:512],
                   AF.Copy, scale=f32[0:16, FMS + 252:FMS + 253]), wait=[(chm, None)], inc=[(cha, 1)])
            l2_done = cha.n
            prev_evac = {}
            subdone = {}
            for oc in range(8):
                p0 = self.ps[oc % 2]
                for sub in range(4):
                    gsub = oc * 4 + sub
                    par = gsub % 2
                    wsb = WSA + 2048 + par * 1024
                    chs = self.ch["wg" + str(par)]
                    w = [(chm, subdone.get(gsub - 2) if gsub >= 2 else chm.n)]
                    if tg == 0 and gsub == 0:
                        w.append((chw, chw.n))
                    rec.do("gpsimd", lambda g, d=wsb, o=oc, s2=sub: g.dma_start(bf[:, d:d + 1024],
                           bass.AP(self.wprT, o * 128 * 4096 + s2 * 1024, [[4096, 128], [1, 1024]])),
                           wait=w, inc=[(chs, 16)])
                    wn = chs.n
                    for hin in range(8):
                        hc = sub * 8 + hin
                        lhsT = bf[:, wsb + hin * 128: wsb + hin * 128 + 128]
                        rhs = bf[:, QQ + hc * 512: QQ + hc * 512 + 512]
                        wm = [(chs, wn), (cha, l2_done)] if hin == 0 else []
                        if oc >= 2 and hin == 0 and sub == 0:
                            wm.append((chv, prev_evac[oc - 2]))
                        rec.do("tensor", lambda t, o=p0, l=lhsT, r=rhs, st=(hc == 0):
                               t.matmul(o[:, 0:512], l, r, start=st, stop=False),
                               wait=wm, inc=[(chm, 1)] if hin == 7 else [])
                    subdone[gsub] = chm.n
                chb = self.ch["bt0"]
                rec.do("gpsimd", lambda g, o=oc: g.dma_start(bf[0:16, BST + 256:BST + 384],
                       bass.AP(self.inp["BprT"], o * 128, [[D, 16], [1, 128]])),
                       wait=[(chm, chm.n)], inc=[(chb, 16)])
                bw = chb.n
                rec.do("tensor", lambda t, o=p0: t.matmul(o[:, 0:512],
                       bf[0:16, BST + 256:BST + 384],
                       bf[0:16, L2T:L2T + 512], start=False, stop=True),
                       wait=[(chb, bw)], inc=[(chm, 1)])
                t1 = A_PT + 1536
                rec.do("vector", lambda v, o=p0, oc_=oc: v.tensor_scalar(
                       bf[:, t1:t1 + 512] if False else bf[:, A_PT + 1536:A_PT + 2048], o[:, 0:512],
                       f32[:, FMS + 244:FMS + 245], f32[:, FMS + 160 + oc_:FMS + 161 + oc_], ALU.mult, ALU.add),
                       wait=[(chm, None)])
                rcol = XB + oc * T + tg * 512
                xcol = XA + oc * T + tg * 512
                rec.do("vector", lambda v, rc=rcol, xc=xcol: v.tensor_add(
                       bf[:, rc:rc + 512], bf[:, A_PT + 1536:A_PT + 2048], bf[:, xc:xc + 512]),
                       inc=[(chv, 1)])
                prev_evac[oc] = chv.n
            raw_chv = chv.n
            self.emit_ln_group(tg, XB, XA, raw_chv)
        if "x5" in self.debug:
            chd2 = self.ch["dma"]
            dn = self.dbg("x5", [128, G * T], BF16)
            dr = self.dbg("x5_raw", [128, G * T], BF16)
            dh2 = self.dbg("h2", [128, 16384], BF16)
            dc_ = self.dbg("cells", [128, 32], F32)
            rec.do("scalar", lambda s: s.dma_start(dn[:, :], bf[:, XA:XA + G * T]),
                   wait=[(chv, None)], inc=[(chd2, 16)])
            rec.do("scalar", lambda s: s.dma_start(dr[:, :], bf[:, XB:XB + G * T]), inc=[(chd2, 16)])
            rec.do("scalar", lambda s: s.dma_start(dh2[:, :], bf[:, QQ:QQ + 16384]), inc=[(chd2, 16)])
            rec.do("scalar", lambda s: s.dma_start(dc_[:, 0:24], f32[:, FMS + 232:FMS + 256]), inc=[(chd2, 16)])

            rec.do("scalar", lambda s: s.wait_ge(chd2.sem, chd2.n), bare=True)
        self.rec.emit_block()

    def phase_output(self):
        nc, rec, bf = self.nc, self.rec, self.bf
        chx, chd = self.ch["xb"], self.ch["dma"]
        for g in range(8):
            ii = bf[:, XA + g * T: XA + (g + 1) * T].rearrange("p (c k) -> p c k", c=16)
            oo = bass.AP(bf, XB + g * 128, [[NBF, 128], [1024, 16], [1, 128]])
            rec.do("sync", lambda s, i=ii, o=oo: s.dma_start(o, i, transpose=True), inc=[(chx, 16)])
        xall = chx.n
        for tc in range(16):
            src = bf[:, XB + tc * 1024: XB + (tc + 1) * 1024]
            dst = bass.AP(self.out, tc * 128 * D, [[D, 128], [1, D]])
            rec.do("gpsimd", lambda g, d=dst, s=src: g.dma_start(d, s),
                   wait=[(chx, xall)] if tc == 0 else [], inc=[(chd, 16)])
        rec.do("gpsimd", lambda g: g.wait_ge(chd.sem, chd.n), bare=True)
        self.rec.emit_block()




def kernel(**inputs):
    n = 8
    x_full = np.ascontiguousarray(inputs["L0_attn_x"])
    b = Builder(stage="all", debug=())
    b.build()
    consts = host_constants()
    lora = layout_lora(inputs)
    weight_names = [k for k in inputs.keys() if k != "L0_attn_x"]
    in_maps = []
    for c in range(n):
        m = {"x": np.ascontiguousarray(x_full[c])}
        for k in weight_names:
            m[k] = inputs[k]
        m.update(consts)
        m.update(lora)
        in_maps.append(m)
    res = run_bass_kernel_spmd(b.nc, in_maps, core_ids=list(range(n)))
    y = np.stack([np.asarray(res.results[c]["y"]) for c in range(n)], axis=0)
    return y.astype(np.float32)


# version 15
# speedup vs baseline: 1.1265x; 1.0066x over previous
import numpy as np
import ml_dtypes
import concourse.bass as bass
import concourse.mybir as mybir
from concourse.bass_utils import run_bass_kernel_spmd

F32 = mybir.dt.float32
BF16 = mybir.dt.bfloat16
I32 = mybir.dt.int32
AF = mybir.ActivationFunctionType
ALU = mybir.AluOpType
BF = ml_dtypes.bfloat16

T = 2048
D = 1024
G = 8
NH = 16
DH = 64
NKC = 16
NQG = 4
PIF = float(np.float32(np.pi))
TWO_PI = 2.0 * np.pi
CW1 = float(np.float32(TWO_PI)); CW2 = float(np.float32(TWO_PI - CW1))
CW3 = float(np.float32(TWO_PI - CW1 - CW2))
C_L0 = 8.0
C_L1 = 8.0
H0 = 3072
H1 = 4096

XA = 0
XB = 16384
KVO = 32768
QQ = 49152
OO = 65536
WSA = 81920
WSB = 86016
AUX = 90112
A_SIN = 90112
A_COS = 92160
A_RT = 94208
A_KVT = 90112
A_PT = 94208
A_LB = 96256
A_BC = 97280
CONST0 = 98304
ONESCOL = 98432
ONESROW = 98440
NBF = 98568

FQA = 0
FQB = 512
FQV = 1024
FQC = 1536
FQD = 2048
FST = 2560
FMS = 3072
NF32 = 3328
TB_TP = 0
TB_KT = 512
TB_TH = 1024
TB_T2 = 1536

NINT = 512


def host_constants():
    c = {}
    c["tpos"] = np.broadcast_to(np.arange(T, dtype=np.float64), (128, T)).astype(np.float32).copy()
    jmod = (np.arange(128) % 32).astype(np.float64)
    c["jarg"] = np.broadcast_to((-(jmod / 32.0)).astype(np.float32).reshape(128, 1), (128, 8)).copy()
    inv_freq = 1.0 / 10000.0 ** (np.arange(0, DH, 2, dtype=np.float64) / DH)
    theta = np.arange(T, dtype=np.float64)[None, :] * inv_freq[np.arange(128) % 32][:, None]
    c["ktab"] = np.round(theta / TWO_PI).astype(np.float32)
    c["ktabc"] = np.round((theta + np.pi / 2) / TWO_PI).astype(np.float32)
    r2t = np.zeros((128, 128), dtype=np.float32)
    for blk in range(2):
        o = 64 * blk
        for j in range(32):
            r2t[o + j + 32, o + j] = -1.0
            r2t[o + j, o + j + 32] = 1.0
    c["r2t"] = r2t.astype(BF)
    c["ident"] = np.eye(128, dtype=np.float32).astype(BF)
    c["identf32"] = np.eye(128, dtype=np.float32)
    ind2 = np.zeros((2, 128), dtype=np.float32)
    ind2[0, 0:64] = 1.0
    ind2[1, 64:128] = 1.0
    c["ind2"] = ind2
    c["ones1x128"] = np.ones((1, 128), dtype=np.float32)
    return c


def layout_lora(ins):
    return {
        "AfcT": np.ascontiguousarray(ins["L1_ffn_A_fc"].T),
        "BfcT": np.ascontiguousarray(ins["L1_ffn_B_fc"].T),
        "AprT": np.ascontiguousarray(ins["L1_ffn_A_proj"].T),
        "BprT": np.ascontiguousarray(ins["L1_ffn_B_proj"].T),
    }


class Chan:
    def __init__(self, sem):
        self.sem = sem
        self.n = 0


class Rec:
    ENGS = ("tensor", "vector", "scalar", "gpsimd", "sync")

    def __init__(self, nc):
        self.nc = nc
        self.ops = {e: [] for e in self.ENGS}
        self.last = {}
        self.drain = {}

    def do(self, eng, fn, wait=(), inc=(), bare=False):
        w = [(ch, (ch.n if v is None else v)) for ch, v in wait]
        ii = []
        for ch, k in inc:
            ch.n += k
            ii.append((ch, k))
        isdma = any(k == 16 for _, k in ii)
        dinc = None
        if not (bare or isdma) and eng in self.drain:
            w = [self.last.get(eng)] + w if self.last.get(eng) else w
            if ii:
                self.last[eng] = (ii[0][0], ii[0][0].n)
            else:
                dch = self.drain[eng]
                dch.n += 1
                dinc = (dch, 1)
                self.last[eng] = (dch, dch.n)

        def run(e, fn=fn, w=w, ii=ii, dinc=dinc):
            for ch, v in w:
                e.wait_ge(ch.sem, v)
            ins = fn(e)
            if dinc is not None:
                ins = ins.then_inc(dinc[0].sem, dinc[1])
            for ch, k in ii:
                ins = ins.then_inc(ch.sem, k)
        self.ops[eng].append(run)

    def emit_block(self):
        with self.nc.Block() as blk:
            for name in self.ENGS:
                ops = self.ops[name]
                if not ops:
                    continue
                getattr(blk, name)(lambda e, ops=ops: [f(e) for f in ops])
        self.ops = {e: [] for e in self.ENGS}


class Builder:
    def __init__(self, stage="all", debug=()):
        self.stage = stage
        self.debug = set(debug)
        self.nc = bass.Bass("TRN2", target_bir_lowering=False)
        nc = self.nc
        self.inp = {}
        self.dbg_out = {}
        self.inp["x"] = nc.dram_tensor("x", [T, D], F32, kind="ExternalInput")
        for nm, shp, dt in [
            ("L0_attn_Wq", [D, D], F32), ("L0_attn_Wkv", [D, D], F32),
            ("L0_attn_Wout", [D, D], F32), ("L0_attn_bout", [D], F32),
            ("L0_ffn_w_gate", [H0, D], I32), ("L0_ffn_w_up", [H0, D], I32),
            ("L0_ffn_w_down", [D, H0], I32),
            ("L1_attn_w_qkv", [3 * D, D], F32), ("L1_attn_w_proj", [D, D], F32),
            ("L1_attn_b_proj", [D], F32),
            ("L1_ffn_W_fc", [H1, D], F32), ("L1_ffn_b_fc", [H1], F32),
            ("L1_ffn_A_fc", [16, D], F32), ("L1_ffn_B_fc", [H1, 16], F32),
            ("L1_ffn_W_proj", [D, H1], F32), ("L1_ffn_b_proj", [D], F32),
            ("L1_ffn_A_proj", [16, H1], F32), ("L1_ffn_B_proj", [D, 16], F32),
        ]:
            self.inp[nm] = nc.dram_tensor(nm, shp, dt, kind="ExternalInput")
        for nm, shp in [("tpos", [128, T]), ("jarg", [128, 8]), ("ktab", [128, T]),
                        ("ktabc", [128, T]), ("ind2", [2, 128]), ("ones1x128", [1, 128])]:
            self.inp[nm] = nc.dram_tensor(nm, shp, F32, kind="ExternalInput")
        for nm, shp in [("r2t", [128, 128]), ("ident", [128, 128])]:
            self.inp[nm] = nc.dram_tensor(nm, shp, BF16, kind="ExternalInput")
        for nm, shp in [("AfcT", [D, 16]), ("BfcT", [16, H1]), ("AprT", [H1, 16]), ("BprT", [16, D]),
                        ("identf32", [128, 128])]:
            self.inp[nm] = nc.dram_tensor(nm, shp, F32, kind="ExternalInput")
        self.out = nc.dram_tensor("y", [T, D], F32, kind="ExternalOutput")
        self.wfcT = nc.dram_tensor("wfcT", [32, 128, 1024], BF16)
        self.wprT = nc.dram_tensor("wprT", [8, 128, 4096], BF16)
        self.rtrip = nc.dram_tensor("rtrip", [1, 512], F32)
        self.ostage = nc.dram_tensor("ostage", [T, D], BF16)

        self._ctx = []

    def dbg(self, name, shape, dtype=F32):
        t = self.nc.dram_tensor("dbg_" + name, shape, dtype, kind="ExternalOutput")
        self.dbg_out[name] = t
        return t

    def build(self):
        nc = self.nc
        import contextlib
        self._stack = contextlib.ExitStack()
        st = self._stack
        self.bf = st.enter_context(nc.sbuf_tensor("bfa", [128, NBF], BF16))
        self.f32 = st.enter_context(nc.sbuf_tensor("f32a", [128, NF32], F32))
        self.i32 = st.enter_context(nc.sbuf_tensor("i32a", [128, NINT], I32))
        self.ps = [st.enter_context(nc.psum_tensor(f"pb{i}", [128, 512], F32)) for i in range(7)]

        self.psb = st.enter_context(nc.psum_tensor("pbf", [128, 1024], BF16))
        self.ch = {}
        for nm in ["dma", "wdma", "mm", "act", "dve", "gp", "xb",
                   "wg0", "wg1", "wu0", "wu1", "bt0", "bt1"]:
            self.ch[nm] = Chan(st.enter_context(nc.semaphore("ch_" + nm)))
        self.rec = Rec(nc)
        for eng in ("vector", "scalar", "gpsimd"):
            self.rec.drain[eng] = Chan(st.enter_context(nc.semaphore("dr_" + eng)))

        self.phase_setup()
        if self.stage == "setup":
            self.finish(); return self
        if self.stage == "quant":
            chd = self.ch["dma"]
            self.emit_quant(self.inp["L1_ffn_W_fc"], 32, 1024, self.wfcT, FMS + 232)
            dq = self.dbg("qcells", [128, 16], F32)
            self.rec.do("scalar", lambda s: s.dma_start(dq[:, :], self.f32[:, FMS + 232:FMS + 248]),
                        wait=[(self.ch["chv"] if False else self.ch["dve"], None),
                              (self.ch["wdma"], None)], inc=[(chd, 16)])
            dw = self.dbg("qtiles", [128, 1024], mybir.dt.bfloat16)
            self.rec.do("gpsimd", lambda g: g.dma_start(dw[:, :],
                        bass.AP(self.wfcT, 0, [[1024, 128], [1, 1024]])), wait=[(self.ch["wdma"], None)], inc=[(chd, 16)])
            self.rec.do("scalar", lambda s: s.wait_ge(chd.sem, chd.n), bare=True)
            self.finish(); return self
        self.phase_l0_proj()
        if self.stage == "l0proj":
            self.finish(); return self
        self.phase_l0_attn()
        self.phase_l0_out_ln()
        if self.stage == "x2":
            self.finish(); return self
        self.phase_l0_ffn()
        if self.stage == "x3":
            self.finish(); return self
        self.phase_l1_proj()
        self.phase_l1_attn()
        self.phase_l1_out_ln()
        if self.stage == "x4":
            self.finish(); return self
        self.phase_l1_ffn()
        self.phase_output()
        self.finish()
        return self

    def finish(self):
        self.rec.emit_block()
        self._stack.close()

    def bfv(self, col, n, p0=0, np_=128):
        return self.bf[p0:p0 + np_, col:col + n]

    def xf(self, base, g, t0, nt, p0=0, np_=128):
        return self.bf[p0:p0 + np_, base + g * T + t0: base + g * T + t0 + nt]

    def phase_setup(self):
        nc, rec, ch, bf, f32 = self.nc, self.rec, self.ch, self.bf, self.f32
        inp = self.inp
        xsrc = inp["x"]
        chd = self.ch["dma"]
        for tc in range(16):
            dst = bf[:, XB + tc * 1024: XB + (tc + 1) * 1024]
            src = bass.AP(xsrc, tc * 128 * D, [[D, 128], [1, D]])
            rec.do("gpsimd", lambda g, d=dst, s=src: g.dma_start(d, s), inc=[(chd, 16)])
        rec.do("gpsimd", lambda g: g.dma_start(bf[:, CONST0:CONST0 + 128], inp["r2t"][:, :]), inc=[(chd, 16)])
        chg = self.ch["gp"]
        for off, val in ((48, -C_L0), (49, 1e-5), (50, 0.125), (51, -C_L1),
                         (52, 2.0), (53, 1.0 / 16129.0), (54, 1.0 / 127.0), (55, -0.5)):
            rec.do("gpsimd", lambda g, o=off, v=val: g.memset(f32[:, FMS + o:FMS + o + 1], v), inc=[(chg, 1)])
        rec.do("gpsimd", lambda g: g.memset(f32[:, FMS + 60:FMS + 61], 127.0), inc=[(chg, 1)])
        rec.do("gpsimd", lambda g: g.memset(bf[:, ONESCOL:ONESCOL + 8], 1.0), inc=[(chg, 1)])
        rec.do("gpsimd", lambda g: g.memset(bf[:, ONESROW:ONESROW + 128], 1.0), inc=[(chg, 1)])
        rec.do("gpsimd", lambda g: g.memset(f32[:, FMS:FMS + 1], 10000.0), inc=[(chg, 1)])
        rec.do("gpsimd", lambda g: g.dma_start(f32[:, FMS + 16:FMS + 24], inp["jarg"][:, :]), inc=[(chd, 16)])
        chm, cha, chv = self.ch["mm"], self.ch["act"], self.ch["dve"]
        rec.do("scalar", lambda s: s.activation(f32[:, FMS + 1:FMS + 2], f32[:, FMS:FMS + 1], AF.Ln),
               wait=[(chd, None), (self.ch["gp"], None)], inc=[(cha, 1)])
        rec.do("vector", lambda v: v.tensor_scalar_mul(f32[:, FMS + 17:FMS + 18], f32[:, FMS + 16:FMS + 17],
                                                       f32[:, FMS + 1:FMS + 2]),
               wait=[(cha, None), (chd, None)], inc=[(chv, 1)])
        rec.do("scalar", lambda s: s.activation(f32[:, FMS + 18:FMS + 19], f32[:, FMS + 17:FMS + 18], AF.Exp),
               wait=[(chv, None)], inc=[(cha, 1)])
        inv_col = FMS + 18
        rec.do("vector", lambda v: v.reciprocal(f32[:, FMS + 61:FMS + 62], f32[:, FMS + 60:FMS + 61]),
               wait=[(self.ch["gp"], None)], inc=[(chv, 1)])
        for qp in range(4):
            t0 = 512 * qp
            tp, kt, th, t2 = TB_TP, TB_KT, TB_TH, TB_T2
            rec.do("gpsimd", lambda g, t0=t0: g.dma_start(f32[:, tp:tp + 512],
                   bass.AP(inp["tpos"], t0, [[T, 128], [1, 512]])), wait=[(chv, None)], inc=[(chd, 16)])
            rec.do("gpsimd", lambda g, t0=t0: g.dma_start(f32[:, kt:kt + 512],
                   bass.AP(inp["ktab"], t0, [[T, 128], [1, 512]])), inc=[(chd, 16)])
            rec.do("vector", lambda v: v.tensor_scalar_mul(f32[:, th:th + 512], f32[:, tp:tp + 512],
                                                           f32[:, inv_col:inv_col + 1]),
                   wait=[(chd, None), (cha, None), (chv, None)], inc=[(chv, 1)])
            rec.do("vector", lambda v: v.scalar_tensor_tensor(f32[:, t2:t2 + 512], f32[:, kt:kt + 512],
                   -CW1, f32[:, th:th + 512], ALU.mult, ALU.add))
            rec.do("vector", lambda v: v.scalar_tensor_tensor(f32[:, tp:tp + 512], f32[:, kt:kt + 512],
                   -CW2, f32[:, t2:t2 + 512], ALU.mult, ALU.add))
            rec.do("vector", lambda v: v.scalar_tensor_tensor(f32[:, t2:t2 + 512], f32[:, kt:kt + 512],
                   -CW3, f32[:, tp:tp + 512], ALU.mult, ALU.add))
            rec.do("vector", lambda v: v.tensor_scalar_min(f32[:, tp:tp + 512], f32[:, t2:t2 + 512], PIF))
            rec.do("vector", lambda v: v.tensor_scalar_max(f32[:, t2:t2 + 512], f32[:, tp:tp + 512], -PIF),
                   inc=[(chv, 1)])
            rec.do("scalar", lambda s, t0=t0: s.activation(bf[:, A_SIN + t0:A_SIN + t0 + 512],
                   f32[:, t2:t2 + 512], AF.Sin), wait=[(chv, None)], inc=[(cha, 1)])
            rec.do("gpsimd", lambda g, t0=t0: g.dma_start(f32[:, kt:kt + 512],
                   bass.AP(inp["ktabc"], t0, [[T, 128], [1, 512]])), wait=[(cha, None)], inc=[(chd, 16)])
            rec.do("vector", lambda v: v.tensor_scalar_add(f32[:, tp:tp + 512], f32[:, th:th + 512],
                   float(np.float32(np.pi / 2))), wait=[(chd, None), (cha, None)])
            rec.do("vector", lambda v: v.scalar_tensor_tensor(f32[:, t2:t2 + 512], f32[:, kt:kt + 512],
                   -CW1, f32[:, tp:tp + 512], ALU.mult, ALU.add))
            rec.do("vector", lambda v: v.scalar_tensor_tensor(f32[:, tp:tp + 512], f32[:, kt:kt + 512],
                   -CW2, f32[:, t2:t2 + 512], ALU.mult, ALU.add))
            rec.do("vector", lambda v: v.scalar_tensor_tensor(f32[:, t2:t2 + 512], f32[:, kt:kt + 512],
                   -CW3, f32[:, tp:tp + 512], ALU.mult, ALU.add))
            rec.do("vector", lambda v: v.tensor_scalar_min(f32[:, tp:tp + 512], f32[:, t2:t2 + 512], PIF))
            rec.do("vector", lambda v: v.tensor_scalar_max(f32[:, t2:t2 + 512], f32[:, tp:tp + 512], -PIF),
                   inc=[(chv, 1)])
            rec.do("scalar", lambda s, t0=t0: s.activation(bf[:, A_COS + t0:A_COS + t0 + 512],
                   f32[:, t2:t2 + 512], AF.Sin), wait=[(chv, None)], inc=[(cha, 1)])
        chx = self.ch["xb"]
        for tc in range(16):
            inap = bf[:, XB + tc * 1024: XB + (tc + 1) * 1024].rearrange("p (g d) -> p g d", g=8)
            outap = bass.AP(self.bf, XA + tc * 128, [[NBF, 128], [T, 8], [1, 128]])
            rec.do("sync", lambda s, i=inap, o=outap: s.dma_start(o, i, transpose=True),
                   wait=[(chd, None)], inc=[(chx, 16)])
        chw = self.ch["wdma"]
        wq = inp["L0_attn_Wq"]
        for h, base in ((0, WSA), (1, WSB)):
            for ic in range(8):
                dst = bf[:, base + ic * 512: base + (ic + 1) * 512]
                src = bass.AP(wq, ic * 128 * D + h * 512, [[D, 128], [1, 512]])
                rec.do("gpsimd", lambda g, d=dst, s=src: g.dma_start(d, s), inc=[(chw, 16)])
        if "tabdeep" in self.debug:
            dinv = self.dbg("invf", [128, 8], F32)
            dth = self.dbg("theta", [128, 512], F32)
            rec.do("scalar", lambda s: s.dma_start(dinv[:, 0:8], f32[:, inv_col:inv_col + 8]),
                   wait=[(self.ch["act"], None)], inc=[(chd, 16)])
            rec.do("scalar", lambda s: s.dma_start(dth[:, :], f32[:, TB_TH:TB_TH + 512]), inc=[(chd, 16)])
            dj = self.dbg("jchain", [128, 8], F32)
            rec.do("vector", lambda v: v.tensor_copy(f32[:, FMS + 300:FMS + 301], self.ps[0][:, 0:1]), wait=[(self.ch["mm"], None)])
            rec.do("vector", lambda v: v.tensor_copy(f32[:, FMS + 301:FMS + 302], f32[:, FMS + 16:FMS + 17]))
            rec.do("vector", lambda v: v.tensor_copy(f32[:, FMS + 302:FMS + 303], f32[:, FMS + 17:FMS + 18]).then_inc(self.ch["dve"].sem, 1))
            rec.do("scalar", lambda s: s.dma_start(dj[:, 0:3], f32[:, FMS + 300:FMS + 303]), wait=[(self.ch["dve"], self.ch["dve"].n + 1)], inc=[(chd, 16)])
        if "tables" in self.debug:
            dsin = self.dbg("sin", [128, T], BF16); dcos = self.dbg("cos", [128, T], BF16)
            dxa = self.dbg("xa", [128, G * T], BF16); dxb = self.dbg("xb", [128, 16384], BF16)
            rec.do("scalar", lambda s: s.dma_start(dsin[:, :], bf[:, A_SIN:A_SIN + T]),
                   wait=[(self.ch["act"], None)], inc=[(chd, 16)])
            rec.do("scalar", lambda s: s.dma_start(dcos[:, :], bf[:, A_COS:A_COS + T]), inc=[(chd, 16)])
            rec.do("scalar", lambda s: s.dma_start(dxa[:, :], bf[:, XA:XA + 16384]),
                   wait=[(chx, None)], inc=[(chd, 16)])
            rec.do("scalar", lambda s: s.dma_start(dxb[:, :], bf[:, XB:XB + 16384]), inc=[(chd, 16)])
            rec.do("scalar", lambda s: s.wait_ge(chd.sem, chd.n), bare=True)
        self.rec.emit_block()

    def phase_l0_proj(self):
        nc, rec, bf = self.nc, self.rec, self.bf
        chm, cha, chv, chw = self.ch["mm"], self.ch["act"], self.ch["dve"], self.ch["wdma"]
        evac1_done = {}
        it = 0
        for proj, dstbase in (("q", QQ), ("kv", KVO)):
            if proj == "kv":
                mm_after_q = chm.n
                wkv = self.inp["L0_attn_Wkv"]
                for h, base in ((0, WSA), (1, WSB)):
                    for ic in range(8):
                        dst = bf[:, base + ic * 512: base + (ic + 1) * 512]
                        src = bass.AP(wkv, ic * 128 * D + h * 512, [[D, 128], [1, 512]])
                        rec.do("gpsimd", lambda g, d=dst, s=src: g.dma_start(d, s),
                               wait=[(chm, mm_after_q)] if ic == 0 and h == 0 else [], inc=[(chw, 16)])
            for oc in range(8):
                wbase = WSA if oc < 4 else WSB
                ocol = (oc % 4) * 128
                for tg in range(4):
                    par = it % 2
                    p0 = self.ps[par]
                    p1 = self.ps[2 + par]
                    w_pe = [(chw, None), (self.ch["xb"], None)]
                    if it - 2 in evac1_done:
                        w_pe.append((cha, evac1_done[it - 2]))
                    for ic in range(8):
                        lhsT = bf[:, wbase + ic * 512 + ocol: wbase + ic * 512 + ocol + 128]
                        rhs = bf[:, XA + ic * T + tg * 512: XA + ic * T + tg * 512 + 512]
                        rec.do("tensor", lambda t, o=p0, l=lhsT, r=rhs, s=(ic == 0), e=(ic == 7):
                               t.matmul(o[:, 0:512], l, r, start=s, stop=e),
                               wait=w_pe if ic == 0 else [],
                               inc=[(chm, 1)] if ic == 7 else [])
                    dcol = dstbase + oc * T + tg * 512
                    rec.do("scalar", lambda s, d=dcol, o=p0: s.activation(bf[:, d:d + 512], o[:, 0:512], AF.Copy),
                           wait=[(chm, None)], inc=[(cha, 1)])
                    rec.do("tensor", lambda t, d=dcol, o=p1: t.matmul(o[:, 0:512],
                           bf[:, CONST0:CONST0 + 128], bf[:, d:d + 512], start=True, stop=True),
                           wait=[(cha, None)], inc=[(chm, 1)])
                    t1c = A_RT + par * 512
                    rec.do("vector", lambda v, d=dcol, t1=t1c, tg=tg: v.tensor_mul(
                           bf[:, t1:t1 + 512], bf[:, d:d + 512], bf[:, A_COS + tg * 512: A_COS + tg * 512 + 512]),
                           wait=[(cha, None)])
                    swc = A_RT + 1024 + par * 512
                    rec.do("scalar", lambda s, o=p1, sw=swc: s.activation(bf[:, sw:sw + 512], o[:, 0:512], AF.Copy),
                           wait=[(chm, None), (chv, None)], inc=[(cha, 1)])
                    evac1_done[it] = cha.n
                    rec.do("vector", lambda v, sw=swc, tg=tg: v.tensor_mul(
                           bf[:, sw:sw + 512], bf[:, sw:sw + 512], bf[:, A_SIN + tg * 512: A_SIN + tg * 512 + 512]),
                           wait=[(cha, None)])
                    rec.do("vector", lambda v, d=dcol, t1=t1c, sw=swc: v.tensor_add(
                           bf[:, d:d + 512], bf[:, t1:t1 + 512], bf[:, sw:sw + 512]), inc=[(chv, 1)])
                    it += 1
        mm_all = chm.n
        wout = self.inp["L0_attn_Wout"]
        for h, base in ((0, WSA), (1, WSB)):
            for ic in range(8):
                dst = bf[:, base + ic * 512: base + (ic + 1) * 512]
                src = bass.AP(wout, ic * 128 * D + h * 512, [[D, 128], [1, 512]])
                rec.do("gpsimd", lambda g, d=dst, s=src: g.dma_start(d, s),
                       wait=[(chm, mm_all)] if ic == 0 and h == 0 else [], inc=[(chw, 16)])
        if "l0proj" in self.debug:
            dt2 = self.dbg("tiles", [128, 4096], BF16)
            rec.do("scalar", lambda s: s.dma_start(dt2[:, 0:2048], bf[:, A_RT:A_RT + 2048]),
                   wait=[(chv, None)], inc=[(self.ch["dma"], 16)])
            rec.do("scalar", lambda s: s.dma_start(dt2[:, 2048:4096], bf[:, A_SIN:A_SIN + 2048]), inc=[(self.ch["dma"], 16)])
            dws = self.dbg("wsa", [128, 8192], BF16)
            rec.do("scalar", lambda s: s.dma_start(dws[:, 0:4096], bf[:, WSA:WSA + 4096]),
                   wait=[(chw, None)], inc=[(self.ch["dma"], 16)])
            rec.do("scalar", lambda s: s.dma_start(dws[:, 4096:8192], bf[:, WSB:WSB + 4096]), inc=[(self.ch["dma"], 16)])
            dq = self.dbg("qrot", [128, G * T], BF16)
            dkv = self.dbg("kvrot", [128, G * T], BF16)
            rec.do("scalar", lambda s: s.dma_start(dq[:, :], bf[:, QQ:QQ + G * T]),
                   wait=[(chv, None)], inc=[(self.ch["dma"], 16)])
            rec.do("scalar", lambda s: s.dma_start(dkv[:, :], bf[:, KVO:KVO + G * T]), inc=[(self.ch["dma"], 16)])
            rec.do("scalar", lambda s: s.wait_ge(self.ch["dma"].sem, self.ch["dma"].n), bare=True)
        self.rec.emit_block()

    def phase_l0_attn(self):
        nc, rec, bf, f32 = self.nc, self.rec, self.bf, self.f32
        chm, cha, chv, chx, chg = self.ch["mm"], self.ch["act"], self.ch["dve"], self.ch["xb"], self.ch["gp"]
        rec.do("gpsimd", lambda g: g.memset(bf[:, A_KVT:A_KVT + 4096], 0.0), inc=[(chg, 1)])
        rec.do("gpsimd", lambda g: g.memset(bass.AP(bf, A_KVT + 64, [[NBF, 128], [256, 16]]), 1.0), inc=[(chg, 1)])
        rec.do("gpsimd", lambda g: g.memset(bass.AP(bf, A_KVT + 128, [[NBF, 128], [256, 16]]), 1.0), inc=[(chg, 1)])
        memset_done = chg.n
        expd = {}
        avd = {}
        evacd = 0
        xbar_n0 = chx.n
        for pr in range(8):
            wv = [(chg, memset_done)]
            if pr > 0:
                wv.append((chm, chm.n))
            for half, (p0, dcol) in enumerate(((0, A_KVT), (64, A_KVT + 192))):
                inap = bass.AP(bf, KVO + pr * T, [[NBF, 64], [128, 16], [1, 128]], ) if False else None
                src = self.bf[p0:p0 + 64, KVO + pr * T: KVO + (pr + 1) * T].rearrange("p (c k) -> p c k", c=16)
                dst = bass.AP(bf, dcol, [[NBF, 128], [256, 16], [1, 64]])
                rec.do("sync", lambda s, d=dst, i=src: s.dma_start(d, i, transpose=True),
                       wait=wv if half == 0 else [], inc=[(chx, 16)])
            for qg in range(4):
                ev_wait = evacd
                for c in range(16):
                    par = c % 2
                    psA, psB = self.ps[par], self.ps[2 + par]
                    w = []
                    if (0, par) in expd:
                        w.append((cha, expd[(0, par)]))
                    if (1, par) in expd:
                        w.append((cha, expd[(1, par)]))
                    if c == 0:
                        w.append((chv, ev_wait))
                        w.append((chx, chx.n))
                    qcol = QQ + pr * T + qg * 512
                    kcol = KVO + pr * T + c * 128
                    rec.do("tensor", lambda t, o=psA, k=kcol, q=qcol: t.matmul(
                           o[:, 0:512], bf[0:64, k:k + 128], bf[0:64, q:q + 512],
                           start=True, stop=True, tile_position=(0, 0)),
                           wait=w, inc=[(chm, 1)])
                    rec.do("tensor", lambda t, o=psB, k=kcol, q=qcol: t.matmul(
                           o[:, 0:512], bf[64:128, k:k + 128], bf[64:128, q:q + 512],
                           start=True, stop=True, tile_position=(64, 0)),
                           inc=[(chm, 1)])
                    mm_after_s = chm.n
                    for h, ps_ in ((0, psA), (1, psB)):
                        ptcol = A_PT + (2 * par + h) * 512
                        w2 = [(chm, mm_after_s)]
                        if c >= 2 and (c - 2) in avd:
                            w2.append((chm, avd[c - 2]))
                        rec.do("scalar", lambda s, p=ps_, pc=ptcol: s.activation(
                               bf[:, pc:pc + 512], p[:, 0:512], AF.Exp,
                               bias=f32[:, FMS + 48:FMS + 49], scale=f32[:, FMS + 50:FMS + 51]),
                               wait=w2, inc=[(cha, 1)])
                        expd[(h, par)] = cha.n
                    for h, avp in ((0, self.ps[4]), (1, self.ps[5])):
                        ptcol = A_PT + (2 * par + h) * 512
                        kv_l = A_KVT + c * 256 + (0 if h == 0 else 128)
                        rec.do("tensor", lambda t, o=avp, kl=kv_l, pc=ptcol, s0=(c == 0), s1=(c == 15):
                               t.matmul(o[:, 0:512], bf[:, kl:kl + 128], bf[:, pc:pc + 512],
                                        start=s0, stop=s1),
                               wait=[(cha, expd[(h, par)])], inc=[(chm, 1)])
                    avd[c] = chm.n
                av_stop = chm.n
                rec.do("vector", lambda v: v.reciprocal(f32[64:65, FQA:FQA + 512], self.ps[4][64:65, 0:512]),
                       wait=[(chm, av_stop)])
                rec.do("vector", lambda v: v.reciprocal(f32[0:1, FQA:FQA + 512], self.ps[5][0:1, 0:512]))
                rec.do("vector", lambda v: v.tensor_copy(bf[64:65, A_LB:A_LB + 512], f32[64:65, FQA:FQA + 512]))
                rec.do("vector", lambda v: v.tensor_copy(bf[0:1, A_LB:A_LB + 512], f32[0:1, FQA:FQA + 512]),
                       inc=[(chv, 1)])
                rows_ready = chv.n
                rec.do("tensor", lambda t: t.matmul(self.ps[0][:, 0:512], bf[64:65, ONESROW:ONESROW + 128],
                       bf[64:65, A_LB:A_LB + 512], start=True, stop=True),
                       wait=[(chv, rows_ready)], inc=[(chm, 1)])
                rec.do("tensor", lambda t: t.matmul(self.ps[1][:, 0:512], bf[0:1, ONESROW:ONESROW + 128],
                       bf[0:1, A_LB:A_LB + 512], start=True, stop=True), inc=[(chm, 1)])
                bc_done = chm.n
                rec.do("scalar", lambda s: s.activation(bf[:, A_BC:A_BC + 512], self.ps[0][:, 0:512], AF.Copy),
                       wait=[(chm, bc_done)])
                rec.do("scalar", lambda s: s.activation(bf[:, A_BC + 512:A_BC + 1024], self.ps[1][:, 0:512], AF.Copy),
                       inc=[(cha, 1)])
                bce = cha.n
                ocol = OO + pr * T + qg * 512
                rec.do("vector", lambda v, oc=ocol: v.tensor_mul(bf[0:64, oc:oc + 512],
                       self.ps[4][0:64, 0:512], bf[0:64, A_BC:A_BC + 512]),
                       wait=[(cha, bce)])
                rec.do("vector", lambda v, oc=ocol: v.tensor_mul(bf[64:128, oc:oc + 512],
                       self.ps[5][64:128, 0:512], bf[64:128, A_BC + 512:A_BC + 1024]),
                       inc=[(chv, 1)])
                evacd = chv.n
        if "l0attn" in self.debug:
            do_ = self.dbg("of", [128, G * T], BF16)
            rec.do("scalar", lambda s: s.dma_start(do_[:, :], bf[:, OO:OO + G * T]),
                   wait=[(chv, None)], inc=[(self.ch["dma"], 16)])
            rec.do("scalar", lambda s: s.wait_ge(self.ch["dma"].sem, self.ch["dma"].n), bare=True)
        self.rec.emit_block()

    def phase_l0_out_ln(self):
        self.emit_proj_residual_ln(src=OO, wname=None, bias="L0_attn_bout",
                                   stream=XA, rawdst=XB, newdst=XA, dbgkey="x2")

    def emit_proj_residual_ln(self, src, wname, bias, stream, rawdst, newdst, dbgkey):
        nc, rec, bf, f32 = self.nc, self.rec, self.bf, self.f32
        chm, cha, chv, chd = self.ch["mm"], self.ch["act"], self.ch["dve"], self.ch["dma"]
        chw = self.ch["wdma"]
        bt = self.inp[bias]
        rec.do("gpsimd", lambda g: g.dma_start(f32[:, FMS + 32:FMS + 40],
               bass.AP(bt, 0, [[1, 128], [128, 8]]), allow_slow_non_contiguous=True), inc=[(chd, 16)])
        bias_done = chd.n
        raw_done = {}
        for tg in range(4):
            for oc in range(8):
                par = (tg * 8 + oc) % 2
                p0 = self.ps[par]
                wbase = WSA if oc < 4 else WSB
                ocol = (oc % 4) * 128
                w = [(chw, None)] if (tg == 0 and oc == 0) else []
                if (tg, oc) not in (None,):
                    prev = tg * 8 + oc - 2
                    if prev >= 0:
                        w.append((chv, raw_done.get(prev, 0)))
                for ic in range(8):
                    lhsT = bf[:, wbase + ic * 512 + ocol: wbase + ic * 512 + ocol + 128]
                    rhs = bf[:, src + ic * T + tg * 512: src + ic * T + tg * 512 + 512]
                    rec.do("tensor", lambda t, o=p0, l=lhsT, r=rhs, s=(ic == 0), e=(ic == 7):
                           t.matmul(o[:, 0:512], l, r, start=s, stop=e),
                           wait=w if ic == 0 else [], inc=[(chm, 1)] if ic == 7 else [])
                rcol = rawdst + oc * T + tg * 512
                xcol = stream + oc * T + tg * 512
                wst = [(chm, None)]
                if tg == 0 and oc == 0:
                    wst.append((chd, bias_done))
                rec.do("vector", lambda v, o=p0, rc=rcol, xc=xcol, occ=oc: v.scalar_tensor_tensor(
                       bf[:, rc:rc + 512], o[:, 0:512], f32[:, FMS + 32 + occ:FMS + 33 + occ],
                       bf[:, xc:xc + 512], ALU.add, ALU.add),
                       wait=wst, inc=[(chv, 1)])
                raw_done[tg * 8 + oc] = chv.n
            self.emit_ln_group(tg, rawdst, newdst, raw_done[tg * 8 + 7])
        if dbgkey in self.debug:
            dr = self.dbg(dbgkey + "_raw", [128, G * T], BF16)
            dn = self.dbg(dbgkey, [128, G * T], BF16)
            rec.do("scalar", lambda s: s.dma_start(dr[:, :], bf[:, rawdst:rawdst + G * T]),
                   wait=[(chv, None)], inc=[(chd, 16)])
            rec.do("scalar", lambda s: s.dma_start(dn[:, :], bf[:, newdst:newdst + G * T]), inc=[(chd, 16)])
            rec.do("scalar", lambda s: s.wait_ge(chd.sem, chd.n), bare=True)
        self.rec.emit_block()

    def emit_ln_group(self, tg, rawdst, newdst, raw_ready_chv):
        rec, bf, f32 = self.rec, self.bf, self.f32
        chm, cha, chv = self.ch["mm"], self.ch["act"], self.ch["dve"]
        for ic in range(8):
            rcol = rawdst + ic * T + tg * 512
            w = [(chv, raw_ready_chv)] if ic == 0 else []
            rec.do("tensor", lambda t, rc=rcol, s=(ic == 0), e=(ic == 7): t.matmul(
                   self.ps[6][0:1, 0:512], bf[:, ONESCOL:ONESCOL + 1], bf[:, rc:rc + 512],
                   start=s, stop=e), wait=w, inc=[(chm, 1)] if ic == 7 else [])
        sqmm = {}
        for ic in range(8):
            rcol = rawdst + ic * T + tg * 512
            sq = A_PT + (ic % 2) * 512
            w = [(chv, raw_ready_chv)] if ic == 0 else []
            if ic >= 2:
                w.append((chm, sqmm[ic - 2]))
            else:
                w.append((chm, chm.n))
            rec.do("scalar", lambda s_, rc=rcol, q=sq: s_.activation(bf[:, q:q + 512], bf[:, rc:rc + 512],
                   AF.Square), wait=w, inc=[(cha, 1)])
            rec.do("tensor", lambda t, q=sq, s=(ic == 0), e=(ic == 7): t.matmul(
                   self.ps[5][0:1, 0:512], bf[:, ONESCOL:ONESCOL + 1], bf[:, q:q + 512],
                   start=s, stop=e), wait=[(cha, None)], inc=[(chm, 1)])
            sqmm[ic] = chm.n
        sums_done = chm.n
        rec.do("vector", lambda v: v.tensor_scalar_mul(f32[0:1, FQA:FQA + 512], self.ps[6][0:1, 0:512],
               1.0 / 1024.0), wait=[(chm, sums_done)])
        rec.do("vector", lambda v: v.tensor_scalar_mul(f32[0:1, FQB:FQB + 512], self.ps[5][0:1, 0:512],
               1.0 / 1024.0))
        rec.do("vector", lambda v: v.tensor_mul(f32[0:1, FQC:FQC + 512], f32[0:1, FQA:FQA + 512],
               f32[0:1, FQA:FQA + 512]))
        rec.do("vector", lambda v: v.tensor_sub(f32[0:1, FQB:FQB + 512], f32[0:1, FQB:FQB + 512],
               f32[0:1, FQC:FQC + 512]), inc=[(chv, 1)])
        rec.do("scalar", lambda s: s.activation(f32[0:1, FQC:FQC + 512], f32[0:1, FQB:FQB + 512],
               AF.Ln, bias=f32[0:1, FMS + 49:FMS + 50]), wait=[(chv, None)])
        rec.do("scalar", lambda s: s.activation(f32[0:1, FQB:FQB + 512], f32[0:1, FQC:FQC + 512],
               AF.Exp, scale=f32[0:1, FMS + 55:FMS + 56]), inc=[(cha, 1)])
        rec.do("vector", lambda v: v.engine_nop(), wait=[(cha, None)])
        rec.do("vector", lambda v: v.tensor_copy(bf[0:1, A_LB:A_LB + 512], f32[0:1, FQA:FQA + 512]))
        rec.do("vector", lambda v: v.tensor_copy(bf[0:1, A_LB + 512:A_LB + 1024],
               f32[0:1, FQB:FQB + 512]), inc=[(chv, 1)])
        rows_done = chv.n
        rec.do("tensor", lambda t: t.matmul(self.ps[2][:, 0:512], bf[0:1, ONESROW:ONESROW + 128],
               bf[0:1, A_LB:A_LB + 512], start=True, stop=True),
               wait=[(chv, rows_done), (cha, cha.n)], inc=[(chm, 1)])
        rec.do("tensor", lambda t: t.matmul(self.ps[3][:, 0:512], bf[0:1, ONESROW:ONESROW + 128],
               bf[0:1, A_LB + 512:A_LB + 1024], start=True, stop=True), inc=[(chm, 1)])
        bc = chm.n
        rec.do("scalar", lambda s: s.activation(bf[:, A_BC:A_BC + 512], self.ps[2][:, 0:512], AF.Copy),
               wait=[(chm, bc), (chv, None)])
        rec.do("scalar", lambda s: s.activation(bf[:, A_BC + 512:A_BC + 1024], self.ps[3][:, 0:512], AF.Copy),
               inc=[(cha, 1)])
        bce = cha.n
        for ic in range(8):
            rcol = rawdst + ic * T + tg * 512
            ncol = newdst + ic * T + tg * 512
            tmp = A_PT + 1024
            rec.do("vector", lambda v, rc=rcol, tm=tmp: v.tensor_sub(bf[:, tm:tm + 512],
                   bf[:, rc:rc + 512], bf[:, A_BC:A_BC + 512]),
                   wait=[(cha, bce)] if ic == 0 else [])
            rec.do("vector", lambda v, nc_=ncol, tm=tmp: v.tensor_mul(bf[:, nc_:nc_ + 512],
                   bf[:, tm:tm + 512], bf[:, A_BC + 512:A_BC + 1024]),
                   inc=[(chv, 1)] if ic == 7 else [])

    def phase_l0_ffn(self):
        nc, rec, bf, f32 = self.nc, self.rec, self.bf, self.f32
        chm, cha, chv, chx, chw = self.ch["mm"], self.ch["act"], self.ch["dve"], self.ch["xb"], self.ch["wdma"]
        FH = KVO
        wg_t, wu_t, wd_t = self.inp["L0_ffn_w_gate"], self.inp["L0_ffn_w_up"], self.inp["L0_ffn_w_down"]
        NH0 = 24
        stg = {"g": WSA, "u": WSB, "d": WSA}
        for tg2 in range(2):
            ln_gate = cha.n
            xb_done = {}
            for hc in range(NH0):
                par = hc % 2
                for nm, wt, psg in (("g", wg_t, (self.ps[0], self.ps[1])), ("u", wu_t, (self.ps[2], self.ps[3]))):
                    sb = stg[nm] + par * 1024
                    tb = stg[nm] + 2048 + par * 1024
                    chs = self.ch[("wg" if nm == "g" else "wu") + str(par)]
                    if hc >= 2:
                        w = [(chx, xb_done[(nm, hc - 2)][0])]
                    else:
                        w = [(chx, chx.n)]
                    src = bass.AP(wt, hc * 128 * D, [[D, 128], [1, D]])
                    rec.do("gpsimd", lambda g, d=sb, s=src: g.dma_start(bf[:, d:d + 1024], s),
                           wait=w, inc=[(chs, 16)])
                    wdma_n = chs.n
                    mmprev = xb_done.get((nm, hc - 2), (0, chm.n))[1]
                    ii = bf[:, sb:sb + 1024].rearrange("p (c k) -> p c k", c=8)
                    oo = bf[:, tb:tb + 1024].rearrange("p (c k) -> p c k", c=8)
                    rec.do("sync", lambda s, i=ii, o=oo, wn=wdma_n, mp=mmprev, cs=chs: (
                           s.wait_ge(cs.sem, wn), s.wait_ge(chm.sem, mp),
                           s.dma_start(o, i, transpose=True))[-1], inc=[(chx, 16)])
                    xbn = chx.n
                    for th in range(2):
                        ps_ = psg[th]
                        for dc in range(8):
                            lhsT = bf[:, tb + dc * 128: tb + dc * 128 + 128]
                            rhs = bf[:, XA + dc * T + tg2 * 1024 + th * 512: XA + dc * T + tg2 * 1024 + th * 512 + 512]
                            wmm = [(chx, xbn)] if dc == 0 else []
                            if dc == 0 and hc == 0 and th == 0 and nm == "g":
                                wmm.append((cha, ln_gate))
                            if dc == 0 and th == 0 and nm == "g" and hc >= 1:
                                wmm.append((chv, chv.n))
                                wmm.append((cha, cha.n))
                            rec.do("tensor", lambda t, o=ps_, l=lhsT, r=rhs, s=(dc == 0), e=(dc == 7):
                                   t.matmul(o[:, 0:512], l, r, start=s, stop=e),
                                   wait=wmm, inc=[(chm, 1)] if dc == 7 else [])
                    xb_done[(nm, hc)] = (chx.n, chm.n)
                gu_done = chm.n
                for th in range(2):
                    gt = A_KVT + th * 512
                    rec.do("scalar", lambda s, g_=gt, p=self.ps[th]: s.activation(
                           bf[:, g_:g_ + 512], p[:, 0:512], AF.Silu, scale=f32[:, FMS + 54:FMS + 55]),
                           wait=[(chm, gu_done)], inc=[(cha, 1)])
                    hcol = FH + hc * 1024 + th * 512
                    rec.do("vector", lambda v, hcc=hcol, g_=gt, p=self.ps[2 + th]: v.tensor_mul(
                           bf[:, hcc:hcc + 512], p[:, 0:512], bf[:, g_:g_ + 512]),
                           wait=[(cha, None)], inc=[(chv, 1)])
            hdone = chv.n
            raw_chv = {}
            for oc in range(8):
                dn_w = [(chv, hdone)]
                if oc >= 1:
                    dn_w.append((chv, raw_chv[oc - 1]))
                mm_start = chm.n
                for p3 in range(3):
                    par = p3 % 2
                    sb = stg["d"] + par * 1024
                    tb = stg["d"] + 2048 + par * 1024
                    w = list(dn_w) if p3 == 0 else []
                    if (oc * 3 + p3) >= 2:
                        w.append((chx, chx.n))
                    src = bass.AP(wd_t, oc * 128 * H0 + p3 * 1024, [[H0, 128], [1, 1024]])
                    chs = self.ch["wg" + str(par)]
                    rec.do("gpsimd", lambda g, d=sb, s=src: g.dma_start(bf[:, d:d + 1024], s),
                           wait=w, inc=[(chs, 16)])
                    wdma_n = chs.n
                    ii = bf[:, sb:sb + 1024].rearrange("p (c k) -> p c k", c=8)
                    oo = bf[:, tb:tb + 1024].rearrange("p (c k) -> p c k", c=8)
                    mmprev = chm.n if (oc * 3 + p3) >= 2 else 0
                    rec.do("sync", lambda s, i=ii, o=oo, wn=wdma_n, mp=mmprev, cs=chs: (
                           s.wait_ge(cs.sem, wn), s.wait_ge(chm.sem, mp),
                           s.dma_start(o, i, transpose=True))[-1], inc=[(chx, 16)])
                    xbn = chx.n
                    for hin in range(8):
                        hc = p3 * 8 + hin
                        lhsT = bf[:, tb + hin * 128: tb + hin * 128 + 128]
                        for th in range(2):
                            ps_ = self.ps[4 + th]
                            rhs = bf[:, FH + hc * 1024 + th * 512: FH + hc * 1024 + th * 512 + 512]
                            wmm = [(chx, xbn)] if hin == 0 else []
                            rec.do("tensor", lambda t, o=ps_, l=lhsT, r=rhs, s=(hc == 0), e=(hc == 23):
                                   t.matmul(o[:, 0:512], l, r, start=s, stop=e),
                                   wait=wmm, inc=[(chm, 1)] if hc == 23 else [])
                for th in range(2):
                    rcol = XB + oc * T + tg2 * 1024 + th * 512
                    xcol = XA + oc * T + tg2 * 1024 + th * 512
                    rec.do("vector", lambda v, o=self.ps[4 + th], rc=rcol, xc=xcol: v.scalar_tensor_tensor(
                           bf[:, rc:rc + 512], o[:, 0:512], f32[:, FMS + 53:FMS + 54],
                           bf[:, xc:xc + 512], ALU.mult, ALU.add),
                           wait=[(chm, None)], inc=[(chv, 1)])
                raw_chv[oc] = chv.n
            for th in range(2):
                self.emit_ln_group(tg2 * 2 + th, XB, XA, raw_chv[7])
        if "x3" in self.debug:
            chd = self.ch["dma"]
            dh = self.dbg("h0", [128, 24 * 1024], BF16)
            rec.do("scalar", lambda s: s.dma_start(dh[:, :], bf[:, FH:FH + 24 * 1024]),
                   wait=[(chv, None)], inc=[(chd, 16)])
            dn = self.dbg("x3", [128, G * T], BF16)
            dr = self.dbg("x3_raw", [128, G * T], BF16)
            rec.do("scalar", lambda s: s.dma_start(dn[:, :], bf[:, XA:XA + G * T]),
                   wait=[(chv, None)], inc=[(chd, 16)])
            rec.do("scalar", lambda s: s.dma_start(dr[:, :], bf[:, XB:XB + G * T]), inc=[(chd, 16)])
            rec.do("scalar", lambda s: s.wait_ge(chd.sem, chd.n), bare=True)
        self.rec.emit_block()

    def phase_l1_proj(self):
        nc, rec, bf, f32 = self.nc, self.rec, self.bf, self.f32
        chm, cha, chv, chx, chw = self.ch["mm"], self.ch["act"], self.ch["dve"], self.ch["xb"], self.ch["wdma"]
        wt = self.inp["L1_attn_w_qkv"]
        xb_done = {}
        evac_done = {}
        for oc in range(24):
            par = oc % 2
            sb = WSA + par * 1024
            tb = WSA + 2048 + par * 1024
            chs = self.ch["wg" + str(par)]
            w = [(chx, xb_done[oc - 2][0])] if oc >= 2 else [(chx, chx.n)]
            src = bass.AP(wt, oc * 128 * D, [[D, 128], [1, D]])
            rec.do("gpsimd", lambda g, d=sb, s=src: g.dma_start(bf[:, d:d + 1024], s),
                   wait=w, inc=[(chs, 16)])
            wdma_n = chs.n
            mmprev = xb_done.get(oc - 2, (0, chm.n))[1]
            ii = bf[:, sb:sb + 1024].rearrange("p (c k) -> p c k", c=8)
            oo = bf[:, tb:tb + 1024].rearrange("p (c k) -> p c k", c=8)
            rec.do("sync", lambda s, i=ii, o=oo, wn=wdma_n, mp=mmprev, cs=chs: (
                   s.wait_ge(cs.sem, wn), s.wait_ge(chm.sem, mp),
                   s.dma_start(o, i, transpose=True))[-1], inc=[(chx, 16)])
            xbn = chx.n
            if oc < 8:
                dstb = QQ + oc * T
            elif oc < 16:
                dstb = KVO + (oc - 8) * T
            else:
                dstb = XB + (oc - 16) * T
            for tg in range(4):
                ppar = (oc * 4 + tg) % 2
                p0 = self.ps[ppar]
                wmm = [(chx, xbn)] if tg == 0 else []
                prev = oc * 4 + tg - 2
                if prev >= 0:
                    wmm.append((cha, evac_done.get(prev, 0)))
                for ic in range(8):
                    lhsT = bf[:, tb + ic * 128: tb + ic * 128 + 128]
                    rhs = bf[:, XA + ic * T + tg * 512: XA + ic * T + tg * 512 + 512]
                    rec.do("tensor", lambda t, o=p0, l=lhsT, r=rhs, st=(ic == 0), e=(ic == 7):
                           t.matmul(o[:, 0:512], l, r, start=st, stop=e),
                           wait=wmm if ic == 0 else [], inc=[(chm, 1)] if ic == 7 else [])
                dcol = dstb + tg * 512
                rec.do("scalar", lambda s, d=dcol, o=p0: s.activation(bf[:, d:d + 512], o[:, 0:512], AF.Copy),
                       wait=[(chm, None)], inc=[(cha, 1)])
                evac_done[oc * 4 + tg] = cha.n
            xb_done[oc] = (chx.n, chm.n)
        vdone = cha.n
        for dc in range(8):
            ii = self.bf[:, XB + dc * T: XB + (dc + 1) * T].rearrange("p (c k) -> p c k", c=16)
            oo = bass.AP(bf, OO + dc * 128, [[NBF, 128], [1024, 16], [1, 128]])
            rec.do("sync", lambda s, i=ii, o=oo: s.dma_start(o, i, transpose=True),
                   wait=[(cha, vdone)] if dc == 0 else [], inc=[(chx, 16)])
        self.rec.emit_block()

    def phase_l1_attn(self):
        nc, rec, bf, f32 = self.nc, self.rec, self.bf, self.f32
        chm, cha, chv, chx = self.ch["mm"], self.ch["act"], self.ch["dve"], self.ch["xb"]
        PT = AUX
        for qg in range(4):
            expd = {}
            avdone_prev = chm.n
            for c in range(16):
                par = c % 2
                psS = self.ps[par]
                w = []
                if (c - 2, "e") in expd:
                    w.append((cha, expd[(c - 2, "e")]))
                if c == 0:
                    w.append((chx, chx.n))
                for ic in range(8):
                    lhsT = bf[:, KVO + ic * T + c * 128: KVO + ic * T + c * 128 + 128]
                    rhs = bf[:, QQ + ic * T + qg * 512: QQ + ic * T + qg * 512 + 512]
                    rec.do("tensor", lambda t, o=psS, l=lhsT, r=rhs, st=(ic == 0), e=(ic == 7):
                           t.matmul(o[:, 0:512], l, r, start=st, stop=e),
                           wait=w if ic == 0 else [], inc=[(chm, 1)] if ic == 7 else [])
                w2 = [(chm, None)]
                if qg > 0 and c == 0:
                    w2.append((chm, avdone_prev))
                rec.do("scalar", lambda s, p=psS, pc=PT + c * 512: s.activation(
                       bf[:, pc:pc + 512], p[:, 0:512], AF.Exp,
                       bias=f32[:, FMS + 51:FMS + 52], scale=f32[:, FMS + 50:FMS + 51]),
                       wait=w2, inc=[(cha, 1)])
                expd[(c, "e")] = cha.n
            exp_all = cha.n
            for qt in range(4):
                for c in range(16):
                    lhsT = bf[:, PT + c * 512 + qt * 128: PT + c * 512 + qt * 128 + 128]
                    w = [(cha, exp_all)] if c == 0 else []
                    if c == 0:
                        w.append((chv, chv.n))
                    for dh in range(2):
                        rhs = bf[:, OO + c * 1024 + dh * 512: OO + c * 1024 + dh * 512 + 512]
                        rec.do("tensor", lambda t, o=self.ps[2 + dh], l=lhsT, r=rhs, st=(c == 0), e=(c == 15):
                               t.matmul(o[:, 0:512], l, r, start=st, stop=e),
                               wait=w if dh == 0 else [], inc=[(chm, 1)] if c == 15 else [])
                    rec.do("tensor", lambda t, l=lhsT, st=(c == 0), e=(c == 15):
                           t.matmul(self.ps[4][:, 0:1], l, bf[:, ONESCOL:ONESCOL + 1], start=st, stop=e),
                           inc=[(chm, 1)] if c == 15 else [])
                av_done = chm.n
                qi = qg * 4 + qt
                rec.do("vector", lambda v: v.reciprocal(f32[:, FMS + 104:FMS + 105], self.ps[4][:, 0:1]),
                       wait=[(chm, av_done)], inc=[(chv, 1)])
                for dh in range(2):
                    dcol = XB + qi * 1024 + dh * 512
                    rec.do("scalar", lambda s, d=dcol, o=self.ps[2 + dh]: s.activation(
                           bf[:, d:d + 512], o[:, 0:512], AF.Copy, scale=f32[:, FMS + 104:FMS + 105]),
                           wait=[(chv, None)], inc=[(cha, 1)] if dh == 1 else [])
                rec.do("vector", lambda v: v.engine_nop(), wait=[(cha, None)])
                rec.do("vector", lambda v: v.engine_nop(), inc=[(chv, 1)])
        if "l1attn" in self.debug:
            chd = self.ch["dma"]
            dob = self.dbg("otok_mid", [128, 16384], BF16)
            rec.do("scalar", lambda s: s.dma_start(dob[:, :], bf[:, XB:XB + 16384]),
                   wait=[(chv, None)], inc=[(chd, 16)])
            rec.do("scalar", lambda s: s.wait_ge(chd.sem, chd.n), bare=True)
        self.rec.emit_block()

    def phase_l1_out_ln(self):
        nc, rec, bf, f32 = self.nc, self.rec, self.bf, self.f32
        chm, cha, chv, chx, chw = self.ch["mm"], self.ch["act"], self.ch["dve"], self.ch["xb"], self.ch["wdma"]
        chd = self.ch["dma"]
        wt = self.inp["L1_attn_w_proj"]
        bt = self.inp["L1_attn_b_proj"]
        rec.do("gpsimd", lambda g: g.dma_start(f32[:, FMS + 96:FMS + 104],
               bass.AP(bt, 0, [[1, 128], [128, 8]]), allow_slow_non_contiguous=True), inc=[(chd, 16)])
        bias_done = chd.n
        xb_done = {}
        raw_chv = {}
        for oc in range(8):
            sb = AUX + (oc % 2) * 1024
            tb = (WSA + oc * 1024) if oc < 4 else (WSB + (oc - 4) * 1024)
            chs = self.ch["wg" + str(oc % 2)]
            w = [(chx, chx.n)] if oc < 2 else [(chx, xb_done[oc - 2][0])]
            rec.do("gpsimd", lambda g, d=sb, s=bass.AP(wt, oc * 128 * D, [[D, 128], [1, D]]):
                   g.dma_start(bf[:, d:d + 1024], s), wait=w, inc=[(chs, 16)])
            wdma_n = chs.n
            ii = bf[:, sb:sb + 1024].rearrange("p (c k) -> p c k", c=8)
            oo = bf[:, tb:tb + 1024].rearrange("p (c k) -> p c k", c=8)
            rec.do("sync", lambda s_, i=ii, o=oo, wn=wdma_n, cs=chs: (
                   s_.wait_ge(cs.sem, wn), s_.dma_start(o, i, transpose=True))[-1], inc=[(chx, 16)])
            xb_done[oc] = (chx.n, chm.n)
        tiles_ready = chx.n
        for ab in range(2):
            for r in range(2):
                for oc in range(8):
                    ppar = (ab * 16 + r * 8 + oc) % 2
                    p0 = self.ps[ppar]
                    tb = (WSA + oc * 1024) if oc < 4 else (WSB + (oc - 4) * 1024)
                    idx = ab * 16 + r * 8 + oc
                    wmm = [(chx, tiles_ready)] if idx == 0 else []
                    if idx - 2 >= 0:
                        wmm.append((chv, raw_chv.get(idx - 2, 0)))
                    for hc in range(8):
                        lhsT = bf[:, tb + hc * 128: tb + hc * 128 + 128]
                        rhs = bf[:, XB + (8 * r + hc) * 1024 + ab * 512: XB + (8 * r + hc) * 1024 + ab * 512 + 512]
                        rec.do("tensor", lambda t, o=p0, l=lhsT, rr=rhs, st=(hc == 0), e=(hc == 7):
                               t.matmul(o[:, 0:512], l, rr, start=st, stop=e),
                               wait=wmm if hc == 0 else [], inc=[(chm, 1)] if hc == 7 else [])
                    wst = [(chm, None)]
                    if idx == 0:
                        wst.append((chd, bias_done))
                    outap = bass.AP(bf, KVO + oc * T + 2 * ab * 512 + r, [[NBF, 128], [2, 512]])
                    xap = bass.AP(bf, XA + oc * T + 2 * ab * 512 + r, [[NBF, 128], [2, 512]])
                    rec.do("vector", lambda v, o=p0, oa=outap, xa_=xap, occ=oc: v.scalar_tensor_tensor(
                           oa, o[:, 0:512], f32[:, FMS + 96 + occ:FMS + 97 + occ], xa_,
                           ALU.add, ALU.add), wait=wst, inc=[(chv, 1)])
                    raw_chv[idx] = chv.n
            for th in range(2):
                self.emit_ln_group(2 * ab + th, KVO, XA, raw_chv[ab * 16 + 15])
            self.rec.emit_block()
        if "x4" in self.debug:
            dob = self.dbg("otok", [128, 16384], BF16)
            rec.do("scalar", lambda s: s.dma_start(dob[:, :], bf[:, XB:XB + 16384]),
                   wait=[(chv, None)], inc=[(chd, 16)])
            dn = self.dbg("x4", [128, G * T], BF16)
            dr = self.dbg("x4_raw", [128, G * T], BF16)
            rec.do("scalar", lambda s: s.dma_start(dn[:, :], bf[:, XA:XA + G * T]),
                   wait=[(chv, None)], inc=[(chd, 16)])
            rec.do("scalar", lambda s: s.dma_start(dr[:, :], bf[:, KVO:KVO + G * T]), inc=[(chd, 16)])
            rec.do("scalar", lambda s: s.wait_ge(chd.sem, chd.n), bare=True)
        self.rec.emit_block()

    def emit_quant(self, wt, nchunks, width, scratch, cells):
        nc, rec, bf, f32 = self.nc, self.rec, self.bf, self.f32
        chm, cha, chv, chx, chw, chd, chg = (self.ch["mm"], self.ch["act"], self.ch["dve"],
                                             self.ch["xb"], self.ch["wdma"], self.ch["dma"], self.ch["gp"])
        npieces = nchunks * (width // 1024)
        for eng in ("gpsimd", "scalar", "sync", "vector", "tensor"):
            rec.do(eng, lambda e, v=chd.n: e.wait_ge(chd.sem, v), bare=True)
            rec.do(eng, lambda e, v=chw.n: e.wait_ge(chw.sem, v), bare=True)
            rec.do(eng, lambda e, v=chx.n: e.wait_ge(chx.sem, v), bare=True)
        rec.do("gpsimd", lambda g: g.memset(f32[:, FMS + 112:FMS + 120], 0.0), inc=[(chg, 1)])
        mx_done = {}
        for pc in range(npieces):
            par = pc % 2
            col0 = par * 1024
            chs = self.ch["wg" + str(par)]
            w = [(chv, mx_done.get(par, 0))]
            if pc < 2:
                w.append((chg, None))
            src = bass.AP(wt, pc * 128 * 1024 if width == 1024 else (pc // 4) * 128 * width + (pc % 4) * 1024,
                          [[width, 128], [1, 1024]])
            rec.do("gpsimd", lambda g, c0=col0, s=src: g.dma_start(f32[:, c0:c0 + 1024], s),
                   wait=w, inc=[(chs, 16)])
            ldn = chs.n
            rec.do("vector", lambda v, c0=col0, cs=chs, ln_=ldn: v.reduce_max(f32[:, FMS + 113:FMS + 114],
                   f32[:, c0:c0 + 1024], mybir.AxisListType.X, apply_absolute_value=True),
                   wait=[(chs, ldn)])

            rec.do("vector", lambda v: v.tensor_max(f32[:, FMS + 112:FMS + 113],
                   f32[:, FMS + 112:FMS + 113], f32[:, FMS + 113:FMS + 114]),
                   wait=[(chg, None)] if pc == 0 else [], inc=[(chv, 1)])
            mx_done[par] = chv.n
        mx_all = chv.n
        QX = FMS + 200
        self.rec.emit_block()
        rec.do("vector", lambda v: v.tensor_copy(f32[:, FQB:FQB + 4], f32[:, FMS + 112:FMS + 116]),
               wait=[(chv, mx_all)], inc=[(chv, 1)])
        rec.do("gpsimd", lambda s_, v=chv.n: (s_.wait_ge(chv.sem, v),
               s_.dma_start(bass.AP(self.rtrip, 0, [[512, 1], [1, 512]]), f32[:, FQB:FQB + 4]))[-1],
               inc=[(chd, 16)])
        rec.do("gpsimd", lambda s_, dv=chd.n: (s_.wait_ge(chd.sem, dv),
               s_.dma_start(f32[:, FQV:FQV + 512],
               bass.AP(self.rtrip, 0, [[0, 128], [1, 512]])))[-1], inc=[(chd, 16)])
        dnr = chd.n
        self.rec.emit_block()
        for j in range(4):
            rec.do("vector", lambda v, jj=j: v.reduce_max(f32[:, QX + jj:QX + jj + 1],
                   f32[:, FQV:FQV + 512], mybir.AxisListType.X),
                   wait=[(chd, dnr)])
        for j in range(4):
            rec.do("vector", lambda v, jj=j: v.tensor_copy(f32[:, QX + 24 + jj:QX + 25 + jj],
                   f32[:, FMS + 61:FMS + 62]))
        rec.do("vector", lambda v: v.tensor_copy(f32[:, FQD:FQD + 512], f32[:, FQV:FQV + 512]))
        rec.do("vector", lambda v, cc=cells: v.tensor_mul(f32[:, cc:cc + 4], f32[:, QX:QX + 4],
               f32[:, QX + 24:QX + 28]))
        rec.do("vector", lambda v: v.tensor_copy(f32[:, FQD:FQD + 512], f32[:, FQV:FQV + 512]))
        self.rec.emit_block()
        rec.do("vector", lambda v, cc=cells: v.reciprocal(f32[:, cc + 4:cc + 8], f32[:, cc:cc + 4]))
        rec.do("vector", lambda v: v.tensor_copy(f32[:, FQD:FQD + 512], f32[:, FQV:FQV + 512]))
        self.rec.emit_block()
        rec.do("vector", lambda v, cc=cells: v.tensor_scalar_mul(f32[:, cc + 8:cc + 12],
               f32[:, cc + 4:cc + 8], 2.0), inc=[(chv, 1)])
        self.rec.emit_block()
        if "qearly" in self.debug:
            dcf = self.dbg("qcf_early", [128, 8], F32)
            rec.do("vector", lambda v: v.tensor_copy(f32[:, QX + 8:QX + 12], f32[:, QX:QX + 4]), inc=[(chv, 1)])
            rec.do("scalar", lambda s_, cc=cells, v=chv.n: (s_.wait_ge(chv.sem, v),
                   s_.dma_start(dcf[:, 0:4], f32[:, cc:cc + 4]))[-1], inc=[(chd, 16)])
            rec.do("scalar", lambda s_: s_.dma_start(dcf[:, 4:8], f32[:, QX + 32:QX + 36]), inc=[(chd, 16)])
        if "qearly" in self.debug:
            dqe = self.dbg("qearly", [128, 90], F32)
            rec.do("scalar", lambda s_: s_.dma_start(dqe[:, 0:56], f32[:, FMS + 200:FMS + 256]),
                   wait=[(chv, None)], inc=[(chd, 16)])
            dbc = self.dbg("qbc", [128, 130], F32)
            rec.do("vector", lambda v: v.tensor_copy(f32[:, FQV:FQV + 128], self.ps[6][:, 0:128]),
                   inc=[(chv, 1)])
            rec.do("scalar", lambda s_: s_.dma_start(dbc[:, 0:128], f32[:, FQV:FQV + 128]),
                   wait=[(chv, None)], inc=[(chd, 16)])
            rec.do("scalar", lambda s_: s_.dma_start(dbc[:, 128:129], f32[:, FQA:FQA + 1], allow_slow_non_contiguous=True), inc=[(chd, 16)])
            rec.do("scalar", lambda s_: s_.dma_start(dbc[:, 129:130], f32[:, FMS + 112:FMS + 113], allow_slow_non_contiguous=True), inc=[(chd, 16)])
        cells_done = chv.n
        buffree = cells_done
        for pc in range(npieces):
            w = [(chv, cells_done if pc == 0 else buffree)]
            src = bass.AP(wt, pc * 128 * 1024 if width == 1024 else (pc // 4) * 128 * width + (pc % 4) * 1024,
                          [[width, 128], [1, 1024]])
            rec.do("gpsimd", lambda g, s=src: g.dma_start(f32[:, 0:1024], s),
                   wait=w, inc=[(chd, 16)])
            dn = chd.n
            sb = WSB + (pc % 2) * 1024
            for hh in range(2):
                c0 = hh * 512
                wch = [(chd, dn), (chx, chx.n)] if hh == 0 else []
                rec.do("vector", lambda v, cc=c0, cell=cells: v.tensor_scalar_mul(
                       f32[:, FQV:FQV + 512], f32[:, cc:cc + 512], f32[:, cell + 4:cell + 5]),
                       wait=wch, inc=[(chv, 1)] if hh == 1 else [])
                if hh == 1:
                    buffree = chv.n
                rec.do("vector", lambda v: v.tensor_copy(self.i32[:, 0:512], f32[:, FQV:FQV + 512]))
                rec.do("vector", lambda v: v.tensor_copy(f32[:, FQC:FQC + 512], self.i32[:, 0:512]))
                rec.do("vector", lambda v: v.tensor_sub(f32[:, FQD:FQD + 512], f32[:, FQV:FQV + 512],
                       f32[:, FQC:FQC + 512]))
                rec.do("vector", lambda v: v.scalar_tensor_tensor(f32[:, FST:FST + 512],
                       f32[:, FQD:FQD + 512], 0.5, f32[:, FQC:FQC + 512], ALU.is_ge, ALU.add))
                rec.do("vector", lambda v: v.tensor_single_scalar(f32[:, FQV:FQV + 512],
                       f32[:, FQD:FQD + 512], -0.5, ALU.is_le))
                rec.do("vector", lambda v, s_=sb, hh_=hh: v.tensor_sub(bf[:, s_ + hh_ * 512:s_ + hh_ * 512 + 512],
                       f32[:, FST:FST + 512], f32[:, FQV:FQV + 512]), inc=[(chv, 1)])
            qv = chv.n
            ii = bf[:, sb:sb + 1024].rearrange("p (c k) -> p c k", c=8)
            oo = bf[:, WSA + (pc % 2) * 1024:WSA + (pc % 2) * 1024 + 1024].rearrange("p (c k) -> p c k", c=8)
            rec.do("sync", lambda s_, i=ii, o=oo, v=qv, wv=chw.n, dv=chd.n: (s_.wait_ge(chv.sem, v),
                   s_.wait_ge(chw.sem, wv), s_.wait_ge(chd.sem, dv),
                   s_.dma_start(o, i, transpose=True))[-1], inc=[(chx, 16)])
            xn = chx.n
            if width == 1024:
                dst = bass.AP(scratch, pc * 128 * 1024, [[1024, 128], [1, 1024]])
            else:
                dst = bass.AP(scratch, (pc // 4) * 128 * 4096 + (pc % 4) * 1024, [[4096, 128], [1, 1024]])
            rec.do("scalar", lambda s_, d=dst, src2=bf[:, WSA + (pc % 2) * 1024:WSA + (pc % 2) * 1024 + 1024], xnn=xn:
                   (s_.wait_ge(chx.sem, xnn), s_.dma_start(d, src2))[-1], inc=[(chw, 16)])
            if pc == 16 and "qearly" in self.debug:
                dmid = self.dbg("qmid", [128, 16], F32)
                rec.do("scalar", lambda s_, cc=cells: s_.dma_start(dmid[:, 0:12], f32[:, cc:cc + 12]),
                       inc=[(chd, 16)])

    def phase_l1_ffn(self):
        nc, rec, bf, f32 = self.nc, self.rec, self.bf, self.f32
        chm, cha, chv, chx, chw, chd, chg = (self.ch["mm"], self.ch["act"], self.ch["dve"],
                                             self.ch["xb"], self.ch["wdma"], self.ch["dma"], self.ch["gp"])
        self.emit_quant(self.inp["L1_ffn_W_fc"], 32, 1024, self.wfcT, FMS + 232)
        self.rec.emit_block()
        self.emit_quant(self.inp["L1_ffn_W_proj"], 8, 4096, self.wprT, FMS + 244)
        self.rec.emit_block()
        AFC = AUX + 512
        APR = AUX + 640
        BST = AUX
        L1T = AUX + 1152
        L2T = AUX + 1664
        rec.do("gpsimd", lambda g: g.dma_start(
               bf[:, AFC:AFC + 128].rearrange("p (b f) -> p b f", b=8),
               bass.AP(self.inp["AfcT"], 0, [[16, 128], [2048, 8], [1, 16]])), inc=[(chw, 16)])
        rec.do("gpsimd", lambda g: g.dma_start(
               bf[:, APR:APR + 512].rearrange("p (b f) -> p b f", b=32),
               bass.AP(self.inp["AprT"], 0, [[16, 128], [2048, 32], [1, 16]])), inc=[(chw, 16)])
        rec.do("gpsimd", lambda g: g.dma_start(f32[:, FMS + 128:FMS + 160],
               bass.AP(self.inp["L1_ffn_b_fc"], 0, [[1, 128], [128, 32]]),
               allow_slow_non_contiguous=True), inc=[(chw, 16)])
        rec.do("gpsimd", lambda g: g.dma_start(f32[:, FMS + 160:FMS + 168],
               bass.AP(self.inp["L1_ffn_b_proj"], 0, [[1, 128], [128, 8]]),
               allow_slow_non_contiguous=True), inc=[(chw, 16)])
        consts_done = chw.n
        raw_chv = 0
        h2_marker = {}
        for tg in range(4):
            for ic in range(8):
                lhsT = bf[:, AFC + ic * 16: AFC + ic * 16 + 16]
                rhs = bf[:, XA + ic * T + tg * 512: XA + ic * T + tg * 512 + 512]
                w = [(chw, consts_done)] if ic == 0 else []
                if ic == 0:
                    w.append((cha, cha.n))
                rec.do("tensor", lambda t, l=lhsT, r=rhs, st=(ic == 0), e=(ic == 7):
                       t.matmul(self.ps[4][0:16, 0:512], l, r, start=st, stop=e),
                       wait=w, inc=[(chm, 1)] if ic == 7 else [])
            rec.do("scalar", lambda s: s.activation(bf[0:16, L1T:L1T + 512], self.ps[4][0:16, 0:512],
                   AF.Copy, scale=f32[0:16, FMS + 240:FMS + 241]), wait=[(chm, None)], inc=[(cha, 1)])
            l1_done = cha.n
            evac_done = {}
            for oc in range(32):
                par = oc % 2
                wsb = WSA + par * 1024
                chs = self.ch["wg" + str(par)]
                chb = self.ch["bt" + str(par)]
                w = [(chm, chm.n)]
                if oc >= 2:
                    w.append((cha, evac_done[oc - 2]))
                rec.do("gpsimd", lambda g, d=wsb, o=oc: g.dma_start(bf[:, d:d + 1024],
                       bass.AP(self.wfcT, o * 128 * 1024, [[1024, 128], [1, 1024]])),
                       wait=w, inc=[(chs, 16)])
                rec.do("gpsimd", lambda g, o=oc, p2=par: g.dma_start(bf[0:16, BST + p2 * 128:BST + p2 * 128 + 128],
                       bass.AP(self.inp["BfcT"], o * 128, [[H1, 16], [1, 128]])), inc=[(chb, 16)])
                wns, bns = chs.n, chb.n
                p0 = self.ps[par]
                for ic in range(8):
                    lhsT = bf[:, wsb + ic * 128: wsb + ic * 128 + 128]
                    rhs = bf[:, XA + ic * T + tg * 512: XA + ic * T + tg * 512 + 512]
                    wm = [(chs, wns), (cha, l1_done)] if ic == 0 else []
                    rec.do("tensor", lambda t, o=p0, l=lhsT, r=rhs, st=(ic == 0):
                           t.matmul(o[:, 0:512], l, r, start=st, stop=False),
                           wait=wm, inc=[])
                rec.do("tensor", lambda t, o=p0, p2=par: t.matmul(o[:, 0:512],
                       bf[0:16, BST + p2 * 128:BST + p2 * 128 + 128], bf[0:16, L1T:L1T + 512],
                       start=False, stop=True), wait=[(chb, bns)], inc=[(chm, 1)])
                rec.do("scalar", lambda s, o=p0, oc_=oc: s.activation(
                       bf[:, QQ + oc_ * 512: QQ + oc_ * 512 + 512], o[:, 0:512], AF.Gelu,
                       scale=f32[:, FMS + 232:FMS + 233], bias=f32[:, FMS + 128 + oc_:FMS + 129 + oc_]),
                       wait=[(chm, None)], inc=[(cha, 1)])
                evac_done[oc] = cha.n
            h2_done = cha.n
            for hc in range(32):
                lhsT = bf[:, APR + hc * 16: APR + hc * 16 + 16]
                rhs = bf[:, QQ + hc * 512: QQ + hc * 512 + 512]
                w = [(cha, h2_done)] if hc == 0 else []
                rec.do("tensor", lambda t, l=lhsT, r=rhs, st=(hc == 0), e=(hc == 31):
                       t.matmul(self.ps[4][0:16, 0:512], l, r, start=st, stop=e),
                       wait=w, inc=[(chm, 1)] if hc == 31 else [])
            rec.do("scalar", lambda s: s.activation(bf[0:16, L2T:L2T + 512], self.ps[4][0:16, 0:512],
                   AF.Copy, scale=f32[0:16, FMS + 252:FMS + 253]), wait=[(chm, None)], inc=[(cha, 1)])
            l2_done = cha.n
            prev_evac = {}
            subdone = {}
            for oc in range(8):
                p0 = self.ps[oc % 2]
                for sub in range(4):
                    gsub = oc * 4 + sub
                    par = gsub % 2
                    wsb = WSA + 2048 + par * 1024
                    chs = self.ch["wg" + str(par)]
                    w = [(chm, subdone.get(gsub - 2) if gsub >= 2 else chm.n)]
                    if tg == 0 and gsub == 0:
                        w.append((chw, chw.n))
                    rec.do("gpsimd", lambda g, d=wsb, o=oc, s2=sub: g.dma_start(bf[:, d:d + 1024],
                           bass.AP(self.wprT, o * 128 * 4096 + s2 * 1024, [[4096, 128], [1, 1024]])),
                           wait=w, inc=[(chs, 16)])
                    wn = chs.n
                    for hin in range(8):
                        hc = sub * 8 + hin
                        lhsT = bf[:, wsb + hin * 128: wsb + hin * 128 + 128]
                        rhs = bf[:, QQ + hc * 512: QQ + hc * 512 + 512]
                        wm = [(chs, wn), (cha, l2_done)] if hin == 0 else []
                        if oc >= 2 and hin == 0 and sub == 0:
                            wm.append((chv, prev_evac[oc - 2]))
                        rec.do("tensor", lambda t, o=p0, l=lhsT, r=rhs, st=(hc == 0):
                               t.matmul(o[:, 0:512], l, r, start=st, stop=False),
                               wait=wm, inc=[(chm, 1)] if hin == 7 else [])
                    subdone[gsub] = chm.n
                chb = self.ch["bt0"]
                rec.do("gpsimd", lambda g, o=oc: g.dma_start(bf[0:16, BST + 256:BST + 384],
                       bass.AP(self.inp["BprT"], o * 128, [[D, 16], [1, 128]])),
                       wait=[(chm, chm.n)], inc=[(chb, 16)])
                bw = chb.n
                rec.do("tensor", lambda t, o=p0: t.matmul(o[:, 0:512],
                       bf[0:16, BST + 256:BST + 384],
                       bf[0:16, L2T:L2T + 512], start=False, stop=True),
                       wait=[(chb, bw)], inc=[(chm, 1)])
                t1 = A_PT + 1536
                rec.do("vector", lambda v, o=p0, oc_=oc: v.tensor_scalar(
                       bf[:, t1:t1 + 512] if False else bf[:, A_PT + 1536:A_PT + 2048], o[:, 0:512],
                       f32[:, FMS + 244:FMS + 245], f32[:, FMS + 160 + oc_:FMS + 161 + oc_], ALU.mult, ALU.add),
                       wait=[(chm, None)])
                rcol = XB + oc * T + tg * 512
                xcol = XA + oc * T + tg * 512
                rec.do("vector", lambda v, rc=rcol, xc=xcol: v.tensor_add(
                       bf[:, rc:rc + 512], bf[:, A_PT + 1536:A_PT + 2048], bf[:, xc:xc + 512]),
                       inc=[(chv, 1)])
                prev_evac[oc] = chv.n
            raw_chv = chv.n
            self.emit_ln_group(tg, XB, XA, raw_chv)
        if "x5" in self.debug:
            chd2 = self.ch["dma"]
            dn = self.dbg("x5", [128, G * T], BF16)
            dr = self.dbg("x5_raw", [128, G * T], BF16)
            dh2 = self.dbg("h2", [128, 16384], BF16)
            dc_ = self.dbg("cells", [128, 32], F32)
            rec.do("scalar", lambda s: s.dma_start(dn[:, :], bf[:, XA:XA + G * T]),
                   wait=[(chv, None)], inc=[(chd2, 16)])
            rec.do("scalar", lambda s: s.dma_start(dr[:, :], bf[:, XB:XB + G * T]), inc=[(chd2, 16)])
            rec.do("scalar", lambda s: s.dma_start(dh2[:, :], bf[:, QQ:QQ + 16384]), inc=[(chd2, 16)])
            rec.do("scalar", lambda s: s.dma_start(dc_[:, 0:24], f32[:, FMS + 232:FMS + 256]), inc=[(chd2, 16)])

            rec.do("scalar", lambda s: s.wait_ge(chd2.sem, chd2.n), bare=True)
        self.rec.emit_block()

    def phase_output(self):
        nc, rec, bf = self.nc, self.rec, self.bf
        chx, chd = self.ch["xb"], self.ch["dma"]
        for g in range(8):
            ii = bf[:, XA + g * T: XA + (g + 1) * T].rearrange("p (c k) -> p c k", c=16)
            oo = bass.AP(bf, XB + g * 128, [[NBF, 128], [1024, 16], [1, 128]])
            rec.do("sync", lambda s, i=ii, o=oo: s.dma_start(o, i, transpose=True), inc=[(chx, 16)])
        xall = chx.n
        for tc in range(16):
            src = bf[:, XB + tc * 1024: XB + (tc + 1) * 1024]
            dst = bass.AP(self.out, tc * 128 * D, [[D, 128], [1, D]])
            rec.do("gpsimd", lambda g, d=dst, s=src: g.dma_start(d, s),
                   wait=[(chx, xall)] if tc == 0 else [], inc=[(chd, 16)])
        rec.do("gpsimd", lambda g: g.wait_ge(chd.sem, chd.n), bare=True)
        self.rec.emit_block()




def kernel(**inputs):
    n = 8
    x_full = np.ascontiguousarray(inputs["L0_attn_x"])
    b = Builder(stage="all", debug=())
    b.build()
    consts = host_constants()
    lora = layout_lora(inputs)
    weight_names = [k for k in inputs.keys() if k != "L0_attn_x"]
    in_maps = []
    for c in range(n):
        m = {"x": np.ascontiguousarray(x_full[c])}
        for k in weight_names:
            m[k] = inputs[k]
        m.update(consts)
        m.update(lora)
        in_maps.append(m)
    res = run_bass_kernel_spmd(b.nc, in_maps, core_ids=list(range(n)))
    y = np.stack([np.asarray(res.results[c]["y"]) for c in range(n)], axis=0)
    return y.astype(np.float32)
